# version 07
# speedup vs baseline: 2.0937x; 1.6853x over previous
import numpy as np
import ml_dtypes
import concourse.bass as bass
import concourse.mybir as mybir
from concourse.bass import IndirectOffsetOnAxis
from concourse.bass_utils import run_bass_kernel_spmd

F32 = mybir.dt.float32
BF16 = mybir.dt.bfloat16
U32 = mybir.dt.uint32
AF = mybir.ActivationFunctionType
ALU = mybir.AluOpType
AX = mybir.AxisListType

D = 1024
NB = 16
T = 2048
NH = 16
DH = 64
NS = 4
WIN = 512
QR0 = 1532
NQ = 516
NCTX = 516
EPS = 1e-5

ENGS = ("pe", "act", "dve", "pool", "sp")


class Trk:

    def __init__(self, nc, es):
        self.nc = nc
        self.sem = {e: es.enter_context(nc.semaphore(f"sem_{e}")) for e in ENGS}
        self.ops = []
        self.fence = 0
        self.regcache = {}

    def _rec(self, eng, fn, deps, inc, est):
        oid = len(self.ops)
        ds = set()
        for d in deps:
            if d is None:
                continue
            if isinstance(d, int):
                ds.add(d)
            else:
                ds.update(d)
        self.ops.append({"id": oid, "eng": eng, "fn": fn, "inc": inc,
                         "deps": ds, "est": est, "fence": self.fence})
        return oid

    def op(self, eng, fn, deps=(), est=400):
        return self._rec(eng, fn, deps, 1, est)

    def dma(self, eng, fn, deps=(), est=2000):
        return self._rec(eng, fn, deps, 16, est)

    def wait(self, eng, deps):
        self._rec(eng, None, deps, 0, 0)

    def barrier(self):
        self.fence += 1

    def replay(self, block):
        order = {e: [] for e in ENGS}
        comp = {}
        nf = self.fence + 1
        groups = [[] for _ in range(nf)]
        for o in self.ops:
            groups[o["fence"]].append(o)
        eng_time = {e: 0.0 for e in ENGS}
        for g in groups:
            per_eng = {e: [o for o in g if o["eng"] == e] for e in ENGS}
            idx = {e: 0 for e in ENGS}
            ndone = 0
            total = len(g)
            gstart = max(eng_time.values())
            for e in ENGS:
                eng_time[e] = max(eng_time[e], gstart)
            scheduled = set()
            while ndone < total:
                best = None
                best_t = None
                for e in ENGS:
                    lst = per_eng[e]
                    i = idx[e]
                    for k in range(i, min(i + 24, len(lst))):
                        o = lst[k]
                        if o["id"] in scheduled:
                            continue
                        if any((d >= g[0]["id"] and d not in scheduled and
                                self.ops[d]["fence"] == o["fence"])
                               for d in o["deps"]):
                            continue
                        ready = eng_time[e]
                        for d in o["deps"]:
                            ready = max(ready, comp.get(d, 0.0))
                        t = ready
                        if best_t is None or t < best_t:
                            best, best_t = o, t
                        break
                if best is None:
                    for o in g:
                        if o["id"] not in scheduled:
                            best = o
                            best_t = eng_time[o["eng"]]
                            break
                e = best["eng"]
                scheduled.add(best["id"])
                comp[best["id"]] = best_t + best["est"]
                eng_time[e] = best_t + best["est"]
                order[e].append(best)
                lst = per_eng[e]
                while idx[e] < len(lst) and lst[idx[e]]["id"] in scheduled:
                    idx[e] += 1
                ndone += 1
        cross_needed = set()
        for o in self.ops:
            for d in o["deps"]:
                if self.ops[d]["eng"] == "pe" and o["eng"] != "pe":
                    cross_needed.add(d)
        last_pe_of_fence = {}
        for o in order["pe"]:
            last_pe_of_fence[o["fence"]] = o["id"]
        keep = cross_needed | set(last_pe_of_fence.values())
        for o in self.ops:
            if o["eng"] == "pe" and o["inc"] == 1 and o["id"] not in keep:
                o["inc"] = 0
        pos_count = {}
        run = {e: 0 for e in ENGS}
        fence_end = [dict() for _ in range(nf)]
        for e in ENGS:
            for o in order[e]:
                run[e] += o["inc"]
                pos_count[o["id"]] = run[e]
                fence_end[o["fence"]][e] = run[e]
        for f in range(1, nf):
            for e in ENGS:
                fence_end[f][e] = max(fence_end[f].get(e, 0), fence_end[f - 1].get(e, 0))
        for f in range(nf):
            for e in ENGS:
                fence_end[f].setdefault(e, 0)

        sem_map = self.sem
        ops_all = self.ops

        def mk(eng_name):
            def body(h):
                lw = {s: 0 for s in ENGS}
                cur_f = 0
                for o in order[eng_name]:
                    if o["fence"] > cur_f:
                        fb = fence_end[o["fence"] - 1]
                        for s in ENGS:
                            v = fb[s]
                            if v > lw[s]:
                                h.wait_ge(sem_map[s], v)
                                lw[s] = v
                        cur_f = o["fence"]
                    need = {}
                    for d in o["deps"]:
                        de = ops_all[d]["eng"]
                        if de == "pe" and eng_name == "pe":
                            continue
                        v = pos_count.get(d, 0)
                        if v > need.get(de, 0):
                            need[de] = v
                    for s in ENGS:
                        v = need.get(s, 0)
                        if v > lw[s]:
                            h.wait_ge(sem_map[s], v)
                            lw[s] = v
                    if o["fn"] is not None:
                        inst = o["fn"](h)
                        inst.then_inc(sem_map[eng_name], o["inc"])
            return body

        block.tensor(mk("pe"))
        block.scalar(mk("act"))
        block.vector(mk("dve"))
        block.gpsimd(mk("pool"))
        block.sync(mk("sp"))


def ceil_div(a, b):
    return (a + b - 1) // b


def bcreg(trk, h, val, eng="pool"):
    key = (eng, val)
    r = trk.regcache.get(key)
    if r is None:
        r = h.to_reg(val)
        trk.regcache[key] = r
    return r


class Rsrc:
    __slots__ = ("w", "r", "name")

    def __init__(self, name=""):
        self.w = None
        self.r = []
        self.name = name


class Emit:
    def __init__(self, trk):
        self.trk = trk

    def do(self, eng, fn, reads=(), writes=(), dma=False, est=None):
        deps = []
        for r in reads:
            if r.w is not None:
                deps.append(r.w)
        for w in writes:
            if w.w is not None:
                deps.append(w.w)
            deps.extend(w.r)
        if dma:
            tok = self.trk.dma(eng, fn, deps, est=est if est else 2000)
        else:
            tok = self.trk.op(eng, fn, deps, est=est if est else 400)
        for r in reads:
            r.r.append(tok)
        for w in writes:
            w.w = tok
            w.r = []
        return tok



L0_LN_TILES = [(0, 4), (1532, 128), (1660, 128), (1788, 128), (1916, 128), (2044, 4)]
L0_LNT_COLS = [(0, 4), (4, 128), (132, 128), (260, 128), (388, 128), (516, 4)]
L0_QT = [(0, 128), (128, 128), (256, 128), (384, 128), (512, 4)]
L0_CT = [(0, 4), (4, 128), (132, 128), (260, 128), (388, 128)]
L0_CT_XLNT = [(0, 4), (8, 128), (136, 128), (264, 128), (392, 128)]


def alloc_l0_attn(nc, es):
    sb = {}
    ps = {}
    sb["xlnT"] = es.enter_context(nc.sbuf_tensor("xlnT", [128, 8, 520], BF16))
    sb["qf"] = es.enter_context(nc.sbuf_tensor("qf", [128, 8, 516], BF16))
    sb["kf"] = es.enter_context(nc.sbuf_tensor("kf", [128, 8, 516], BF16))
    sb["vt"] = es.enter_context(nc.sbuf_tensor("vt", [128, 5, 1024], BF16))
    sb["ao"] = es.enter_context(nc.sbuf_tensor("ao", [128, 8, 516], BF16))
    for w in ("wq", "wk", "wv", "wo"):
        sb[w] = es.enter_context(nc.sbuf_tensor(w + "_sb", [128, 8, 1024], BF16))
    sb["xt"] = es.enter_context(nc.sbuf_tensor("xt", [128, 1024], F32))
    sb["xsq"] = es.enter_context(nc.sbuf_tensor("xsq", [128, 1024], F32))
    sb["xlnb"] = es.enter_context(nc.sbuf_tensor("xlnb", [128, 1024], BF16))
    sb["st"] = es.enter_context(nc.sbuf_tensor("st", [128, 8], F32))
    sb["lg"] = es.enter_context(nc.sbuf_tensor("lg", [128, 516], F32))
    sb["esc"] = es.enter_context(nc.sbuf_tensor("esc", [128, 516], BF16))
    sb["dg"] = es.enter_context(nc.sbuf_tensor("dg", [128, 128], BF16))
    sb["wT"] = es.enter_context(nc.sbuf_tensor("wT", [128, 5 * 128], BF16))
    sb["xr"] = es.enter_context(nc.sbuf_tensor("xr", [128, 1024], F32))
    sb["xo"] = es.enter_context(nc.sbuf_tensor("xo", [128, 1024], F32))
    sb["ident"] = es.enter_context(nc.sbuf_tensor("ident", [128, 128], BF16))
    sb["konst"] = es.enter_context(nc.sbuf_tensor("konst", [128, 4], F32))
    sb["mtri"] = es.enter_context(nc.sbuf_tensor("mtri", [128, 128], F32))
    ps["t"] = es.enter_context(nc.psum_tensor("ps_t", [128, 128], BF16))
    ps["a"] = es.enter_context(nc.psum_tensor("ps_a", [128, 512], F32))
    ps["b"] = es.enter_context(nc.psum_tensor("ps_b", [128, 512], F32))
    ps["s"] = es.enter_context(nc.psum_tensor("ps_s", [128, 4], F32))
    ps["h"] = es.enter_context(nc.psum_tensor("ps_h", [64, 128], F32))
    ps["t2"] = es.enter_context(nc.psum_tensor("ps_t2", [128, 128], F32))
    return sb, ps


def l0_attn_consts(nc):
    ident = np.eye(128, dtype=np.float32)
    tri = np.where(np.arange(128)[None, :] <= np.arange(128)[:, None], 0.0, -1e30).astype(np.float32)
    h_ident = nc.inline_tensor(ident.astype(ml_dtypes.bfloat16), name="c_ident")
    h_tri = nc.inline_tensor(tri, name="c_tri")
    kv = np.broadcast_to(np.array([EPS, -1.0, 0.125, 0.0], dtype=np.float32), (128, 4)).copy()
    h_konst = nc.inline_tensor(kv, name="c_konst")
    return h_ident, h_tri, h_konst


def emit_l0_attn(nc, em, sbh, psh, rs, x_in, xq_out, w_aps, nb, h_ident, h_tri, h_konst, dbg=None, out_base=0):
    sb = sbh
    ps = psh

    em.do("sp", lambda h: h.dma_start(out=sb["ident"][:], in_=h_ident.ap()), writes=[rs["ident"]], dma=True)
    em.do("sp", lambda h: h.dma_start(out=sb["mtri"][:], in_=h_tri.ap()), writes=[rs["mtri"]], dma=True)
    em.do("sp", lambda h: h.dma_start(out=sb["konst"][:], in_=h_konst.ap()), writes=[rs["konst"]], dma=True)
    for w in ("wq", "wk", "wv", "wo"):
        src = w_aps[w].rearrange("(k p) n -> p k n", p=128)
        em.do("pool", lambda h, w=w, src=src: h.dma_start(out=sb[w][:, :, :], in_=src),
              writes=[rs[w]], dma=True)

    for b in range(nb):
        for ti, (r0, p) in enumerate(L0_LN_TILES):
            em.do("sp", lambda h, b=b, r0=r0, p=p: h.dma_start(out=sb["xt"][0:p, :], in_=x_in[b, r0:r0 + p, :]),
                  writes=[rs["xt"]], dma=True)
            em.do("act", lambda h, p=p: h.activation(out=sb["xsq"][0:p, :], in_=sb["xt"][0:p, :],
                                                     func=AF.Square, accum_out=sb["st"][0:p, 0:1]),
                  reads=[rs["xt"]], writes=[rs["xsq"], rs["st"]])
            em.do("dve", lambda h, p=p: h.reduce_sum(sb["st"][0:p, 1:2], sb["xt"][0:p, :], AX.X, negate=True),
                  reads=[rs["xt"]], writes=[rs["st"]])
            em.do("dve", lambda h, p=p: h.tensor_scalar(sb["st"][0:p, 2:3], sb["st"][0:p, 1:2], 1.0 / D, None, ALU.mult),
                  reads=[rs["st"]], writes=[rs["st"]])
            em.do("dve", lambda h, p=p: h.tensor_scalar(sb["st"][0:p, 3:4], sb["st"][0:p, 0:1], 1.0 / D, None, ALU.mult),
                  reads=[rs["st"]], writes=[rs["st"]])
            em.do("dve", lambda h, p=p: h.scalar_tensor_tensor(
                out=sb["st"][0:p, 4:5], in0=sb["st"][0:p, 2:3], scalar=sb["st"][0:p, 2:3],
                in1=sb["st"][0:p, 3:4], op0=ALU.mult, op1=ALU.subtract),
                reads=[rs["st"]], writes=[rs["st"]])
            em.do("act", lambda h, p=p: h.activation(out=sb["st"][0:p, 5:6], in_=sb["st"][0:p, 4:5],
                                                     func=AF.Sqrt, bias=sb["konst"][0:p, 0:1],
                                                     scale=sb["konst"][0:p, 1:2]),
                  reads=[rs["st"], rs["konst"]], writes=[rs["st"]])
            em.do("dve", lambda h, p=p: h.reciprocal(sb["st"][0:p, 6:7], sb["st"][0:p, 5:6]),
                  reads=[rs["st"]], writes=[rs["st"]])
            em.do("dve", lambda h, p=p: h.tensor_scalar(sb["xlnb"][0:p, :], sb["xt"][0:p, :],
                                                        sb["st"][0:p, 2:3], sb["st"][0:p, 6:7],
                                                        ALU.add, ALU.mult),
                  reads=[rs["xt"], rs["st"]], writes=[rs["xlnb"]])
            c0, cn = L0_LNT_COLS[ti]
            for ki in range(8):
                em.do("pe", lambda h, ki=ki, p=p: h.transpose(ps["t"][0:128, 0:p],
                                                              sb["xlnb"][0:p, ki * 128:(ki + 1) * 128],
                                                              sb["ident"][0:p, 0:p]),
                      reads=[rs["xlnb"], rs["ident"]], writes=[rs["ps_t"]])
                eng = "dve" if ki % 2 == 0 else "act"
                fn = (lambda h, ki=ki, c0=c0, p=p: h.tensor_copy(sb["xlnT"][:, ki, c0:c0 + p], ps["t"][0:128, 0:p])) \
                    if eng == "dve" else \
                     (lambda h, ki=ki, c0=c0, p=p: h.copy(out=sb["xlnT"][:, ki, c0:c0 + p], in_=ps["t"][0:128, 0:p]))
                em.do(eng, fn, reads=[rs["ps_t"]], writes=[rs["xlnT"]])

        for mo in range(8):
            for ki in range(8):
                st_, sp_ = (ki == 0), (ki == 7)
                em.do("pe", lambda h, mo=mo, ki=ki, st_=st_, sp_=sp_: h.matmul(
                    ps["a"][:, 0:512], sb["wq"][:, ki, mo * 128:(mo + 1) * 128],
                    sb["xlnT"][:, ki, 4:516], start=st_, stop=sp_),
                    reads=[rs["wq"], rs["xlnT"]], writes=[rs["ps_a"]])
                em.do("pe", lambda h, mo=mo, ki=ki, st_=st_, sp_=sp_: h.matmul(
                    ps["s"][:, 0:4], sb["wq"][:, ki, mo * 128:(mo + 1) * 128],
                    sb["xlnT"][:, ki, 516:520], start=st_, stop=sp_),
                    reads=[rs["wq"], rs["xlnT"]], writes=[rs["ps_s"]])
            em.do("act", lambda h, mo=mo: h.activation(out=sb["qf"][:, mo, 0:512], in_=ps["a"][:, 0:512],
                                                       func=AF.Copy, scale=sb["konst"][:, 2:3]),
                  reads=[rs["ps_a"], rs["konst"]], writes=[rs["qf"]])
            em.do("act", lambda h, mo=mo: h.activation(out=sb["qf"][:, mo, 512:516], in_=ps["s"][:, 0:4],
                                                       func=AF.Copy, scale=sb["konst"][:, 2:3]),
                  reads=[rs["ps_s"], rs["konst"]], writes=[rs["qf"]])
        for mo in range(8):
            for ki in range(8):
                st_, sp_ = (ki == 0), (ki == 7)
                em.do("pe", lambda h, mo=mo, ki=ki, st_=st_, sp_=sp_: h.matmul(
                    ps["a"][:, 0:512], sb["wk"][:, ki, mo * 128:(mo + 1) * 128],
                    sb["xlnT"][:, ki, 8:520], start=st_, stop=sp_),
                    reads=[rs["wk"], rs["xlnT"]], writes=[rs["ps_a"]])
                em.do("pe", lambda h, mo=mo, ki=ki, st_=st_, sp_=sp_: h.matmul(
                    ps["s"][:, 0:4], sb["wk"][:, ki, mo * 128:(mo + 1) * 128],
                    sb["xlnT"][:, ki, 0:4], start=st_, stop=sp_),
                    reads=[rs["wk"], rs["xlnT"]], writes=[rs["ps_s"]])
            em.do("dve", lambda h, mo=mo: h.tensor_copy(sb["kf"][:, mo, 4:516], ps["a"][:, 0:512]),
                  reads=[rs["ps_a"]], writes=[rs["kf"]])
            em.do("dve", lambda h, mo=mo: h.tensor_copy(sb["kf"][:, mo, 0:4], ps["s"][:, 0:4]),
                  reads=[rs["ps_s"]], writes=[rs["kf"]])
        for cti, (c0, ck) in enumerate(L0_CT_XLNT):
            for half in range(2):
                pstgt = ps["a"] if half == 0 else ps["b"]
                psname = "ps_a" if half == 0 else "ps_b"
                for ki in range(8):
                    st_, sp_ = (ki == 0), (ki == 7)
                    em.do("pe", lambda h, cti=cti, c0=c0, ck=ck, half=half, ki=ki, st_=st_, sp_=sp_, pstgt=pstgt: h.matmul(
                        pstgt[0:ck, 0:512], sb["xlnT"][:, ki, c0:c0 + ck],
                        sb["wv"][:, ki, half * 512:(half + 1) * 512], start=st_, stop=sp_),
                        reads=[rs["wv"], rs["xlnT"]], writes=[rs[psname]])
                eng = "dve" if half == 0 else "act"
                fn = (lambda h, cti=cti, ck=ck, half=half, pstgt=pstgt: h.tensor_copy(
                    sb["vt"][0:ck, cti, half * 512:(half + 1) * 512], pstgt[0:ck, 0:512])) if eng == "dve" else \
                     (lambda h, cti=cti, ck=ck, half=half, pstgt=pstgt: h.copy(
                    out=sb["vt"][0:ck, cti, half * 512:(half + 1) * 512], in_=pstgt[0:ck, 0:512]))
                em.do(eng, fn, reads=[rs[psname]], writes=[rs["vt"]])

        if dbg and b == 0:
            for nm, buf in [("d_xlnT", "xlnT"), ("d_qf", "qf"), ("d_kf", "kf"), ("d_vt", "vt")]:
                em.do("sp", lambda h, nm=nm, buf=buf: h.dma_start(out=dbg[nm], in_=sb[buf][:, :, :]),
                      reads=[rs[buf]], writes=[rs["out_0"]], dma=True)
        for hd in range(16):
            mo, po = hd // 2, (hd % 2) * 64
            for qt, (qoff, qsz) in enumerate(L0_QT):
                wq_ = qoff + qsz
                em.do("pe", lambda h, mo=mo, po=po, qoff=qoff, qsz=qsz: h.matmul(
                    ps["s"][0:qsz, 0:4], sb["qf"][po:po + 64, mo, qoff:qoff + qsz],
                    sb["kf"][po:po + 64, mo, 0:4], start=True, stop=True),
                    reads=[rs["qf"], rs["kf"]], writes=[rs["ps_s"]])
                em.do("pe", lambda h, mo=mo, po=po, qoff=qoff, qsz=qsz, wq_=wq_: h.matmul(
                    ps["a"][0:qsz, 0:wq_ - 4], sb["qf"][po:po + 64, mo, qoff:qoff + qsz],
                    sb["kf"][po:po + 64, mo, 4:wq_], start=True, stop=True),
                    reads=[rs["qf"], rs["kf"]], writes=[rs["ps_a"]])
                if qt == 0:
                    em.do("dve", lambda h, qsz=qsz: h.tensor_add(sb["lg"][0:qsz, 0:4], ps["s"][0:qsz, 0:4],
                                                                 sb["mtri"][0:qsz, 0:4]),
                          reads=[rs["ps_s"], rs["mtri"]], writes=[rs["lg"]])
                    em.do("dve", lambda h, qsz=qsz: h.tensor_add(sb["lg"][0:qsz, 4:128], ps["a"][0:qsz, 0:124],
                                                                 sb["mtri"][0:qsz, 4:128]),
                          reads=[rs["ps_a"], rs["mtri"]], writes=[rs["lg"]])
                else:
                    em.do("dve", lambda h, qsz=qsz: h.tensor_copy(sb["lg"][0:qsz, 0:4], ps["s"][0:qsz, 0:4]),
                          reads=[rs["ps_s"]], writes=[rs["lg"]])
                    em.do("act", lambda h, qsz=qsz, qoff=qoff: h.copy(out=sb["lg"][0:qsz, 4:qoff],
                                                                      in_=ps["a"][0:qsz, 0:qoff - 4]),
                          reads=[rs["ps_a"]], writes=[rs["lg"]])
                    em.do("dve", lambda h, qsz=qsz, qoff=qoff: h.tensor_add(
                        sb["lg"][0:qsz, qoff:qoff + qsz], ps["a"][0:qsz, qoff - 4:qoff - 4 + qsz],
                        sb["mtri"][0:qsz, 0:qsz]),
                        reads=[rs["ps_a"], rs["mtri"]], writes=[rs["lg"]])
                em.do("dve", lambda h, qsz=qsz, wq_=wq_: h.reduce_max(sb["st"][0:qsz, 0:1], sb["lg"][0:qsz, 0:wq_],
                                                                      AX.X, negate=True),
                      reads=[rs["lg"]], writes=[rs["st"]])
                em.do("act", lambda h, qsz=qsz, wq_=wq_: h.activation(out=sb["esc"][0:qsz, 0:wq_],
                                                                      in_=sb["lg"][0:qsz, 0:wq_], func=AF.Exp,
                                                                      bias=sb["st"][0:qsz, 0:1],
                                                                      accum_out=sb["st"][0:qsz, 1:2]),
                      reads=[rs["lg"], rs["st"]], writes=[rs["esc"], rs["st"]])
                em.do("dve", lambda h, qsz=qsz: h.reciprocal(sb["st"][0:qsz, 2:3], sb["st"][0:qsz, 1:2]),
                      reads=[rs["st"]], writes=[rs["st"]])
                em.do("dve", lambda h, qsz=qsz: h.tensor_scalar(sb["dg"][0:qsz, 0:qsz], sb["ident"][0:qsz, 0:qsz],
                                                                sb["st"][0:qsz, 2:3], None, ALU.mult),
                      reads=[rs["ident"], rs["st"]], writes=[rs["dg"]])
                pieces = []
                for cti, (c0, ck) in enumerate(L0_CT):
                    if c0 >= wq_:
                        break
                    pieces.append((cti, c0, min(ck, wq_ - c0)))
                for cti, c0, ck_ in pieces:
                    em.do("pe", lambda h, c0=c0, ck_=ck_, qsz=qsz: h.matmul(
                        ps["t2"][0:ck_, 0:qsz], sb["esc"][0:qsz, c0:c0 + ck_], sb["dg"][0:qsz, 0:qsz],
                        start=True, stop=True),
                        reads=[rs["esc"], rs["dg"]], writes=[rs["ps_t2"]])
                    eng = "dve" if cti % 2 == 0 else "act"
                    fn = (lambda h, cti=cti, ck_=ck_, qsz=qsz: h.tensor_copy(
                        sb["wT"][0:ck_, cti * 128:cti * 128 + qsz], ps["t2"][0:ck_, 0:qsz])) if eng == "dve" else \
                         (lambda h, cti=cti, ck_=ck_, qsz=qsz: h.copy(
                        out=sb["wT"][0:ck_, cti * 128:cti * 128 + qsz], in_=ps["t2"][0:ck_, 0:qsz]))
                    em.do(eng, fn, reads=[rs["ps_t2"]], writes=[rs["wT"]])
                for j, (cti, c0, ck_) in enumerate(pieces):
                    em.do("pe", lambda h, cti=cti, ck_=ck_, qsz=qsz, hd=hd,
                          st_=(j == 0), sp_=(j == len(pieces) - 1): h.matmul(
                        ps["h"][0:64, 0:qsz], sb["vt"][0:ck_, cti, hd * 64:(hd + 1) * 64],
                        sb["wT"][0:ck_, cti * 128:cti * 128 + qsz], start=st_, stop=sp_),
                        reads=[rs["vt"], rs["wT"]], writes=[rs["ps_h"]])
                em.do("act", lambda h, qsz=qsz, mo=mo, po=po, qoff=qoff: h.copy(
                    out=sb["ao"][po:po + 64, mo, qoff:qoff + qsz], in_=ps["h"][0:64, 0:qsz]),
                    reads=[rs["ps_h"]], writes=[rs["ao"]])
                if dbg and b == 0 and hd == 0 and qt == 1:
                    em.do("sp", lambda h: h.dma_start(out=dbg["d_lg"], in_=sb["lg"][:, :]),
                          reads=[rs["lg"]], writes=[rs["out_0"]], dma=True)
                    em.do("sp", lambda h: h.dma_start(out=dbg["d_esc"], in_=sb["esc"][:, :]),
                          reads=[rs["esc"]], writes=[rs["out_0"]], dma=True)
                    em.do("sp", lambda h: h.dma_start(out=dbg["d_wT"], in_=sb["wT"][:, :]),
                          reads=[rs["wT"]], writes=[rs["out_0"]], dma=True)

        if dbg and b == 0:
            em.do("sp", lambda h: h.dma_start(out=dbg["d_ao"], in_=sb["ao"][:, :, :]),
                  reads=[rs["ao"]], writes=[rs["out_0"]], dma=True)
        for qt, (qoff, qsz) in enumerate(L0_QT):
            em.do("sp", lambda h, b=b, qoff=qoff, qsz=qsz: h.dma_start(
                out=sb["xr"][0:qsz, :], in_=x_in[b, QR0 + qoff:QR0 + qoff + qsz, :]),
                writes=[rs["xr"]], dma=True)
            for half in range(2):
                pstgt = ps["a"] if half == 0 else ps["b"]
                psname = "ps_a" if half == 0 else "ps_b"
                for ki in range(8):
                    st_, sp_ = (ki == 0), (ki == 7)
                    em.do("pe", lambda h, qoff=qoff, qsz=qsz, half=half, ki=ki, st_=st_, sp_=sp_, pstgt=pstgt: h.matmul(
                        pstgt[0:qsz, 0:512], sb["ao"][:, ki, qoff:qoff + qsz],
                        sb["wo"][:, ki, half * 512:(half + 1) * 512], start=st_, stop=sp_),
                        reads=[rs["ao"], rs["wo"]], writes=[rs[psname]])
                em.do("dve", lambda h, qsz=qsz, half=half, pstgt=pstgt: h.tensor_add(
                    sb["xo"][0:qsz, half * 512:(half + 1) * 512], pstgt[0:qsz, 0:512],
                    sb["xr"][0:qsz, half * 512:(half + 1) * 512]),
                    reads=[rs[psname], rs["xr"]], writes=[rs["xo"]])
            em.do("sp", lambda h, b=b, qoff=qoff, qsz=qsz: h.dma_start(
                out=xq_out[b, out_base + qoff:out_base + qoff + qsz, :], in_=sb["xo"][0:qsz, :]),
                reads=[rs["xo"]], writes=[rs["out_" + str(b % 2)]], dma=True)


L0_RS_KEYS = ["ident", "mtri", "konst", "ps_t2", "wq", "wk", "wv", "wo", "xt", "xsq", "xlnb", "st",
              "xlnT", "qf", "kf", "vt", "ao", "lg", "esc", "dg", "wT", "xr", "xo",
              "ps_t", "ps_a", "ps_b", "ps_s", "ps_h", "out_0", "out_1"]


def build_l0_attn_nc(nb, debug=False):
    from contextlib import ExitStack
    nc = bass.Bass("TRN2", target_bir_lowering=False)
    x_in = nc.dram_tensor("x_in", [nb, T, D], F32, kind="ExternalInput").ap()
    xq_out = nc.dram_tensor("xq_out", [nb, NQ, D], F32, kind="ExternalOutput").ap()
    w_aps = {w: nc.dram_tensor(w, [D, D], F32, kind="ExternalInput").ap()
             for w in ("wq", "wk", "wv", "wo")}
    dbg = {}
    if debug:
        for nm, shp, dt_ in [("d_xlnT", [128, 8, 520], BF16), ("d_qf", [128, 8, 516], BF16),
                             ("d_kf", [128, 8, 516], BF16), ("d_vt", [128, 5, 1024], BF16),
                             ("d_ao", [128, 8, 516], BF16), ("d_lg", [128, 516], F32),
                             ("d_esc", [128, 516], BF16), ("d_wT", [128, 640], BF16)]:
            dbg[nm] = nc.dram_tensor(nm, shp, dt_, kind="ExternalOutput").ap()
    h_ident, h_tri, h_konst = l0_attn_consts(nc)
    es = ExitStack()
    with es:
        sb, ps = alloc_l0_attn(nc, es)
        trk = Trk(nc, es)
        em = Emit(trk)
        rs = {k: Rsrc(k) for k in L0_RS_KEYS}
        block = es.enter_context(nc.Block())
        emit_l0_attn(nc, em, sb, ps, rs, x_in, xq_out, w_aps, nb, h_ident, h_tri, h_konst,
                     dbg=dbg)
        trk.wait("sp", [rs["out_0"].w, rs["out_1"].w])
        trk.replay(block)
    return nc



E64 = 64
CAP = 512
TOPK = 8
NT_GLOB = 256
NT_OWN = 32
HUGE_SLOT = 65536.0


def l0ffn_consts(nc):
    ident = np.eye(128, dtype=np.float32)
    ones_full = np.ones((128, 128), dtype=np.float32)
    triu_strict = np.triu(np.ones((128, 128), dtype=np.float32), 1)
    iota512 = np.broadcast_to((np.arange(E64) * CAP).astype(np.float32), (128, E64)).copy()
    iota_e = np.broadcast_to(np.arange(E64, dtype=np.float32), (128, E64)).copy()
    h = {}
    h["ones_full"] = nc.inline_tensor(ones_full.astype(ml_dtypes.bfloat16), name="c_onesf")
    h["ident"] = nc.inline_tensor(ident.astype(ml_dtypes.bfloat16), name="c_identf")
    h["triu"] = nc.inline_tensor(triu_strict.astype(ml_dtypes.bfloat16), name="c_triu")
    h["iota512"] = nc.inline_tensor(iota512, name="c_iota512")
    h["iota_e"] = nc.inline_tensor(iota_e, name="c_iotae")
    kv = np.broadcast_to(np.array([EPS, -1.0, 0.125, 0.0], dtype=np.float32), (128, 4)).copy()
    h["konst"] = nc.inline_tensor(kv, name="c_konstf")
    return h


def alloc_l0ffn(nc, es):
    sb, ps = {}, {}
    sb["rwT"] = es.enter_context(nc.sbuf_tensor("rwT_sb", [128, 8, E64], BF16))
    sb["ident"] = es.enter_context(nc.sbuf_tensor("f_ident", [128, 128], BF16))
    sb["onesf"] = es.enter_context(nc.sbuf_tensor("onesf", [128, 128], BF16))
    sb["triu"] = es.enter_context(nc.sbuf_tensor("triu_sb", [128, 128], BF16))
    sb["iota512"] = es.enter_context(nc.sbuf_tensor("iota512_sb", [128, E64], F32))
    sb["iotae"] = es.enter_context(nc.sbuf_tensor("iotae_sb", [128, E64], F32))
    sb["colsumB"] = es.enter_context(nc.sbuf_tensor("colsumB", [128, E64], F32))
    sb["xt"] = es.enter_context(nc.sbuf_tensor("f_xt", [128, 1024], F32))
    sb["xbf"] = es.enter_context(nc.sbuf_tensor("f_xbf", [128, 1024], BF16))
    sb["xT"] = es.enter_context(nc.sbuf_tensor("f_xT", [128, 8, 128], BF16))
    sb["st"] = es.enter_context(nc.sbuf_tensor("f_st", [128, 8], F32))
    sb["lgr"] = es.enter_context(nc.sbuf_tensor("f_lgr", [128, E64], F32))
    sb["m8"] = es.enter_context(nc.sbuf_tensor("f_m8", [128, 8], F32))
    sb["rep"] = es.enter_context(nc.sbuf_tensor("f_rep", [128, E64], F32))
    sb["Mbf"] = es.enter_context(nc.sbuf_tensor("f_Mbf", [128, E64], BF16))
    sb["runrow"] = es.enter_context(nc.sbuf_tensor("f_runrow", [1, E64], BF16))
    sb["exc"] = es.enter_context(nc.sbuf_tensor("f_exc", [1, 8, E64], F32))
    sb["rankoh"] = es.enter_context(nc.sbuf_tensor("f_rankoh", [1, 512], F32))
    sb["baseown"] = es.enter_context(nc.sbuf_tensor("f_baseown", [1, E64], BF16))
    sb["i8u"] = es.enter_context(nc.sbuf_tensor("f_i8u", [128, 8], U32))
    sb["i8f"] = es.enter_context(nc.sbuf_tensor("f_i8f", [128, 8], F32))
    sb["G"] = es.enter_context(nc.sbuf_tensor("f_G", [128, E64], F32))
    sb["S"] = es.enter_context(nc.sbuf_tensor("f_S", [128, E64], F32))
    sb["slot"] = es.enter_context(nc.sbuf_tensor("f_slot", [128, E64], F32))
    sb["inv"] = es.enter_context(nc.sbuf_tensor("f_inv", [128, E64], F32))
    sb["s8"] = es.enter_context(nc.sbuf_tensor("f_s8", [128, 8], F32))
    sb["w8"] = es.enter_context(nc.sbuf_tensor("f_w8", [128, 8], F32))
    sb["w8x2"] = es.enter_context(nc.sbuf_tensor("f_w8x2", [128, 8, 2], F32))
    sb["s8u"] = es.enter_context(nc.sbuf_tensor("f_s8u", [128, 8], U32))
    sb["scr"] = es.enter_context(nc.sbuf_tensor("f_scr", [128, E64], F32))
    sb["xlnT"] = es.enter_context(nc.sbuf_tensor("f_xlnT", [128, 8, 4096], BF16))
    sb["scr2ln"] = es.enter_context(nc.sbuf_tensor("f_scr2ln", [128, 1024], F32))
    sb["konst"] = es.enter_context(nc.sbuf_tensor("f_konst", [128, 4], F32))
    sb["excm"] = es.enter_context(nc.sbuf_tensor("f_excm", [1, 8, E64], F32))
    sb["wsgu"] = es.enter_context(nc.sbuf_tensor("f_wsgu", [128, 8, 1536], BF16))
    sb["wsd"] = es.enter_context(nc.sbuf_tensor("f_wsd", [128, 6, 1024], BF16))
    sb["hsh"] = es.enter_context(nc.sbuf_tensor("f_hsh", [128, 6, 512], BF16))
    sb["gbf"] = es.enter_context(nc.sbuf_tensor("f_gbf", [128, 512], BF16))
    sb["xr"] = es.enter_context(nc.sbuf_tensor("f_xr", [128, 1024], F32))
    sb["acc"] = es.enter_context(nc.sbuf_tensor("f_acc", [128, 1024], F32))
    sb["wg"] = es.enter_context(nc.sbuf_tensor("f_wg", [128, 8, 768], BF16))
    sb["wu"] = es.enter_context(nc.sbuf_tensor("f_wu", [128, 8, 768], BF16))
    sb["wd"] = es.enter_context(nc.sbuf_tensor("f_wd", [128, 6, 1024], BF16))
    sb["xbT"] = es.enter_context(nc.sbuf_tensor("f_xbT", [128, 8, 512], BF16))
    sb["hE"] = es.enter_context(nc.sbuf_tensor("f_hE", [128, 6, 512], BF16))
    sb["ybf"] = es.enter_context(nc.sbuf_tensor("f_ybf", [128, 1024], BF16))
    sb["wcol"] = es.enter_context(nc.sbuf_tensor("f_wcol", [128, 8], F32))
    sb["oidx"] = es.enter_context(nc.sbuf_tensor("f_oidx", [128, 1], U32))
    ps["a"] = es.enter_context(nc.psum_tensor("f_ps_a", [128, 512], F32))
    ps["b"] = es.enter_context(nc.psum_tensor("f_ps_b", [128, 512], F32))
    ps["t"] = es.enter_context(nc.psum_tensor("f_ps_t", [128, 128], BF16))
    ps["lg"] = es.enter_context(nc.psum_tensor("f_ps_lg", [128, E64], F32))
    ps["run"] = es.enter_context(nc.psum_tensor("f_ps_run", [1, E64], F32))
    ps["runP"] = es.enter_context(nc.psum_tensor("f_ps_runP", [1, E64], F32))
    ps["S"] = es.enter_context(nc.psum_tensor("f_ps_S", [128, E64], F32))
    return sb, ps


L0FFN_RS = ["rwT", "ident", "onesf", "scr2ln", "konst", "excm", "triu", "iota512", "iotae", "colsumB", "xt", "xbf", "xT",
            "st", "lgr", "m8", "rep", "Mbf", "runrow", "exc", "rankoh", "baseown",
            "i8u", "i8f", "G", "S", "slot", "inv", "s8", "w8", "w8x2", "s8u", "scr", "xlnT",
            "ps_a", "ps_b", "ps_t", "ps_lg", "ps_run", "ps_runP", "ps_S",
            "xb_d", "w_d", "y_d", "s8u_d", "rbase_d", "x2_o",
            "wsgu", "wsd", "hsh", "xr", "acc",
            "wg", "wu", "wd", "xbT", "hE", "gbf", "ybf", "wcol", "oidx", "x1_d"]


def emit_l0ffn_R(nc, em, sb, ps, rs, x1_full, rankoh_ap, rwT_ap, hc):
    em.do("sp", lambda h: h.dma_start(out=sb["onesf"][:], in_=hc["ones_full"].ap()), writes=[rs["onesf"]], dma=True)
    em.do("sp", lambda h: h.dma_start(out=sb["triu"][:], in_=hc["triu"].ap()), writes=[rs["triu"]], dma=True)
    em.do("sp", lambda h: h.dma_start(out=sb["ident"][:], in_=hc["ident"].ap()), writes=[rs["ident"]], dma=True)
    em.do("sp", lambda h: h.dma_start(out=sb["konst"][:], in_=hc["konst"].ap()), writes=[rs["konst"]], dma=True)
    em.do("sp", lambda h: h.dma_start(out=sb["iota512"][:], in_=hc["iota512"].ap()), writes=[rs["iota512"]], dma=True)
    em.do("sp", lambda h: h.dma_start(out=sb["iotae"][:], in_=hc["iota_e"].ap()), writes=[rs["iotae"]], dma=True)
    em.do("sp", lambda h: h.dma_start(out=sb["rankoh"][0:1, :], in_=rankoh_ap[0:1, :]), writes=[rs["rankoh"]], dma=True)
    em.do("pool", lambda h: h.dma_start(out=sb["rwT"][:, :, :], in_=rwT_ap.rearrange("(k p) n -> p k n", p=128)),
          writes=[rs["rwT"]], dma=True)
    for dk in range(8):
        em.do("pe", lambda h, dk=dk: h.matmul(ps["lg"][:, :], sb["onesf"][:, :], sb["rwT"][:, dk, :],
                                              start=(dk == 0), stop=(dk == 7)),
              reads=[rs["onesf"], rs["rwT"]], writes=[rs["ps_lg"]])
    em.do("dve", lambda h: h.tensor_copy(sb["colsumB"][:], ps["lg"][:, :]), reads=[rs["ps_lg"]], writes=[rs["colsumB"]])

    for ti in range(NT_GLOB):
        g, gj = ti // 32, ti % 32
        em.do("sp", lambda h, ti=ti: h.dma_start(out=sb["xt"][:, :], in_=x1_full[ti * 128:(ti + 1) * 128, :]),
              writes=[rs["xt"]], dma=True)
        em.do("act", lambda h: h.activation(out=sb["xbf"][:, :], in_=sb["xt"][:, :], func=AF.Copy,
                                            accum_out=sb["st"][:, 0:1]),
              reads=[rs["xt"]], writes=[rs["xbf"], rs["st"]])
        em.do("dve", lambda h: h.tensor_scalar(sb["st"][:, 1:2], sb["st"][:, 0:1], -1.0 / D, None, ALU.mult),
              reads=[rs["st"]], writes=[rs["st"]])
        for ki in range(8):
            em.do("pe", lambda h, ki=ki: h.transpose(ps["t"][:, :], sb["xbf"][:, ki * 128:(ki + 1) * 128],
                                                      sb["ident"][:, :]),
                  reads=[rs["xbf"], rs["ident"]], writes=[rs["ps_t"]])
            eng = "dve" if ki % 2 == 0 else "act"
            fn = (lambda h, ki=ki: h.tensor_copy(sb["xT"][:, ki, :], ps["t"][:, :])) if eng == "dve" else \
                 (lambda h, ki=ki: h.copy(out=sb["xT"][:, ki, :], in_=ps["t"][:, :]))
            em.do(eng, fn, reads=[rs["ps_t"]], writes=[rs["xT"]])
        for ki in range(8):
            em.do("pe", lambda h, ki=ki: h.matmul(ps["lg"][:, :], sb["xT"][:, ki, :], sb["rwT"][:, ki, :],
                                                  start=(ki == 0), stop=(ki == 7)),
                  reads=[rs["xT"], rs["rwT"]], writes=[rs["ps_lg"]])
        em.do("dve", lambda h: h.scalar_tensor_tensor(out=sb["lgr"][:, :], in0=sb["colsumB"][:, :],
                                                      scalar=sb["st"][:, 1:2], in1=ps["lg"][:, :],
                                                      op0=ALU.mult, op1=ALU.add),
              reads=[rs["colsumB"], rs["st"], rs["ps_lg"]], writes=[rs["lgr"]])
        em.do("dve", lambda h: h.max(out=sb["m8"][:, :], in_=sb["lgr"][:, :]), reads=[rs["lgr"]], writes=[rs["m8"]])
        em.do("dve", lambda h: h.match_replace(out=sb["rep"][:, :], in_to_replace=sb["m8"][:, :],
                                               in_values=sb["lgr"][:, :], imm_value=-1e30),
              reads=[rs["lgr"], rs["m8"]], writes=[rs["rep"]])
        em.do("dve", lambda h: h.tensor_tensor(sb["Mbf"][:, :], sb["lgr"][:, :], sb["rep"][:, :], ALU.not_equal),
              reads=[rs["lgr"], rs["rep"]], writes=[rs["Mbf"]])
        em.do("pe", lambda h, ti=ti: h.matmul(ps["run"][0:1, :], sb["onesf"][:, 0:1], sb["Mbf"][:, :],
                                              start=(ti == 0), stop=(ti == NT_GLOB - 1), skip_group_check=True),
              reads=[rs["onesf"], rs["Mbf"]], writes=[rs["ps_run"]])
        if gj == 31:
            em.do("act", lambda h, g=g: h.copy(out=sb["exc"][0:1, g, :], in_=ps["run"][0:1, :]),
                  reads=[rs["ps_run"]], writes=[rs["exc"]])
    em.do("dve", lambda h: h.tensor_mul(sb["excm"][0:1, :, :], sb["exc"][0:1, :, :],
                                        sb["rankoh"][0:1, :].rearrange("a (g e) -> a g e", e=E64)),
          reads=[rs["rankoh"], rs["exc"]], writes=[rs["excm"]])
    em.do("dve", lambda h: h.tensor_reduce(sb["scr"][0:1, 0:E64],
                                           sb["excm"][0:1, :, :].transpose([0, 2, 1]), AX.X, ALU.add),
          reads=[rs["excm"]], writes=[rs["scr"]])
    em.do("dve", lambda h: h.tensor_copy(sb["baseown"][0:1, :], sb["scr"][0:1, 0:E64]),
          reads=[rs["scr"]], writes=[rs["baseown"]])


def emit_l0ffn_P(nc, em, sb, ps, rs, x1_own, rwT_ap, xb_d, w_d, s8u_d, hc, ownrows=None, x1_dram=None):
    for j in range(NT_OWN):
        if ownrows is None:
            em.do("sp", lambda h, j=j: h.dma_start(out=sb["xt"][:, :], in_=x1_own[j * 128:(j + 1) * 128, :]),
                  writes=[rs["xt"]], dma=True)
        else:
            em.do("sp", lambda h, j=j: h.dma_start(out=sb["oidx"][:, :], in_=ownrows[j, :, :]),
                  writes=[rs["oidx"]], dma=True)
            em.do("pool", lambda h, j=j: h.indirect_dma_start(
                sb["xt"][:, :], None, x1_dram[0:128, :], IndirectOffsetOnAxis(sb["oidx"][:, :], 0),
                bounds_check=bcreg(em.trk, h, NB * T - 1), oob_is_err=False),
                reads=[rs["oidx"], rs["x1_d"]], writes=[rs["xt"]], dma=True)
        em.do("act", lambda h: h.activation(out=sb["scr2ln"][:, :], in_=sb["xt"][:, :], func=AF.Square,
                                            accum_out=sb["st"][:, 0:1]),
              reads=[rs["xt"]], writes=[rs["scr2ln"], rs["st"]])
        em.do("dve", lambda h: h.reduce_sum(sb["st"][:, 1:2], sb["xt"][:, :], AX.X, negate=True),
              reads=[rs["xt"]], writes=[rs["st"]])
        em.do("dve", lambda h: h.tensor_scalar(sb["st"][:, 2:3], sb["st"][:, 1:2], 1.0 / D, None, ALU.mult),
              reads=[rs["st"]], writes=[rs["st"]])
        em.do("dve", lambda h: h.tensor_scalar(sb["st"][:, 3:4], sb["st"][:, 0:1], 1.0 / D, None, ALU.mult),
              reads=[rs["st"]], writes=[rs["st"]])
        em.do("dve", lambda h: h.scalar_tensor_tensor(out=sb["st"][:, 4:5], in0=sb["st"][:, 2:3],
                                                      scalar=sb["st"][:, 2:3], in1=sb["st"][:, 3:4],
                                                      op0=ALU.mult, op1=ALU.subtract),
              reads=[rs["st"]], writes=[rs["st"]])
        em.do("act", lambda h: h.activation(out=sb["st"][:, 5:6], in_=sb["st"][:, 4:5], func=AF.Sqrt,
                                            bias=sb["konst"][:, 0:1], scale=sb["konst"][:, 1:2]),
              reads=[rs["st"], rs["konst"]], writes=[rs["st"]])
        em.do("dve", lambda h: h.reciprocal(sb["st"][:, 6:7], sb["st"][:, 5:6]),
              reads=[rs["st"]], writes=[rs["st"]])
        em.do("dve", lambda h: h.tensor_scalar(sb["xbf"][:, :], sb["xt"][:, :], sb["st"][:, 2:3],
                                               sb["st"][:, 6:7], ALU.add, ALU.mult),
              reads=[rs["xt"], rs["st"]], writes=[rs["xbf"]])
        for ki in range(8):
            em.do("pe", lambda h, ki=ki: h.transpose(ps["t"][:, :], sb["xbf"][:, ki * 128:(ki + 1) * 128],
                                                     sb["ident"][:, :]),
                  reads=[rs["xbf"], rs["ident"]], writes=[rs["ps_t"]])
            eng = "dve" if ki % 2 == 0 else "act"
            fn = (lambda h, ki=ki, j=j: h.tensor_copy(sb["xlnT"][:, ki, j * 128:(j + 1) * 128], ps["t"][:, :])) \
                if eng == "dve" else \
                 (lambda h, ki=ki, j=j: h.copy(out=sb["xlnT"][:, ki, j * 128:(j + 1) * 128], in_=ps["t"][:, :]))
            em.do(eng, fn, reads=[rs["ps_t"]], writes=[rs["xlnT"]])
        for ki in range(8):
            em.do("pe", lambda h, ki=ki, j=j: h.matmul(ps["lg"][:, :], sb["xlnT"][:, ki, j * 128:(j + 1) * 128],
                                                       sb["rwT"][:, ki, :], start=(ki == 0), stop=(ki == 7)),
                  reads=[rs["xlnT"], rs["rwT"]], writes=[rs["ps_lg"]])
        em.do("dve", lambda h: h.tensor_copy(sb["lgr"][:, :], ps["lg"][:, :]), reads=[rs["ps_lg"]], writes=[rs["lgr"]])
        em.do("dve", lambda h: h.max(out=sb["m8"][:, :], in_=sb["lgr"][:, :]), reads=[rs["lgr"]], writes=[rs["m8"]])
        em.do("dve", lambda h: h.max_index(out=sb["i8u"][:, :], in_max=sb["m8"][:, :], in_values=sb["lgr"][:, :]),
              reads=[rs["lgr"], rs["m8"]], writes=[rs["i8u"]])
        em.do("dve", lambda h: h.tensor_copy(sb["i8f"][:, :], sb["i8u"][:, :]), reads=[rs["i8u"]], writes=[rs["i8f"]])
        em.do("dve", lambda h: h.match_replace(out=sb["rep"][:, :], in_to_replace=sb["m8"][:, :],
                                               in_values=sb["lgr"][:, :], imm_value=-1e30),
              reads=[rs["lgr"], rs["m8"]], writes=[rs["rep"]])
        em.do("dve", lambda h: h.tensor_tensor(sb["scr"][:, :], sb["lgr"][:, :], sb["rep"][:, :], ALU.not_equal),
              reads=[rs["lgr"], rs["rep"]], writes=[rs["scr"]])
        em.do("dve", lambda h: h.tensor_copy(sb["Mbf"][:, :], sb["scr"][:, :]), reads=[rs["scr"]], writes=[rs["Mbf"]])
        em.do("dve", lambda h: h.tensor_scalar(sb["st"][:, 7:8], sb["m8"][:, 0:1], -1.0, None, ALU.mult),
              reads=[rs["m8"]], writes=[rs["st"]])
        em.do("act", lambda h: h.activation(out=sb["rep"][:, :], in_=sb["lgr"][:, :], func=AF.Exp,
                                            bias=sb["st"][:, 7:8]),
              reads=[rs["lgr"], rs["st"]], writes=[rs["rep"]])
        em.do("dve", lambda h: h.tensor_mul(sb["G"][:, :], sb["scr"][:, :], sb["rep"][:, :]),
              reads=[rs["scr"], rs["rep"]], writes=[rs["G"]])
        em.do("dve", lambda h: h.tensor_reduce(sb["st"][:, 7:8], sb["G"][:, :], AX.X, ALU.add),
              reads=[rs["G"]], writes=[rs["st"]])
        em.do("dve", lambda h: h.reciprocal(sb["st"][:, 7:8], sb["st"][:, 7:8]),
              reads=[rs["st"]], writes=[rs["st"]])
        em.do("dve", lambda h: h.tensor_scalar(sb["G"][:, :], sb["G"][:, :], sb["st"][:, 7:8], None, ALU.mult),
              reads=[rs["st"]], writes=[rs["G"]])
        if j == 0:
            em.do("dve", lambda h: h.memset(sb["runrow"][0:1, :], 0.0), writes=[rs["runrow"]])
        else:
            em.do("act", lambda h: h.copy(out=sb["runrow"][0:1, :], in_=ps["runP"][0:1, :]),
                  reads=[rs["ps_runP"]], writes=[rs["runrow"]])
        em.do("pe", lambda h: h.matmul(ps["S"][:, :], sb["triu"][:, :], sb["Mbf"][:, :], start=True, stop=False),
              reads=[rs["triu"], rs["Mbf"]], writes=[rs["ps_S"]])
        em.do("pe", lambda h: h.matmul(ps["S"][:, :], sb["onesf"][0:1, :], sb["runrow"][0:1, :],
                                       start=False, stop=False),
              reads=[rs["onesf"], rs["runrow"]], writes=[rs["ps_S"]])
        em.do("pe", lambda h: h.matmul(ps["S"][:, :], sb["onesf"][0:1, :], sb["baseown"][0:1, :],
                                       start=False, stop=True),
              reads=[rs["onesf"], rs["baseown"]], writes=[rs["ps_S"]])
        em.do("pe", lambda h, j=j: h.matmul(ps["runP"][0:1, :], sb["onesf"][:, 0:1], sb["Mbf"][:, :],
                                            start=(j == 0), stop=(j == NT_OWN - 1), skip_group_check=True),
              reads=[rs["onesf"], rs["Mbf"]], writes=[rs["ps_runP"]])
        em.do("dve", lambda h: h.tensor_copy(sb["S"][:, :], ps["S"][:, :]), reads=[rs["ps_S"]], writes=[rs["S"]])
        em.do("dve", lambda h: h.tensor_scalar(sb["inv"][:, :], sb["S"][:, :], float(CAP), None, ALU.is_ge),
              reads=[rs["S"]], writes=[rs["inv"]])
        em.do("dve", lambda h: h.tensor_add(sb["slot"][:, :], sb["S"][:, :], sb["iota512"][:, :]),
              reads=[rs["S"], rs["iota512"]], writes=[rs["slot"]])
        em.do("dve", lambda h: h.scalar_tensor_tensor(out=sb["slot"][:, :], in0=sb["inv"][:, :],
                                                      scalar=HUGE_SLOT, in1=sb["slot"][:, :],
                                                      op0=ALU.mult, op1=ALU.add),
              reads=[rs["inv"]], writes=[rs["slot"]])
        for k in range(8):
            em.do("dve", lambda h, k=k: h.scalar_tensor_tensor(out=sb["scr"][:, :], in0=sb["iotae"][:, :],
                                                               scalar=sb["i8f"][:, k:k + 1], in1=sb["slot"][:, :],
                                                               op0=ALU.is_equal, op1=ALU.mult,
                                                               accum_out=sb["s8"][:, k:k + 1]),
                  reads=[rs["iotae"], rs["i8f"], rs["slot"]], writes=[rs["scr"], rs["s8"]])
            em.do("dve", lambda h, k=k: h.scalar_tensor_tensor(out=sb["scr"][:, :], in0=sb["iotae"][:, :],
                                                               scalar=sb["i8f"][:, k:k + 1], in1=sb["G"][:, :],
                                                               op0=ALU.is_equal, op1=ALU.mult,
                                                               accum_out=sb["w8"][:, k:k + 1]),
                  reads=[rs["iotae"], rs["i8f"], rs["G"]], writes=[rs["scr"], rs["w8"]])
        em.do("dve", lambda h: h.tensor_copy(sb["s8u"][:, :], sb["s8"][:, :]), reads=[rs["s8"]], writes=[rs["s8u"]])
        em.do("dve", lambda h: h.tensor_copy(sb["w8x2"][:, :, 0], sb["w8"][:, :]), reads=[rs["w8"]], writes=[rs["w8x2"]])
        em.do("dve", lambda h: h.tensor_copy(sb["w8x2"][:, :, 1], sb["w8"][:, :]), reads=[rs["w8"]], writes=[rs["w8x2"]])
        em.do("sp", lambda h, j=j: h.dma_start(out=s8u_d[j, :, :], in_=sb["s8u"][:, :]),
              reads=[rs["s8u"]], writes=[rs["s8u_d"]], dma=True)
        for k in range(8):
            em.do("pool", lambda h, k=k: h.indirect_dma_start(
                xb_d[0:128, :], IndirectOffsetOnAxis(sb["s8u"][:, k:k + 1], 0),
                sb["xbf"][:, :], None, bounds_check=bcreg(em.trk, h, E64 * CAP - 1), oob_is_err=False),
                reads=[rs["s8u"], rs["xbf"]], writes=[rs["xb_d"]], dma=True)
            em.do("pool", lambda h, k=k: h.indirect_dma_start(
                w_d[0:128, :], IndirectOffsetOnAxis(sb["s8u"][:, k:k + 1], 0),
                sb["w8x2"][:, k, :], None, bounds_check=bcreg(em.trk, h, E64 * CAP - 1), oob_is_err=False),
                reads=[rs["s8u"], rs["w8x2"]], writes=[rs["w_d"]], dma=True)


def emit_l0ffn_S(nc, em, sb, ps, rs, x1_own, wsgu_ap, wsd_ap, rbase_d, ownrows=None, x1_dram=None):
    em.do("pool", lambda h: h.dma_start(out=sb["wsgu"][:, :, :], in_=wsgu_ap.rearrange("(k p) n -> p k n", p=128)),
          writes=[rs["wsgu"]], dma=True)
    em.do("pool", lambda h: h.dma_start(out=sb["wsd"][:, :, :], in_=wsd_ap.rearrange("(k p) n -> p k n", p=128)),
          writes=[rs["wsd"]], dma=True)
    SF = 768
    for sp_ in range(8):
        t0 = sp_ * 512
        for fc in range(6):
            for ki in range(8):
                em.do("pe", lambda h, fc=fc, ki=ki, t0=t0: h.matmul(
                    ps["a"][:, :], sb["wsgu"][:, ki, fc * 128:(fc + 1) * 128],
                    sb["xlnT"][:, ki, t0:t0 + 512], start=(ki == 0), stop=(ki == 7)),
                    reads=[rs["wsgu"], rs["xlnT"]], writes=[rs["ps_a"]])
                em.do("pe", lambda h, fc=fc, ki=ki, t0=t0: h.matmul(
                    ps["b"][:, :], sb["wsgu"][:, ki, SF + fc * 128:SF + (fc + 1) * 128],
                    sb["xlnT"][:, ki, t0:t0 + 512], start=(ki == 0), stop=(ki == 7)),
                    reads=[rs["wsgu"], rs["xlnT"]], writes=[rs["ps_b"]])
            em.do("act", lambda h: h.activation(out=sb["gbf"][:, :], in_=ps["a"][:, :], func=AF.Silu),
                  reads=[rs["ps_a"]], writes=[rs["gbf"]])
            em.do("dve", lambda h, fc=fc: h.tensor_mul(sb["hsh"][:, fc, :], sb["gbf"][:, :], ps["b"][:, :]),
                  reads=[rs["gbf"], rs["ps_b"]], writes=[rs["hsh"]])
        for tt in range(4):
            tg = t0 + tt * 128
            if ownrows is None:
                em.do("sp", lambda h, tg=tg: h.dma_start(out=sb["xr"][:, :], in_=x1_own[tg:tg + 128, :]),
                      writes=[rs["xr"]], dma=True)
            else:
                em.do("sp", lambda h, tg=tg: h.dma_start(out=sb["oidx"][:, :], in_=ownrows[tg // 128, :, :]),
                      writes=[rs["oidx"]], dma=True)
                em.do("pool", lambda h: h.indirect_dma_start(
                    sb["xr"][:, :], None, x1_dram[0:128, :], IndirectOffsetOnAxis(sb["oidx"][:, :], 0),
                    bounds_check=bcreg(em.trk, h, NB * T - 1), oob_is_err=False),
                    reads=[rs["oidx"], rs["x1_d"]], writes=[rs["xr"]], dma=True)
            for half in range(2):
                pst, psn = (ps["a"], "ps_a") if half == 0 else (ps["b"], "ps_b")
                for fc in range(6):
                    em.do("pe", lambda h, fc=fc, tt=tt, half=half, pst=pst: h.matmul(
                        pst[:, :], sb["hsh"][:, fc, tt * 128:(tt + 1) * 128],
                        sb["wsd"][:, fc, half * 512:(half + 1) * 512], start=(fc == 0), stop=(fc == 5)),
                        reads=[rs["hsh"], rs["wsd"]], writes=[rs[psn]])
                em.do("dve", lambda h, half=half, pst=pst: h.tensor_add(
                    sb["acc"][:, half * 512:(half + 1) * 512], pst[:, :],
                    sb["xr"][:, half * 512:(half + 1) * 512]),
                    reads=[rs[psn], rs["xr"]], writes=[rs["acc"]])
            em.do("sp", lambda h, tg=tg: h.dma_start(out=rbase_d[tg:tg + 128, :], in_=sb["acc"][:, :]),
                  reads=[rs["acc"]], writes=[rs["rbase_d"]], dma=True)


def emit_l0ffn_E(nc, em, sb, ps, rs, wg_ap, wu_ap, wd_ap, xb_d, w_d, y_d):
    for e in range(E64):
        em.do("pool", lambda h, e=e: h.dma_start(out=sb["wg"][:, :, :],
                                                 in_=wg_ap[e].rearrange("(k p) n -> p k n", p=128)),
              writes=[rs["wg"]], dma=True)
        em.do("pool", lambda h, e=e: h.dma_start(out=sb["wu"][:, :, :],
                                                 in_=wu_ap[e].rearrange("(k p) n -> p k n", p=128)),
              writes=[rs["wu"]], dma=True)
        em.do("pool", lambda h, e=e: h.dma_start(out=sb["wd"][:, :, :],
                                                 in_=wd_ap[e].rearrange("(k p) n -> p k n", p=128)),
              writes=[rs["wd"]], dma=True)
        for dk in range(8):
            em.do("sp", lambda h, e=e, dk=dk: h.dma_start(
                out=sb["xbT"][:, dk, :], in_=xb_d[e * CAP:(e + 1) * CAP, dk * 128:(dk + 1) * 128],
                transpose=True),
                reads=[rs["xb_d"]], writes=[rs["xbT"]], dma=True)
        for st_ in range(4):
            em.do("sp", lambda h, e=e, st_=st_: h.dma_start(
                out=sb["wcol"][:, 2 * st_:2 * st_ + 2],
                in_=w_d[e * CAP + st_ * 128:e * CAP + (st_ + 1) * 128, 0:2]),
                reads=[rs["w_d"]], writes=[rs["wcol"]], dma=True)
        for fc in range(6):
            for ki in range(8):
                em.do("pe", lambda h, fc=fc, ki=ki: h.matmul(
                    ps["a"][:, :], sb["wg"][:, ki, fc * 128:(fc + 1) * 128],
                    sb["xbT"][:, ki, :], start=(ki == 0), stop=(ki == 7)),
                    reads=[rs["wg"], rs["xbT"]], writes=[rs["ps_a"]])
                em.do("pe", lambda h, fc=fc, ki=ki: h.matmul(
                    ps["b"][:, :], sb["wu"][:, ki, fc * 128:(fc + 1) * 128],
                    sb["xbT"][:, ki, :], start=(ki == 0), stop=(ki == 7)),
                    reads=[rs["wu"], rs["xbT"]], writes=[rs["ps_b"]])
            em.do("act", lambda h: h.activation(out=sb["gbf"][:, :], in_=ps["a"][:, :], func=AF.Silu),
                  reads=[rs["ps_a"]], writes=[rs["gbf"]])
            em.do("dve", lambda h, fc=fc: h.tensor_mul(sb["hE"][:, fc, :], sb["gbf"][:, :], ps["b"][:, :]),
                  reads=[rs["gbf"], rs["ps_b"]], writes=[rs["hE"]])
        for st_ in range(4):
            for half in range(2):
                pst, psn = (ps["a"], "ps_a") if half == 0 else (ps["b"], "ps_b")
                for fc in range(6):
                    em.do("pe", lambda h, fc=fc, st_=st_, half=half, pst=pst: h.matmul(
                        pst[:, :], sb["hE"][:, fc, st_ * 128:(st_ + 1) * 128],
                        sb["wd"][:, fc, half * 512:(half + 1) * 512], start=(fc == 0), stop=(fc == 5)),
                        reads=[rs["hE"], rs["wd"]], writes=[rs[psn]])
                em.do("dve", lambda h, half=half, st_=st_, pst=pst: h.tensor_scalar(
                    sb["ybf"][:, half * 512:(half + 1) * 512], pst[:, :],
                    sb["wcol"][:, 2 * st_:2 * st_ + 1], None, ALU.mult),
                    reads=[rs[psn], rs["wcol"]], writes=[rs["ybf"]])
            em.do("pool", lambda h, e=e, st_=st_: h.dma_start(
                out=y_d[e * CAP + st_ * 128:e * CAP + (st_ + 1) * 128, :], in_=sb["ybf"][:, :]),
                reads=[rs["ybf"]], writes=[rs["y_d"]], dma=True)


def emit_l0ffn_G(nc, em, sb, ps, rs, rbase_d, s8u_d, y_d, x2_out):
    for j in range(NT_OWN):
        em.do("sp", lambda h, j=j: h.dma_start(out=sb["acc"][:, :], in_=rbase_d[j * 128:(j + 1) * 128, :]),
              reads=[rs["rbase_d"]], writes=[rs["acc"]], dma=True)
        em.do("sp", lambda h, j=j: h.dma_start(out=sb["s8u"][:, :], in_=s8u_d[j, :, :]),
              reads=[rs["s8u_d"]], writes=[rs["s8u"]], dma=True)
        for k in range(8):
            em.do("pool", lambda h, k=k: h.indirect_dma_start(
                sb["acc"][:, :], None, y_d[0:128, :], IndirectOffsetOnAxis(sb["s8u"][:, k:k + 1], 0),
                bounds_check=bcreg(em.trk, h, E64 * CAP - 1), oob_is_err=False, compute_op=ALU.add),
                reads=[rs["s8u"], rs["y_d"]], writes=[rs["acc"]], dma=True)
        em.do("sp", lambda h, j=j: h.dma_start(out=x2_out[j * 128:(j + 1) * 128, :], in_=sb["acc"][:, :]),
              reads=[rs["acc"]], writes=[rs["x2_o"]], dma=True)


def build_l0ffn_nc():
    from contextlib import ExitStack
    nc = bass.Bass("TRN2", target_bir_lowering=False)
    x1_full = nc.dram_tensor("x1_full", [NB * T, D], F32, kind="ExternalInput").ap()
    x1_own = nc.dram_tensor("x1_own", [4096, D], F32, kind="ExternalInput").ap()
    rankoh = nc.dram_tensor("rankoh", [1, 512], F32, kind="ExternalInput").ap()
    rwT = nc.dram_tensor("rwT", [D, E64], F32, kind="ExternalInput").ap()
    wsgu = nc.dram_tensor("wsgu", [D, 1536], F32, kind="ExternalInput").ap()
    wsd = nc.dram_tensor("wsd", [768, D], F32, kind="ExternalInput").ap()
    wg = nc.dram_tensor("wg", [E64, D, 768], F32, kind="ExternalInput").ap()
    wu = nc.dram_tensor("wu", [E64, D, 768], F32, kind="ExternalInput").ap()
    wd = nc.dram_tensor("wd", [E64, 768, D], F32, kind="ExternalInput").ap()
    x2_out = nc.dram_tensor("x2_out", [4096, D], F32, kind="ExternalOutput").ap()
    xb_d = nc.dram_tensor("xb_d", [E64 * CAP, D], BF16, kind="Internal").ap()
    w_d = nc.dram_tensor("w_d", [E64 * CAP, 2], F32, kind="Internal").ap()
    y_d = nc.dram_tensor("y_d", [E64 * CAP, D], BF16, kind="Internal").ap()
    s8u_d = nc.dram_tensor("s8u_d", [NT_OWN, 128, 8], U32, kind="Internal").ap()
    rbase_d = nc.dram_tensor("rbase_d", [4096, D], F32, kind="Internal").ap()
    hc = l0ffn_consts(nc)
    es = ExitStack()
    with es:
        sb, ps = alloc_l0ffn(nc, es)
        trk = Trk(nc, es)
        em = Emit(trk)
        rs = {k: Rsrc(k) for k in L0FFN_RS}
        block = es.enter_context(nc.Block())
        emit_l0ffn_R(nc, em, sb, ps, rs, x1_full, rankoh, rwT, hc)
        emit_l0ffn_P(nc, em, sb, ps, rs, x1_own, rwT, xb_d, w_d, s8u_d, hc)
        emit_l0ffn_S(nc, em, sb, ps, rs, x1_own, wsgu, wsd, rbase_d)
        emit_l0ffn_E(nc, em, sb, ps, rs, wg, wu, wd, xb_d, w_d, y_d)
        emit_l0ffn_G(nc, em, sb, ps, rs, rbase_d, s8u_d, y_d, x2_out)
        trk.wait("sp", [rs["x2_o"].w])
        trk.replay(block)
    return nc



L1A_RS = ["ident", "onesf", "konst", "wqkv", "wout", "bq_row", "bo_row",
          "xt", "xbf", "xT", "st", "scr2ln", "qbf", "kbf", "vbf", "prod", "obf",
          "kvrow", "kvb", "rsq", "xr", "acc",
          "ps_a", "ps_b", "ps_t", "ps_kv1", "ps_kv2", "x3_o"]


def alloc_l1attn(nc, es):
    sb, ps = {}, {}
    sb["ident"] = es.enter_context(nc.sbuf_tensor("a_ident", [128, 128], BF16))
    sb["onesf"] = es.enter_context(nc.sbuf_tensor("a_onesf", [128, 128], BF16))
    sb["konst"] = es.enter_context(nc.sbuf_tensor("a_konst", [128, 4], F32))
    sb["wqkv"] = es.enter_context(nc.sbuf_tensor("a_wqkv", [128, 8, 3072], BF16))
    sb["wout"] = es.enter_context(nc.sbuf_tensor("a_wout", [128, 8, 1024], BF16))
    sb["bq_row"] = es.enter_context(nc.sbuf_tensor("a_bqrow", [1, 3072], BF16))
    sb["bo_row"] = es.enter_context(nc.sbuf_tensor("a_borow", [1, 1024], BF16))
    sb["bq_row32"] = es.enter_context(nc.sbuf_tensor("a_bqrow32", [1, 3072], F32))
    sb["bo_row32"] = es.enter_context(nc.sbuf_tensor("a_borow32", [1, 1024], F32))
    sb["xt"] = es.enter_context(nc.sbuf_tensor("a_xt", [128, 1024], F32))
    sb["xbf"] = es.enter_context(nc.sbuf_tensor("a_xbf", [128, 1024], BF16))
    sb["xT"] = es.enter_context(nc.sbuf_tensor("a_xT", [128, 8, 128], BF16))
    sb["st"] = es.enter_context(nc.sbuf_tensor("a_st", [128, 16], F32))
    sb["scr2ln"] = es.enter_context(nc.sbuf_tensor("a_scr2ln", [128, 1024], F32))
    sb["qbf"] = es.enter_context(nc.sbuf_tensor("a_qbf", [128, 16, 1024], BF16))
    sb["kbf"] = es.enter_context(nc.sbuf_tensor("a_kbf", [128, 1024], BF16))
    sb["vbf"] = es.enter_context(nc.sbuf_tensor("a_vbf", [128, 1024], BF16))
    sb["prod"] = es.enter_context(nc.sbuf_tensor("a_prod", [128, 1024], BF16))
    sb["obf"] = es.enter_context(nc.sbuf_tensor("a_obf", [128, 1024], BF16))
    sb["kvrow"] = es.enter_context(nc.sbuf_tensor("a_kvrow", [1, 1024], BF16))
    sb["kvb"] = es.enter_context(nc.sbuf_tensor("a_kvb", [128, 1024], F32))
    sb["rsq"] = es.enter_context(nc.sbuf_tensor("a_rsq", [128, 16], F32))
    sb["xr"] = es.enter_context(nc.sbuf_tensor("a_xr", [128, 1024], F32))
    sb["acc"] = es.enter_context(nc.sbuf_tensor("a_acc", [128, 1024], F32))
    ps["a"] = es.enter_context(nc.psum_tensor("a_ps_a", [128, 512], F32))
    ps["b"] = es.enter_context(nc.psum_tensor("a_ps_b", [128, 512], F32))
    ps["t"] = es.enter_context(nc.psum_tensor("a_ps_t", [128, 128], BF16))
    ps["kv1"] = es.enter_context(nc.psum_tensor("a_ps_kv1", [1, 512], F32))
    ps["kv2"] = es.enter_context(nc.psum_tensor("a_ps_kv2", [1, 512], F32))
    return sb, ps


def emit_ln128(em, sb, rs, xkey="xt", okey="xbf"):
    em.do("act", lambda h: h.activation(out=sb["scr2ln"][:, :], in_=sb[xkey][:, :], func=AF.Square,
                                        accum_out=sb["st"][:, 0:1]),
          reads=[rs[xkey]], writes=[rs["scr2ln"], rs["st"]])
    em.do("dve", lambda h: h.reduce_sum(sb["st"][:, 1:2], sb[xkey][:, :], AX.X, negate=True),
          reads=[rs[xkey]], writes=[rs["st"]])
    em.do("dve", lambda h: h.tensor_scalar(sb["st"][:, 2:3], sb["st"][:, 1:2], 1.0 / D, None, ALU.mult),
          reads=[rs["st"]], writes=[rs["st"]])
    em.do("dve", lambda h: h.tensor_scalar(sb["st"][:, 3:4], sb["st"][:, 0:1], 1.0 / D, None, ALU.mult),
          reads=[rs["st"]], writes=[rs["st"]])
    em.do("dve", lambda h: h.scalar_tensor_tensor(out=sb["st"][:, 4:5], in0=sb["st"][:, 2:3],
                                                  scalar=sb["st"][:, 2:3], in1=sb["st"][:, 3:4],
                                                  op0=ALU.mult, op1=ALU.subtract),
          reads=[rs["st"]], writes=[rs["st"]])
    em.do("act", lambda h: h.activation(out=sb["st"][:, 5:6], in_=sb["st"][:, 4:5], func=AF.Sqrt,
                                        bias=sb["konst"][:, 0:1], scale=sb["konst"][:, 1:2]),
          reads=[rs["st"], rs["konst"]], writes=[rs["st"]])
    em.do("dve", lambda h: h.reciprocal(sb["st"][:, 6:7], sb["st"][:, 5:6]),
          reads=[rs["st"]], writes=[rs["st"]])
    em.do("dve", lambda h: h.tensor_scalar(sb[okey][:, :], sb[xkey][:, :], sb["st"][:, 2:3],
                                           sb["st"][:, 6:7], ALU.add, ALU.mult),
          reads=[rs[xkey], rs["st"]], writes=[rs[okey]])


def emit_l1_attn(nc, em, sb, ps, rs, x_in, x3_out, wqkv_ap, bqkv_ap, wout_ap, bout_ap, nb, hc):
    em.do("sp", lambda h: h.dma_start(out=sb["ident"][:], in_=hc["ident"].ap()), writes=[rs["ident"]], dma=True)
    em.do("sp", lambda h: h.dma_start(out=sb["onesf"][:], in_=hc["ones_full"].ap()), writes=[rs["onesf"]], dma=True)
    em.do("sp", lambda h: h.dma_start(out=sb["konst"][:], in_=hc["konst"].ap()), writes=[rs["konst"]], dma=True)
    em.do("pool", lambda h: h.dma_start(out=sb["wqkv"][:, :, :], in_=wqkv_ap.rearrange("(k p) n -> p k n", p=128)),
          writes=[rs["wqkv"]], dma=True)
    em.do("pool", lambda h: h.dma_start(out=sb["wout"][:, :, :], in_=wout_ap.rearrange("(k p) n -> p k n", p=128)),
          writes=[rs["wout"]], dma=True)
    em.do("sp", lambda h: h.dma_start(out=sb["bq_row32"][0:1, :], in_=bqkv_ap[None, :]), writes=[rs["bq_row"]], dma=True)
    em.do("sp", lambda h: h.dma_start(out=sb["bo_row32"][0:1, :], in_=bout_ap[None, :]), writes=[rs["bo_row"]], dma=True)
    em.do("dve", lambda h: h.tensor_copy(sb["bq_row"][0:1, :], sb["bq_row32"][0:1, :]),
          reads=[rs["bq_row"]], writes=[rs["bq_row"]])
    em.do("dve", lambda h: h.tensor_copy(sb["bo_row"][0:1, :], sb["bo_row32"][0:1, :]),
          reads=[rs["bo_row"]], writes=[rs["bo_row"]])

    NTT = 16
    for b in range(nb):
        for tt in range(NTT):
            em.do("sp", lambda h, b=b, tt=tt: h.dma_start(out=sb["xt"][:, :], in_=x_in[b, tt * 128:(tt + 1) * 128, :]),
                  writes=[rs["xt"]], dma=True)
            emit_ln128(em, sb, rs)
            for ki in range(8):
                em.do("pe", lambda h, ki=ki: h.transpose(ps["t"][:, :], sb["xbf"][:, ki * 128:(ki + 1) * 128],
                                                         sb["ident"][:, :]),
                      reads=[rs["xbf"], rs["ident"]], writes=[rs["ps_t"]])
                eng = "dve" if ki % 2 == 0 else "act"
                fn = (lambda h, ki=ki: h.tensor_copy(sb["xT"][:, ki, :], ps["t"][:, :])) if eng == "dve" else \
                     (lambda h, ki=ki: h.copy(out=sb["xT"][:, ki, :], in_=ps["t"][:, :]))
                em.do(eng, fn, reads=[rs["ps_t"]], writes=[rs["xT"]])
            for s in range(3):
                for half in range(2):
                    pst, psn = (ps["a"], "ps_a") if half == 0 else (ps["b"], "ps_b")
                    c0 = s * 1024 + half * 512
                    for ki in range(8):
                        em.do("pe", lambda h, ki=ki, c0=c0, pst=pst: h.matmul(
                            pst[:, :], sb["xT"][:, ki, :], sb["wqkv"][:, ki, c0:c0 + 512],
                            start=(ki == 0), stop=False),
                            reads=[rs["xT"], rs["wqkv"]], writes=[rs[psn]])
                    em.do("pe", lambda h, c0=c0, pst=pst: h.matmul(
                        pst[:, :], sb["onesf"][0:1, :], sb["bq_row"][0:1, c0:c0 + 512],
                        start=False, stop=True),
                        reads=[rs["onesf"], rs["bq_row"]], writes=[rs[psn]])
                for half in range(2):
                    pst, psn = (ps["a"], "ps_a") if half == 0 else (ps["b"], "ps_b")
                    if s == 0:
                        em.do("act", lambda h, half=half, pst=pst: h.activation(
                            out=sb["scr2ln"][:, half * 512:(half + 1) * 512], in_=pst[:, :], func=AF.Square,
                            accum_out=sb["st"][:, 8 + half:9 + half]),
                            reads=[rs[psn]], writes=[rs["scr2ln"], rs["st"]])
                        em.do("act", lambda h, half=half, tt=tt, pst=pst: h.copy(
                            out=sb["qbf"][:, tt, half * 512:(half + 1) * 512], in_=pst[:, :]),
                            reads=[rs[psn]], writes=[rs["qbf"]])
                    elif s == 1:
                        em.do("act", lambda h, half=half, pst=pst: h.activation(
                            out=sb["scr2ln"][:, half * 512:(half + 1) * 512], in_=pst[:, :], func=AF.Square,
                            accum_out=sb["st"][:, 10 + half:11 + half]),
                            reads=[rs[psn]], writes=[rs["scr2ln"], rs["st"]])
                        em.do("act", lambda h, half=half, pst=pst: h.copy(
                            out=sb["kbf"][:, half * 512:(half + 1) * 512], in_=pst[:, :]),
                            reads=[rs[psn]], writes=[rs["kbf"]])
                    else:
                        em.do("act", lambda h, half=half, pst=pst: h.copy(
                            out=sb["vbf"][:, half * 512:(half + 1) * 512], in_=pst[:, :]),
                            reads=[rs[psn]], writes=[rs["vbf"]])
            em.do("dve", lambda h: h.tensor_add(sb["st"][:, 12:13], sb["st"][:, 8:9], sb["st"][:, 9:10]),
                  reads=[rs["st"]], writes=[rs["st"]])
            em.do("dve", lambda h: h.tensor_add(sb["st"][:, 13:14], sb["st"][:, 10:11], sb["st"][:, 11:12]),
                  reads=[rs["st"]], writes=[rs["st"]])
            em.do("act", lambda h: h.activation(out=sb["st"][:, 14:15], in_=sb["st"][:, 12:13], func=AF.Sqrt),
                  reads=[rs["st"]], writes=[rs["st"]])
            em.do("act", lambda h: h.activation(out=sb["st"][:, 15:16], in_=sb["st"][:, 13:14], func=AF.Sqrt),
                  reads=[rs["st"]], writes=[rs["st"]])
            em.do("dve", lambda h, tt=tt: h.reciprocal(sb["rsq"][:, tt:tt + 1], sb["st"][:, 14:15]),
                  reads=[rs["st"]], writes=[rs["rsq"]])
            em.do("dve", lambda h: h.reciprocal(sb["st"][:, 15:16], sb["st"][:, 15:16]),
                  reads=[rs["st"]], writes=[rs["st"]])
            em.do("dve", lambda h: h.scalar_tensor_tensor(out=sb["prod"][:, :], in0=sb["kbf"][:, :],
                                                          scalar=sb["st"][:, 15:16], in1=sb["vbf"][:, :],
                                                          op0=ALU.mult, op1=ALU.mult),
                  reads=[rs["kbf"], rs["st"], rs["vbf"]], writes=[rs["prod"]])
            em.do("pe", lambda h, b=b, tt=tt: h.matmul(ps["kv1"][0:1, :], sb["onesf"][:, 0:1],
                                                       sb["prod"][:, 0:512],
                                                       start=(tt == 0), stop=(tt == NTT - 1), skip_group_check=True),
                  reads=[rs["onesf"], rs["prod"]], writes=[rs["ps_kv1"]])
            em.do("pe", lambda h, b=b, tt=tt: h.matmul(ps["kv2"][0:1, :], sb["onesf"][:, 0:1],
                                                       sb["prod"][:, 512:1024],
                                                       start=(tt == 0), stop=(tt == NTT - 1), skip_group_check=True),
                  reads=[rs["onesf"], rs["prod"]], writes=[rs["ps_kv2"]])
        em.do("act", lambda h: h.copy(out=sb["kvrow"][0:1, 0:512], in_=ps["kv1"][0:1, :]),
              reads=[rs["ps_kv1"]], writes=[rs["kvrow"]])
        em.do("act", lambda h: h.copy(out=sb["kvrow"][0:1, 512:1024], in_=ps["kv2"][0:1, :]),
              reads=[rs["ps_kv2"]], writes=[rs["kvrow"]])
        for half in range(2):
            pst, psn = (ps["a"], "ps_a") if half == 0 else (ps["b"], "ps_b")
            em.do("pe", lambda h, half=half, pst=pst: h.matmul(
                pst[:, :], sb["onesf"][0:1, :], sb["kvrow"][0:1, half * 512:(half + 1) * 512],
                start=True, stop=True),
                reads=[rs["onesf"], rs["kvrow"]], writes=[rs[psn]])
            em.do("act", lambda h, half=half, pst=pst: h.copy(
                out=sb["kvb"][:, half * 512:(half + 1) * 512], in_=pst[:, :]),
                reads=[rs[psn]], writes=[rs["kvb"]])
        for tt in range(NTT):
            em.do("dve", lambda h, tt=tt: h.scalar_tensor_tensor(
                out=sb["obf"][:, :], in0=sb["qbf"][:, tt, :], scalar=sb["rsq"][:, tt:tt + 1],
                in1=sb["kvb"][:, :], op0=ALU.mult, op1=ALU.mult),
                reads=[rs["qbf"], rs["rsq"], rs["kvb"]], writes=[rs["obf"]])
            for ki in range(8):
                em.do("pe", lambda h, ki=ki: h.transpose(ps["t"][:, :], sb["obf"][:, ki * 128:(ki + 1) * 128],
                                                         sb["ident"][:, :]),
                      reads=[rs["obf"], rs["ident"]], writes=[rs["ps_t"]])
                eng = "dve" if ki % 2 == 0 else "act"
                fn = (lambda h, ki=ki: h.tensor_copy(sb["xT"][:, ki, :], ps["t"][:, :])) if eng == "dve" else \
                     (lambda h, ki=ki: h.copy(out=sb["xT"][:, ki, :], in_=ps["t"][:, :]))
                em.do(eng, fn, reads=[rs["ps_t"]], writes=[rs["xT"]])
            em.do("sp", lambda h, b=b, tt=tt: h.dma_start(out=sb["xr"][:, :], in_=x_in[b, tt * 128:(tt + 1) * 128, :]),
                  writes=[rs["xr"]], dma=True)
            for half in range(2):
                pst, psn = (ps["a"], "ps_a") if half == 0 else (ps["b"], "ps_b")
                for ki in range(8):
                    em.do("pe", lambda h, ki=ki, half=half, pst=pst: h.matmul(
                        pst[:, :], sb["xT"][:, ki, :], sb["wout"][:, ki, half * 512:(half + 1) * 512],
                        start=(ki == 0), stop=False),
                        reads=[rs["xT"], rs["wout"]], writes=[rs[psn]])
                em.do("pe", lambda h, half=half, pst=pst: h.matmul(
                    pst[:, :], sb["onesf"][0:1, :], sb["bo_row"][0:1, half * 512:(half + 1) * 512],
                    start=False, stop=True),
                    reads=[rs["onesf"], rs["bo_row"]], writes=[rs[psn]])
                em.do("dve", lambda h, half=half, pst=pst: h.tensor_add(
                    sb["acc"][:, half * 512:(half + 1) * 512], pst[:, :],
                    sb["xr"][:, half * 512:(half + 1) * 512]),
                    reads=[rs[psn], rs["xr"]], writes=[rs["acc"]])
            em.do("sp", lambda h, b=b, tt=tt: h.dma_start(out=x3_out[b, tt * 128:(tt + 1) * 128, :],
                                                          in_=sb["acc"][:, :]),
                  reads=[rs["acc"]], writes=[rs["x3_o"]], dma=True)


def build_l1attn_nc(nb):
    from contextlib import ExitStack
    nc = bass.Bass("TRN2", target_bir_lowering=False)
    x_in = nc.dram_tensor("x_in", [nb, T, D], F32, kind="ExternalInput").ap()
    x3_out = nc.dram_tensor("x3_out", [nb, T, D], F32, kind="ExternalOutput").ap()
    wqkv = nc.dram_tensor("wqkv", [D, 3 * D], F32, kind="ExternalInput").ap()
    bqkv = nc.dram_tensor("bqkv", [3 * D], F32, kind="ExternalInput").ap()
    wout = nc.dram_tensor("wout", [D, D], F32, kind="ExternalInput").ap()
    bout = nc.dram_tensor("bout", [D], F32, kind="ExternalInput").ap()
    hc = l0ffn_consts(nc)
    es = ExitStack()
    with es:
        sb, ps = alloc_l1attn(nc, es)
        trk = Trk(nc, es)
        em = Emit(trk)
        rs = {k: Rsrc(k) for k in L1A_RS}
        block = es.enter_context(nc.Block())
        emit_l1_attn(nc, em, sb, ps, rs, x_in, x3_out, wqkv, bqkv, wout, bout, nb, hc)
        trk.wait("sp", [rs["x3_o"].w])
        trk.replay(block)
    return nc



L1F_RS = ["ident", "onesf", "konst", "rw8", "w1sb", "w2sb", "xlnT", "gate", "gatebf", "gb",
          "grow", "hbf", "hsp", "yev", "xt", "xbf", "xT", "st", "scr2ln", "xr", "acc",
          "neginf", "l2", "is1", "is2", "gcol",
          "ps_a", "ps_b", "ps_t", "ps_lg8", "y_acc", "out_o"]


def alloc_l1ffn(nc, es):
    sb, ps = {}, {}
    sb["ident"] = es.enter_context(nc.sbuf_tensor("m_ident", [128, 128], BF16))
    sb["onesf"] = es.enter_context(nc.sbuf_tensor("m_onesf", [128, 128], BF16))
    sb["konst"] = es.enter_context(nc.sbuf_tensor("m_konst", [128, 4], F32))
    sb["rw8"] = es.enter_context(nc.sbuf_tensor("m_rw8", [128, 8, 8], BF16))
    sb["w1sb"] = es.enter_context(nc.sbuf_tensor("m_w1sb", [128, 8, 2048], BF16))
    sb["w2sb"] = es.enter_context(nc.sbuf_tensor("m_w2sb", [128, 16, 1024], BF16))
    sb["xlnT"] = es.enter_context(nc.sbuf_tensor("m_xlnT", [128, 8, 4096], BF16))
    sb["gate"] = es.enter_context(nc.sbuf_tensor("m_gate", [128, 32, 8], F32))
    sb["gatebf"] = es.enter_context(nc.sbuf_tensor("m_gatebf", [128, 32, 8], BF16))
    sb["gb"] = es.enter_context(nc.sbuf_tensor("m_gb", [128, 512], BF16))
    sb["grow"] = es.enter_context(nc.sbuf_tensor("m_grow", [1, 512], BF16))
    sb["hbf"] = es.enter_context(nc.sbuf_tensor("m_hbf", [128, 512], BF16))
    sb["hsp"] = es.enter_context(nc.sbuf_tensor("m_hsp", [128, 16, 512], BF16))
    sb["yev"] = es.enter_context(nc.sbuf_tensor("m_yev", [128, 1024], F32))
    sb["xt"] = es.enter_context(nc.sbuf_tensor("m_xt", [128, 1024], F32))
    sb["xbf"] = es.enter_context(nc.sbuf_tensor("m_xbf", [128, 1024], BF16))
    sb["xT"] = es.enter_context(nc.sbuf_tensor("m_xT", [128, 8, 128], BF16))
    sb["st"] = es.enter_context(nc.sbuf_tensor("m_st", [128, 16], F32))
    sb["scr2ln"] = es.enter_context(nc.sbuf_tensor("m_scr2ln", [128, 1024], F32))
    sb["xr"] = es.enter_context(nc.sbuf_tensor("m_xr", [128, 1024], F32))
    sb["acc"] = es.enter_context(nc.sbuf_tensor("m_acc", [128, 1024], F32))
    sb["neginf"] = es.enter_context(nc.sbuf_tensor("m_neginf", [128, 8], F32))
    sb["l2"] = es.enter_context(nc.sbuf_tensor("m_l2", [128, 8], F32))
    sb["is1"] = es.enter_context(nc.sbuf_tensor("m_is1", [128, 8], F32))
    sb["is2"] = es.enter_context(nc.sbuf_tensor("m_is2", [128, 8], F32))
    sb["gcol"] = es.enter_context(nc.sbuf_tensor("m_gcol", [128, 8], F32))
    ps["a"] = es.enter_context(nc.psum_tensor("m_ps_a", [128, 512], F32))
    ps["b"] = es.enter_context(nc.psum_tensor("m_ps_b", [128, 512], F32))
    ps["t"] = es.enter_context(nc.psum_tensor("m_ps_t", [128, 128], BF16))
    ps["lg8"] = es.enter_context(nc.psum_tensor("m_ps_lg8", [128, 8], F32))
    ps["gr"] = es.enter_context(nc.psum_tensor("m_ps_gr", [1, 128], BF16))
    ps["gbp"] = es.enter_context(nc.psum_tensor("m_ps_gbp", [128, 512], F32))
    return sb, ps


def emit_l1_ffn(nc, em, sb, ps, rs, x_in, out_ap, rw8_ap, w1T_ap, w2T_ap, y_acc, hc):
    em.do("sp", lambda h: h.dma_start(out=sb["ident"][:], in_=hc["ident"].ap()), writes=[rs["ident"]], dma=True)
    em.do("sp", lambda h: h.dma_start(out=sb["onesf"][:], in_=hc["ones_full"].ap()), writes=[rs["onesf"]], dma=True)
    em.do("sp", lambda h: h.dma_start(out=sb["konst"][:], in_=hc["konst"].ap()), writes=[rs["konst"]], dma=True)
    em.do("pool", lambda h: h.dma_start(out=sb["rw8"][:, :, :], in_=rw8_ap.rearrange("(k p) n -> p k n", p=128)),
          writes=[rs["rw8"]], dma=True)
    em.do("dve", lambda h: h.memset(sb["neginf"][:, :], -1e30), writes=[rs["neginf"]])

    for j in range(NT_OWN):
        em.do("sp", lambda h, j=j: h.dma_start(out=sb["xt"][:, :], in_=x_in[j * 128:(j + 1) * 128, :]),
              writes=[rs["xt"]], dma=True)
        emit_ln128(em, sb, rs)
        for ki in range(8):
            em.do("pe", lambda h, ki=ki: h.transpose(ps["t"][:, :], sb["xbf"][:, ki * 128:(ki + 1) * 128],
                                                     sb["ident"][:, :]),
                  reads=[rs["xbf"], rs["ident"]], writes=[rs["ps_t"]])
            eng = "dve" if ki % 2 == 0 else "act"
            fn = (lambda h, ki=ki, j=j: h.tensor_copy(sb["xlnT"][:, ki, j * 128:(j + 1) * 128], ps["t"][:, :])) \
                if eng == "dve" else \
                 (lambda h, ki=ki, j=j: h.copy(out=sb["xlnT"][:, ki, j * 128:(j + 1) * 128], in_=ps["t"][:, :]))
            em.do(eng, fn, reads=[rs["ps_t"]], writes=[rs["xlnT"]])
        for ki in range(8):
            em.do("pe", lambda h, ki=ki, j=j: h.matmul(ps["lg8"][:, :], sb["xlnT"][:, ki, j * 128:(j + 1) * 128],
                                                       sb["rw8"][:, ki, :], start=(ki == 0), stop=(ki == 7)),
                  reads=[rs["xlnT"], rs["rw8"]], writes=[rs["ps_lg8"]])
        em.do("dve", lambda h: h.tensor_copy(sb["l2"][:, :], ps["lg8"][:, :]), reads=[rs["ps_lg8"]], writes=[rs["l2"]])
        em.do("dve", lambda h: h.reduce_max(sb["st"][:, 0:1], sb["l2"][:, :], AX.X), reads=[rs["l2"]], writes=[rs["st"]])
        em.do("dve", lambda h: h.tensor_scalar(sb["is1"][:, :], sb["l2"][:, :], sb["st"][:, 0:1], None, ALU.is_equal),
              reads=[rs["l2"], rs["st"]], writes=[rs["is1"]])
        em.do("dve", lambda h: h.scalar_tensor_tensor(out=sb["l2"][:, :], in0=sb["is1"][:, :],
                                                      scalar=-1e30, in1=sb["l2"][:, :],
                                                      op0=ALU.mult, op1=ALU.add),
              reads=[rs["is1"]], writes=[rs["l2"]])
        em.do("dve", lambda h: h.reduce_max(sb["st"][:, 1:2], sb["l2"][:, :], AX.X), reads=[rs["l2"]], writes=[rs["st"]])
        em.do("dve", lambda h: h.tensor_scalar(sb["is2"][:, :], sb["l2"][:, :], sb["st"][:, 1:2], None, ALU.is_equal),
              reads=[rs["l2"], rs["st"]], writes=[rs["is2"]])
        em.do("dve", lambda h: h.tensor_sub(sb["st"][:, 2:3], sb["st"][:, 1:2], sb["st"][:, 0:1]),
              reads=[rs["st"]], writes=[rs["st"]])
        em.do("act", lambda h: h.activation(out=sb["st"][:, 3:4], in_=sb["st"][:, 2:3], func=AF.Exp),
              reads=[rs["st"]], writes=[rs["st"]])
        em.do("dve", lambda h: h.tensor_scalar(sb["st"][:, 4:5], sb["st"][:, 3:4], 1.0, None, ALU.add),
              reads=[rs["st"]], writes=[rs["st"]])
        em.do("dve", lambda h: h.reciprocal(sb["st"][:, 5:6], sb["st"][:, 4:5]), reads=[rs["st"]], writes=[rs["st"]])
        em.do("dve", lambda h: h.tensor_mul(sb["st"][:, 6:7], sb["st"][:, 3:4], sb["st"][:, 5:6]),
              reads=[rs["st"]], writes=[rs["st"]])
        em.do("dve", lambda h: h.tensor_scalar(sb["gcol"][:, :], sb["is1"][:, :], sb["st"][:, 5:6], None, ALU.mult),
              reads=[rs["is1"], rs["st"]], writes=[rs["gcol"]])
        em.do("dve", lambda h, j=j: h.scalar_tensor_tensor(out=sb["gate"][:, j, :], in0=sb["is2"][:, :],
                                                           scalar=sb["st"][:, 6:7], in1=sb["gcol"][:, :],
                                                           op0=ALU.mult, op1=ALU.add),
              reads=[rs["is2"], rs["st"], rs["gcol"]], writes=[rs["gate"]])
        em.do("dve", lambda h, j=j: h.tensor_copy(sb["gatebf"][:, j, :], sb["gate"][:, j, :]),
              reads=[rs["gate"]], writes=[rs["gatebf"]])

    for e in range(8):
        em.do("pool", lambda h, e=e: h.dma_start(out=sb["w1sb"][:, :, :],
                                                 in_=w1T_ap[e].rearrange("(k p) n -> p k n", p=128)),
              writes=[rs["w1sb"]], dma=True)
        em.do("pool", lambda h, e=e: h.dma_start(out=sb["w2sb"][:, :, :],
                                                 in_=w2T_ap[e].rearrange("(k p) n -> p k n", p=128)),
              writes=[rs["w2sb"]], dma=True)
        for sp_ in range(8):
            t0 = sp_ * 512
            for q in range(4):
                em.do("pe", lambda h, e=e, sp_=sp_, q=q: h.transpose(
                    ps["gr"][0:1, :], sb["gatebf"][:, sp_ * 4 + q, e:e + 1], sb["ident"][:, :]),
                    reads=[rs["gatebf"], rs["ident"]], writes=[rs["ps_t"]])
                em.do("act", lambda h, q=q: h.copy(out=sb["grow"][0:1, q * 128:(q + 1) * 128], in_=ps["gr"][0:1, :]),
                      reads=[rs["ps_t"]], writes=[rs["grow"]])
            em.do("pe", lambda h: h.matmul(ps["gbp"][:, :], sb["onesf"][0:1, :], sb["grow"][0:1, :],
                                           start=True, stop=True),
                  reads=[rs["onesf"], rs["grow"]], writes=[rs["ps_a"]])
            em.do("act", lambda h: h.copy(out=sb["gb"][:, :], in_=ps["gbp"][:, :]),
                  reads=[rs["ps_a"]], writes=[rs["gb"]])
            for f in range(16):
                for ki in range(8):
                    em.do("pe", lambda h, f=f, ki=ki, t0=t0: h.matmul(
                        ps["a"][:, :], sb["w1sb"][:, ki, f * 128:(f + 1) * 128],
                        sb["xlnT"][:, ki, t0:t0 + 512], start=(ki == 0), stop=(ki == 7)),
                        reads=[rs["w1sb"], rs["xlnT"]], writes=[rs["ps_a"]])
                em.do("act", lambda h: h.activation(out=sb["hbf"][:, :], in_=ps["a"][:, :], func=AF.Gelu),
                      reads=[rs["ps_a"]], writes=[rs["hbf"]])
                em.do("dve", lambda h, f=f: h.tensor_mul(sb["hsp"][:, f, :], sb["hbf"][:, :], sb["gb"][:, :]),
                      reads=[rs["hbf"], rs["gb"]], writes=[rs["hsp"]])
            for tq in range(4):
                for half in range(2):
                    pst, psn = (ps["a"], "ps_a") if half == 0 else (ps["b"], "ps_b")
                    for f in range(16):
                        em.do("pe", lambda h, f=f, tq=tq, half=half, pst=pst: h.matmul(
                            pst[:, :], sb["hsp"][:, f, tq * 128:(tq + 1) * 128],
                            sb["w2sb"][:, f, half * 512:(half + 1) * 512], start=(f == 0), stop=(f == 15)),
                            reads=[rs["hsp"], rs["w2sb"]], writes=[rs[psn]])
                    em.do("dve", lambda h, half=half, pst=pst: h.tensor_copy(
                        sb["yev"][:, half * 512:(half + 1) * 512], pst[:, :]),
                        reads=[rs[psn]], writes=[rs["yev"]])
                tg = t0 + tq * 128
                if e == 0:
                    em.do("pool", lambda h, tg=tg: h.dma_start(out=y_acc[tg:tg + 128, :], in_=sb["yev"][:, :]),
                          reads=[rs["yev"]], writes=[rs["y_acc"]], dma=True)
                else:
                    em.do("pool", lambda h, tg=tg: h.dma_start(out=y_acc[tg:tg + 128, :], in_=sb["yev"][:, :],
                                                               accum_op=ALU.add),
                          reads=[rs["yev"]], writes=[rs["y_acc"]], dma=True)

    for j in range(NT_OWN):
        em.do("sp", lambda h, j=j: h.dma_start(out=sb["xr"][:, :], in_=x_in[j * 128:(j + 1) * 128, :]),
              writes=[rs["xr"]], dma=True)
        em.do("sp", lambda h, j=j: h.dma_start(out=sb["xt"][:, :], in_=y_acc[j * 128:(j + 1) * 128, :]),
              reads=[rs["y_acc"]], writes=[rs["xt"]], dma=True)
        em.do("dve", lambda h: h.tensor_add(sb["acc"][:, :], sb["xt"][:, :], sb["xr"][:, :]),
              reads=[rs["xt"], rs["xr"]], writes=[rs["acc"]])
        em.do("sp", lambda h, j=j: h.dma_start(out=out_ap[j * 128:(j + 1) * 128, :], in_=sb["acc"][:, :]),
              reads=[rs["acc"]], writes=[rs["out_o"]], dma=True)


def build_l1ffn_nc():
    from contextlib import ExitStack
    nc = bass.Bass("TRN2", target_bir_lowering=False)
    x_in = nc.dram_tensor("x_in", [4096, D], F32, kind="ExternalInput").ap()
    out_ap = nc.dram_tensor("out", [4096, D], F32, kind="ExternalOutput").ap()
    rw8 = nc.dram_tensor("rw8", [D, 8], F32, kind="ExternalInput").ap()
    w1T = nc.dram_tensor("w1T", [8, D, 2048], F32, kind="ExternalInput").ap()
    w2T = nc.dram_tensor("w2T", [8, 2048, D], F32, kind="ExternalInput").ap()
    y_acc = nc.dram_tensor("y_acc", [4096, D], F32, kind="Internal").ap()
    hc = l0ffn_consts(nc)
    es = ExitStack()
    with es:
        sb, ps = alloc_l1ffn(nc, es)
        trk = Trk(nc, es)
        em = Emit(trk)
        rs = {k: Rsrc(k) for k in L1F_RS}
        block = es.enter_context(nc.Block())
        emit_l1_ffn(nc, em, sb, ps, rs, x_in, out_ap, rw8, w1T, w2T, y_acc, hc)
        trk.wait("sp", [rs["out_o"].w])
        trk.replay(block)
    return nc


def trk_barrier(trk):
    trk.barrier()


def build_fused3_nc():
    from contextlib import ExitStack
    nc = bass.Bass("TRN2", target_bir_lowering=False)
    x1_full = nc.dram_tensor("x1_full", [NB * T, D], F32, kind="ExternalInput").ap()
    x1_own = nc.dram_tensor("x1_own", [4096, D], F32, kind="ExternalInput").ap()
    rankoh = nc.dram_tensor("rankoh", [1, 512], F32, kind="ExternalInput").ap()
    rwT = nc.dram_tensor("rwT", [D, E64], F32, kind="ExternalInput").ap()
    wsgu = nc.dram_tensor("wsgu", [D, 1536], F32, kind="ExternalInput").ap()
    wsd = nc.dram_tensor("wsd", [768, D], F32, kind="ExternalInput").ap()
    wg = nc.dram_tensor("wg", [E64, D, 768], F32, kind="ExternalInput").ap()
    wu = nc.dram_tensor("wu", [E64, D, 768], F32, kind="ExternalInput").ap()
    wd = nc.dram_tensor("wd", [E64, 768, D], F32, kind="ExternalInput").ap()
    wqkv = nc.dram_tensor("wqkv", [D, 3 * D], F32, kind="ExternalInput").ap()
    bqkv = nc.dram_tensor("bqkv", [3 * D], F32, kind="ExternalInput").ap()
    wout = nc.dram_tensor("wout", [D, D], F32, kind="ExternalInput").ap()
    bout = nc.dram_tensor("bout", [D], F32, kind="ExternalInput").ap()
    rw8 = nc.dram_tensor("rw8", [D, 8], F32, kind="ExternalInput").ap()
    w1T = nc.dram_tensor("w1T", [8, D, 2048], F32, kind="ExternalInput").ap()
    w2T = nc.dram_tensor("w2T", [8, 2048, D], F32, kind="ExternalInput").ap()
    out_ap = nc.dram_tensor("out", [4096, D], F32, kind="ExternalOutput").ap()

    xb_d = nc.dram_tensor("xb_d", [E64 * CAP, D], BF16, kind="Internal").ap()
    w_d = nc.dram_tensor("w_d", [E64 * CAP, 2], F32, kind="Internal").ap()
    y_d = nc.dram_tensor("y_d", [E64 * CAP, D], BF16, kind="Internal").ap()
    s8u_d = nc.dram_tensor("s8u_d", [NT_OWN, 128, 8], U32, kind="Internal").ap()
    rbase_d = nc.dram_tensor("rbase_d", [4096, D], F32, kind="Internal").ap()
    x2_d = nc.dram_tensor("x2_d", [4096, D], F32, kind="Internal")
    x3_d = nc.dram_tensor("x3_d", [4096, D], F32, kind="Internal")
    y_acc = nc.dram_tensor("y_acc", [4096, D], F32, kind="Internal").ap()

    hc = l0ffn_consts(nc)
    es = ExitStack()
    with es:
        trk = Trk(nc, es)
        em = Emit(trk)
        with ExitStack() as es1:
            sb, ps = alloc_l0ffn(nc, es1)
            rs = {k: Rsrc(k) for k in L0FFN_RS}
            emit_l0ffn_R(nc, em, sb, ps, rs, x1_full, rankoh, rwT, hc)
            emit_l0ffn_P(nc, em, sb, ps, rs, x1_own, rwT, xb_d, w_d, s8u_d, hc)
            emit_l0ffn_S(nc, em, sb, ps, rs, x1_own, wsgu, wsd, rbase_d)
            emit_l0ffn_E(nc, em, sb, ps, rs, wg, wu, wd, xb_d, w_d, y_d)
            emit_l0ffn_G(nc, em, sb, ps, rs, rbase_d, s8u_d, y_d, x2_d.ap())
            trk_barrier(trk)
        with ExitStack() as es2:
            sb, ps = alloc_l1attn(nc, es2)
            rs = {k: Rsrc(k) for k in L1A_RS}
            x2v = x2_d.ap().rearrange("(b t) d -> b t d", b=2)
            x3v = x3_d.ap().rearrange("(b t) d -> b t d", b=2)
            emit_l1_attn(nc, em, sb, ps, rs, x2v, x3v, wqkv, bqkv, wout, bout, 2, hc)
            trk_barrier(trk)
        with ExitStack() as es3:
            sb, ps = alloc_l1ffn(nc, es3)
            rs = {k: Rsrc(k) for k in L1F_RS}
            emit_l1_ffn(nc, em, sb, ps, rs, x3_d.ap(), out_ap, rw8, w1T, w2T, y_acc, hc)
            trk_barrier(trk)
        block = es.enter_context(nc.Block())
        trk.replay(block)
    return nc


def build_fullfused_nc():
    from contextlib import ExitStack
    nc = bass.Bass("TRN2", target_bir_lowering=False)
    x_in = nc.dram_tensor("x_in", [NB, T, D], F32, kind="ExternalInput").ap()
    ownrows = nc.dram_tensor("ownrows", [NT_OWN, 128, 1], U32, kind="ExternalInput").ap()
    rankoh = nc.dram_tensor("rankoh", [1, 512], F32, kind="ExternalInput").ap()
    wq_a = {w: nc.dram_tensor(w, [D, D], F32, kind="ExternalInput").ap()
            for w in ("wq", "wk", "wv", "wo")}
    rwT = nc.dram_tensor("rwT", [D, E64], F32, kind="ExternalInput").ap()
    wsgu = nc.dram_tensor("wsgu", [D, 1536], F32, kind="ExternalInput").ap()
    wsd = nc.dram_tensor("wsd", [768, D], F32, kind="ExternalInput").ap()
    wg = nc.dram_tensor("wg", [E64, D, 768], F32, kind="ExternalInput").ap()
    wu = nc.dram_tensor("wu", [E64, D, 768], F32, kind="ExternalInput").ap()
    wd = nc.dram_tensor("wd", [E64, 768, D], F32, kind="ExternalInput").ap()
    wqkv = nc.dram_tensor("wqkv", [D, 3 * D], F32, kind="ExternalInput").ap()
    bqkv = nc.dram_tensor("bqkv", [3 * D], F32, kind="ExternalInput").ap()
    wout = nc.dram_tensor("wout", [D, D], F32, kind="ExternalInput").ap()
    bout = nc.dram_tensor("bout", [D], F32, kind="ExternalInput").ap()
    rw8 = nc.dram_tensor("rw8", [D, 8], F32, kind="ExternalInput").ap()
    w1T = nc.dram_tensor("w1T", [8, D, 2048], F32, kind="ExternalInput").ap()
    w2T = nc.dram_tensor("w2T", [8, 2048, D], F32, kind="ExternalInput").ap()
    out_ap = nc.dram_tensor("out", [4096, D], F32, kind="ExternalOutput").ap()

    x1_d = nc.dram_tensor("x1_d", [NB * T, D], F32, kind="Internal")
    xb_d = nc.dram_tensor("xb_d", [E64 * CAP, D], BF16, kind="Internal").ap()
    w_d = nc.dram_tensor("w_d", [E64 * CAP, 2], F32, kind="Internal").ap()
    y_d = nc.dram_tensor("y_d", [E64 * CAP, D], BF16, kind="Internal").ap()
    s8u_d = nc.dram_tensor("s8u_d", [NT_OWN, 128, 8], U32, kind="Internal").ap()
    rbase_d = nc.dram_tensor("rbase_d", [4096, D], F32, kind="Internal").ap()
    x2_d = nc.dram_tensor("x2_d", [4096, D], F32, kind="Internal")
    x3_d = nc.dram_tensor("x3_d", [4096, D], F32, kind="Internal")
    y_acc = nc.dram_tensor("y_acc", [4096, D], F32, kind="Internal").ap()

    h_ident, h_tri, h_konst = l0_attn_consts(nc)
    hc = l0ffn_consts(nc)
    es = ExitStack()
    with es:
        trk = Trk(nc, es)
        em = Emit(trk)
        with ExitStack() as es0:
            sb, ps = alloc_l0_attn(nc, es0)
            rs = {k: Rsrc(k) for k in L0_RS_KEYS}
            x1v = x1_d.ap().rearrange("(b t) d -> b t d", b=NB)
            for b in range(NB):
                em.do("pool", lambda h, b=b: h.dma_start(out=x1v[b, 0:QR0, :], in_=x_in[b, 0:QR0, :]),
                      writes=[rs["out_0"]], dma=True)
            emit_l0_attn(nc, em, sb, ps, rs, x_in, x1v, wq_a, NB, h_ident, h_tri, h_konst,
                         out_base=QR0)
            trk_barrier(trk)
        with ExitStack() as es1:
            sb, ps = alloc_l0ffn(nc, es1)
            rs = {k: Rsrc(k) for k in L0FFN_RS}
            emit_l0ffn_R(nc, em, sb, ps, rs, x1_d.ap(), rankoh, rwT, hc)
            emit_l0ffn_P(nc, em, sb, ps, rs, None, rwT, xb_d, w_d, s8u_d, hc,
                         ownrows=ownrows, x1_dram=x1_d.ap())
            emit_l0ffn_S(nc, em, sb, ps, rs, None, wsgu, wsd, rbase_d,
                         ownrows=ownrows, x1_dram=x1_d.ap())
            emit_l0ffn_E(nc, em, sb, ps, rs, wg, wu, wd, xb_d, w_d, y_d)
            emit_l0ffn_G(nc, em, sb, ps, rs, rbase_d, s8u_d, y_d, x2_d.ap())
            trk_barrier(trk)
        with ExitStack() as es2:
            sb, ps = alloc_l1attn(nc, es2)
            rs = {k: Rsrc(k) for k in L1A_RS}
            x2v = x2_d.ap().rearrange("(b t) d -> b t d", b=2)
            x3v = x3_d.ap().rearrange("(b t) d -> b t d", b=2)
            emit_l1_attn(nc, em, sb, ps, rs, x2v, x3v, wqkv, bqkv, wout, bout, 2, hc)
            trk_barrier(trk)
        with ExitStack() as es3:
            sb, ps = alloc_l1ffn(nc, es3)
            rs = {k: Rsrc(k) for k in L1F_RS}
            emit_l1_ffn(nc, em, sb, ps, rs, x3_d.ap(), out_ap, rw8, w1T, w2T, y_acc, hc)
            trk_barrier(trk)
        block = es.enter_context(nc.Block())
        trk.replay(block)
    return nc



def alloc_l0_attn2(nc, es):
    sb, ps = {}, {}
    sb["xlnT"] = es.enter_context(nc.sbuf_tensor("xlnT", [128, 8, 520], BF16))
    sb["qf"] = es.enter_context(nc.sbuf_tensor("qf", [128, 8, 516], BF16))
    sb["kf"] = es.enter_context(nc.sbuf_tensor("kf", [128, 8, 516], BF16))
    sb["vt"] = es.enter_context(nc.sbuf_tensor("vt", [128, 5, 1024], BF16))
    sb["ao"] = es.enter_context(nc.sbuf_tensor("ao", [128, 8, 516], BF16))
    for w in ("wq", "wk", "wv", "wo"):
        sb[w] = es.enter_context(nc.sbuf_tensor(w + "_sb", [128, 8, 1024], BF16))
    for p in range(2):
        sb[f"xt{p}"] = es.enter_context(nc.sbuf_tensor(f"xt{p}", [128, 1024], F32))
        sb[f"xsq{p}"] = es.enter_context(nc.sbuf_tensor(f"xsq{p}", [128, 1024], F32))
        sb[f"xlnb{p}"] = es.enter_context(nc.sbuf_tensor(f"xlnb{p}", [128, 1024], BF16))
        sb[f"st{p}"] = es.enter_context(nc.sbuf_tensor(f"st{p}", [128, 8], F32))
        sb[f"sta{p}"] = es.enter_context(nc.sbuf_tensor(f"sta{p}", [128, 4], F32))
        sb[f"lg{p}"] = es.enter_context(nc.sbuf_tensor(f"lg{p}", [128, 516], F32))
        sb[f"esc{p}"] = es.enter_context(nc.sbuf_tensor(f"esc{p}", [128, 516], BF16))
        sb[f"dg{p}"] = es.enter_context(nc.sbuf_tensor(f"dg{p}", [128, 128], BF16))
        sb[f"wT{p}"] = es.enter_context(nc.sbuf_tensor(f"wT{p}", [128, 5 * 128], BF16))
        sb[f"xr{p}"] = es.enter_context(nc.sbuf_tensor(f"xr{p}", [128, 1024], F32))
        sb[f"xo{p}"] = es.enter_context(nc.sbuf_tensor(f"xo{p}", [128, 1024], F32))
    sb["ident"] = es.enter_context(nc.sbuf_tensor("ident", [128, 128], BF16))
    sb["mtri"] = es.enter_context(nc.sbuf_tensor("mtri", [128, 128], F32))
    sb["konst"] = es.enter_context(nc.sbuf_tensor("konst", [128, 4], F32))
    ps["t"] = es.enter_context(nc.psum_tensor("ps_t", [128, 128], BF16))
    ps["s"] = es.enter_context(nc.psum_tensor("ps_s", [128, 8], F32))
    for p in range(2):
        ps[f"comb{p}"] = es.enter_context(nc.psum_tensor(f"ps_comb{p}", [128, 512], F32))
        ps[f"t2{p}"] = es.enter_context(nc.psum_tensor(f"ps_t2{p}", [128, 128], F32))
        ps[f"h{p}"] = es.enter_context(nc.psum_tensor(f"ps_h{p}", [64, 128], F32))
    return sb, ps


L0_RS2 = (["ident", "mtri", "konst", "xlnT", "qf", "kf", "vt", "ao",
           "wq", "wk", "wv", "wo", "ps_t", "ps_s", "out_0", "out_1"] +
          [f"{k}{p}" for p in range(2) for k in
           ("xt", "xsq", "xlnb", "st", "sta", "lg", "esc", "dg", "wT", "xr", "xo",
            "ps_comb", "ps_t2", "ps_h")])


def emit_l0_attn2(nc, em, sb, ps, rs, x_in, xq_out, w_aps, nb, h_ident, h_tri, h_konst,
                  out_base=0):
    em.do("sp", lambda h: h.dma_start(out=sb["ident"][:], in_=h_ident.ap()), writes=[rs["ident"]], dma=True)
    em.do("sp", lambda h: h.dma_start(out=sb["mtri"][:], in_=h_tri.ap()), writes=[rs["mtri"]], dma=True)
    em.do("sp", lambda h: h.dma_start(out=sb["konst"][:], in_=h_konst.ap()), writes=[rs["konst"]], dma=True)
    for w in ("wq", "wk", "wv", "wo"):
        src = w_aps[w].rearrange("(k p) n -> p k n", p=128)
        em.do("pool", lambda h, w=w, src=src: h.dma_start(out=sb[w][:, :, :], in_=src),
              writes=[rs[w]], dma=True)

    for b in range(nb):
        for ti, (r0, p_) in enumerate(L0_LN_TILES):
            c = ti % 2
            xt, xsq, xlnb, st = f"xt{c}", f"xsq{c}", f"xlnb{c}", f"st{c}"
            em.do("sp", lambda h, b=b, r0=r0, p_=p_, xt=xt: h.dma_start(
                out=sb[xt][0:p_, :], in_=x_in[b, r0:r0 + p_, :]), writes=[rs[xt]], dma=True)
            em.do("act", lambda h, p_=p_, xt=xt, xsq=xsq, st=st: h.activation(
                out=sb[xsq][0:p_, :], in_=sb[xt][0:p_, :], func=AF.Square, accum_out=sb[st][0:p_, 0:1]),
                reads=[rs[xt]], writes=[rs[xsq], rs[st]])
            em.do("dve", lambda h, p_=p_, xt=xt, st=st: h.reduce_sum(
                sb[st][0:p_, 1:2], sb[xt][0:p_, :], AX.X, negate=True),
                reads=[rs[xt]], writes=[rs[st]])
            em.do("dve", lambda h, p_=p_, st=st: h.tensor_scalar(
                sb[st][0:p_, 2:3], sb[st][0:p_, 1:2], 1.0 / D, None, ALU.mult),
                reads=[rs[st]], writes=[rs[st]])
            em.do("dve", lambda h, p_=p_, st=st: h.tensor_scalar(
                sb[st][0:p_, 3:4], sb[st][0:p_, 0:1], 1.0 / D, None, ALU.mult),
                reads=[rs[st]], writes=[rs[st]])
            em.do("dve", lambda h, p_=p_, st=st: h.scalar_tensor_tensor(
                out=sb[st][0:p_, 4:5], in0=sb[st][0:p_, 2:3], scalar=sb[st][0:p_, 2:3],
                in1=sb[st][0:p_, 3:4], op0=ALU.mult, op1=ALU.subtract),
                reads=[rs[st]], writes=[rs[st]])
            em.do("act", lambda h, p_=p_, st=st: h.activation(
                out=sb[st][0:p_, 5:6], in_=sb[st][0:p_, 4:5], func=AF.Sqrt,
                bias=sb["konst"][0:p_, 0:1], scale=sb["konst"][0:p_, 1:2]),
                reads=[rs[st], rs["konst"]], writes=[rs[st]])
            em.do("dve", lambda h, p_=p_, st=st: h.reciprocal(sb[st][0:p_, 6:7], sb[st][0:p_, 5:6]),
                reads=[rs[st]], writes=[rs[st]])
            em.do("dve", lambda h, p_=p_, xt=xt, xlnb=xlnb, st=st: h.tensor_scalar(
                sb[xlnb][0:p_, :], sb[xt][0:p_, :], sb[st][0:p_, 2:3], sb[st][0:p_, 6:7],
                ALU.add, ALU.mult),
                reads=[rs[xt], rs[st]], writes=[rs[xlnb]])
            c0, cn = L0_LNT_COLS[ti]
            for ki in range(8):
                tp = f"ps_t2{ki % 2}"
                tpb = f"t2{ki % 2}"
                em.do("pe", lambda h, ki=ki, p_=p_, xlnb=xlnb, tpb=tpb: h.matmul(
                    ps[tpb][0:128, 0:p_], sb[xlnb][0:p_, ki * 128:(ki + 1) * 128],
                    sb["ident"][0:p_, 0:p_], start=True, stop=True),
                    reads=[rs[xlnb], rs["ident"]], writes=[rs[tp]])
                eng = "dve" if ki % 2 == 0 else "act"
                fn = (lambda h, ki=ki, c0=c0, p_=p_, tpb=tpb: h.tensor_copy(
                    sb["xlnT"][:, ki, c0:c0 + p_], ps[tpb][0:128, 0:p_])) if eng == "dve" else \
                     (lambda h, ki=ki, c0=c0, p_=p_, tpb=tpb: h.copy(
                    out=sb["xlnT"][:, ki, c0:c0 + p_], in_=ps[tpb][0:128, 0:p_]))
                em.do(eng, fn, reads=[rs[tp]], writes=[rs["xlnT"]])

        for mo in range(8):
            c = mo % 2
            cb, cbr = ps[f"comb{c}"], rs[f"ps_comb{c}"]
            for ki in range(8):
                em.do("pe", lambda h, mo=mo, ki=ki, cb=cb: h.matmul(
                    cb[:, 0:512], sb["wq"][:, ki, mo * 128:(mo + 1) * 128],
                    sb["xlnT"][:, ki, 4:516], start=(ki == 0), stop=(ki == 7)),
                    reads=[rs["wq"], rs["xlnT"]], writes=[cbr])
                em.do("pe", lambda h, mo=mo, ki=ki: h.matmul(
                    ps["s"][:, 0:4], sb["wq"][:, ki, mo * 128:(mo + 1) * 128],
                    sb["xlnT"][:, ki, 516:520], start=(ki == 0), stop=(ki == 7),
                    skip_group_check=True),
                    reads=[rs["wq"], rs["xlnT"]], writes=[rs["ps_s"]])
            em.do("act", lambda h, mo=mo, cb=cb: h.activation(
                out=sb["qf"][:, mo, 0:512], in_=cb[:, 0:512], func=AF.Copy, scale=sb["konst"][:, 2:3]),
                reads=[cbr, rs["konst"]], writes=[rs["qf"]])
            em.do("act", lambda h, mo=mo: h.activation(
                out=sb["qf"][:, mo, 512:516], in_=ps["s"][:, 0:4], func=AF.Copy, scale=sb["konst"][:, 2:3]),
                reads=[rs["ps_s"], rs["konst"]], writes=[rs["qf"]])
        for mo in range(8):
            c = mo % 2
            cb, cbr = ps[f"comb{c}"], rs[f"ps_comb{c}"]
            for ki in range(8):
                em.do("pe", lambda h, mo=mo, ki=ki, cb=cb: h.matmul(
                    cb[:, 0:512], sb["wk"][:, ki, mo * 128:(mo + 1) * 128],
                    sb["xlnT"][:, ki, 8:520], start=(ki == 0), stop=(ki == 7)),
                    reads=[rs["wk"], rs["xlnT"]], writes=[cbr])
                em.do("pe", lambda h, mo=mo, ki=ki: h.matmul(
                    ps["s"][:, 0:4], sb["wk"][:, ki, mo * 128:(mo + 1) * 128],
                    sb["xlnT"][:, ki, 0:4], start=(ki == 0), stop=(ki == 7),
                    skip_group_check=True),
                    reads=[rs["wk"], rs["xlnT"]], writes=[rs["ps_s"]])
            em.do("dve", lambda h, mo=mo, cb=cb: h.tensor_copy(sb["kf"][:, mo, 4:516], cb[:, 0:512]),
                  reads=[cbr], writes=[rs["kf"]])
            em.do("dve", lambda h, mo=mo: h.tensor_copy(sb["kf"][:, mo, 0:4], ps["s"][:, 0:4]),
                  reads=[rs["ps_s"]], writes=[rs["kf"]])
        for cti, (c0, ck) in enumerate(L0_CT_XLNT):
            for half in range(2):
                cb, cbr = ps[f"comb{half}"], rs[f"ps_comb{half}"]
                for ki in range(8):
                    em.do("pe", lambda h, cti=cti, c0=c0, ck=ck, half=half, ki=ki, cb=cb: h.matmul(
                        cb[0:ck, 0:512], sb["xlnT"][:, ki, c0:c0 + ck],
                        sb["wv"][:, ki, half * 512:(half + 1) * 512], start=(ki == 0), stop=(ki == 7)),
                        reads=[rs["wv"], rs["xlnT"]], writes=[cbr])
                eng = "dve" if half == 0 else "act"
                fn = (lambda h, cti=cti, ck=ck, half=half, cb=cb: h.tensor_copy(
                    sb["vt"][0:ck, cti, half * 512:(half + 1) * 512], cb[0:ck, 0:512])) if eng == "dve" else \
                     (lambda h, cti=cti, ck=ck, half=half, cb=cb: h.copy(
                    out=sb["vt"][0:ck, cti, half * 512:(half + 1) * 512], in_=cb[0:ck, 0:512]))
                em.do(eng, fn, reads=[cbr], writes=[rs["vt"]])

        it = 0
        for hd in range(16):
            mo, po = hd // 2, (hd % 2) * 64
            for qt, (qoff, qsz) in enumerate(L0_QT):
                c = it % 2
                it += 1
                wq_ = qoff + qsz
                cb, cbr = ps[f"comb{c}"], rs[f"ps_comb{c}"]
                hb, hbr = ps[f"h{c}"], rs[f"ps_h{c}"]
                tb, tbr = ps[f"t2{c}"], rs[f"ps_t2{c}"]
                lg, esc, dg, sta, wT = f"lg{c}", f"esc{c}", f"dg{c}", f"sta{c}", f"wT{c}"
                if qt < 4:
                    em.do("pe", lambda h, mo=mo, po=po, qoff=qoff, qsz=qsz, cb=cb: h.matmul(
                        cb[0:qsz, 0:4], sb["qf"][po:po + 64, mo, qoff:qoff + qsz],
                        sb["kf"][po:po + 64, mo, 0:4], start=True, stop=True, skip_group_check=True),
                        reads=[rs["qf"], rs["kf"]], writes=[cbr])
                    em.do("pe", lambda h, mo=mo, po=po, qoff=qoff, qsz=qsz, wq_=wq_, cb=cb: h.matmul(
                        cb[0:qsz, 4:wq_], sb["qf"][po:po + 64, mo, qoff:qoff + qsz],
                        sb["kf"][po:po + 64, mo, 4:wq_], start=True, stop=True, skip_group_check=True),
                        reads=[rs["qf"], rs["kf"]], writes=[cbr])
                    if qt == 0:
                        em.do("dve", lambda h, qsz=qsz, lg=lg, cb=cb: h.tensor_add(
                            sb[lg][0:qsz, 0:128], cb[0:qsz, 0:128], sb["mtri"][0:qsz, 0:128]),
                            reads=[cbr, rs["mtri"]], writes=[rs[lg]])
                    else:
                        em.do("act", lambda h, qsz=qsz, qoff=qoff, lg=lg, cb=cb: h.copy(
                            out=sb[lg][0:qsz, 0:qoff], in_=cb[0:qsz, 0:qoff]),
                            reads=[cbr], writes=[rs[lg]])
                        em.do("dve", lambda h, qsz=qsz, qoff=qoff, lg=lg, cb=cb: h.tensor_add(
                            sb[lg][0:qsz, qoff:qoff + qsz], cb[0:qsz, qoff:qoff + qsz],
                            sb["mtri"][0:qsz, 0:qsz]),
                            reads=[cbr, rs["mtri"]], writes=[rs[lg]])
                else:
                    em.do("pe", lambda h, mo=mo, po=po, qoff=qoff, qsz=qsz, cb=cb: h.matmul(
                        cb[0:qsz, 0:4], sb["qf"][po:po + 64, mo, qoff:qoff + qsz],
                        sb["kf"][po:po + 64, mo, 0:4], start=True, stop=True, skip_group_check=True),
                        reads=[rs["qf"], rs["kf"]], writes=[cbr])
                    em.do("pe", lambda h, mo=mo, po=po, qoff=qoff, qsz=qsz, cb=cb: h.matmul(
                        cb[0:qsz, 4:512], sb["qf"][po:po + 64, mo, qoff:qoff + qsz],
                        sb["kf"][po:po + 64, mo, 4:512], start=True, stop=True, skip_group_check=True),
                        reads=[rs["qf"], rs["kf"]], writes=[cbr])
                    em.do("pe", lambda h, mo=mo, po=po, qoff=qoff, qsz=qsz: h.matmul(
                        ps["s"][0:qsz, 0:4], sb["qf"][po:po + 64, mo, qoff:qoff + qsz],
                        sb["kf"][po:po + 64, mo, 512:516], start=True, stop=True, skip_group_check=True),
                        reads=[rs["qf"], rs["kf"]], writes=[rs["ps_s"]])
                    em.do("act", lambda h, qsz=qsz, lg=lg, cb=cb: h.copy(
                        out=sb[lg][0:qsz, 0:512], in_=cb[0:qsz, 0:512]),
                        reads=[cbr], writes=[rs[lg]])
                    em.do("dve", lambda h, qsz=qsz, lg=lg: h.tensor_add(
                        sb[lg][0:qsz, 512:516], ps["s"][0:qsz, 0:4], sb["mtri"][0:qsz, 0:qsz]),
                        reads=[rs["ps_s"], rs["mtri"]], writes=[rs[lg]])
                em.do("dve", lambda h, qsz=qsz, wq_=wq_, lg=lg, sta=sta: h.reduce_max(
                    sb[sta][0:qsz, 0:1], sb[lg][0:qsz, 0:wq_], AX.X, negate=True),
                    reads=[rs[lg]], writes=[rs[sta]])
                em.do("act", lambda h, qsz=qsz, wq_=wq_, lg=lg, esc=esc, sta=sta: h.activation(
                    out=sb[esc][0:qsz, 0:wq_], in_=sb[lg][0:qsz, 0:wq_], func=AF.Exp,
                    bias=sb[sta][0:qsz, 0:1], accum_out=sb[sta][0:qsz, 1:2]),
                    reads=[rs[lg], rs[sta]], writes=[rs[esc], rs[sta]])
                em.do("dve", lambda h, qsz=qsz, sta=sta: h.reciprocal(sb[sta][0:qsz, 2:3], sb[sta][0:qsz, 1:2]),
                      reads=[rs[sta]], writes=[rs[sta]])
                em.do("dve", lambda h, qsz=qsz, dg=dg, sta=sta: h.tensor_scalar(
                    sb[dg][0:qsz, 0:qsz], sb["ident"][0:qsz, 0:qsz], sb[sta][0:qsz, 2:3], None, ALU.mult),
                    reads=[rs["ident"], rs[sta]], writes=[rs[dg]])
                pieces = []
                for cti, (c0, ck) in enumerate(L0_CT):
                    if c0 >= wq_:
                        break
                    pieces.append((cti, c0, min(ck, wq_ - c0)))
                for cti, c0, ck_ in pieces:
                    em.do("pe", lambda h, c0=c0, ck_=ck_, qsz=qsz, esc=esc, dg=dg, tb=tb: h.matmul(
                        tb[0:ck_, 0:qsz], sb[esc][0:qsz, c0:c0 + ck_], sb[dg][0:qsz, 0:qsz],
                        start=True, stop=True),
                        reads=[rs[esc], rs[dg]], writes=[tbr])
                    eng = "dve" if cti % 2 == 0 else "act"
                    fn = (lambda h, cti=cti, ck_=ck_, qsz=qsz, wT=wT, tb=tb: h.tensor_copy(
                        sb[wT][0:ck_, cti * 128:cti * 128 + qsz], tb[0:ck_, 0:qsz])) if eng == "dve" else \
                         (lambda h, cti=cti, ck_=ck_, qsz=qsz, wT=wT, tb=tb: h.copy(
                        out=sb[wT][0:ck_, cti * 128:cti * 128 + qsz], in_=tb[0:ck_, 0:qsz]))
                    em.do(eng, fn, reads=[tbr], writes=[rs[wT]])
                for j, (cti, c0, ck_) in enumerate(pieces):
                    em.do("pe", lambda h, cti=cti, ck_=ck_, qsz=qsz, hd=hd, wT=wT, hb=hb,
                          st_=(j == 0), sp_=(j == len(pieces) - 1): h.matmul(
                        hb[0:64, 0:qsz], sb["vt"][0:ck_, cti, hd * 64:(hd + 1) * 64],
                        sb[wT][0:ck_, cti * 128:cti * 128 + qsz], start=st_, stop=sp_),
                        reads=[rs["vt"], rs[wT]], writes=[hbr])
                em.do("act", lambda h, qsz=qsz, mo=mo, po=po, qoff=qoff, hb=hb: h.copy(
                    out=sb["ao"][po:po + 64, mo, qoff:qoff + qsz], in_=hb[0:64, 0:qsz]),
                    reads=[hbr], writes=[rs["ao"]])

        for qt, (qoff, qsz) in enumerate(L0_QT):
            c = qt % 2
            xr, xo = f"xr{c}", f"xo{c}"
            em.do("sp", lambda h, b=b, qoff=qoff, qsz=qsz, xr=xr: h.dma_start(
                out=sb[xr][0:qsz, :], in_=x_in[b, QR0 + qoff:QR0 + qoff + qsz, :]),
                writes=[rs[xr]], dma=True)
            for half in range(2):
                cb, cbr = ps[f"comb{half}"], rs[f"ps_comb{half}"]
                for ki in range(8):
                    em.do("pe", lambda h, qoff=qoff, qsz=qsz, half=half, ki=ki, cb=cb: h.matmul(
                        cb[0:qsz, 0:512], sb["ao"][:, ki, qoff:qoff + qsz],
                        sb["wo"][:, ki, half * 512:(half + 1) * 512], start=(ki == 0), stop=(ki == 7)),
                        reads=[rs["ao"], rs["wo"]], writes=[cbr])
                em.do("dve", lambda h, qsz=qsz, half=half, cb=cb, xr=xr, xo=xo: h.tensor_add(
                    sb[xo][0:qsz, half * 512:(half + 1) * 512], cb[0:qsz, 0:512],
                    sb[xr][0:qsz, half * 512:(half + 1) * 512]),
                    reads=[cbr, rs[xr]], writes=[rs[xo]])
            em.do("sp", lambda h, b=b, qoff=qoff, qsz=qsz, xo=xo: h.dma_start(
                out=xq_out[b, out_base + qoff:out_base + qoff + qsz, :], in_=sb[xo][0:qsz, :]),
                reads=[rs[xo]], writes=[rs["out_" + str(b % 2)]], dma=True)


def build_l0_attn2_nc(nb):
    from contextlib import ExitStack
    nc = bass.Bass("TRN2", target_bir_lowering=False)
    x_in = nc.dram_tensor("x_in", [nb, T, D], F32, kind="ExternalInput").ap()
    xq_out = nc.dram_tensor("xq_out", [nb, NQ, D], F32, kind="ExternalOutput").ap()
    w_aps = {w: nc.dram_tensor(w, [D, D], F32, kind="ExternalInput").ap()
             for w in ("wq", "wk", "wv", "wo")}
    h_ident, h_tri, h_konst = l0_attn_consts(nc)
    es = ExitStack()
    with es:
        sb, ps = alloc_l0_attn2(nc, es)
        trk = Trk(nc, es)
        em = Emit(trk)
        rs = {k: Rsrc(k) for k in L0_RS2}
        block = es.enter_context(nc.Block())
        emit_l0_attn2(nc, em, sb, ps, rs, x_in, xq_out, w_aps, nb, h_ident, h_tri, h_konst)
        trk.wait("sp", [rs["out_0"].w, rs["out_1"].w])
        trk.replay(block)
    return nc



L1F_RS2 = (["ident", "onesf", "konst", "rw8", "w1sb0", "w1sb1", "w2sb0", "w2sb1",
            "xlnT", "gate", "gatebf", "grow", "hsp",
            "neginf", "l2", "is1", "is2", "gcol",
            "ps_t", "ps_lg8", "ps_gr", "y_acc", "out_o"] +
           [f"{k}{p}" for p in range(2) for k in
            ("xt", "xbf", "xT", "st", "scr2ln", "xr", "acc", "hbf", "yev", "gb",
             "ps_a", "ps_b")])


def alloc_l1ffn2(nc, es):
    sb, ps = {}, {}
    sb["ident"] = es.enter_context(nc.sbuf_tensor("m_ident", [128, 128], BF16))
    sb["onesf"] = es.enter_context(nc.sbuf_tensor("m_onesf", [128, 128], BF16))
    sb["konst"] = es.enter_context(nc.sbuf_tensor("m_konst", [128, 4], F32))
    sb["rw8"] = es.enter_context(nc.sbuf_tensor("m_rw8", [128, 8, 8], BF16))
    for p in range(2):
        sb[f"w1sb{p}"] = es.enter_context(nc.sbuf_tensor(f"m_w1sb{p}", [128, 8, 2048], BF16))
    w2s = es.enter_context(nc.sbuf_tensor("m_w2sb", [128, 16, 1024], BF16))
    sb["w2sb0"] = w2s
    sb["w2sb1"] = w2s
    sb["xlnT"] = es.enter_context(nc.sbuf_tensor("m_xlnT", [128, 8, 4096], BF16))
    sb["gate"] = es.enter_context(nc.sbuf_tensor("m_gate", [128, 32, 8], F32))
    sb["gatebf"] = es.enter_context(nc.sbuf_tensor("m_gatebf", [128, 32, 8], BF16))
    sb["grow"] = es.enter_context(nc.sbuf_tensor("m_grow", [1, 512], BF16))
    sb["hsp"] = es.enter_context(nc.sbuf_tensor("m_hsp", [128, 16, 512], BF16))
    sb["neginf"] = es.enter_context(nc.sbuf_tensor("m_neginf", [128, 8], F32))
    sb["l2"] = es.enter_context(nc.sbuf_tensor("m_l2", [128, 8], F32))
    sb["is1"] = es.enter_context(nc.sbuf_tensor("m_is1", [128, 8], F32))
    sb["is2"] = es.enter_context(nc.sbuf_tensor("m_is2", [128, 8], F32))
    sb["gcol"] = es.enter_context(nc.sbuf_tensor("m_gcol", [128, 8], F32))
    for p in range(2):
        sb[f"xt{p}"] = es.enter_context(nc.sbuf_tensor(f"m_xt{p}", [128, 1024], F32))
        sb[f"xbf{p}"] = es.enter_context(nc.sbuf_tensor(f"m_xbf{p}", [128, 1024], BF16))
        sb[f"st{p}"] = es.enter_context(nc.sbuf_tensor(f"m_st{p}", [128, 16], F32))

        sb[f"hbf{p}"] = es.enter_context(nc.sbuf_tensor(f"m_hbf{p}", [128, 512], BF16))
        sb[f"gb{p}"] = es.enter_context(nc.sbuf_tensor(f"m_gb{p}", [128, 512], BF16))
        ps[f"a{p}"] = es.enter_context(nc.psum_tensor(f"m_ps_a{p}", [128, 512], F32))
        ps[f"b{p}"] = es.enter_context(nc.psum_tensor(f"m_ps_b{p}", [128, 512], F32))
    xr = es.enter_context(nc.sbuf_tensor("m_xr", [128, 1024], F32))
    sb["xr0"] = xr
    sb["xr1"] = xr
    sb["scr2ln0"] = xr
    sb["scr2ln1"] = xr
    acc = es.enter_context(nc.sbuf_tensor("m_acc", [128, 1024], F32))
    sb["acc0"] = acc
    sb["acc1"] = acc
    sb["yev0"] = acc
    sb["yev1"] = acc
    ps["t"] = es.enter_context(nc.psum_tensor("m_ps_t", [128, 128], BF16))
    ps["lg8"] = es.enter_context(nc.psum_tensor("m_ps_lg8", [128, 8], F32))
    ps["gr"] = es.enter_context(nc.psum_tensor("m_ps_gr", [1, 128], BF16))
    return sb, ps


def emit_ln128v(em, sb, rs, c, xkey="xt", okey="xbf"):
    xk, ok, st, sc = f"{xkey}{c}", f"{okey}{c}", f"st{c}", f"scr2ln{c}"
    em.do("act", lambda h: h.activation(out=sb[sc][:, :], in_=sb[xk][:, :], func=AF.Square,
                                        accum_out=sb[st][:, 0:1]),
          reads=[rs[xk]], writes=[rs[sc], rs[st]])
    em.do("dve", lambda h: h.reduce_sum(sb[st][:, 1:2], sb[xk][:, :], AX.X, negate=True),
          reads=[rs[xk]], writes=[rs[st]])
    em.do("dve", lambda h: h.tensor_scalar(sb[st][:, 2:3], sb[st][:, 1:2], 1.0 / D, None, ALU.mult),
          reads=[rs[st]], writes=[rs[st]])
    em.do("dve", lambda h: h.tensor_scalar(sb[st][:, 3:4], sb[st][:, 0:1], 1.0 / D, None, ALU.mult),
          reads=[rs[st]], writes=[rs[st]])
    em.do("dve", lambda h: h.scalar_tensor_tensor(out=sb[st][:, 4:5], in0=sb[st][:, 2:3],
                                                  scalar=sb[st][:, 2:3], in1=sb[st][:, 3:4],
                                                  op0=ALU.mult, op1=ALU.subtract),
          reads=[rs[st]], writes=[rs[st]])
    em.do("act", lambda h: h.activation(out=sb[st][:, 5:6], in_=sb[st][:, 4:5], func=AF.Sqrt,
                                        bias=sb["konst"][:, 0:1], scale=sb["konst"][:, 1:2]),
          reads=[rs[st], rs["konst"]], writes=[rs[st]])
    em.do("dve", lambda h: h.reciprocal(sb[st][:, 6:7], sb[st][:, 5:6]),
          reads=[rs[st]], writes=[rs[st]])
    em.do("dve", lambda h: h.tensor_scalar(sb[ok][:, :], sb[xk][:, :], sb[st][:, 2:3],
                                           sb[st][:, 6:7], ALU.add, ALU.mult),
          reads=[rs[xk], rs[st]], writes=[rs[ok]])


def emit_l1_ffn2(nc, em, sb, ps, rs, x_in, out_ap, rw8_ap, w1T_ap, w2T_ap, y_acc, hc):
    em.do("sp", lambda h: h.dma_start(out=sb["ident"][:], in_=hc["ident"].ap()), writes=[rs["ident"]], dma=True)
    em.do("sp", lambda h: h.dma_start(out=sb["onesf"][:], in_=hc["ones_full"].ap()), writes=[rs["onesf"]], dma=True)
    em.do("sp", lambda h: h.dma_start(out=sb["konst"][:], in_=hc["konst"].ap()), writes=[rs["konst"]], dma=True)
    em.do("pool", lambda h: h.dma_start(out=sb["rw8"][:, :, :], in_=rw8_ap.rearrange("(k p) n -> p k n", p=128)),
          writes=[rs["rw8"]], dma=True)
    em.do("dve", lambda h: h.memset(sb["neginf"][:, :], -1e30), writes=[rs["neginf"]])

    for j in range(NT_OWN):
        c = j % 2
        xbf = f"xbf{c}"
        em.do("sp", lambda h, j=j, c=c: h.dma_start(out=sb[f"xt{c}"][:, :], in_=x_in[j * 128:(j + 1) * 128, :]),
              writes=[rs[f"xt{c}"]], dma=True)
        emit_ln128v(em, sb, rs, c)
        for ki in range(8):
            pc = ki % 2
            tb, tbr = ps[f"b{pc}"], rs[f"ps_b{pc}"]
            em.do("pe", lambda h, ki=ki, xbf=xbf, tb=tb: h.matmul(
                tb[0:128, 0:128], sb[xbf][:, ki * 128:(ki + 1) * 128], sb["ident"][:, :],
                start=True, stop=True),
                reads=[rs[xbf], rs["ident"]], writes=[tbr])
            eng = "dve" if ki % 2 == 0 else "act"
            fn = (lambda h, ki=ki, j=j, tb=tb: h.tensor_copy(
                sb["xlnT"][:, ki, j * 128:(j + 1) * 128], tb[0:128, 0:128])) \
                if eng == "dve" else \
                 (lambda h, ki=ki, j=j, tb=tb: h.copy(
                out=sb["xlnT"][:, ki, j * 128:(j + 1) * 128], in_=tb[0:128, 0:128]))
            em.do(eng, fn, reads=[tbr], writes=[rs["xlnT"]])
            em.do("pe", lambda h, ki=ki, j=j: h.matmul(
                ps["lg8"][:, :], sb["xlnT"][:, ki, j * 128:(j + 1) * 128], sb["rw8"][:, ki, :],
                start=(ki == 0), stop=(ki == 7), skip_group_check=True),
                reads=[rs["xlnT"], rs["rw8"]], writes=[rs["ps_lg8"]])
        st = f"st{c}"
        em.do("dve", lambda h: h.tensor_copy(sb["l2"][:, :], ps["lg8"][:, :]), reads=[rs["ps_lg8"]], writes=[rs["l2"]])
        em.do("dve", lambda h, st=st: h.reduce_max(sb[st][:, 8:9], sb["l2"][:, :], AX.X),
              reads=[rs["l2"]], writes=[rs[st]])
        em.do("dve", lambda h, st=st: h.tensor_scalar(sb["is1"][:, :], sb["l2"][:, :], sb[st][:, 8:9], None, ALU.is_equal),
              reads=[rs["l2"], rs[st]], writes=[rs["is1"]])
        em.do("dve", lambda h: h.scalar_tensor_tensor(out=sb["l2"][:, :], in0=sb["is1"][:, :],
                                                      scalar=-1e30, in1=sb["l2"][:, :],
                                                      op0=ALU.mult, op1=ALU.add),
              reads=[rs["is1"]], writes=[rs["l2"]])
        em.do("dve", lambda h, st=st: h.reduce_max(sb[st][:, 9:10], sb["l2"][:, :], AX.X),
              reads=[rs["l2"]], writes=[rs[st]])
        em.do("dve", lambda h, st=st: h.tensor_scalar(sb["is2"][:, :], sb["l2"][:, :], sb[st][:, 9:10], None, ALU.is_equal),
              reads=[rs["l2"], rs[st]], writes=[rs["is2"]])
        em.do("dve", lambda h, st=st: h.tensor_sub(sb[st][:, 10:11], sb[st][:, 9:10], sb[st][:, 8:9]),
              reads=[rs[st]], writes=[rs[st]])
        em.do("act", lambda h, st=st: h.activation(out=sb[st][:, 11:12], in_=sb[st][:, 10:11], func=AF.Exp),
              reads=[rs[st]], writes=[rs[st]])
        em.do("dve", lambda h, st=st: h.tensor_scalar(sb[st][:, 12:13], sb[st][:, 11:12], 1.0, None, ALU.add),
              reads=[rs[st]], writes=[rs[st]])
        em.do("dve", lambda h, st=st: h.reciprocal(sb[st][:, 13:14], sb[st][:, 12:13]),
              reads=[rs[st]], writes=[rs[st]])
        em.do("dve", lambda h, st=st: h.tensor_mul(sb[st][:, 14:15], sb[st][:, 11:12], sb[st][:, 13:14]),
              reads=[rs[st]], writes=[rs[st]])
        em.do("dve", lambda h, st=st: h.tensor_scalar(sb["gcol"][:, :], sb["is1"][:, :], sb[st][:, 13:14], None, ALU.mult),
              reads=[rs["is1"], rs[st]], writes=[rs["gcol"]])
        em.do("dve", lambda h, j=j, st=st: h.scalar_tensor_tensor(out=sb["gate"][:, j, :], in0=sb["is2"][:, :],
                                                                  scalar=sb[st][:, 14:15], in1=sb["gcol"][:, :],
                                                                  op0=ALU.mult, op1=ALU.add),
              reads=[rs["is2"], rs[st], rs["gcol"]], writes=[rs["gate"]])
        em.do("dve", lambda h, j=j: h.tensor_copy(sb["gatebf"][:, j, :], sb["gate"][:, j, :]),
              reads=[rs["gate"]], writes=[rs["gatebf"]])

    for e in range(8):
        we = e % 2
        w1sb, w2sb = f"w1sb{we}", f"w2sb{we}"
        em.do("pool", lambda h, e=e, w1sb=w1sb: h.dma_start(out=sb[w1sb][:, :, :],
                                                            in_=w1T_ap[e].rearrange("(k p) n -> p k n", p=128)),
              writes=[rs[w1sb]], dma=True)
        em.do("pool", lambda h, e=e, w2sb=w2sb: h.dma_start(out=sb[w2sb][:, :, :],
                                                            in_=w2T_ap[e].rearrange("(k p) n -> p k n", p=128)),
              writes=[rs[w2sb]], dma=True)
        for sp_ in range(8):
            t0 = sp_ * 512
            gc = sp_ % 2
            gb = f"gb{gc}"
            for q in range(4):
                em.do("pe", lambda h, e=e, sp_=sp_, q=q: h.transpose(
                    ps["gr"][0:1, :], sb["gatebf"][:, sp_ * 4 + q, e:e + 1], sb["ident"][:, :]),
                    reads=[rs["gatebf"], rs["ident"]], writes=[rs["ps_gr"]])
                em.do("act", lambda h, q=q: h.copy(out=sb["grow"][0:1, q * 128:(q + 1) * 128], in_=ps["gr"][0:1, :]),
                      reads=[rs["ps_gr"]], writes=[rs["grow"]])
            em.do("pe", lambda h, gc=gc: h.matmul(ps[f"b{gc}"][:, :], sb["onesf"][0:1, :], sb["grow"][0:1, :],
                                                  start=True, stop=True),
                  reads=[rs["onesf"], rs["grow"]], writes=[rs[f"ps_b{gc}"]])
            em.do("act", lambda h, gb=gb, gc=gc: h.copy(out=sb[gb][:, :], in_=ps[f"b{gc}"][:, :]),
                  reads=[rs[f"ps_b{gc}"]], writes=[rs[gb]])
            for f in range(16):
                fc = f % 2
                pa, par = ps[f"a{fc}"], rs[f"ps_a{fc}"]
                hbf = f"hbf{fc}"
                for ki in range(8):
                    em.do("pe", lambda h, f=f, ki=ki, t0=t0, pa=pa, w1sb=w1sb: h.matmul(
                        pa[:, :], sb[w1sb][:, ki, f * 128:(f + 1) * 128],
                        sb["xlnT"][:, ki, t0:t0 + 512], start=(ki == 0), stop=(ki == 7)),
                        reads=[rs[w1sb], rs["xlnT"]], writes=[par])
                em.do("act", lambda h, pa=pa, hbf=hbf: h.activation(out=sb[hbf][:, :], in_=pa[:, :], func=AF.Gelu),
                      reads=[par], writes=[rs[hbf]])
                em.do("dve", lambda h, f=f, hbf=hbf, gb=gb: h.tensor_mul(sb["hsp"][:, f, :], sb[hbf][:, :], sb[gb][:, :]),
                      reads=[rs[hbf], rs[gb]], writes=[rs["hsp"]])
            for tq in range(4):
                yc = tq % 2
                yev = f"yev{yc}"
                for half in range(2):
                    pb, pbr = ps[f"b{half}"], rs[f"ps_b{half}"]
                    for f in range(16):
                        em.do("pe", lambda h, f=f, tq=tq, half=half, pb=pb, w2sb=w2sb: h.matmul(
                            pb[:, :], sb["hsp"][:, f, tq * 128:(tq + 1) * 128],
                            sb[w2sb][:, f, half * 512:(half + 1) * 512], start=(f == 0), stop=(f == 15)),
                            reads=[rs["hsp"], rs[w2sb]], writes=[pbr])
                    em.do("dve", lambda h, half=half, pb=pb, yev=yev: h.tensor_copy(
                        sb[yev][:, half * 512:(half + 1) * 512], pb[:, :]),
                        reads=[pbr], writes=[rs[yev]])
                tg = t0 + tq * 128
                if e == 0:
                    em.do("pool", lambda h, tg=tg, yev=yev: h.dma_start(out=y_acc[tg:tg + 128, :], in_=sb[yev][:, :]),
                          reads=[rs[yev]], writes=[rs["y_acc"]], dma=True)
                else:
                    em.do("pool", lambda h, tg=tg, yev=yev: h.dma_start(out=y_acc[tg:tg + 128, :], in_=sb[yev][:, :],
                                                                        accum_op=ALU.add),
                          reads=[rs[yev]], writes=[rs["y_acc"]], dma=True)

    for j in range(NT_OWN):
        c = j % 2
        xr, xt, acc = f"xr{c}", f"xt{c}", f"acc{c}"
        em.do("sp", lambda h, j=j, xr=xr: h.dma_start(out=sb[xr][:, :], in_=x_in[j * 128:(j + 1) * 128, :]),
              writes=[rs[xr]], dma=True)
        em.do("sp", lambda h, j=j, xt=xt: h.dma_start(out=sb[xt][:, :], in_=y_acc[j * 128:(j + 1) * 128, :]),
              reads=[rs["y_acc"]], writes=[rs[xt]], dma=True)
        em.do("dve", lambda h, xt=xt, xr=xr, acc=acc: h.tensor_add(sb[acc][:, :], sb[xt][:, :], sb[xr][:, :]),
              reads=[rs[xt], rs[xr]], writes=[rs[acc]])
        em.do("sp", lambda h, j=j, acc=acc: h.dma_start(out=out_ap[j * 128:(j + 1) * 128, :], in_=sb[acc][:, :]),
              reads=[rs[acc]], writes=[rs["out_o"]], dma=True)


def build_fullfused2_nc():
    from contextlib import ExitStack
    nc = bass.Bass("TRN2", target_bir_lowering=False)
    x_in = nc.dram_tensor("x_in", [NB, T, D], F32, kind="ExternalInput").ap()
    ownrows = nc.dram_tensor("ownrows", [NT_OWN, 128, 1], U32, kind="ExternalInput").ap()
    rankoh = nc.dram_tensor("rankoh", [1, 512], F32, kind="ExternalInput").ap()
    wq_a = {w: nc.dram_tensor(w, [D, D], F32, kind="ExternalInput").ap()
            for w in ("wq", "wk", "wv", "wo")}
    rwT = nc.dram_tensor("rwT", [D, E64], F32, kind="ExternalInput").ap()
    wsgu = nc.dram_tensor("wsgu", [D, 1536], F32, kind="ExternalInput").ap()
    wsd = nc.dram_tensor("wsd", [768, D], F32, kind="ExternalInput").ap()
    wg = nc.dram_tensor("wg", [E64, D, 768], F32, kind="ExternalInput").ap()
    wu = nc.dram_tensor("wu", [E64, D, 768], F32, kind="ExternalInput").ap()
    wd = nc.dram_tensor("wd", [E64, 768, D], F32, kind="ExternalInput").ap()
    wqkv = nc.dram_tensor("wqkv", [D, 3 * D], F32, kind="ExternalInput").ap()
    bqkv = nc.dram_tensor("bqkv", [3 * D], F32, kind="ExternalInput").ap()
    wout = nc.dram_tensor("wout", [D, D], F32, kind="ExternalInput").ap()
    bout = nc.dram_tensor("bout", [D], F32, kind="ExternalInput").ap()
    rw8 = nc.dram_tensor("rw8", [D, 8], F32, kind="ExternalInput").ap()
    w1T = nc.dram_tensor("w1T", [8, D, 2048], F32, kind="ExternalInput").ap()
    w2T = nc.dram_tensor("w2T", [8, 2048, D], F32, kind="ExternalInput").ap()
    out_ap = nc.dram_tensor("out", [4096, D], F32, kind="ExternalOutput").ap()

    x1_d = nc.dram_tensor("x1_d", [NB * T, D], F32, kind="Internal")
    xb_d = nc.dram_tensor("xb_d", [E64 * CAP, D], BF16, kind="Internal").ap()
    w_d = nc.dram_tensor("w_d", [E64 * CAP, 2], F32, kind="Internal").ap()
    y_d = nc.dram_tensor("y_d", [E64 * CAP, D], BF16, kind="Internal").ap()
    s8u_d = nc.dram_tensor("s8u_d", [NT_OWN, 128, 8], U32, kind="Internal").ap()
    rbase_d = nc.dram_tensor("rbase_d", [4096, D], F32, kind="Internal").ap()
    x2_d = nc.dram_tensor("x2_d", [4096, D], F32, kind="Internal")
    x3_d = nc.dram_tensor("x3_d", [4096, D], F32, kind="Internal")
    y_acc = nc.dram_tensor("y_acc", [4096, D], F32, kind="Internal").ap()

    h_ident, h_tri, h_konst = l0_attn_consts(nc)
    hc = l0ffn_consts(nc)
    es = ExitStack()
    with es:
        trk = Trk(nc, es)
        em = Emit(trk)
        with ExitStack() as es0:
            sb, ps = alloc_l0_attn2(nc, es0)
            rs = {k: Rsrc(k) for k in L0_RS2}
            x1v = x1_d.ap().rearrange("(b t) d -> b t d", b=NB)
            for b in range(NB):
                em.do("pool", lambda h, b=b: h.dma_start(out=x1v[b, 0:QR0, :], in_=x_in[b, 0:QR0, :]),
                      writes=[rs["out_0"]], dma=True)
            emit_l0_attn2(nc, em, sb, ps, rs, x_in, x1v, wq_a, NB, h_ident, h_tri, h_konst,
                          out_base=QR0)
            trk_barrier(trk)
        with ExitStack() as es1:
            sb, ps = alloc_l0ffn(nc, es1)
            rs = {k: Rsrc(k) for k in L0FFN_RS}
            emit_l0ffn_R(nc, em, sb, ps, rs, x1_d.ap(), rankoh, rwT, hc)
            emit_l0ffn_P(nc, em, sb, ps, rs, None, rwT, xb_d, w_d, s8u_d, hc,
                         ownrows=ownrows, x1_dram=x1_d.ap())
            emit_l0ffn_S(nc, em, sb, ps, rs, None, wsgu, wsd, rbase_d,
                         ownrows=ownrows, x1_dram=x1_d.ap())
            emit_l0ffn_E(nc, em, sb, ps, rs, wg, wu, wd, xb_d, w_d, y_d)
            emit_l0ffn_G(nc, em, sb, ps, rs, rbase_d, s8u_d, y_d, x2_d.ap())
            trk_barrier(trk)
        with ExitStack() as es2:
            sb, ps = alloc_l1attn(nc, es2)
            rs = {k: Rsrc(k) for k in L1A_RS}
            x2v = x2_d.ap().rearrange("(b t) d -> b t d", b=2)
            x3v = x3_d.ap().rearrange("(b t) d -> b t d", b=2)
            emit_l1_attn(nc, em, sb, ps, rs, x2v, x3v, wqkv, bqkv, wout, bout, 2, hc)
            trk_barrier(trk)
        with ExitStack() as es3:
            sb, ps = alloc_l1ffn2(nc, es3)
            rs = {k: Rsrc(k) for k in L1F_RS2}
            rs["w2sb1"] = rs["w2sb0"]
            rs["scr2ln1"] = rs["scr2ln0"]
            rs["scr2ln0"] = rs["xr0"]
            rs["scr2ln1"] = rs["xr0"]
            rs["xr1"] = rs["xr0"]
            rs["acc1"] = rs["acc0"]
            rs["yev0"] = rs["acc0"]
            rs["yev1"] = rs["acc0"]
            emit_l1_ffn2(nc, em, sb, ps, rs, x3_d.ap(), out_ap, rw8, w1T, w2T, y_acc, hc)
            trk_barrier(trk)
        block = es.enter_context(nc.Block())
        trk.replay(block)
    return nc


def build_l1ffn2_nc():
    from contextlib import ExitStack
    nc = bass.Bass("TRN2", target_bir_lowering=False)
    x_in = nc.dram_tensor("x_in", [4096, D], F32, kind="ExternalInput").ap()
    out_ap = nc.dram_tensor("out", [4096, D], F32, kind="ExternalOutput").ap()
    rw8 = nc.dram_tensor("rw8", [D, 8], F32, kind="ExternalInput").ap()
    w1T = nc.dram_tensor("w1T", [8, D, 2048], F32, kind="ExternalInput").ap()
    w2T = nc.dram_tensor("w2T", [8, 2048, D], F32, kind="ExternalInput").ap()
    y_acc = nc.dram_tensor("y_acc", [4096, D], F32, kind="Internal").ap()
    hc = l0ffn_consts(nc)
    es = ExitStack()
    with es:
        sb, ps = alloc_l1ffn2(nc, es)
        trk = Trk(nc, es)
        em = Emit(trk)
        rs = {k: Rsrc(k) for k in L1F_RS2}
        rs["w2sb1"] = rs["w2sb0"]
        rs["scr2ln0"] = rs["xr0"]
        rs["scr2ln1"] = rs["xr0"]
        rs["xr1"] = rs["xr0"]
        rs["acc1"] = rs["acc0"]
        rs["yev0"] = rs["acc0"]
        rs["yev1"] = rs["acc0"]
        block = es.enter_context(nc.Block())
        emit_l1_ffn2(nc, em, sb, ps, rs, x_in, out_ap, rw8, w1T, w2T, y_acc, hc)
        trk.wait("sp", [rs["out_o"].w])
        trk.replay(block)
    return nc



L1A_RS2 = (["ident", "onesf", "konst", "wqkv", "wout", "bq_row", "bo_row",
            "qbf", "kvrow", "kvb", "rsq", "ps_kv1", "ps_kv2", "x3_o"] +
           [f"{k}{p}" for p in range(2) for k in
            ("xt", "xbf", "xT", "st", "scr2ln", "kbf", "vbf", "prod", "obf", "xr", "acc",
             "ps_a", "ps_b", "ps_t")])


def alloc_l1attn2(nc, es):
    sb, ps = {}, {}
    sb["ident"] = es.enter_context(nc.sbuf_tensor("a_ident", [128, 128], BF16))
    sb["onesf"] = es.enter_context(nc.sbuf_tensor("a_onesf", [128, 128], BF16))
    sb["konst"] = es.enter_context(nc.sbuf_tensor("a_konst", [128, 4], F32))
    sb["wqkv"] = es.enter_context(nc.sbuf_tensor("a_wqkv", [128, 8, 3072], BF16))
    sb["wout"] = es.enter_context(nc.sbuf_tensor("a_wout", [128, 8, 1024], BF16))
    sb["bq_row"] = es.enter_context(nc.sbuf_tensor("a_bqrow", [1, 3072], BF16))
    sb["bo_row"] = es.enter_context(nc.sbuf_tensor("a_borow", [1, 1024], BF16))
    sb["bq_row32"] = es.enter_context(nc.sbuf_tensor("a_bqrow32", [1, 3072], F32))
    sb["bo_row32"] = es.enter_context(nc.sbuf_tensor("a_borow32", [1, 1024], F32))
    sb["qbf"] = es.enter_context(nc.sbuf_tensor("a_qbf", [128, 16, 1024], BF16))
    sb["kvrow"] = es.enter_context(nc.sbuf_tensor("a_kvrow", [1, 1024], BF16))
    sb["kvb"] = es.enter_context(nc.sbuf_tensor("a_kvb", [128, 1024], F32))
    sb["rsq"] = es.enter_context(nc.sbuf_tensor("a_rsq", [128, 16], F32))
    for p in range(2):
        sb[f"xt{p}"] = es.enter_context(nc.sbuf_tensor(f"a_xt{p}", [128, 1024], F32))
        sb[f"xbf{p}"] = es.enter_context(nc.sbuf_tensor(f"a_xbf{p}", [128, 1024], BF16))
        sb[f"xT{p}"] = es.enter_context(nc.sbuf_tensor(f"a_xT{p}", [128, 8, 128], BF16))
        sb[f"st{p}"] = es.enter_context(nc.sbuf_tensor(f"a_st{p}", [128, 16], F32))
        sb[f"scr2ln{p}"] = es.enter_context(nc.sbuf_tensor(f"a_scr2ln{p}", [128, 1024], F32))
        sb[f"kbf{p}"] = es.enter_context(nc.sbuf_tensor(f"a_kbf{p}", [128, 1024], BF16))
        sb[f"vbf{p}"] = es.enter_context(nc.sbuf_tensor(f"a_vbf{p}", [128, 1024], BF16))
        sb[f"prod{p}"] = es.enter_context(nc.sbuf_tensor(f"a_prod{p}", [128, 1024], BF16))
        sb[f"obf{p}"] = es.enter_context(nc.sbuf_tensor(f"a_obf{p}", [128, 1024], BF16))
        sb[f"xr{p}"] = es.enter_context(nc.sbuf_tensor(f"a_xr{p}", [128, 1024], F32))
        sb[f"acc{p}"] = es.enter_context(nc.sbuf_tensor(f"a_acc{p}", [128, 1024], F32))
        ps[f"a{p}"] = es.enter_context(nc.psum_tensor(f"a_ps_a{p}", [128, 512], F32))
        ps[f"b{p}"] = es.enter_context(nc.psum_tensor(f"a_ps_b{p}", [128, 512], F32))
        ps[f"t{p}"] = es.enter_context(nc.psum_tensor(f"a_ps_t{p}", [128, 128], F32))
    ps["kv1"] = es.enter_context(nc.psum_tensor("a_ps_kv1", [1, 512], F32))
    ps["kv2"] = es.enter_context(nc.psum_tensor("a_ps_kv2", [1, 512], F32))
    return sb, ps


def emit_l1_attn2(nc, em, sb, ps, rs, x_in, x3_out, wqkv_ap, bqkv_ap, wout_ap, bout_ap, nb, hc):
    em.do("sp", lambda h: h.dma_start(out=sb["ident"][:], in_=hc["ident"].ap()), writes=[rs["ident"]], dma=True)
    em.do("sp", lambda h: h.dma_start(out=sb["onesf"][:], in_=hc["ones_full"].ap()), writes=[rs["onesf"]], dma=True)
    em.do("sp", lambda h: h.dma_start(out=sb["konst"][:], in_=hc["konst"].ap()), writes=[rs["konst"]], dma=True)
    em.do("pool", lambda h: h.dma_start(out=sb["wqkv"][:, :, :], in_=wqkv_ap.rearrange("(k p) n -> p k n", p=128)),
          writes=[rs["wqkv"]], dma=True)
    em.do("pool", lambda h: h.dma_start(out=sb["wout"][:, :, :], in_=wout_ap.rearrange("(k p) n -> p k n", p=128)),
          writes=[rs["wout"]], dma=True)
    em.do("sp", lambda h: h.dma_start(out=sb["bq_row32"][0:1, :], in_=bqkv_ap[None, :]), writes=[rs["bq_row"]], dma=True)
    em.do("sp", lambda h: h.dma_start(out=sb["bo_row32"][0:1, :], in_=bout_ap[None, :]), writes=[rs["bo_row"]], dma=True)
    em.do("dve", lambda h: h.tensor_copy(sb["bq_row"][0:1, :], sb["bq_row32"][0:1, :]),
          reads=[rs["bq_row"]], writes=[rs["bq_row"]])
    em.do("dve", lambda h: h.tensor_copy(sb["bo_row"][0:1, :], sb["bo_row32"][0:1, :]),
          reads=[rs["bo_row"]], writes=[rs["bo_row"]])

    NTT = 16
    for b in range(nb):
        for tt in range(NTT):
            c = tt % 2
            xt, xbf, xT, st = f"xt{c}", f"xbf{c}", f"xT{c}", f"st{c}"
            kbf, vbf, prod = f"kbf{c}", f"vbf{c}", f"prod{c}"
            em.do("sp", lambda h, b=b, tt=tt, xt=xt: h.dma_start(out=sb[xt][:, :], in_=x_in[b, tt * 128:(tt + 1) * 128, :]),
                  writes=[rs[xt]], dma=True)
            em.do("act", lambda h, xt=xt, st=st, c=c: h.activation(
                out=sb[f"scr2ln{c}"][:, :], in_=sb[xt][:, :], func=AF.Square, accum_out=sb[st][:, 0:1]),
                reads=[rs[xt]], writes=[rs[f"scr2ln{c}"], rs[st]])
            em.do("dve", lambda h, xt=xt, st=st: h.reduce_sum(sb[st][:, 1:2], sb[xt][:, :], AX.X, negate=True),
                  reads=[rs[xt]], writes=[rs[st]])
            em.do("dve", lambda h, st=st: h.tensor_scalar(sb[st][:, 2:3], sb[st][:, 1:2], 1.0 / D, None, ALU.mult),
                  reads=[rs[st]], writes=[rs[st]])
            em.do("dve", lambda h, st=st: h.tensor_scalar(sb[st][:, 3:4], sb[st][:, 0:1], 1.0 / D, None, ALU.mult),
                  reads=[rs[st]], writes=[rs[st]])
            em.do("dve", lambda h, st=st: h.scalar_tensor_tensor(
                out=sb[st][:, 4:5], in0=sb[st][:, 2:3], scalar=sb[st][:, 2:3],
                in1=sb[st][:, 3:4], op0=ALU.mult, op1=ALU.subtract),
                reads=[rs[st]], writes=[rs[st]])
            em.do("act", lambda h, st=st: h.activation(out=sb[st][:, 5:6], in_=sb[st][:, 4:5], func=AF.Sqrt,
                                                       bias=sb["konst"][:, 0:1], scale=sb["konst"][:, 1:2]),
                  reads=[rs[st], rs["konst"]], writes=[rs[st]])
            em.do("dve", lambda h, st=st: h.reciprocal(sb[st][:, 6:7], sb[st][:, 5:6]),
                  reads=[rs[st]], writes=[rs[st]])
            em.do("dve", lambda h, xt=xt, xbf=xbf, st=st: h.tensor_scalar(
                sb[xbf][:, :], sb[xt][:, :], sb[st][:, 2:3], sb[st][:, 6:7], ALU.add, ALU.mult),
                reads=[rs[xt], rs[st]], writes=[rs[xbf]])
            for ki in range(8):
                tc = ki % 2
                tb, tbr = ps[f"t{tc}"], rs[f"ps_t{tc}"]
                em.do("pe", lambda h, ki=ki, xbf=xbf, tb=tb: h.matmul(
                    tb[0:128, 0:128], sb[xbf][:, ki * 128:(ki + 1) * 128], sb["ident"][:, :],
                    start=True, stop=True),
                    reads=[rs[xbf], rs["ident"]], writes=[tbr])
                eng = "dve" if ki % 2 == 0 else "act"
                fn = (lambda h, ki=ki, xT=xT, tb=tb: h.tensor_copy(sb[xT][:, ki, :], tb[0:128, 0:128])) \
                    if eng == "dve" else \
                     (lambda h, ki=ki, xT=xT, tb=tb: h.copy(out=sb[xT][:, ki, :], in_=tb[0:128, 0:128]))
                em.do(eng, fn, reads=[tbr], writes=[rs[xT]])
            for s in range(3):
                for half in range(2):
                    pn = f"{'a' if half == 0 else 'b'}{c}"
                    pst, psn = ps[pn], rs[f"ps_{pn}"]
                    c0 = s * 1024 + half * 512
                    for ki in range(8):
                        em.do("pe", lambda h, ki=ki, c0=c0, pst=pst, xT=xT: h.matmul(
                            pst[:, :], sb[xT][:, ki, :], sb["wqkv"][:, ki, c0:c0 + 512],
                            start=(ki == 0), stop=False),
                            reads=[rs[xT], rs["wqkv"]], writes=[psn])
                    em.do("pe", lambda h, c0=c0, pst=pst: h.matmul(
                        pst[:, :], sb["onesf"][0:1, :], sb["bq_row"][0:1, c0:c0 + 512],
                        start=False, stop=True),
                        reads=[rs["onesf"], rs["bq_row"]], writes=[psn])
                    if s == 0:
                        em.do("act", lambda h, half=half, pst=pst, st=st, c=c: h.activation(
                            out=sb[f"scr2ln{c}"][:, half * 512:(half + 1) * 512], in_=pst[:, :], func=AF.Square,
                            accum_out=sb[st][:, 8 + half:9 + half]),
                            reads=[psn], writes=[rs[f"scr2ln{c}"], rs[st]])
                        em.do("act", lambda h, half=half, tt=tt, pst=pst: h.copy(
                            out=sb["qbf"][:, tt, half * 512:(half + 1) * 512], in_=pst[:, :]),
                            reads=[psn], writes=[rs["qbf"]])
                    elif s == 1:
                        em.do("act", lambda h, half=half, pst=pst, st=st, c=c: h.activation(
                            out=sb[f"scr2ln{c}"][:, half * 512:(half + 1) * 512], in_=pst[:, :], func=AF.Square,
                            accum_out=sb[st][:, 10 + half:11 + half]),
                            reads=[psn], writes=[rs[f"scr2ln{c}"], rs[st]])
                        em.do("act", lambda h, half=half, pst=pst, kbf=kbf: h.copy(
                            out=sb[kbf][:, half * 512:(half + 1) * 512], in_=pst[:, :]),
                            reads=[psn], writes=[rs[kbf]])
                    else:
                        em.do("act", lambda h, half=half, pst=pst, vbf=vbf: h.copy(
                            out=sb[vbf][:, half * 512:(half + 1) * 512], in_=pst[:, :]),
                            reads=[psn], writes=[rs[vbf]])
            em.do("dve", lambda h, st=st: h.tensor_add(sb[st][:, 12:13], sb[st][:, 8:9], sb[st][:, 9:10]),
                  reads=[rs[st]], writes=[rs[st]])
            em.do("dve", lambda h, st=st: h.tensor_add(sb[st][:, 13:14], sb[st][:, 10:11], sb[st][:, 11:12]),
                  reads=[rs[st]], writes=[rs[st]])
            em.do("act", lambda h, st=st: h.activation(out=sb[st][:, 14:15], in_=sb[st][:, 12:13], func=AF.Sqrt),
                  reads=[rs[st]], writes=[rs[st]])
            em.do("act", lambda h, st=st: h.activation(out=sb[st][:, 15:16], in_=sb[st][:, 13:14], func=AF.Sqrt),
                  reads=[rs[st]], writes=[rs[st]])
            em.do("dve", lambda h, tt=tt, st=st: h.reciprocal(sb["rsq"][:, tt:tt + 1], sb[st][:, 14:15]),
                  reads=[rs[st]], writes=[rs["rsq"]])
            em.do("dve", lambda h, st=st: h.reciprocal(sb[st][:, 15:16], sb[st][:, 15:16]),
                  reads=[rs[st]], writes=[rs[st]])
            em.do("dve", lambda h, kbf=kbf, vbf=vbf, prod=prod, st=st: h.scalar_tensor_tensor(
                out=sb[prod][:, :], in0=sb[kbf][:, :], scalar=sb[st][:, 15:16], in1=sb[vbf][:, :],
                op0=ALU.mult, op1=ALU.mult),
                reads=[rs[kbf], rs[st], rs[vbf]], writes=[rs[prod]])
            em.do("pe", lambda h, b=b, tt=tt, prod=prod: h.matmul(
                ps["kv1"][0:1, :], sb["onesf"][:, 0:1], sb[prod][:, 0:512],
                start=(tt == 0), stop=(tt == NTT - 1), skip_group_check=True),
                reads=[rs["onesf"], rs[prod]], writes=[rs["ps_kv1"]])
            em.do("pe", lambda h, b=b, tt=tt, prod=prod: h.matmul(
                ps["kv2"][0:1, :], sb["onesf"][:, 0:1], sb[prod][:, 512:1024],
                start=(tt == 0), stop=(tt == NTT - 1), skip_group_check=True),
                reads=[rs["onesf"], rs[prod]], writes=[rs["ps_kv2"]])
        em.do("act", lambda h: h.copy(out=sb["kvrow"][0:1, 0:512], in_=ps["kv1"][0:1, :]),
              reads=[rs["ps_kv1"]], writes=[rs["kvrow"]])
        em.do("act", lambda h: h.copy(out=sb["kvrow"][0:1, 512:1024], in_=ps["kv2"][0:1, :]),
              reads=[rs["ps_kv2"]], writes=[rs["kvrow"]])
        for half in range(2):
            pn = f"{'a' if half == 0 else 'b'}0"
            pst, psn = ps[pn], rs[f"ps_{pn}"]
            em.do("pe", lambda h, half=half, pst=pst: h.matmul(
                pst[:, :], sb["onesf"][0:1, :], sb["kvrow"][0:1, half * 512:(half + 1) * 512],
                start=True, stop=True),
                reads=[rs["onesf"], rs["kvrow"]], writes=[psn])
            em.do("act", lambda h, half=half, pst=pst: h.copy(
                out=sb["kvb"][:, half * 512:(half + 1) * 512], in_=pst[:, :]),
                reads=[psn], writes=[rs["kvb"]])
        for tt in range(NTT):
            c = tt % 2
            obf, xT, xr, acc = f"obf{c}", f"xT{c}", f"xr{c}", f"acc{c}"
            em.do("dve", lambda h, tt=tt, obf=obf: h.scalar_tensor_tensor(
                out=sb[obf][:, :], in0=sb["qbf"][:, tt, :], scalar=sb["rsq"][:, tt:tt + 1],
                in1=sb["kvb"][:, :], op0=ALU.mult, op1=ALU.mult),
                reads=[rs["qbf"], rs["rsq"], rs["kvb"]], writes=[rs[obf]])
            for ki in range(8):
                tc = ki % 2
                tb, tbr = ps[f"t{tc}"], rs[f"ps_t{tc}"]
                em.do("pe", lambda h, ki=ki, obf=obf, tb=tb: h.matmul(
                    tb[0:128, 0:128], sb[obf][:, ki * 128:(ki + 1) * 128], sb["ident"][:, :],
                    start=True, stop=True),
                    reads=[rs[obf], rs["ident"]], writes=[tbr])
                eng = "dve" if ki % 2 == 0 else "act"
                fn = (lambda h, ki=ki, xT=xT, tb=tb: h.tensor_copy(sb[xT][:, ki, :], tb[0:128, 0:128])) \
                    if eng == "dve" else \
                     (lambda h, ki=ki, xT=xT, tb=tb: h.copy(out=sb[xT][:, ki, :], in_=tb[0:128, 0:128]))
                em.do(eng, fn, reads=[tbr], writes=[rs[xT]])
            em.do("sp", lambda h, b=b, tt=tt, xr=xr: h.dma_start(out=sb[xr][:, :], in_=x_in[b, tt * 128:(tt + 1) * 128, :]),
                  writes=[rs[xr]], dma=True)
            for half in range(2):
                pn = f"{'a' if half == 0 else 'b'}{c}"
                pst, psn = ps[pn], rs[f"ps_{pn}"]
                for ki in range(8):
                    em.do("pe", lambda h, ki=ki, half=half, pst=pst, xT=xT: h.matmul(
                        pst[:, :], sb[xT][:, ki, :], sb["wout"][:, ki, half * 512:(half + 1) * 512],
                        start=(ki == 0), stop=False),
                        reads=[rs[xT], rs["wout"]], writes=[psn])
                em.do("pe", lambda h, half=half, pst=pst: h.matmul(
                    pst[:, :], sb["onesf"][0:1, :], sb["bo_row"][0:1, half * 512:(half + 1) * 512],
                    start=False, stop=True),
                    reads=[rs["onesf"], rs["bo_row"]], writes=[psn])
                em.do("dve", lambda h, half=half, pst=pst, xr=xr, acc=acc: h.tensor_add(
                    sb[acc][:, half * 512:(half + 1) * 512], pst[:, :],
                    sb[xr][:, half * 512:(half + 1) * 512]),
                    reads=[psn, rs[xr]], writes=[rs[acc]])
            em.do("sp", lambda h, b=b, tt=tt, acc=acc: h.dma_start(out=x3_out[b, tt * 128:(tt + 1) * 128, :],
                                                                   in_=sb[acc][:, :]),
                  reads=[rs[acc]], writes=[rs["x3_o"]], dma=True)



L0FFN_RS2 = (["rwT", "ident", "onesf", "triu", "iota512", "iotae", "colsumB", "konst",
              "runrow", "exc", "excm", "rankoh", "baseown",
              "i8u", "i8f", "G", "S", "slot", "inv", "s8", "w8", "w8x2", "s8u", "scr",
              "xlnT", "wsgu", "wsd", "hsh", "xr", "acc", "oidx", "x1_d", "m80", "m81",
              "gbf0", "gbf1",
              "ps_lg", "ps_lg2", "ps_run", "ps_runP", "ps_S",
              "xb_d", "w_d", "y_d", "s8u_d", "rbase_d", "x2_o", "hE", "wcol"] +
             [f"{k}{p}" for p in range(2) for k in
              ("xt", "xbf", "xT", "st", "lgr", "rep", "Mbf", "scr2",
               "ps_a", "ps_b", "ps_c", "ps_d", "wg", "wu", "wd", "xbT", "gbf", "ybf")])


def alloc_l0ffn_core(nc, es):
    sb, ps = {}, {}
    sb["rwT"] = es.enter_context(nc.sbuf_tensor("rwT_sb", [128, 8, E64], BF16))
    sb["ident"] = es.enter_context(nc.sbuf_tensor("f_ident", [128, 128], BF16))
    sb["onesf"] = es.enter_context(nc.sbuf_tensor("f_onesf", [128, 128], BF16))
    sb["triu"] = es.enter_context(nc.sbuf_tensor("f_triu", [128, 128], BF16))
    sb["iota512"] = es.enter_context(nc.sbuf_tensor("f_iota512", [128, E64], F32))
    sb["iotae"] = es.enter_context(nc.sbuf_tensor("f_iotae", [128, E64], F32))
    sb["colsumB"] = es.enter_context(nc.sbuf_tensor("f_colsumB", [128, E64], F32))
    sb["konst"] = es.enter_context(nc.sbuf_tensor("f_konst", [128, 4], F32))
    sb["runrow"] = es.enter_context(nc.sbuf_tensor("f_runrow", [1, E64], BF16))
    sb["exc"] = es.enter_context(nc.sbuf_tensor("f_exc", [1, 8, E64], F32))
    sb["excm"] = es.enter_context(nc.sbuf_tensor("f_excm", [1, 8, E64], F32))
    sb["rankoh"] = es.enter_context(nc.sbuf_tensor("f_rankoh", [1, 512], F32))
    sb["baseown"] = es.enter_context(nc.sbuf_tensor("f_baseown", [1, E64], BF16))
    sb["i8u"] = es.enter_context(nc.sbuf_tensor("f_i8u", [128, 8], U32))
    sb["i8f"] = es.enter_context(nc.sbuf_tensor("f_i8f", [128, 8], F32))
    sb["G"] = es.enter_context(nc.sbuf_tensor("f_G", [128, E64], F32))
    sb["S"] = es.enter_context(nc.sbuf_tensor("f_S", [128, E64], F32))
    sb["slot"] = es.enter_context(nc.sbuf_tensor("f_slot", [128, E64], F32))
    sb["inv"] = es.enter_context(nc.sbuf_tensor("f_inv", [128, E64], F32))
    sb["s8"] = es.enter_context(nc.sbuf_tensor("f_s8", [128, 8], F32))
    sb["w8"] = es.enter_context(nc.sbuf_tensor("f_w8", [128, 8], F32))
    sb["w8x2"] = es.enter_context(nc.sbuf_tensor("f_w8x2", [128, 8, 2], F32))
    sb["s8u"] = es.enter_context(nc.sbuf_tensor("f_s8u", [128, 8], U32))
    sb["scr"] = es.enter_context(nc.sbuf_tensor("f_scr", [128, E64], F32))
    sb["oidx"] = es.enter_context(nc.sbuf_tensor("f_oidx", [128, 1], U32))
    sb["xr"] = es.enter_context(nc.sbuf_tensor("f_xr", [128, 1024], F32))
    sb["acc"] = es.enter_context(nc.sbuf_tensor("f_acc", [128, 1024], F32))
    sb["wcol"] = es.enter_context(nc.sbuf_tensor("f_wcol", [128, 8], F32))
    sb["gbf0"] = es.enter_context(nc.sbuf_tensor("f_gbf0", [128, 512], BF16))
    sb["gbf1"] = es.enter_context(nc.sbuf_tensor("f_gbf1", [128, 512], BF16))
    for p in range(2):
        sb[f"xt{p}"] = es.enter_context(nc.sbuf_tensor(f"f_xt{p}", [128, 1024], F32))
        sb[f"xbf{p}"] = es.enter_context(nc.sbuf_tensor(f"f_xbf{p}", [128, 1024], BF16))
        sb[f"xT{p}"] = es.enter_context(nc.sbuf_tensor(f"f_xT{p}", [128, 8, 128], BF16))
        sb[f"st{p}"] = es.enter_context(nc.sbuf_tensor(f"f_st{p}", [128, 8], F32))
        sb[f"lgr{p}"] = es.enter_context(nc.sbuf_tensor(f"f_lgr{p}", [128, E64], F32))
        sb[f"rep{p}"] = es.enter_context(nc.sbuf_tensor(f"f_rep{p}", [128, E64], F32))
        sb[f"Mbf{p}"] = es.enter_context(nc.sbuf_tensor(f"f_Mbf{p}", [128, E64], BF16))
        sb[f"scr2{p}"] = es.enter_context(nc.sbuf_tensor(f"f_scr2{p}", [128, 1024], F32))
        sb[f"m8{p}"] = es.enter_context(nc.sbuf_tensor(f"f_m8{p}", [128, 8], F32))
        ps[f"a{p}"] = es.enter_context(nc.psum_tensor(f"f_ps_a{p}", [128, 512], F32))
    return sb, ps


def emit_l0ffn_R2(nc, em, sb, ps, rs, x1_full, rankoh_ap, rwT_ap, hc):
    em.do("sp", lambda h: h.dma_start(out=sb["onesf"][:], in_=hc["ones_full"].ap()), writes=[rs["onesf"]], dma=True)
    em.do("sp", lambda h: h.dma_start(out=sb["triu"][:], in_=hc["triu"].ap()), writes=[rs["triu"]], dma=True)
    em.do("sp", lambda h: h.dma_start(out=sb["ident"][:], in_=hc["ident"].ap()), writes=[rs["ident"]], dma=True)
    em.do("sp", lambda h: h.dma_start(out=sb["konst"][:], in_=hc["konst"].ap()), writes=[rs["konst"]], dma=True)
    em.do("sp", lambda h: h.dma_start(out=sb["iota512"][:], in_=hc["iota512"].ap()), writes=[rs["iota512"]], dma=True)
    em.do("sp", lambda h: h.dma_start(out=sb["iotae"][:], in_=hc["iota_e"].ap()), writes=[rs["iotae"]], dma=True)
    em.do("sp", lambda h: h.dma_start(out=sb["rankoh"][0:1, :], in_=rankoh_ap[0:1, :]), writes=[rs["rankoh"]], dma=True)
    em.do("pool", lambda h: h.dma_start(out=sb["rwT"][:, :, :], in_=rwT_ap.rearrange("(k p) n -> p k n", p=128)),
          writes=[rs["rwT"]], dma=True)
    for dk in range(8):
        em.do("pe", lambda h, dk=dk: h.matmul(ps["lg"][:, :], sb["onesf"][:, :], sb["rwT"][:, dk, :],
                                              start=(dk == 0), stop=(dk == 7)),
              reads=[rs["onesf"], rs["rwT"]], writes=[rs["ps_lg"]])
    em.do("dve", lambda h: h.tensor_copy(sb["colsumB"][:], ps["lg"][:, :]), reads=[rs["ps_lg"]], writes=[rs["colsumB"]])

    for ti in range(NT_GLOB):
        g, gj = ti // 32, ti % 32
        c = ti % 2
        xt, xbf, xT, st = f"xt{c}", f"xbf{c}", f"xT{c}", f"st{c}"
        lgr, rep, Mbf = f"lgr{c}", f"rep{c}", f"Mbf{c}"
        lgp, lgpr = (ps["lg"], rs["ps_lg"]) if c == 0 else (ps["lg2"], rs["ps_lg2"])
        em.do("sp", lambda h, ti=ti, xt=xt: h.dma_start(out=sb[xt][:, :], in_=x1_full[ti * 128:(ti + 1) * 128, :]),
              writes=[rs[xt]], dma=True)
        em.do("act", lambda h, xt=xt, xbf=xbf, st=st: h.activation(out=sb[xbf][:, :], in_=sb[xt][:, :], func=AF.Copy,
                                                                   accum_out=sb[st][:, 0:1]),
              reads=[rs[xt]], writes=[rs[xbf], rs[st]])
        em.do("dve", lambda h, st=st: h.tensor_scalar(sb[st][:, 1:2], sb[st][:, 0:1], -1.0 / D, None, ALU.mult),
              reads=[rs[st]], writes=[rs[st]])
        for ki in range(8):
            tc = ki % 2
            tb, tbr = ps[f"a{tc}"], rs[f"ps_a{tc}"]
            em.do("pe", lambda h, ki=ki, xbf=xbf, tb=tb: h.matmul(
                tb[0:128, 0:128], sb[xbf][:, ki * 128:(ki + 1) * 128], sb["ident"][:, :],
                start=True, stop=True),
                reads=[rs[xbf], rs["ident"]], writes=[tbr])
            eng = "dve" if ki % 2 == 0 else "act"
            fn = (lambda h, ki=ki, xT=xT, tb=tb: h.tensor_copy(sb[xT][:, ki, :], tb[0:128, 0:128])) \
                if eng == "dve" else \
                 (lambda h, ki=ki, xT=xT, tb=tb: h.copy(out=sb[xT][:, ki, :], in_=tb[0:128, 0:128]))
            em.do(eng, fn, reads=[tbr], writes=[rs[xT]])
        for ki in range(8):
            em.do("pe", lambda h, ki=ki, xT=xT, lgp=lgp: h.matmul(lgp[:, :], sb[xT][:, ki, :], sb["rwT"][:, ki, :],
                                                                  start=(ki == 0), stop=(ki == 7),
                                                                  skip_group_check=True),
                  reads=[rs[xT], rs["rwT"]], writes=[lgpr])
        em.do("dve", lambda h, st=st, lgr=lgr, lgp=lgp: h.scalar_tensor_tensor(
            out=sb[lgr][:, :], in0=sb["colsumB"][:, :], scalar=sb[st][:, 1:2], in1=lgp[:, :],
            op0=ALU.mult, op1=ALU.add),
            reads=[rs["colsumB"], rs[st], lgpr], writes=[rs[lgr]])
        m8 = f"m8{c}"
        em.do("dve", lambda h, lgr=lgr, m8=m8: h.max(out=sb[m8][:, :], in_=sb[lgr][:, :]),
              reads=[rs[lgr]], writes=[rs[m8]])
        em.do("dve", lambda h, lgr=lgr, rep=rep, m8=m8: h.match_replace(out=sb[rep][:, :], in_to_replace=sb[m8][:, :],
                                                                        in_values=sb[lgr][:, :], imm_value=-1e30),
              reads=[rs[lgr], rs[m8]], writes=[rs[rep]])
        em.do("dve", lambda h, lgr=lgr, rep=rep, Mbf=Mbf: h.tensor_tensor(sb[Mbf][:, :], sb[lgr][:, :], sb[rep][:, :], ALU.not_equal),
              reads=[rs[lgr], rs[rep]], writes=[rs[Mbf]])
        em.do("pe", lambda h, ti=ti, Mbf=Mbf: h.matmul(ps["run"][0:1, :], sb["onesf"][:, 0:1], sb[Mbf][:, :],
                                                       start=(ti == 0), stop=(ti == NT_GLOB - 1), skip_group_check=True),
              reads=[rs["onesf"], rs[Mbf]], writes=[rs["ps_run"]])
        if gj == 31:
            em.do("act", lambda h, g=g: h.copy(out=sb["exc"][0:1, g, :], in_=ps["run"][0:1, :]),
                  reads=[rs["ps_run"]], writes=[rs["exc"]])
    em.do("dve", lambda h: h.tensor_mul(sb["excm"][0:1, :, :], sb["exc"][0:1, :, :],
                                        sb["rankoh"][0:1, :].rearrange("a (g e) -> a g e", e=E64)),
          reads=[rs["rankoh"], rs["exc"]], writes=[rs["excm"]])
    em.do("dve", lambda h: h.tensor_reduce(sb["scr"][0:1, 0:E64],
                                           sb["excm"][0:1, :, :].transpose([0, 2, 1]), AX.X, ALU.add),
          reads=[rs["excm"]], writes=[rs["scr"]])
    em.do("dve", lambda h: h.tensor_copy(sb["baseown"][0:1, :], sb["scr"][0:1, 0:E64]),
          reads=[rs["scr"]], writes=[rs["baseown"]])


def emit_l0ffn_P2(nc, em, sb, ps, rs, rwT_ap, xb_d, w_d, s8u_d, hc, ownrows, x1_dram):
    for j in range(NT_OWN):
        c = j % 2
        xt, xbf, st = f"xt{c}", f"xbf{c}", f"st{c}"
        lgr, rep, m8 = f"lgr{c}", f"rep{c}", f"m8{c}"
        lgp, lgpr = (ps["lg"], rs["ps_lg"]) if c == 0 else (ps["lg2"], rs["ps_lg2"])
        em.do("sp", lambda h, j=j: h.dma_start(out=sb["oidx"][:, :], in_=ownrows[j, :, :]),
              writes=[rs["oidx"]], dma=True)
        em.do("pool", lambda h, j=j, xt=xt: h.indirect_dma_start(
            sb[xt][:, :], None, x1_dram[0:128, :], IndirectOffsetOnAxis(sb["oidx"][:, :], 0),
            bounds_check=bcreg(em.trk, h, NB * T - 1), oob_is_err=False),
            reads=[rs["oidx"], rs["x1_d"]], writes=[rs[xt]], dma=True)
        em.do("act", lambda h, xt=xt, st=st, c=c: h.activation(out=sb[f"scr2{c}"][:, :], in_=sb[xt][:, :],
                                                               func=AF.Square, accum_out=sb[st][:, 0:1]),
              reads=[rs[xt]], writes=[rs[f"scr2{c}"], rs[st]])
        em.do("dve", lambda h, xt=xt, st=st: h.reduce_sum(sb[st][:, 1:2], sb[xt][:, :], AX.X, negate=True),
              reads=[rs[xt]], writes=[rs[st]])
        em.do("dve", lambda h, st=st: h.tensor_scalar(sb[st][:, 2:3], sb[st][:, 1:2], 1.0 / D, None, ALU.mult),
              reads=[rs[st]], writes=[rs[st]])
        em.do("dve", lambda h, st=st: h.tensor_scalar(sb[st][:, 3:4], sb[st][:, 0:1], 1.0 / D, None, ALU.mult),
              reads=[rs[st]], writes=[rs[st]])
        em.do("dve", lambda h, st=st: h.scalar_tensor_tensor(out=sb[st][:, 4:5], in0=sb[st][:, 2:3],
                                                             scalar=sb[st][:, 2:3], in1=sb[st][:, 3:4],
                                                             op0=ALU.mult, op1=ALU.subtract),
              reads=[rs[st]], writes=[rs[st]])
        em.do("act", lambda h, st=st: h.activation(out=sb[st][:, 5:6], in_=sb[st][:, 4:5], func=AF.Sqrt,
                                                   bias=sb["konst"][:, 0:1], scale=sb["konst"][:, 1:2]),
              reads=[rs[st], rs["konst"]], writes=[rs[st]])
        em.do("dve", lambda h, st=st: h.reciprocal(sb[st][:, 6:7], sb[st][:, 5:6]),
              reads=[rs[st]], writes=[rs[st]])
        em.do("dve", lambda h, xt=xt, xbf=xbf, st=st: h.tensor_scalar(
            sb[xbf][:, :], sb[xt][:, :], sb[st][:, 2:3], sb[st][:, 6:7], ALU.add, ALU.mult),
            reads=[rs[xt], rs[st]], writes=[rs[xbf]])
        for ki in range(8):
            tc = ki % 2
            tb, tbr = ps[f"a{tc}"], rs[f"ps_a{tc}"]
            em.do("pe", lambda h, ki=ki, xbf=xbf, tb=tb: h.matmul(
                tb[0:128, 0:128], sb[xbf][:, ki * 128:(ki + 1) * 128], sb["ident"][:, :],
                start=True, stop=True),
                reads=[rs[xbf], rs["ident"]], writes=[tbr])
            eng = "dve" if ki % 2 == 0 else "act"
            fn = (lambda h, ki=ki, j=j, tb=tb: h.tensor_copy(sb["xlnT"][:, ki, j * 128:(j + 1) * 128], tb[0:128, 0:128])) \
                if eng == "dve" else \
                 (lambda h, ki=ki, j=j, tb=tb: h.copy(out=sb["xlnT"][:, ki, j * 128:(j + 1) * 128], in_=tb[0:128, 0:128]))
            em.do(eng, fn, reads=[tbr], writes=[rs["xlnT"]])
        for ki in range(8):
            em.do("pe", lambda h, ki=ki, j=j, lgp=lgp: h.matmul(
                lgp[:, :], sb["xlnT"][:, ki, j * 128:(j + 1) * 128], sb["rwT"][:, ki, :],
                start=(ki == 0), stop=(ki == 7), skip_group_check=True),
                reads=[rs["xlnT"], rs["rwT"]], writes=[lgpr])
        em.do("dve", lambda h, lgr=lgr, lgp=lgp: h.tensor_copy(sb[lgr][:, :], lgp[:, :]),
              reads=[lgpr], writes=[rs[lgr]])
        em.do("dve", lambda h, lgr=lgr, m8=m8: h.max(out=sb[m8][:, :], in_=sb[lgr][:, :]),
              reads=[rs[lgr]], writes=[rs[m8]])
        em.do("dve", lambda h, lgr=lgr, m8=m8: h.max_index(out=sb["i8u"][:, :], in_max=sb[m8][:, :], in_values=sb[lgr][:, :]),
              reads=[rs[lgr], rs[m8]], writes=[rs["i8u"]])
        em.do("dve", lambda h: h.tensor_copy(sb["i8f"][:, :], sb["i8u"][:, :]), reads=[rs["i8u"]], writes=[rs["i8f"]])
        em.do("dve", lambda h, lgr=lgr, rep=rep, m8=m8: h.match_replace(out=sb[rep][:, :], in_to_replace=sb[m8][:, :],
                                                                        in_values=sb[lgr][:, :], imm_value=-1e30),
              reads=[rs[lgr], rs[m8]], writes=[rs[rep]])
        em.do("dve", lambda h, lgr=lgr, rep=rep: h.tensor_tensor(sb["scr"][:, :], sb[lgr][:, :], sb[rep][:, :], ALU.not_equal),
              reads=[rs[lgr], rs[rep]], writes=[rs["scr"]])
        em.do("dve", lambda h, c=c: h.tensor_copy(sb[f"Mbf{c}"][:, :], sb["scr"][:, :]),
              reads=[rs["scr"]], writes=[rs[f"Mbf{c}"]])
        em.do("dve", lambda h, st=st, m8=m8: h.tensor_scalar(sb[st][:, 7:8], sb[m8][:, 0:1], -1.0, None, ALU.mult),
              reads=[rs[m8]], writes=[rs[st]])
        em.do("act", lambda h, lgr=lgr, rep=rep, st=st: h.activation(out=sb[rep][:, :], in_=sb[lgr][:, :], func=AF.Exp,
                                                                     bias=sb[st][:, 7:8]),
              reads=[rs[lgr], rs[st]], writes=[rs[rep]])
        em.do("dve", lambda h, rep=rep: h.tensor_mul(sb["G"][:, :], sb["scr"][:, :], sb[rep][:, :]),
              reads=[rs["scr"], rs[rep]], writes=[rs["G"]])
        em.do("dve", lambda h, st=st: h.tensor_reduce(sb[st][:, 7:8], sb["G"][:, :], AX.X, ALU.add),
              reads=[rs["G"]], writes=[rs[st]])
        em.do("dve", lambda h, st=st: h.reciprocal(sb[st][:, 7:8], sb[st][:, 7:8]),
              reads=[rs[st]], writes=[rs[st]])
        em.do("dve", lambda h, st=st: h.tensor_scalar(sb["G"][:, :], sb["G"][:, :], sb[st][:, 7:8], None, ALU.mult),
              reads=[rs[st]], writes=[rs["G"]])
        if j == 0:
            em.do("dve", lambda h: h.memset(sb["runrow"][0:1, :], 0.0), writes=[rs["runrow"]])
        else:
            em.do("act", lambda h: h.copy(out=sb["runrow"][0:1, :], in_=ps["runP"][0:1, :]),
                  reads=[rs["ps_runP"]], writes=[rs["runrow"]])
        em.do("pe", lambda h, c=c: h.matmul(ps["S"][:, :], sb["triu"][:, :], sb[f"Mbf{c}"][:, :], start=True, stop=False),
              reads=[rs["triu"], rs[f"Mbf{c}"]], writes=[rs["ps_S"]])
        em.do("pe", lambda h: h.matmul(ps["S"][:, :], sb["onesf"][0:1, :], sb["runrow"][0:1, :],
                                       start=False, stop=False),
              reads=[rs["onesf"], rs["runrow"]], writes=[rs["ps_S"]])
        em.do("pe", lambda h: h.matmul(ps["S"][:, :], sb["onesf"][0:1, :], sb["baseown"][0:1, :],
                                       start=False, stop=True),
              reads=[rs["onesf"], rs["baseown"]], writes=[rs["ps_S"]])
        em.do("pe", lambda h, j=j, c=c: h.matmul(ps["runP"][0:1, :], sb["onesf"][:, 0:1], sb[f"Mbf{c}"][:, :],
                                                 start=(j == 0), stop=(j == NT_OWN - 1), skip_group_check=True),
              reads=[rs["onesf"], rs[f"Mbf{c}"]], writes=[rs["ps_runP"]])
        em.do("dve", lambda h: h.tensor_copy(sb["S"][:, :], ps["S"][:, :]), reads=[rs["ps_S"]], writes=[rs["S"]])
        em.do("dve", lambda h: h.tensor_scalar(sb["inv"][:, :], sb["S"][:, :], float(CAP), None, ALU.is_ge),
              reads=[rs["S"]], writes=[rs["inv"]])
        em.do("dve", lambda h: h.tensor_add(sb["slot"][:, :], sb["S"][:, :], sb["iota512"][:, :]),
              reads=[rs["S"], rs["iota512"]], writes=[rs["slot"]])
        em.do("dve", lambda h: h.scalar_tensor_tensor(out=sb["slot"][:, :], in0=sb["inv"][:, :],
                                                      scalar=HUGE_SLOT, in1=sb["slot"][:, :],
                                                      op0=ALU.mult, op1=ALU.add),
              reads=[rs["inv"]], writes=[rs["slot"]])
        for k in range(8):
            em.do("dve", lambda h, k=k: h.scalar_tensor_tensor(out=sb["scr"][:, :], in0=sb["iotae"][:, :],
                                                               scalar=sb["i8f"][:, k:k + 1], in1=sb["slot"][:, :],
                                                               op0=ALU.is_equal, op1=ALU.mult,
                                                               accum_out=sb["s8"][:, k:k + 1]),
                  reads=[rs["iotae"], rs["i8f"], rs["slot"]], writes=[rs["scr"], rs["s8"]])
            em.do("dve", lambda h, k=k: h.scalar_tensor_tensor(out=sb["scr"][:, :], in0=sb["iotae"][:, :],
                                                               scalar=sb["i8f"][:, k:k + 1], in1=sb["G"][:, :],
                                                               op0=ALU.is_equal, op1=ALU.mult,
                                                               accum_out=sb["w8"][:, k:k + 1]),
                  reads=[rs["iotae"], rs["i8f"], rs["G"]], writes=[rs["scr"], rs["w8"]])
        em.do("dve", lambda h: h.tensor_copy(sb["s8u"][:, :], sb["s8"][:, :]), reads=[rs["s8"]], writes=[rs["s8u"]])
        em.do("dve", lambda h: h.tensor_copy(sb["w8x2"][:, :, 0], sb["w8"][:, :]), reads=[rs["w8"]], writes=[rs["w8x2"]])
        em.do("dve", lambda h: h.tensor_copy(sb["w8x2"][:, :, 1], sb["w8"][:, :]), reads=[rs["w8"]], writes=[rs["w8x2"]])
        em.do("sp", lambda h, j=j: h.dma_start(out=s8u_d[j, :, :], in_=sb["s8u"][:, :]),
              reads=[rs["s8u"]], writes=[rs["s8u_d"]], dma=True)
        for k in range(8):
            em.do("pool", lambda h, k=k, xbf=xbf: h.indirect_dma_start(
                xb_d[0:128, :], IndirectOffsetOnAxis(sb["s8u"][:, k:k + 1], 0),
                sb[xbf][:, :], None, bounds_check=bcreg(em.trk, h, E64 * CAP - 1), oob_is_err=False),
                reads=[rs["s8u"], rs[xbf]], writes=[rs["xb_d"]], dma=True)
            em.do("pool", lambda h, k=k: h.indirect_dma_start(
                w_d[0:128, :], IndirectOffsetOnAxis(sb["s8u"][:, k:k + 1], 0),
                sb["w8x2"][:, k, :], None, bounds_check=bcreg(em.trk, h, E64 * CAP - 1), oob_is_err=False),
                reads=[rs["s8u"], rs["w8x2"]], writes=[rs["w_d"]], dma=True)


def emit_l0ffn_S2(nc, em, sb, ps, rs, wsgu_ap, wsd_ap, rbase_d, ownrows, x1_dram):
    em.do("pool", lambda h: h.dma_start(out=sb["wsgu"][:, :, :], in_=wsgu_ap.rearrange("(k p) n -> p k n", p=128)),
          writes=[rs["wsgu"]], dma=True)
    em.do("pool", lambda h: h.dma_start(out=sb["wsd"][:, :, :], in_=wsd_ap.rearrange("(k p) n -> p k n", p=128)),
          writes=[rs["wsd"]], dma=True)
    SF = 768
    for sp_ in range(8):
        t0 = sp_ * 512
        for fc in range(6):
            pg, pgr = ps[f"a{0}"], rs["ps_a0"]
            pu, pur = ps[f"a{1}"], rs["ps_a1"]
            for ki in range(8):
                em.do("pe", lambda h, fc=fc, ki=ki, t0=t0, pg=pg: h.matmul(
                    pg[:, :], sb["wsgu"][:, ki, fc * 128:(fc + 1) * 128],
                    sb["xlnT"][:, ki, t0:t0 + 512], start=(ki == 0), stop=(ki == 7)),
                    reads=[rs["wsgu"], rs["xlnT"]], writes=[pgr])
                em.do("pe", lambda h, fc=fc, ki=ki, t0=t0, pu=pu: h.matmul(
                    pu[:, :], sb["wsgu"][:, ki, SF + fc * 128:SF + (fc + 1) * 128],
                    sb["xlnT"][:, ki, t0:t0 + 512], start=(ki == 0), stop=(ki == 7)),
                    reads=[rs["wsgu"], rs["xlnT"]], writes=[pur])
            em.do("act", lambda h, fc=fc, pg=pg: h.activation(out=sb[f"gbf{fc % 2}"][:, :], in_=pg[:, :], func=AF.Silu),
                  reads=[pgr], writes=[rs[f"gbf{fc % 2}"]])
            em.do("dve", lambda h, fc=fc, pu=pu: h.tensor_mul(sb["hsh"][:, fc, :], sb[f"gbf{fc % 2}"][:, :], pu[:, :]),
                  reads=[rs[f"gbf{fc % 2}"], pur], writes=[rs["hsh"]])
        for tt in range(4):
            tg = t0 + tt * 128
            if ownrows is None:
                pass
            else:
                em.do("sp", lambda h, tg=tg: h.dma_start(out=sb["oidx"][:, :], in_=ownrows[tg // 128, :, :]),
                      writes=[rs["oidx"]], dma=True)
                em.do("pool", lambda h: h.indirect_dma_start(
                    sb["xr"][:, :], None, x1_dram[0:128, :], IndirectOffsetOnAxis(sb["oidx"][:, :], 0),
                    bounds_check=bcreg(em.trk, h, NB * T - 1), oob_is_err=False),
                    reads=[rs["oidx"], rs["x1_d"]], writes=[rs["xr"]], dma=True)
            for half in range(2):
                pt, ptr = ps[f"a{half}"], rs[f"ps_a{half}"]
                for fc in range(6):
                    em.do("pe", lambda h, fc=fc, tt=tt, half=half, pt=pt: h.matmul(
                        pt[:, :], sb["hsh"][:, fc, tt * 128:(tt + 1) * 128],
                        sb["wsd"][:, fc, half * 512:(half + 1) * 512], start=(fc == 0), stop=(fc == 5)),
                        reads=[rs["hsh"], rs["wsd"]], writes=[ptr])
                em.do("dve", lambda h, half=half, pt=pt: h.tensor_add(
                    sb["acc"][:, half * 512:(half + 1) * 512], pt[:, :],
                    sb["xr"][:, half * 512:(half + 1) * 512]),
                    reads=[ptr, rs["xr"]], writes=[rs["acc"]])
            em.do("sp", lambda h, tg=tg: h.dma_start(out=rbase_d[tg:tg + 128, :], in_=sb["acc"][:, :]),
                  reads=[rs["acc"]], writes=[rs["rbase_d"]], dma=True)


def emit_l0ffn_E2(nc, em, sb, ps, rs, wg_ap, wu_ap, wd_ap, xb_d, w_d, y_d):
    for e in range(E64):
        we = e % 2
        wg, wu, wd = f"wg{we}", f"wu{we}", f"wd{we}"
        xbT = f"xbT{we}"
        em.do("pool", lambda h, e=e, wg=wg: h.dma_start(out=sb[wg][:, :, :],
                                                        in_=wg_ap[e].rearrange("(k p) n -> p k n", p=128)),
              writes=[rs[wg]], dma=True)
        em.do("pool", lambda h, e=e, wu=wu: h.dma_start(out=sb[wu][:, :, :],
                                                        in_=wu_ap[e].rearrange("(k p) n -> p k n", p=128)),
              writes=[rs[wu]], dma=True)
        em.do("pool", lambda h, e=e, wd=wd: h.dma_start(out=sb[wd][:, :, :],
                                                        in_=wd_ap[e].rearrange("(k p) n -> p k n", p=128)),
              writes=[rs[wd]], dma=True)
        for dk in range(8):
            em.do("sp", lambda h, e=e, dk=dk, xbT=xbT: h.dma_start(
                out=sb[xbT][:, dk, :], in_=xb_d[e * CAP:(e + 1) * CAP, dk * 128:(dk + 1) * 128],
                transpose=True),
                reads=[rs["xb_d"]], writes=[rs[xbT]], dma=True)
        for st_ in range(4):
            em.do("sp", lambda h, e=e, st_=st_: h.dma_start(
                out=sb["wcol"][:, 2 * st_:2 * st_ + 2],
                in_=w_d[e * CAP + st_ * 128:e * CAP + (st_ + 1) * 128, 0:2]),
                reads=[rs["w_d"]], writes=[rs["wcol"]], dma=True)
        for fc in range(6):
            pc = fc % 2
            pg, pgr = ps[f"c{pc}"], rs[f"ps_c{pc}"]
            pu, pur = ps[f"d{pc}"], rs[f"ps_d{pc}"]
            gbf = f"gbf{pc}"
            for ki in range(8):
                em.do("pe", lambda h, fc=fc, ki=ki, pg=pg, wg=wg, xbT=xbT: h.matmul(
                    pg[:, :], sb[wg][:, ki, fc * 128:(fc + 1) * 128],
                    sb[xbT][:, ki, :], start=(ki == 0), stop=(ki == 7)),
                    reads=[rs[wg], rs[xbT]], writes=[pgr])
                em.do("pe", lambda h, fc=fc, ki=ki, pu=pu, wu=wu, xbT=xbT: h.matmul(
                    pu[:, :], sb[wu][:, ki, fc * 128:(fc + 1) * 128],
                    sb[xbT][:, ki, :], start=(ki == 0), stop=(ki == 7)),
                    reads=[rs[wu], rs[xbT]], writes=[pur])
            em.do("act", lambda h, pg=pg, gbf=gbf: h.activation(out=sb[gbf][:, :], in_=pg[:, :], func=AF.Silu),
                  reads=[pgr], writes=[rs[gbf]])
            em.do("dve", lambda h, fc=fc, pu=pu, gbf=gbf: h.tensor_mul(sb["hE"][:, fc, :], sb[gbf][:, :], pu[:, :]),
                  reads=[rs[gbf], pur], writes=[rs["hE"]])
        for st_ in range(4):
            yc = st_ % 2
            ybf = f"ybf{yc}"
            for half in range(2):
                pt, ptr = ps[f"a{half}"], rs[f"ps_a{half}"]
                for fc in range(6):
                    em.do("pe", lambda h, fc=fc, st_=st_, half=half, pt=pt, wd=wd: h.matmul(
                        pt[:, :], sb["hE"][:, fc, st_ * 128:(st_ + 1) * 128],
                        sb[wd][:, fc, half * 512:(half + 1) * 512], start=(fc == 0), stop=(fc == 5)),
                        reads=[rs["hE"], rs[wd]], writes=[ptr])
                em.do("dve", lambda h, half=half, st_=st_, pt=pt, ybf=ybf: h.tensor_scalar(
                    sb[ybf][:, half * 512:(half + 1) * 512], pt[:, :],
                    sb["wcol"][:, 2 * st_:2 * st_ + 1], None, ALU.mult),
                    reads=[ptr, rs["wcol"]], writes=[rs[ybf]])
            em.do("pool", lambda h, e=e, st_=st_, ybf=ybf: h.dma_start(
                out=y_d[e * CAP + st_ * 128:e * CAP + (st_ + 1) * 128, :], in_=sb[ybf][:, :]),
                reads=[rs[ybf]], writes=[rs["y_d"]], dma=True)


def emit_l0ffn_G2(nc, em, sb, ps, rs, rbase_d, s8u_d, y_d, x2_out):
    for j in range(NT_OWN):
        c = j % 2
        xt = f"xt{c}"
        em.do("sp", lambda h, j=j, xt=xt: h.dma_start(out=sb[xt][:, :], in_=rbase_d[j * 128:(j + 1) * 128, :]),
              reads=[rs["rbase_d"]], writes=[rs[xt]], dma=True)
        em.do("sp", lambda h, j=j: h.dma_start(out=sb["s8u"][:, :], in_=s8u_d[j, :, :]),
              reads=[rs["s8u_d"]], writes=[rs["s8u"]], dma=True)
        for k in range(8):
            em.do("pool", lambda h, k=k, xt=xt: h.indirect_dma_start(
                sb[xt][:, :], None, y_d[0:128, :], IndirectOffsetOnAxis(sb["s8u"][:, k:k + 1], 0),
                bounds_check=bcreg(em.trk, h, E64 * CAP - 1), oob_is_err=False, compute_op=ALU.add),
                reads=[rs["s8u"], rs["y_d"]], writes=[rs[xt]], dma=True)
        em.do("sp", lambda h, j=j, xt=xt: h.dma_start(out=x2_out[j * 128:(j + 1) * 128, :], in_=sb[xt][:, :]),
              reads=[rs[xt]], writes=[rs["x2_o"]], dma=True)


def build_fullfused3_nc():
    from contextlib import ExitStack
    nc = bass.Bass("TRN2", target_bir_lowering=False)
    x_in = nc.dram_tensor("x_in", [NB, T, D], F32, kind="ExternalInput").ap()
    ownrows = nc.dram_tensor("ownrows", [NT_OWN, 128, 1], U32, kind="ExternalInput").ap()
    rankoh = nc.dram_tensor("rankoh", [1, 512], F32, kind="ExternalInput").ap()
    wq_a = {w: nc.dram_tensor(w, [D, D], F32, kind="ExternalInput").ap()
            for w in ("wq", "wk", "wv", "wo")}
    rwT = nc.dram_tensor("rwT", [D, E64], F32, kind="ExternalInput").ap()
    wsgu = nc.dram_tensor("wsgu", [D, 1536], F32, kind="ExternalInput").ap()
    wsd = nc.dram_tensor("wsd", [768, D], F32, kind="ExternalInput").ap()
    wg = nc.dram_tensor("wg", [E64, D, 768], F32, kind="ExternalInput").ap()
    wu = nc.dram_tensor("wu", [E64, D, 768], F32, kind="ExternalInput").ap()
    wd = nc.dram_tensor("wd", [E64, 768, D], F32, kind="ExternalInput").ap()
    wqkv = nc.dram_tensor("wqkv", [D, 3 * D], F32, kind="ExternalInput").ap()
    bqkv = nc.dram_tensor("bqkv", [3 * D], F32, kind="ExternalInput").ap()
    wout = nc.dram_tensor("wout", [D, D], F32, kind="ExternalInput").ap()
    bout = nc.dram_tensor("bout", [D], F32, kind="ExternalInput").ap()
    rw8 = nc.dram_tensor("rw8", [D, 8], F32, kind="ExternalInput").ap()
    w1T = nc.dram_tensor("w1T", [8, D, 2048], F32, kind="ExternalInput").ap()
    w2T = nc.dram_tensor("w2T", [8, 2048, D], F32, kind="ExternalInput").ap()
    out_ap = nc.dram_tensor("out", [4096, D], F32, kind="ExternalOutput").ap()

    x1_d = nc.dram_tensor("x1_d", [NB * T, D], F32, kind="Internal")
    xb_d = nc.dram_tensor("xb_d", [E64 * CAP, D], BF16, kind="Internal").ap()
    w_d = nc.dram_tensor("w_d", [E64 * CAP, 2], F32, kind="Internal").ap()
    y_d = nc.dram_tensor("y_d", [E64 * CAP, D], BF16, kind="Internal").ap()
    s8u_d = nc.dram_tensor("s8u_d", [NT_OWN, 128, 8], U32, kind="Internal").ap()
    rbase_d = nc.dram_tensor("rbase_d", [4096, D], F32, kind="Internal").ap()
    x2_d = nc.dram_tensor("x2_d", [4096, D], F32, kind="Internal")
    x3_d = nc.dram_tensor("x3_d", [4096, D], F32, kind="Internal")
    y_acc = nc.dram_tensor("y_acc", [4096, D], F32, kind="Internal").ap()

    h_ident, h_tri, h_konst = l0_attn_consts(nc)
    hc = l0ffn_consts(nc)
    es = ExitStack()
    with es:
        trk = Trk(nc, es)
        em = Emit(trk)
        with ExitStack() as es0:
            sb, ps = alloc_l0_attn2(nc, es0)
            rs = {k: Rsrc(k) for k in L0_RS2}
            x1v = x1_d.ap().rearrange("(b t) d -> b t d", b=NB)
            for b in range(NB):
                em.do("pool", lambda h, b=b: h.dma_start(out=x1v[b, 0:QR0, :], in_=x_in[b, 0:QR0, :]),
                      writes=[rs["out_0"]], dma=True)
            emit_l0_attn2(nc, em, sb, ps, rs, x_in, x1v, wq_a, NB, h_ident, h_tri, h_konst,
                          out_base=QR0)
            trk.barrier()
        with ExitStack() as es1:
            sb, ps = alloc_l0ffn_core(nc, es1)
            rs = {k: Rsrc(k) for k in L0FFN_RS2}
            with ExitStack() as esR:
                ps["lg"] = esR.enter_context(nc.psum_tensor("f_ps_lg", [128, E64], F32))
                ps["lg2"] = esR.enter_context(nc.psum_tensor("f_ps_lg2", [128, E64], F32))
                ps["run"] = esR.enter_context(nc.psum_tensor("f_ps_run", [1, E64], F32))
                emit_l0ffn_R2(nc, em, sb, ps, rs, x1_d.ap(), rankoh, rwT, hc)
                trk.barrier()
            with ExitStack() as esP:
                ps["lg"] = esP.enter_context(nc.psum_tensor("f_ps_lgp", [128, E64], F32))
                ps["lg2"] = esP.enter_context(nc.psum_tensor("f_ps_lg2p", [128, E64], F32))
                ps["S"] = esP.enter_context(nc.psum_tensor("f_ps_S", [128, E64], F32))
                ps["runP"] = esP.enter_context(nc.psum_tensor("f_ps_runP", [1, E64], F32))
                sb["xlnT"] = esP.enter_context(nc.sbuf_tensor("f_xlnT", [128, 8, 4096], BF16))
                sb["wsgu"] = esP.enter_context(nc.sbuf_tensor("f_wsgu", [128, 8, 1536], BF16))
                sb["wsd"] = esP.enter_context(nc.sbuf_tensor("f_wsd", [128, 6, 1024], BF16))
                sb["hsh"] = esP.enter_context(nc.sbuf_tensor("f_hsh", [128, 6, 512], BF16))
                rs["ps_lg"] = Rsrc("ps_lg")
                rs["ps_lg2"] = Rsrc("ps_lg2")
                emit_l0ffn_P2(nc, em, sb, ps, rs, rwT, xb_d, w_d, s8u_d, hc, ownrows, x1_d.ap())
                emit_l0ffn_S2(nc, em, sb, ps, rs, wsgu, wsd, rbase_d, ownrows, x1_d.ap())
                trk.barrier()
            with ExitStack() as esE:
                for p in range(2):
                    ps[f"c{p}"] = esE.enter_context(nc.psum_tensor(f"f_ps_c{p}", [128, 512], F32))
                    ps[f"d{p}"] = esE.enter_context(nc.psum_tensor(f"f_ps_d{p}", [128, 512], F32))
                    sb[f"wg{p}"] = esE.enter_context(nc.sbuf_tensor(f"f_wg{p}", [128, 8, 768], BF16))
                    sb[f"wu{p}"] = esE.enter_context(nc.sbuf_tensor(f"f_wu{p}", [128, 8, 768], BF16))
                    sb[f"wd{p}"] = esE.enter_context(nc.sbuf_tensor(f"f_wd{p}", [128, 6, 1024], BF16))
                    sb[f"xbT{p}"] = esE.enter_context(nc.sbuf_tensor(f"f_xbT{p}", [128, 8, 512], BF16))
                    sb[f"ybf{p}"] = esE.enter_context(nc.sbuf_tensor(f"f_ybf{p}", [128, 1024], BF16))
                sb["hE"] = esE.enter_context(nc.sbuf_tensor("f_hE", [128, 6, 512], BF16))
                emit_l0ffn_E2(nc, em, sb, ps, rs, wg, wu, wd, xb_d, w_d, y_d)
                emit_l0ffn_G2(nc, em, sb, ps, rs, rbase_d, s8u_d, y_d, x2_d.ap())
                trk.barrier()
        with ExitStack() as es2:
            sb, ps = alloc_l1attn2(nc, es2)
            rs = {k: Rsrc(k) for k in L1A_RS2}
            x2v = x2_d.ap().rearrange("(b t) d -> b t d", b=2)
            x3v = x3_d.ap().rearrange("(b t) d -> b t d", b=2)
            emit_l1_attn2(nc, em, sb, ps, rs, x2v, x3v, wqkv, bqkv, wout, bout, 2, hc)
            trk.barrier()
        with ExitStack() as es3:
            sb, ps = alloc_l1ffn2(nc, es3)
            rs = {k: Rsrc(k) for k in L1F_RS2}
            rs["w2sb1"] = rs["w2sb0"]
            rs["scr2ln0"] = rs["xr0"]
            rs["scr2ln1"] = rs["xr0"]
            rs["xr1"] = rs["xr0"]
            rs["acc1"] = rs["acc0"]
            rs["yev0"] = rs["acc0"]
            rs["yev1"] = rs["acc0"]
            emit_l1_ffn2(nc, em, sb, ps, rs, x3_d.ap(), out_ap, rw8, w1T, w2T, y_acc, hc)
            trk.barrier()
        block = es.enter_context(nc.Block())
        trk.replay(block)
    return nc



_NC_CACHE = {}


def _get_nc(name, builder):
    if name not in _NC_CACHE:
        _NC_CACHE[name] = builder()
    return _NC_CACHE[name]


def kernel(L0_attn_x, L0_attn_Wq, L0_attn_Wk, L0_attn_Wv, L0_attn_Wo,
           L0_ffn_router_w, L0_ffn_w_gate, L0_ffn_w_up, L0_ffn_w_down,
           L0_ffn_ws_gate_up, L0_ffn_ws_down,
           L1_attn_Wqkv, L1_attn_bqkv, L1_attn_Wout, L1_attn_bout,
           L1_ffn_router_w, L1_ffn_w1, L1_ffn_w2):
    x = np.ascontiguousarray(L0_attn_x.astype(np.float32, copy=False))
    nc = _get_nc("full3", build_fullfused3_nc)
    rwT = np.ascontiguousarray(L0_ffn_router_w.T)
    rw8 = np.ascontiguousarray(L1_ffn_router_w.T)
    w1T = np.ascontiguousarray(L1_ffn_w1.transpose(0, 2, 1))
    w2T = np.ascontiguousarray(L1_ffn_w2.transpose(0, 2, 1))
    in_maps = []
    for r in range(8):
        oh = np.zeros((8, E64), np.float32)
        if r > 0:
            oh[r - 1] = 1.0
        own = (np.arange(4096, dtype=np.uint32) + 4096 * r).reshape(NT_OWN, 128, 1)
        in_maps.append({
            "x_in": x, "ownrows": own,
            "rankoh": np.ascontiguousarray(oh.reshape(1, 512)),
            "wq": L0_attn_Wq, "wk": L0_attn_Wk, "wv": L0_attn_Wv, "wo": L0_attn_Wo,
            "rwT": rwT, "wsgu": L0_ffn_ws_gate_up, "wsd": L0_ffn_ws_down,
            "wg": L0_ffn_w_gate, "wu": L0_ffn_w_up, "wd": L0_ffn_w_down,
            "wqkv": L1_attn_Wqkv, "bqkv": L1_attn_bqkv,
            "wout": L1_attn_Wout, "bout": L1_attn_bout,
            "rw8": rw8, "w1T": w1T, "w2T": w2T,
        })
    res = run_bass_kernel_spmd(nc, in_maps, core_ids=list(range(8)))
    out = np.concatenate([res.results[r]["out"] for r in range(8)], axis=0)
    return np.ascontiguousarray(out.reshape(NB, T, D)).astype(np.float32, copy=False)


# version 08
# speedup vs baseline: 2.0973x; 1.0017x over previous
import numpy as np
import ml_dtypes
import concourse.bass as bass
import concourse.mybir as mybir
from concourse.bass import IndirectOffsetOnAxis
from concourse.bass_utils import run_bass_kernel_spmd

F32 = mybir.dt.float32
BF16 = mybir.dt.bfloat16
U32 = mybir.dt.uint32
AF = mybir.ActivationFunctionType
ALU = mybir.AluOpType
AX = mybir.AxisListType

D = 1024
NB = 16
T = 2048
NH = 16
DH = 64
NS = 4
WIN = 512
QR0 = 1532
NQ = 516
NCTX = 516
EPS = 1e-5

ENGS = ("pe", "act", "dve", "pool", "sp")


class Trk:

    NSUB = 8

    def __init__(self, nc, es):
        self.nc = nc
        self.sem = {e: es.enter_context(nc.semaphore(f"sem_{e}")) for e in ENGS}
        self.dsub = {e: [es.enter_context(nc.semaphore(f"sem_{e}{j}")) for j in range(self.NSUB)]
                     for e in ("sp", "pool")}
        self.ops = []
        self.fence = 0
        self.regcache = {}

    def _rec(self, eng, fn, deps, inc, est):
        oid = len(self.ops)
        ds = set()
        for d in deps:
            if d is None:
                continue
            if isinstance(d, int):
                ds.add(d)
            else:
                ds.update(d)
        self.ops.append({"id": oid, "eng": eng, "fn": fn, "inc": inc,
                         "deps": ds, "est": est, "fence": self.fence})
        return oid

    def op(self, eng, fn, deps=(), est=400):
        return self._rec(eng, fn, deps, 1, est)

    def dma(self, eng, fn, deps=(), est=2000):
        return self._rec(eng, fn, deps, 16, est)

    def wait(self, eng, deps):
        self._rec(eng, None, deps, 0, 0)

    def barrier(self):
        self.fence += 1

    def replay(self, block):
        order = {e: [] for e in ENGS}
        comp = {}
        nf = self.fence + 1
        groups = [[] for _ in range(nf)]
        for o in self.ops:
            groups[o["fence"]].append(o)
        eng_time = {e: 0.0 for e in ENGS}
        for g in groups:
            per_eng = {e: [o for o in g if o["eng"] == e] for e in ENGS}
            idx = {e: 0 for e in ENGS}
            ndone = 0
            total = len(g)
            gstart = max(eng_time.values())
            for e in ENGS:
                eng_time[e] = max(eng_time[e], gstart)
            scheduled = set()
            while ndone < total:
                best = None
                best_t = None
                for e in ENGS:
                    lst = per_eng[e]
                    i = idx[e]
                    for k in range(i, min(i + 24, len(lst))):
                        o = lst[k]
                        if o["id"] in scheduled:
                            continue
                        if any((d >= g[0]["id"] and d not in scheduled and
                                self.ops[d]["fence"] == o["fence"])
                               for d in o["deps"]):
                            continue
                        ready = eng_time[e]
                        for d in o["deps"]:
                            ready = max(ready, comp.get(d, 0.0))
                        t = ready
                        if best_t is None or t < best_t:
                            best, best_t = o, t
                        break
                if best is None:
                    for o in g:
                        if o["id"] not in scheduled:
                            best = o
                            best_t = eng_time[o["eng"]]
                            break
                e = best["eng"]
                scheduled.add(best["id"])
                comp[best["id"]] = best_t + best["est"]
                eng_time[e] = best_t + best["est"]
                order[e].append(best)
                lst = per_eng[e]
                while idx[e] < len(lst) and lst[idx[e]]["id"] in scheduled:
                    idx[e] += 1
                ndone += 1
        cross_needed = set()
        for o in self.ops:
            for d in o["deps"]:
                if self.ops[d]["eng"] == "pe" and o["eng"] != "pe":
                    cross_needed.add(d)
        last_pe_of_fence = {}
        for o in order["pe"]:
            last_pe_of_fence[o["fence"]] = o["id"]
        keep = cross_needed | set(last_pe_of_fence.values())
        for o in self.ops:
            if o["eng"] == "pe" and o["inc"] == 1 and o["id"] not in keep:
                o["inc"] = 0
        NSUB = self.NSUB
        pos_count = {}
        skey = {}
        allkeys = [(e, None) for e in ENGS] +                   [(e, j) for e in ("sp", "pool") for j in range(NSUB)]
        run = {k: 0 for k in allkeys}
        dma_ord = {"sp": 0, "pool": 0}
        fence_end = [dict() for _ in range(nf)]
        for e in ENGS:
            for o in order[e]:
                if o["inc"] == 16 and e in ("sp", "pool"):
                    j = dma_ord[e] % NSUB
                    dma_ord[e] += 1
                    k = (e, j)
                else:
                    k = (e, None)
                run[k] += o["inc"]
                pos_count[o["id"]] = run[k]
                skey[o["id"]] = k
                fence_end[o["fence"]][k] = run[k]
        for f in range(1, nf):
            for k in allkeys:
                fence_end[f][k] = max(fence_end[f].get(k, 0), fence_end[f - 1].get(k, 0))
        for f in range(nf):
            for k in allkeys:
                fence_end[f].setdefault(k, 0)

        sem_map = dict(self.sem)
        for e in ("sp", "pool"):
            for j in range(NSUB):
                sem_map[(e, j)] = self.dsub[e][j]

        def semof(k):
            return sem_map[k[0]] if k[1] is None else sem_map[k]

        ops_all = self.ops

        def mk(eng_name):
            def body(h):
                lw = {k: 0 for k in allkeys}
                cur_f = 0
                for o in order[eng_name]:
                    if o["fence"] > cur_f:
                        fb = fence_end[o["fence"] - 1]
                        for k in allkeys:
                            v = fb[k]
                            if v > lw[k]:
                                h.wait_ge(semof(k), v)
                                lw[k] = v
                        cur_f = o["fence"]
                    need = {}
                    for d in o["deps"]:
                        k = skey.get(d)
                        if k is None:
                            continue
                        if k[0] == "pe" and eng_name == "pe":
                            continue
                        v = pos_count.get(d, 0)
                        if v > need.get(k, 0):
                            need[k] = v
                    for k, v in need.items():
                        if v > lw[k]:
                            h.wait_ge(semof(k), v)
                            lw[k] = v
                    if o["fn"] is not None:
                        inst = o["fn"](h)
                        inst.then_inc(semof(skey[o["id"]]), o["inc"])
            return body

        block.tensor(mk("pe"))
        block.scalar(mk("act"))
        block.vector(mk("dve"))
        block.gpsimd(mk("pool"))
        block.sync(mk("sp"))


def ceil_div(a, b):
    return (a + b - 1) // b


def bcreg(trk, h, val, eng="pool"):
    key = (eng, val)
    r = trk.regcache.get(key)
    if r is None:
        r = h.to_reg(val)
        trk.regcache[key] = r
    return r


class Rsrc:
    __slots__ = ("w", "r", "name")

    def __init__(self, name=""):
        self.w = None
        self.r = []
        self.name = name


class Emit:
    def __init__(self, trk):
        self.trk = trk

    def do(self, eng, fn, reads=(), writes=(), dma=False, est=None):
        deps = []
        for r in reads:
            if r.w is not None:
                deps.append(r.w)
        for w in writes:
            if w.w is not None:
                deps.append(w.w)
            deps.extend(w.r)
        if dma:
            tok = self.trk.dma(eng, fn, deps, est=est if est else 2000)
        else:
            tok = self.trk.op(eng, fn, deps, est=est if est else 400)
        for r in reads:
            r.r.append(tok)
        for w in writes:
            w.w = tok
            w.r = []
        return tok



L0_LN_TILES = [(0, 4), (1532, 128), (1660, 128), (1788, 128), (1916, 128), (2044, 4)]
L0_LNT_COLS = [(0, 4), (4, 128), (132, 128), (260, 128), (388, 128), (516, 4)]
L0_QT = [(0, 128), (128, 128), (256, 128), (384, 128), (512, 4)]
L0_CT = [(0, 4), (4, 128), (132, 128), (260, 128), (388, 128)]
L0_CT_XLNT = [(0, 4), (8, 128), (136, 128), (264, 128), (392, 128)]


def alloc_l0_attn(nc, es):
    sb = {}
    ps = {}
    sb["xlnT"] = es.enter_context(nc.sbuf_tensor("xlnT", [128, 8, 520], BF16))
    sb["qf"] = es.enter_context(nc.sbuf_tensor("qf", [128, 8, 516], BF16))
    sb["kf"] = es.enter_context(nc.sbuf_tensor("kf", [128, 8, 516], BF16))
    sb["vt"] = es.enter_context(nc.sbuf_tensor("vt", [128, 5, 1024], BF16))
    sb["ao"] = es.enter_context(nc.sbuf_tensor("ao", [128, 8, 516], BF16))
    for w in ("wq", "wk", "wv", "wo"):
        sb[w] = es.enter_context(nc.sbuf_tensor(w + "_sb", [128, 8, 1024], BF16))
    sb["xt"] = es.enter_context(nc.sbuf_tensor("xt", [128, 1024], F32))
    sb["xsq"] = es.enter_context(nc.sbuf_tensor("xsq", [128, 1024], F32))
    sb["xlnb"] = es.enter_context(nc.sbuf_tensor("xlnb", [128, 1024], BF16))
    sb["st"] = es.enter_context(nc.sbuf_tensor("st", [128, 8], F32))
    sb["lg"] = es.enter_context(nc.sbuf_tensor("lg", [128, 516], F32))
    sb["esc"] = es.enter_context(nc.sbuf_tensor("esc", [128, 516], BF16))
    sb["dg"] = es.enter_context(nc.sbuf_tensor("dg", [128, 128], BF16))
    sb["wT"] = es.enter_context(nc.sbuf_tensor("wT", [128, 5 * 128], BF16))
    sb["xr"] = es.enter_context(nc.sbuf_tensor("xr", [128, 1024], F32))
    sb["xo"] = es.enter_context(nc.sbuf_tensor("xo", [128, 1024], F32))
    sb["ident"] = es.enter_context(nc.sbuf_tensor("ident", [128, 128], BF16))
    sb["konst"] = es.enter_context(nc.sbuf_tensor("konst", [128, 4], F32))
    sb["mtri"] = es.enter_context(nc.sbuf_tensor("mtri", [128, 128], F32))
    ps["t"] = es.enter_context(nc.psum_tensor("ps_t", [128, 128], BF16))
    ps["a"] = es.enter_context(nc.psum_tensor("ps_a", [128, 512], F32))
    ps["b"] = es.enter_context(nc.psum_tensor("ps_b", [128, 512], F32))
    ps["s"] = es.enter_context(nc.psum_tensor("ps_s", [128, 4], F32))
    ps["h"] = es.enter_context(nc.psum_tensor("ps_h", [64, 128], F32))
    ps["t2"] = es.enter_context(nc.psum_tensor("ps_t2", [128, 128], F32))
    return sb, ps


def l0_attn_consts(nc):
    ident = np.eye(128, dtype=np.float32)
    tri = np.where(np.arange(128)[None, :] <= np.arange(128)[:, None], 0.0, -1e30).astype(np.float32)
    h_ident = nc.inline_tensor(ident.astype(ml_dtypes.bfloat16), name="c_ident")
    h_tri = nc.inline_tensor(tri, name="c_tri")
    kv = np.broadcast_to(np.array([EPS, -1.0, 0.125, 0.0], dtype=np.float32), (128, 4)).copy()
    h_konst = nc.inline_tensor(kv, name="c_konst")
    return h_ident, h_tri, h_konst


def emit_l0_attn(nc, em, sbh, psh, rs, x_in, xq_out, w_aps, nb, h_ident, h_tri, h_konst, dbg=None, out_base=0):
    sb = sbh
    ps = psh

    em.do("sp", lambda h: h.dma_start(out=sb["ident"][:], in_=h_ident.ap()), writes=[rs["ident"]], dma=True)
    em.do("sp", lambda h: h.dma_start(out=sb["mtri"][:], in_=h_tri.ap()), writes=[rs["mtri"]], dma=True)
    em.do("sp", lambda h: h.dma_start(out=sb["konst"][:], in_=h_konst.ap()), writes=[rs["konst"]], dma=True)
    for w in ("wq", "wk", "wv", "wo"):
        src = w_aps[w].rearrange("(k p) n -> p k n", p=128)
        em.do("pool", lambda h, w=w, src=src: h.dma_start(out=sb[w][:, :, :], in_=src),
              writes=[rs[w]], dma=True)

    for b in range(nb):
        for ti, (r0, p) in enumerate(L0_LN_TILES):
            em.do("sp", lambda h, b=b, r0=r0, p=p: h.dma_start(out=sb["xt"][0:p, :], in_=x_in[b, r0:r0 + p, :]),
                  writes=[rs["xt"]], dma=True)
            em.do("act", lambda h, p=p: h.activation(out=sb["xsq"][0:p, :], in_=sb["xt"][0:p, :],
                                                     func=AF.Square, accum_out=sb["st"][0:p, 0:1]),
                  reads=[rs["xt"]], writes=[rs["xsq"], rs["st"]])
            em.do("dve", lambda h, p=p: h.reduce_sum(sb["st"][0:p, 1:2], sb["xt"][0:p, :], AX.X, negate=True),
                  reads=[rs["xt"]], writes=[rs["st"]])
            em.do("dve", lambda h, p=p: h.tensor_scalar(sb["st"][0:p, 2:3], sb["st"][0:p, 1:2], 1.0 / D, None, ALU.mult),
                  reads=[rs["st"]], writes=[rs["st"]])
            em.do("dve", lambda h, p=p: h.tensor_scalar(sb["st"][0:p, 3:4], sb["st"][0:p, 0:1], 1.0 / D, None, ALU.mult),
                  reads=[rs["st"]], writes=[rs["st"]])
            em.do("dve", lambda h, p=p: h.scalar_tensor_tensor(
                out=sb["st"][0:p, 4:5], in0=sb["st"][0:p, 2:3], scalar=sb["st"][0:p, 2:3],
                in1=sb["st"][0:p, 3:4], op0=ALU.mult, op1=ALU.subtract),
                reads=[rs["st"]], writes=[rs["st"]])
            em.do("act", lambda h, p=p: h.activation(out=sb["st"][0:p, 5:6], in_=sb["st"][0:p, 4:5],
                                                     func=AF.Sqrt, bias=sb["konst"][0:p, 0:1],
                                                     scale=sb["konst"][0:p, 1:2]),
                  reads=[rs["st"], rs["konst"]], writes=[rs["st"]])
            em.do("dve", lambda h, p=p: h.reciprocal(sb["st"][0:p, 6:7], sb["st"][0:p, 5:6]),
                  reads=[rs["st"]], writes=[rs["st"]])
            em.do("dve", lambda h, p=p: h.tensor_scalar(sb["xlnb"][0:p, :], sb["xt"][0:p, :],
                                                        sb["st"][0:p, 2:3], sb["st"][0:p, 6:7],
                                                        ALU.add, ALU.mult),
                  reads=[rs["xt"], rs["st"]], writes=[rs["xlnb"]])
            c0, cn = L0_LNT_COLS[ti]
            for ki in range(8):
                em.do("pe", lambda h, ki=ki, p=p: h.transpose(ps["t"][0:128, 0:p],
                                                              sb["xlnb"][0:p, ki * 128:(ki + 1) * 128],
                                                              sb["ident"][0:p, 0:p]),
                      reads=[rs["xlnb"], rs["ident"]], writes=[rs["ps_t"]])
                eng = "dve" if ki % 2 == 0 else "act"
                fn = (lambda h, ki=ki, c0=c0, p=p: h.tensor_copy(sb["xlnT"][:, ki, c0:c0 + p], ps["t"][0:128, 0:p])) \
                    if eng == "dve" else \
                     (lambda h, ki=ki, c0=c0, p=p: h.copy(out=sb["xlnT"][:, ki, c0:c0 + p], in_=ps["t"][0:128, 0:p]))
                em.do(eng, fn, reads=[rs["ps_t"]], writes=[rs["xlnT"]])

        for mo in range(8):
            for ki in range(8):
                st_, sp_ = (ki == 0), (ki == 7)
                em.do("pe", lambda h, mo=mo, ki=ki, st_=st_, sp_=sp_: h.matmul(
                    ps["a"][:, 0:512], sb["wq"][:, ki, mo * 128:(mo + 1) * 128],
                    sb["xlnT"][:, ki, 4:516], start=st_, stop=sp_),
                    reads=[rs["wq"], rs["xlnT"]], writes=[rs["ps_a"]])
                em.do("pe", lambda h, mo=mo, ki=ki, st_=st_, sp_=sp_: h.matmul(
                    ps["s"][:, 0:4], sb["wq"][:, ki, mo * 128:(mo + 1) * 128],
                    sb["xlnT"][:, ki, 516:520], start=st_, stop=sp_),
                    reads=[rs["wq"], rs["xlnT"]], writes=[rs["ps_s"]])
            em.do("act", lambda h, mo=mo: h.activation(out=sb["qf"][:, mo, 0:512], in_=ps["a"][:, 0:512],
                                                       func=AF.Copy, scale=sb["konst"][:, 2:3]),
                  reads=[rs["ps_a"], rs["konst"]], writes=[rs["qf"]])
            em.do("act", lambda h, mo=mo: h.activation(out=sb["qf"][:, mo, 512:516], in_=ps["s"][:, 0:4],
                                                       func=AF.Copy, scale=sb["konst"][:, 2:3]),
                  reads=[rs["ps_s"], rs["konst"]], writes=[rs["qf"]])
        for mo in range(8):
            for ki in range(8):
                st_, sp_ = (ki == 0), (ki == 7)
                em.do("pe", lambda h, mo=mo, ki=ki, st_=st_, sp_=sp_: h.matmul(
                    ps["a"][:, 0:512], sb["wk"][:, ki, mo * 128:(mo + 1) * 128],
                    sb["xlnT"][:, ki, 8:520], start=st_, stop=sp_),
                    reads=[rs["wk"], rs["xlnT"]], writes=[rs["ps_a"]])
                em.do("pe", lambda h, mo=mo, ki=ki, st_=st_, sp_=sp_: h.matmul(
                    ps["s"][:, 0:4], sb["wk"][:, ki, mo * 128:(mo + 1) * 128],
                    sb["xlnT"][:, ki, 0:4], start=st_, stop=sp_),
                    reads=[rs["wk"], rs["xlnT"]], writes=[rs["ps_s"]])
            em.do("dve", lambda h, mo=mo: h.tensor_copy(sb["kf"][:, mo, 4:516], ps["a"][:, 0:512]),
                  reads=[rs["ps_a"]], writes=[rs["kf"]])
            em.do("dve", lambda h, mo=mo: h.tensor_copy(sb["kf"][:, mo, 0:4], ps["s"][:, 0:4]),
                  reads=[rs["ps_s"]], writes=[rs["kf"]])
        for cti, (c0, ck) in enumerate(L0_CT_XLNT):
            for half in range(2):
                pstgt = ps["a"] if half == 0 else ps["b"]
                psname = "ps_a" if half == 0 else "ps_b"
                for ki in range(8):
                    st_, sp_ = (ki == 0), (ki == 7)
                    em.do("pe", lambda h, cti=cti, c0=c0, ck=ck, half=half, ki=ki, st_=st_, sp_=sp_, pstgt=pstgt: h.matmul(
                        pstgt[0:ck, 0:512], sb["xlnT"][:, ki, c0:c0 + ck],
                        sb["wv"][:, ki, half * 512:(half + 1) * 512], start=st_, stop=sp_),
                        reads=[rs["wv"], rs["xlnT"]], writes=[rs[psname]])
                eng = "dve" if half == 0 else "act"
                fn = (lambda h, cti=cti, ck=ck, half=half, pstgt=pstgt: h.tensor_copy(
                    sb["vt"][0:ck, cti, half * 512:(half + 1) * 512], pstgt[0:ck, 0:512])) if eng == "dve" else \
                     (lambda h, cti=cti, ck=ck, half=half, pstgt=pstgt: h.copy(
                    out=sb["vt"][0:ck, cti, half * 512:(half + 1) * 512], in_=pstgt[0:ck, 0:512]))
                em.do(eng, fn, reads=[rs[psname]], writes=[rs["vt"]])

        if dbg and b == 0:
            for nm, buf in [("d_xlnT", "xlnT"), ("d_qf", "qf"), ("d_kf", "kf"), ("d_vt", "vt")]:
                em.do("sp", lambda h, nm=nm, buf=buf: h.dma_start(out=dbg[nm], in_=sb[buf][:, :, :]),
                      reads=[rs[buf]], writes=[rs["out_0"]], dma=True)
        for hd in range(16):
            mo, po = hd // 2, (hd % 2) * 64
            for qt, (qoff, qsz) in enumerate(L0_QT):
                wq_ = qoff + qsz
                em.do("pe", lambda h, mo=mo, po=po, qoff=qoff, qsz=qsz: h.matmul(
                    ps["s"][0:qsz, 0:4], sb["qf"][po:po + 64, mo, qoff:qoff + qsz],
                    sb["kf"][po:po + 64, mo, 0:4], start=True, stop=True),
                    reads=[rs["qf"], rs["kf"]], writes=[rs["ps_s"]])
                em.do("pe", lambda h, mo=mo, po=po, qoff=qoff, qsz=qsz, wq_=wq_: h.matmul(
                    ps["a"][0:qsz, 0:wq_ - 4], sb["qf"][po:po + 64, mo, qoff:qoff + qsz],
                    sb["kf"][po:po + 64, mo, 4:wq_], start=True, stop=True),
                    reads=[rs["qf"], rs["kf"]], writes=[rs["ps_a"]])
                if qt == 0:
                    em.do("dve", lambda h, qsz=qsz: h.tensor_add(sb["lg"][0:qsz, 0:4], ps["s"][0:qsz, 0:4],
                                                                 sb["mtri"][0:qsz, 0:4]),
                          reads=[rs["ps_s"], rs["mtri"]], writes=[rs["lg"]])
                    em.do("dve", lambda h, qsz=qsz: h.tensor_add(sb["lg"][0:qsz, 4:128], ps["a"][0:qsz, 0:124],
                                                                 sb["mtri"][0:qsz, 4:128]),
                          reads=[rs["ps_a"], rs["mtri"]], writes=[rs["lg"]])
                else:
                    em.do("dve", lambda h, qsz=qsz: h.tensor_copy(sb["lg"][0:qsz, 0:4], ps["s"][0:qsz, 0:4]),
                          reads=[rs["ps_s"]], writes=[rs["lg"]])
                    em.do("act", lambda h, qsz=qsz, qoff=qoff: h.copy(out=sb["lg"][0:qsz, 4:qoff],
                                                                      in_=ps["a"][0:qsz, 0:qoff - 4]),
                          reads=[rs["ps_a"]], writes=[rs["lg"]])
                    em.do("dve", lambda h, qsz=qsz, qoff=qoff: h.tensor_add(
                        sb["lg"][0:qsz, qoff:qoff + qsz], ps["a"][0:qsz, qoff - 4:qoff - 4 + qsz],
                        sb["mtri"][0:qsz, 0:qsz]),
                        reads=[rs["ps_a"], rs["mtri"]], writes=[rs["lg"]])
                em.do("dve", lambda h, qsz=qsz, wq_=wq_: h.reduce_max(sb["st"][0:qsz, 0:1], sb["lg"][0:qsz, 0:wq_],
                                                                      AX.X, negate=True),
                      reads=[rs["lg"]], writes=[rs["st"]])
                em.do("act", lambda h, qsz=qsz, wq_=wq_: h.activation(out=sb["esc"][0:qsz, 0:wq_],
                                                                      in_=sb["lg"][0:qsz, 0:wq_], func=AF.Exp,
                                                                      bias=sb["st"][0:qsz, 0:1],
                                                                      accum_out=sb["st"][0:qsz, 1:2]),
                      reads=[rs["lg"], rs["st"]], writes=[rs["esc"], rs["st"]])
                em.do("dve", lambda h, qsz=qsz: h.reciprocal(sb["st"][0:qsz, 2:3], sb["st"][0:qsz, 1:2]),
                      reads=[rs["st"]], writes=[rs["st"]])
                em.do("dve", lambda h, qsz=qsz: h.tensor_scalar(sb["dg"][0:qsz, 0:qsz], sb["ident"][0:qsz, 0:qsz],
                                                                sb["st"][0:qsz, 2:3], None, ALU.mult),
                      reads=[rs["ident"], rs["st"]], writes=[rs["dg"]])
                pieces = []
                for cti, (c0, ck) in enumerate(L0_CT):
                    if c0 >= wq_:
                        break
                    pieces.append((cti, c0, min(ck, wq_ - c0)))
                for cti, c0, ck_ in pieces:
                    em.do("pe", lambda h, c0=c0, ck_=ck_, qsz=qsz: h.matmul(
                        ps["t2"][0:ck_, 0:qsz], sb["esc"][0:qsz, c0:c0 + ck_], sb["dg"][0:qsz, 0:qsz],
                        start=True, stop=True),
                        reads=[rs["esc"], rs["dg"]], writes=[rs["ps_t2"]])
                    eng = "dve" if cti % 2 == 0 else "act"
                    fn = (lambda h, cti=cti, ck_=ck_, qsz=qsz: h.tensor_copy(
                        sb["wT"][0:ck_, cti * 128:cti * 128 + qsz], ps["t2"][0:ck_, 0:qsz])) if eng == "dve" else \
                         (lambda h, cti=cti, ck_=ck_, qsz=qsz: h.copy(
                        out=sb["wT"][0:ck_, cti * 128:cti * 128 + qsz], in_=ps["t2"][0:ck_, 0:qsz]))
                    em.do(eng, fn, reads=[rs["ps_t2"]], writes=[rs["wT"]])
                for j, (cti, c0, ck_) in enumerate(pieces):
                    em.do("pe", lambda h, cti=cti, ck_=ck_, qsz=qsz, hd=hd,
                          st_=(j == 0), sp_=(j == len(pieces) - 1): h.matmul(
                        ps["h"][0:64, 0:qsz], sb["vt"][0:ck_, cti, hd * 64:(hd + 1) * 64],
                        sb["wT"][0:ck_, cti * 128:cti * 128 + qsz], start=st_, stop=sp_),
                        reads=[rs["vt"], rs["wT"]], writes=[rs["ps_h"]])
                em.do("act", lambda h, qsz=qsz, mo=mo, po=po, qoff=qoff: h.copy(
                    out=sb["ao"][po:po + 64, mo, qoff:qoff + qsz], in_=ps["h"][0:64, 0:qsz]),
                    reads=[rs["ps_h"]], writes=[rs["ao"]])
                if dbg and b == 0 and hd == 0 and qt == 1:
                    em.do("sp", lambda h: h.dma_start(out=dbg["d_lg"], in_=sb["lg"][:, :]),
                          reads=[rs["lg"]], writes=[rs["out_0"]], dma=True)
                    em.do("sp", lambda h: h.dma_start(out=dbg["d_esc"], in_=sb["esc"][:, :]),
                          reads=[rs["esc"]], writes=[rs["out_0"]], dma=True)
                    em.do("sp", lambda h: h.dma_start(out=dbg["d_wT"], in_=sb["wT"][:, :]),
                          reads=[rs["wT"]], writes=[rs["out_0"]], dma=True)

        if dbg and b == 0:
            em.do("sp", lambda h: h.dma_start(out=dbg["d_ao"], in_=sb["ao"][:, :, :]),
                  reads=[rs["ao"]], writes=[rs["out_0"]], dma=True)
        for qt, (qoff, qsz) in enumerate(L0_QT):
            em.do("sp", lambda h, b=b, qoff=qoff, qsz=qsz: h.dma_start(
                out=sb["xr"][0:qsz, :], in_=x_in[b, QR0 + qoff:QR0 + qoff + qsz, :]),
                writes=[rs["xr"]], dma=True)
            for half in range(2):
                pstgt = ps["a"] if half == 0 else ps["b"]
                psname = "ps_a" if half == 0 else "ps_b"
                for ki in range(8):
                    st_, sp_ = (ki == 0), (ki == 7)
                    em.do("pe", lambda h, qoff=qoff, qsz=qsz, half=half, ki=ki, st_=st_, sp_=sp_, pstgt=pstgt: h.matmul(
                        pstgt[0:qsz, 0:512], sb["ao"][:, ki, qoff:qoff + qsz],
                        sb["wo"][:, ki, half * 512:(half + 1) * 512], start=st_, stop=sp_),
                        reads=[rs["ao"], rs["wo"]], writes=[rs[psname]])
                em.do("dve", lambda h, qsz=qsz, half=half, pstgt=pstgt: h.tensor_add(
                    sb["xo"][0:qsz, half * 512:(half + 1) * 512], pstgt[0:qsz, 0:512],
                    sb["xr"][0:qsz, half * 512:(half + 1) * 512]),
                    reads=[rs[psname], rs["xr"]], writes=[rs["xo"]])
            em.do("sp", lambda h, b=b, qoff=qoff, qsz=qsz: h.dma_start(
                out=xq_out[b, out_base + qoff:out_base + qoff + qsz, :], in_=sb["xo"][0:qsz, :]),
                reads=[rs["xo"]], writes=[rs["out_" + str(b % 2)]], dma=True)


L0_RS_KEYS = ["ident", "mtri", "konst", "ps_t2", "wq", "wk", "wv", "wo", "xt", "xsq", "xlnb", "st",
              "xlnT", "qf", "kf", "vt", "ao", "lg", "esc", "dg", "wT", "xr", "xo",
              "ps_t", "ps_a", "ps_b", "ps_s", "ps_h", "out_0", "out_1"]


def build_l0_attn_nc(nb, debug=False):
    from contextlib import ExitStack
    nc = bass.Bass("TRN2", target_bir_lowering=False)
    x_in = nc.dram_tensor("x_in", [nb, T, D], F32, kind="ExternalInput").ap()
    xq_out = nc.dram_tensor("xq_out", [nb, NQ, D], F32, kind="ExternalOutput").ap()
    w_aps = {w: nc.dram_tensor(w, [D, D], F32, kind="ExternalInput").ap()
             for w in ("wq", "wk", "wv", "wo")}
    dbg = {}
    if debug:
        for nm, shp, dt_ in [("d_xlnT", [128, 8, 520], BF16), ("d_qf", [128, 8, 516], BF16),
                             ("d_kf", [128, 8, 516], BF16), ("d_vt", [128, 5, 1024], BF16),
                             ("d_ao", [128, 8, 516], BF16), ("d_lg", [128, 516], F32),
                             ("d_esc", [128, 516], BF16), ("d_wT", [128, 640], BF16)]:
            dbg[nm] = nc.dram_tensor(nm, shp, dt_, kind="ExternalOutput").ap()
    h_ident, h_tri, h_konst = l0_attn_consts(nc)
    es = ExitStack()
    with es:
        sb, ps = alloc_l0_attn(nc, es)
        trk = Trk(nc, es)
        em = Emit(trk)
        rs = {k: Rsrc(k) for k in L0_RS_KEYS}
        block = es.enter_context(nc.Block())
        emit_l0_attn(nc, em, sb, ps, rs, x_in, xq_out, w_aps, nb, h_ident, h_tri, h_konst,
                     dbg=dbg)
        trk.wait("sp", [rs["out_0"].w, rs["out_1"].w])
        trk.replay(block)
    return nc



E64 = 64
CAP = 512
TOPK = 8
NT_GLOB = 256
NT_OWN = 32
HUGE_SLOT = 65536.0


def l0ffn_consts(nc):
    ident = np.eye(128, dtype=np.float32)
    ones_full = np.ones((128, 128), dtype=np.float32)
    triu_strict = np.triu(np.ones((128, 128), dtype=np.float32), 1)
    iota512 = np.broadcast_to((np.arange(E64) * CAP).astype(np.float32), (128, E64)).copy()
    iota_e = np.broadcast_to(np.arange(E64, dtype=np.float32), (128, E64)).copy()
    h = {}
    h["ones_full"] = nc.inline_tensor(ones_full.astype(ml_dtypes.bfloat16), name="c_onesf")
    h["ident"] = nc.inline_tensor(ident.astype(ml_dtypes.bfloat16), name="c_identf")
    h["triu"] = nc.inline_tensor(triu_strict.astype(ml_dtypes.bfloat16), name="c_triu")
    h["iota512"] = nc.inline_tensor(iota512, name="c_iota512")
    h["iota_e"] = nc.inline_tensor(iota_e, name="c_iotae")
    kv = np.broadcast_to(np.array([EPS, -1.0, 0.125, 0.0], dtype=np.float32), (128, 4)).copy()
    h["konst"] = nc.inline_tensor(kv, name="c_konstf")
    return h


def alloc_l0ffn(nc, es):
    sb, ps = {}, {}
    sb["rwT"] = es.enter_context(nc.sbuf_tensor("rwT_sb", [128, 8, E64], BF16))
    sb["ident"] = es.enter_context(nc.sbuf_tensor("f_ident", [128, 128], BF16))
    sb["onesf"] = es.enter_context(nc.sbuf_tensor("onesf", [128, 128], BF16))
    sb["triu"] = es.enter_context(nc.sbuf_tensor("triu_sb", [128, 128], BF16))
    sb["iota512"] = es.enter_context(nc.sbuf_tensor("iota512_sb", [128, E64], F32))
    sb["iotae"] = es.enter_context(nc.sbuf_tensor("iotae_sb", [128, E64], F32))
    sb["colsumB"] = es.enter_context(nc.sbuf_tensor("colsumB", [128, E64], F32))
    sb["xt"] = es.enter_context(nc.sbuf_tensor("f_xt", [128, 1024], F32))
    sb["xbf"] = es.enter_context(nc.sbuf_tensor("f_xbf", [128, 1024], BF16))
    sb["xT"] = es.enter_context(nc.sbuf_tensor("f_xT", [128, 8, 128], BF16))
    sb["st"] = es.enter_context(nc.sbuf_tensor("f_st", [128, 8], F32))
    sb["lgr"] = es.enter_context(nc.sbuf_tensor("f_lgr", [128, E64], F32))
    sb["m8"] = es.enter_context(nc.sbuf_tensor("f_m8", [128, 8], F32))
    sb["rep"] = es.enter_context(nc.sbuf_tensor("f_rep", [128, E64], F32))
    sb["Mbf"] = es.enter_context(nc.sbuf_tensor("f_Mbf", [128, E64], BF16))
    sb["runrow"] = es.enter_context(nc.sbuf_tensor("f_runrow", [1, E64], BF16))
    sb["exc"] = es.enter_context(nc.sbuf_tensor("f_exc", [1, 8, E64], F32))
    sb["rankoh"] = es.enter_context(nc.sbuf_tensor("f_rankoh", [1, 512], F32))
    sb["baseown"] = es.enter_context(nc.sbuf_tensor("f_baseown", [1, E64], BF16))
    sb["i8u"] = es.enter_context(nc.sbuf_tensor("f_i8u", [128, 8], U32))
    sb["i8f"] = es.enter_context(nc.sbuf_tensor("f_i8f", [128, 8], F32))
    sb["G"] = es.enter_context(nc.sbuf_tensor("f_G", [128, E64], F32))
    sb["S"] = es.enter_context(nc.sbuf_tensor("f_S", [128, E64], F32))
    sb["slot"] = es.enter_context(nc.sbuf_tensor("f_slot", [128, E64], F32))
    sb["inv"] = es.enter_context(nc.sbuf_tensor("f_inv", [128, E64], F32))
    sb["s8"] = es.enter_context(nc.sbuf_tensor("f_s8", [128, 8], F32))
    sb["w8"] = es.enter_context(nc.sbuf_tensor("f_w8", [128, 8], F32))
    sb["w8x2"] = es.enter_context(nc.sbuf_tensor("f_w8x2", [128, 8, 2], F32))
    sb["s8u"] = es.enter_context(nc.sbuf_tensor("f_s8u", [128, 8], U32))
    sb["scr"] = es.enter_context(nc.sbuf_tensor("f_scr", [128, E64], F32))
    sb["xlnT"] = es.enter_context(nc.sbuf_tensor("f_xlnT", [128, 8, 4096], BF16))
    sb["scr2ln"] = es.enter_context(nc.sbuf_tensor("f_scr2ln", [128, 1024], F32))
    sb["konst"] = es.enter_context(nc.sbuf_tensor("f_konst", [128, 4], F32))
    sb["excm"] = es.enter_context(nc.sbuf_tensor("f_excm", [1, 8, E64], F32))
    sb["wsgu"] = es.enter_context(nc.sbuf_tensor("f_wsgu", [128, 8, 1536], BF16))
    sb["wsd"] = es.enter_context(nc.sbuf_tensor("f_wsd", [128, 6, 1024], BF16))
    sb["hsh"] = es.enter_context(nc.sbuf_tensor("f_hsh", [128, 6, 512], BF16))
    sb["gbf"] = es.enter_context(nc.sbuf_tensor("f_gbf", [128, 512], BF16))
    sb["xr"] = es.enter_context(nc.sbuf_tensor("f_xr", [128, 1024], F32))
    sb["acc"] = es.enter_context(nc.sbuf_tensor("f_acc", [128, 1024], F32))
    sb["wg"] = es.enter_context(nc.sbuf_tensor("f_wg", [128, 8, 768], BF16))
    sb["wu"] = es.enter_context(nc.sbuf_tensor("f_wu", [128, 8, 768], BF16))
    sb["wd"] = es.enter_context(nc.sbuf_tensor("f_wd", [128, 6, 1024], BF16))
    sb["xbT"] = es.enter_context(nc.sbuf_tensor("f_xbT", [128, 8, 512], BF16))
    sb["hE"] = es.enter_context(nc.sbuf_tensor("f_hE", [128, 6, 512], BF16))
    sb["ybf"] = es.enter_context(nc.sbuf_tensor("f_ybf", [128, 1024], BF16))
    sb["wcol"] = es.enter_context(nc.sbuf_tensor("f_wcol", [128, 8], F32))
    sb["oidx"] = es.enter_context(nc.sbuf_tensor("f_oidx", [128, 1], U32))
    ps["a"] = es.enter_context(nc.psum_tensor("f_ps_a", [128, 512], F32))
    ps["b"] = es.enter_context(nc.psum_tensor("f_ps_b", [128, 512], F32))
    ps["t"] = es.enter_context(nc.psum_tensor("f_ps_t", [128, 128], BF16))
    ps["lg"] = es.enter_context(nc.psum_tensor("f_ps_lg", [128, E64], F32))
    ps["run"] = es.enter_context(nc.psum_tensor("f_ps_run", [1, E64], F32))
    ps["runP"] = es.enter_context(nc.psum_tensor("f_ps_runP", [1, E64], F32))
    ps["S"] = es.enter_context(nc.psum_tensor("f_ps_S", [128, E64], F32))
    return sb, ps


L0FFN_RS = ["rwT", "ident", "onesf", "scr2ln", "konst", "excm", "triu", "iota512", "iotae", "colsumB", "xt", "xbf", "xT",
            "st", "lgr", "m8", "rep", "Mbf", "runrow", "exc", "rankoh", "baseown",
            "i8u", "i8f", "G", "S", "slot", "inv", "s8", "w8", "w8x2", "s8u", "scr", "xlnT",
            "ps_a", "ps_b", "ps_t", "ps_lg", "ps_run", "ps_runP", "ps_S",
            "xb_d", "w_d", "y_d", "s8u_d", "rbase_d", "x2_o",
            "wsgu", "wsd", "hsh", "xr", "acc",
            "wg", "wu", "wd", "xbT", "hE", "gbf", "ybf", "wcol", "oidx", "x1_d"]


def emit_l0ffn_R(nc, em, sb, ps, rs, x1_full, rankoh_ap, rwT_ap, hc):
    em.do("sp", lambda h: h.dma_start(out=sb["onesf"][:], in_=hc["ones_full"].ap()), writes=[rs["onesf"]], dma=True)
    em.do("sp", lambda h: h.dma_start(out=sb["triu"][:], in_=hc["triu"].ap()), writes=[rs["triu"]], dma=True)
    em.do("sp", lambda h: h.dma_start(out=sb["ident"][:], in_=hc["ident"].ap()), writes=[rs["ident"]], dma=True)
    em.do("sp", lambda h: h.dma_start(out=sb["konst"][:], in_=hc["konst"].ap()), writes=[rs["konst"]], dma=True)
    em.do("sp", lambda h: h.dma_start(out=sb["iota512"][:], in_=hc["iota512"].ap()), writes=[rs["iota512"]], dma=True)
    em.do("sp", lambda h: h.dma_start(out=sb["iotae"][:], in_=hc["iota_e"].ap()), writes=[rs["iotae"]], dma=True)
    em.do("sp", lambda h: h.dma_start(out=sb["rankoh"][0:1, :], in_=rankoh_ap[0:1, :]), writes=[rs["rankoh"]], dma=True)
    em.do("pool", lambda h: h.dma_start(out=sb["rwT"][:, :, :], in_=rwT_ap.rearrange("(k p) n -> p k n", p=128)),
          writes=[rs["rwT"]], dma=True)
    for dk in range(8):
        em.do("pe", lambda h, dk=dk: h.matmul(ps["lg"][:, :], sb["onesf"][:, :], sb["rwT"][:, dk, :],
                                              start=(dk == 0), stop=(dk == 7)),
              reads=[rs["onesf"], rs["rwT"]], writes=[rs["ps_lg"]])
    em.do("dve", lambda h: h.tensor_copy(sb["colsumB"][:], ps["lg"][:, :]), reads=[rs["ps_lg"]], writes=[rs["colsumB"]])

    for ti in range(NT_GLOB):
        g, gj = ti // 32, ti % 32
        em.do("sp", lambda h, ti=ti: h.dma_start(out=sb["xt"][:, :], in_=x1_full[ti * 128:(ti + 1) * 128, :]),
              writes=[rs["xt"]], dma=True)
        em.do("act", lambda h: h.activation(out=sb["xbf"][:, :], in_=sb["xt"][:, :], func=AF.Copy,
                                            accum_out=sb["st"][:, 0:1]),
              reads=[rs["xt"]], writes=[rs["xbf"], rs["st"]])
        em.do("dve", lambda h: h.tensor_scalar(sb["st"][:, 1:2], sb["st"][:, 0:1], -1.0 / D, None, ALU.mult),
              reads=[rs["st"]], writes=[rs["st"]])
        for ki in range(8):
            em.do("pe", lambda h, ki=ki: h.transpose(ps["t"][:, :], sb["xbf"][:, ki * 128:(ki + 1) * 128],
                                                      sb["ident"][:, :]),
                  reads=[rs["xbf"], rs["ident"]], writes=[rs["ps_t"]])
            eng = "dve" if ki % 2 == 0 else "act"
            fn = (lambda h, ki=ki: h.tensor_copy(sb["xT"][:, ki, :], ps["t"][:, :])) if eng == "dve" else \
                 (lambda h, ki=ki: h.copy(out=sb["xT"][:, ki, :], in_=ps["t"][:, :]))
            em.do(eng, fn, reads=[rs["ps_t"]], writes=[rs["xT"]])
        for ki in range(8):
            em.do("pe", lambda h, ki=ki: h.matmul(ps["lg"][:, :], sb["xT"][:, ki, :], sb["rwT"][:, ki, :],
                                                  start=(ki == 0), stop=(ki == 7)),
                  reads=[rs["xT"], rs["rwT"]], writes=[rs["ps_lg"]])
        em.do("dve", lambda h: h.scalar_tensor_tensor(out=sb["lgr"][:, :], in0=sb["colsumB"][:, :],
                                                      scalar=sb["st"][:, 1:2], in1=ps["lg"][:, :],
                                                      op0=ALU.mult, op1=ALU.add),
              reads=[rs["colsumB"], rs["st"], rs["ps_lg"]], writes=[rs["lgr"]])
        em.do("dve", lambda h: h.max(out=sb["m8"][:, :], in_=sb["lgr"][:, :]), reads=[rs["lgr"]], writes=[rs["m8"]])
        em.do("dve", lambda h: h.match_replace(out=sb["rep"][:, :], in_to_replace=sb["m8"][:, :],
                                               in_values=sb["lgr"][:, :], imm_value=-1e30),
              reads=[rs["lgr"], rs["m8"]], writes=[rs["rep"]])
        em.do("dve", lambda h: h.tensor_tensor(sb["Mbf"][:, :], sb["lgr"][:, :], sb["rep"][:, :], ALU.not_equal),
              reads=[rs["lgr"], rs["rep"]], writes=[rs["Mbf"]])
        em.do("pe", lambda h, ti=ti: h.matmul(ps["run"][0:1, :], sb["onesf"][:, 0:1], sb["Mbf"][:, :],
                                              start=(ti == 0), stop=(ti == NT_GLOB - 1), skip_group_check=True),
              reads=[rs["onesf"], rs["Mbf"]], writes=[rs["ps_run"]])
        if gj == 31:
            em.do("act", lambda h, g=g: h.copy(out=sb["exc"][0:1, g, :], in_=ps["run"][0:1, :]),
                  reads=[rs["ps_run"]], writes=[rs["exc"]])
    em.do("dve", lambda h: h.tensor_mul(sb["excm"][0:1, :, :], sb["exc"][0:1, :, :],
                                        sb["rankoh"][0:1, :].rearrange("a (g e) -> a g e", e=E64)),
          reads=[rs["rankoh"], rs["exc"]], writes=[rs["excm"]])
    em.do("dve", lambda h: h.tensor_reduce(sb["scr"][0:1, 0:E64],
                                           sb["excm"][0:1, :, :].transpose([0, 2, 1]), AX.X, ALU.add),
          reads=[rs["excm"]], writes=[rs["scr"]])
    em.do("dve", lambda h: h.tensor_copy(sb["baseown"][0:1, :], sb["scr"][0:1, 0:E64]),
          reads=[rs["scr"]], writes=[rs["baseown"]])


def emit_l0ffn_P(nc, em, sb, ps, rs, x1_own, rwT_ap, xb_d, w_d, s8u_d, hc, ownrows=None, x1_dram=None):
    for j in range(NT_OWN):
        if ownrows is None:
            em.do("sp", lambda h, j=j: h.dma_start(out=sb["xt"][:, :], in_=x1_own[j * 128:(j + 1) * 128, :]),
                  writes=[rs["xt"]], dma=True)
        else:
            em.do("sp", lambda h, j=j: h.dma_start(out=sb["oidx"][:, :], in_=ownrows[j, :, :]),
                  writes=[rs["oidx"]], dma=True)
            em.do("pool", lambda h, j=j: h.indirect_dma_start(
                sb["xt"][:, :], None, x1_dram[0:128, :], IndirectOffsetOnAxis(sb["oidx"][:, :], 0),
                bounds_check=bcreg(em.trk, h, NB * T - 1), oob_is_err=False),
                reads=[rs["oidx"], rs["x1_d"]], writes=[rs["xt"]], dma=True)
        em.do("act", lambda h: h.activation(out=sb["scr2ln"][:, :], in_=sb["xt"][:, :], func=AF.Square,
                                            accum_out=sb["st"][:, 0:1]),
              reads=[rs["xt"]], writes=[rs["scr2ln"], rs["st"]])
        em.do("dve", lambda h: h.reduce_sum(sb["st"][:, 1:2], sb["xt"][:, :], AX.X, negate=True),
              reads=[rs["xt"]], writes=[rs["st"]])
        em.do("dve", lambda h: h.tensor_scalar(sb["st"][:, 2:3], sb["st"][:, 1:2], 1.0 / D, None, ALU.mult),
              reads=[rs["st"]], writes=[rs["st"]])
        em.do("dve", lambda h: h.tensor_scalar(sb["st"][:, 3:4], sb["st"][:, 0:1], 1.0 / D, None, ALU.mult),
              reads=[rs["st"]], writes=[rs["st"]])
        em.do("dve", lambda h: h.scalar_tensor_tensor(out=sb["st"][:, 4:5], in0=sb["st"][:, 2:3],
                                                      scalar=sb["st"][:, 2:3], in1=sb["st"][:, 3:4],
                                                      op0=ALU.mult, op1=ALU.subtract),
              reads=[rs["st"]], writes=[rs["st"]])
        em.do("act", lambda h: h.activation(out=sb["st"][:, 5:6], in_=sb["st"][:, 4:5], func=AF.Sqrt,
                                            bias=sb["konst"][:, 0:1], scale=sb["konst"][:, 1:2]),
              reads=[rs["st"], rs["konst"]], writes=[rs["st"]])
        em.do("dve", lambda h: h.reciprocal(sb["st"][:, 6:7], sb["st"][:, 5:6]),
              reads=[rs["st"]], writes=[rs["st"]])
        em.do("dve", lambda h: h.tensor_scalar(sb["xbf"][:, :], sb["xt"][:, :], sb["st"][:, 2:3],
                                               sb["st"][:, 6:7], ALU.add, ALU.mult),
              reads=[rs["xt"], rs["st"]], writes=[rs["xbf"]])
        for ki in range(8):
            em.do("pe", lambda h, ki=ki: h.transpose(ps["t"][:, :], sb["xbf"][:, ki * 128:(ki + 1) * 128],
                                                     sb["ident"][:, :]),
                  reads=[rs["xbf"], rs["ident"]], writes=[rs["ps_t"]])
            eng = "dve" if ki % 2 == 0 else "act"
            fn = (lambda h, ki=ki, j=j: h.tensor_copy(sb["xlnT"][:, ki, j * 128:(j + 1) * 128], ps["t"][:, :])) \
                if eng == "dve" else \
                 (lambda h, ki=ki, j=j: h.copy(out=sb["xlnT"][:, ki, j * 128:(j + 1) * 128], in_=ps["t"][:, :]))
            em.do(eng, fn, reads=[rs["ps_t"]], writes=[rs["xlnT"]])
        for ki in range(8):
            em.do("pe", lambda h, ki=ki, j=j: h.matmul(ps["lg"][:, :], sb["xlnT"][:, ki, j * 128:(j + 1) * 128],
                                                       sb["rwT"][:, ki, :], start=(ki == 0), stop=(ki == 7)),
                  reads=[rs["xlnT"], rs["rwT"]], writes=[rs["ps_lg"]])
        em.do("dve", lambda h: h.tensor_copy(sb["lgr"][:, :], ps["lg"][:, :]), reads=[rs["ps_lg"]], writes=[rs["lgr"]])
        em.do("dve", lambda h: h.max(out=sb["m8"][:, :], in_=sb["lgr"][:, :]), reads=[rs["lgr"]], writes=[rs["m8"]])
        em.do("dve", lambda h: h.max_index(out=sb["i8u"][:, :], in_max=sb["m8"][:, :], in_values=sb["lgr"][:, :]),
              reads=[rs["lgr"], rs["m8"]], writes=[rs["i8u"]])
        em.do("dve", lambda h: h.tensor_copy(sb["i8f"][:, :], sb["i8u"][:, :]), reads=[rs["i8u"]], writes=[rs["i8f"]])
        em.do("dve", lambda h: h.match_replace(out=sb["rep"][:, :], in_to_replace=sb["m8"][:, :],
                                               in_values=sb["lgr"][:, :], imm_value=-1e30),
              reads=[rs["lgr"], rs["m8"]], writes=[rs["rep"]])
        em.do("dve", lambda h: h.tensor_tensor(sb["scr"][:, :], sb["lgr"][:, :], sb["rep"][:, :], ALU.not_equal),
              reads=[rs["lgr"], rs["rep"]], writes=[rs["scr"]])
        em.do("dve", lambda h: h.tensor_copy(sb["Mbf"][:, :], sb["scr"][:, :]), reads=[rs["scr"]], writes=[rs["Mbf"]])
        em.do("dve", lambda h: h.tensor_scalar(sb["st"][:, 7:8], sb["m8"][:, 0:1], -1.0, None, ALU.mult),
              reads=[rs["m8"]], writes=[rs["st"]])
        em.do("act", lambda h: h.activation(out=sb["rep"][:, :], in_=sb["lgr"][:, :], func=AF.Exp,
                                            bias=sb["st"][:, 7:8]),
              reads=[rs["lgr"], rs["st"]], writes=[rs["rep"]])
        em.do("dve", lambda h: h.tensor_mul(sb["G"][:, :], sb["scr"][:, :], sb["rep"][:, :]),
              reads=[rs["scr"], rs["rep"]], writes=[rs["G"]])
        em.do("dve", lambda h: h.tensor_reduce(sb["st"][:, 7:8], sb["G"][:, :], AX.X, ALU.add),
              reads=[rs["G"]], writes=[rs["st"]])
        em.do("dve", lambda h: h.reciprocal(sb["st"][:, 7:8], sb["st"][:, 7:8]),
              reads=[rs["st"]], writes=[rs["st"]])
        em.do("dve", lambda h: h.tensor_scalar(sb["G"][:, :], sb["G"][:, :], sb["st"][:, 7:8], None, ALU.mult),
              reads=[rs["st"]], writes=[rs["G"]])
        if j == 0:
            em.do("dve", lambda h: h.memset(sb["runrow"][0:1, :], 0.0), writes=[rs["runrow"]])
        else:
            em.do("act", lambda h: h.copy(out=sb["runrow"][0:1, :], in_=ps["runP"][0:1, :]),
                  reads=[rs["ps_runP"]], writes=[rs["runrow"]])
        em.do("pe", lambda h: h.matmul(ps["S"][:, :], sb["triu"][:, :], sb["Mbf"][:, :], start=True, stop=False),
              reads=[rs["triu"], rs["Mbf"]], writes=[rs["ps_S"]])
        em.do("pe", lambda h: h.matmul(ps["S"][:, :], sb["onesf"][0:1, :], sb["runrow"][0:1, :],
                                       start=False, stop=False),
              reads=[rs["onesf"], rs["runrow"]], writes=[rs["ps_S"]])
        em.do("pe", lambda h: h.matmul(ps["S"][:, :], sb["onesf"][0:1, :], sb["baseown"][0:1, :],
                                       start=False, stop=True),
              reads=[rs["onesf"], rs["baseown"]], writes=[rs["ps_S"]])
        em.do("pe", lambda h, j=j: h.matmul(ps["runP"][0:1, :], sb["onesf"][:, 0:1], sb["Mbf"][:, :],
                                            start=(j == 0), stop=(j == NT_OWN - 1), skip_group_check=True),
              reads=[rs["onesf"], rs["Mbf"]], writes=[rs["ps_runP"]])
        em.do("dve", lambda h: h.tensor_copy(sb["S"][:, :], ps["S"][:, :]), reads=[rs["ps_S"]], writes=[rs["S"]])
        em.do("dve", lambda h: h.tensor_scalar(sb["inv"][:, :], sb["S"][:, :], float(CAP), None, ALU.is_ge),
              reads=[rs["S"]], writes=[rs["inv"]])
        em.do("dve", lambda h: h.tensor_add(sb["slot"][:, :], sb["S"][:, :], sb["iota512"][:, :]),
              reads=[rs["S"], rs["iota512"]], writes=[rs["slot"]])
        em.do("dve", lambda h: h.scalar_tensor_tensor(out=sb["slot"][:, :], in0=sb["inv"][:, :],
                                                      scalar=HUGE_SLOT, in1=sb["slot"][:, :],
                                                      op0=ALU.mult, op1=ALU.add),
              reads=[rs["inv"]], writes=[rs["slot"]])
        for k in range(8):
            em.do("dve", lambda h, k=k: h.scalar_tensor_tensor(out=sb["scr"][:, :], in0=sb["iotae"][:, :],
                                                               scalar=sb["i8f"][:, k:k + 1], in1=sb["slot"][:, :],
                                                               op0=ALU.is_equal, op1=ALU.mult,
                                                               accum_out=sb["s8"][:, k:k + 1]),
                  reads=[rs["iotae"], rs["i8f"], rs["slot"]], writes=[rs["scr"], rs["s8"]])
            em.do("dve", lambda h, k=k: h.scalar_tensor_tensor(out=sb["scr"][:, :], in0=sb["iotae"][:, :],
                                                               scalar=sb["i8f"][:, k:k + 1], in1=sb["G"][:, :],
                                                               op0=ALU.is_equal, op1=ALU.mult,
                                                               accum_out=sb["w8"][:, k:k + 1]),
                  reads=[rs["iotae"], rs["i8f"], rs["G"]], writes=[rs["scr"], rs["w8"]])
        em.do("dve", lambda h: h.tensor_copy(sb["s8u"][:, :], sb["s8"][:, :]), reads=[rs["s8"]], writes=[rs["s8u"]])
        em.do("dve", lambda h: h.tensor_copy(sb["w8x2"][:, :, 0], sb["w8"][:, :]), reads=[rs["w8"]], writes=[rs["w8x2"]])
        em.do("dve", lambda h: h.tensor_copy(sb["w8x2"][:, :, 1], sb["w8"][:, :]), reads=[rs["w8"]], writes=[rs["w8x2"]])
        em.do("sp", lambda h, j=j: h.dma_start(out=s8u_d[j, :, :], in_=sb["s8u"][:, :]),
              reads=[rs["s8u"]], writes=[rs["s8u_d"]], dma=True)
        for k in range(8):
            em.do("pool", lambda h, k=k: h.indirect_dma_start(
                xb_d[0:128, :], IndirectOffsetOnAxis(sb["s8u"][:, k:k + 1], 0),
                sb["xbf"][:, :], None, bounds_check=bcreg(em.trk, h, E64 * CAP - 1), oob_is_err=False),
                reads=[rs["s8u"], rs["xbf"]], writes=[rs["xb_d"]], dma=True)
            em.do("pool", lambda h, k=k: h.indirect_dma_start(
                w_d[0:128, :], IndirectOffsetOnAxis(sb["s8u"][:, k:k + 1], 0),
                sb["w8x2"][:, k, :], None, bounds_check=bcreg(em.trk, h, E64 * CAP - 1), oob_is_err=False),
                reads=[rs["s8u"], rs["w8x2"]], writes=[rs["w_d"]], dma=True)


def emit_l0ffn_S(nc, em, sb, ps, rs, x1_own, wsgu_ap, wsd_ap, rbase_d, ownrows=None, x1_dram=None):
    em.do("pool", lambda h: h.dma_start(out=sb["wsgu"][:, :, :], in_=wsgu_ap.rearrange("(k p) n -> p k n", p=128)),
          writes=[rs["wsgu"]], dma=True)
    em.do("pool", lambda h: h.dma_start(out=sb["wsd"][:, :, :], in_=wsd_ap.rearrange("(k p) n -> p k n", p=128)),
          writes=[rs["wsd"]], dma=True)
    SF = 768
    for sp_ in range(8):
        t0 = sp_ * 512
        for fc in range(6):
            for ki in range(8):
                em.do("pe", lambda h, fc=fc, ki=ki, t0=t0: h.matmul(
                    ps["a"][:, :], sb["wsgu"][:, ki, fc * 128:(fc + 1) * 128],
                    sb["xlnT"][:, ki, t0:t0 + 512], start=(ki == 0), stop=(ki == 7)),
                    reads=[rs["wsgu"], rs["xlnT"]], writes=[rs["ps_a"]])
                em.do("pe", lambda h, fc=fc, ki=ki, t0=t0: h.matmul(
                    ps["b"][:, :], sb["wsgu"][:, ki, SF + fc * 128:SF + (fc + 1) * 128],
                    sb["xlnT"][:, ki, t0:t0 + 512], start=(ki == 0), stop=(ki == 7)),
                    reads=[rs["wsgu"], rs["xlnT"]], writes=[rs["ps_b"]])
            em.do("act", lambda h: h.activation(out=sb["gbf"][:, :], in_=ps["a"][:, :], func=AF.Silu),
                  reads=[rs["ps_a"]], writes=[rs["gbf"]])
            em.do("dve", lambda h, fc=fc: h.tensor_mul(sb["hsh"][:, fc, :], sb["gbf"][:, :], ps["b"][:, :]),
                  reads=[rs["gbf"], rs["ps_b"]], writes=[rs["hsh"]])
        for tt in range(4):
            tg = t0 + tt * 128
            if ownrows is None:
                em.do("sp", lambda h, tg=tg: h.dma_start(out=sb["xr"][:, :], in_=x1_own[tg:tg + 128, :]),
                      writes=[rs["xr"]], dma=True)
            else:
                em.do("sp", lambda h, tg=tg: h.dma_start(out=sb["oidx"][:, :], in_=ownrows[tg // 128, :, :]),
                      writes=[rs["oidx"]], dma=True)
                em.do("pool", lambda h: h.indirect_dma_start(
                    sb["xr"][:, :], None, x1_dram[0:128, :], IndirectOffsetOnAxis(sb["oidx"][:, :], 0),
                    bounds_check=bcreg(em.trk, h, NB * T - 1), oob_is_err=False),
                    reads=[rs["oidx"], rs["x1_d"]], writes=[rs["xr"]], dma=True)
            for half in range(2):
                pst, psn = (ps["a"], "ps_a") if half == 0 else (ps["b"], "ps_b")
                for fc in range(6):
                    em.do("pe", lambda h, fc=fc, tt=tt, half=half, pst=pst: h.matmul(
                        pst[:, :], sb["hsh"][:, fc, tt * 128:(tt + 1) * 128],
                        sb["wsd"][:, fc, half * 512:(half + 1) * 512], start=(fc == 0), stop=(fc == 5)),
                        reads=[rs["hsh"], rs["wsd"]], writes=[rs[psn]])
                em.do("dve", lambda h, half=half, pst=pst: h.tensor_add(
                    sb["acc"][:, half * 512:(half + 1) * 512], pst[:, :],
                    sb["xr"][:, half * 512:(half + 1) * 512]),
                    reads=[rs[psn], rs["xr"]], writes=[rs["acc"]])
            em.do("sp", lambda h, tg=tg: h.dma_start(out=rbase_d[tg:tg + 128, :], in_=sb["acc"][:, :]),
                  reads=[rs["acc"]], writes=[rs["rbase_d"]], dma=True)


def emit_l0ffn_E(nc, em, sb, ps, rs, wg_ap, wu_ap, wd_ap, xb_d, w_d, y_d):
    for e in range(E64):
        em.do("pool", lambda h, e=e: h.dma_start(out=sb["wg"][:, :, :],
                                                 in_=wg_ap[e].rearrange("(k p) n -> p k n", p=128)),
              writes=[rs["wg"]], dma=True)
        em.do("pool", lambda h, e=e: h.dma_start(out=sb["wu"][:, :, :],
                                                 in_=wu_ap[e].rearrange("(k p) n -> p k n", p=128)),
              writes=[rs["wu"]], dma=True)
        em.do("pool", lambda h, e=e: h.dma_start(out=sb["wd"][:, :, :],
                                                 in_=wd_ap[e].rearrange("(k p) n -> p k n", p=128)),
              writes=[rs["wd"]], dma=True)
        for dk in range(8):
            em.do("sp", lambda h, e=e, dk=dk: h.dma_start(
                out=sb["xbT"][:, dk, :], in_=xb_d[e * CAP:(e + 1) * CAP, dk * 128:(dk + 1) * 128],
                transpose=True),
                reads=[rs["xb_d"]], writes=[rs["xbT"]], dma=True)
        for st_ in range(4):
            em.do("sp", lambda h, e=e, st_=st_: h.dma_start(
                out=sb["wcol"][:, 2 * st_:2 * st_ + 2],
                in_=w_d[e * CAP + st_ * 128:e * CAP + (st_ + 1) * 128, 0:2]),
                reads=[rs["w_d"]], writes=[rs["wcol"]], dma=True)
        for fc in range(6):
            for ki in range(8):
                em.do("pe", lambda h, fc=fc, ki=ki: h.matmul(
                    ps["a"][:, :], sb["wg"][:, ki, fc * 128:(fc + 1) * 128],
                    sb["xbT"][:, ki, :], start=(ki == 0), stop=(ki == 7)),
                    reads=[rs["wg"], rs["xbT"]], writes=[rs["ps_a"]])
                em.do("pe", lambda h, fc=fc, ki=ki: h.matmul(
                    ps["b"][:, :], sb["wu"][:, ki, fc * 128:(fc + 1) * 128],
                    sb["xbT"][:, ki, :], start=(ki == 0), stop=(ki == 7)),
                    reads=[rs["wu"], rs["xbT"]], writes=[rs["ps_b"]])
            em.do("act", lambda h: h.activation(out=sb["gbf"][:, :], in_=ps["a"][:, :], func=AF.Silu),
                  reads=[rs["ps_a"]], writes=[rs["gbf"]])
            em.do("dve", lambda h, fc=fc: h.tensor_mul(sb["hE"][:, fc, :], sb["gbf"][:, :], ps["b"][:, :]),
                  reads=[rs["gbf"], rs["ps_b"]], writes=[rs["hE"]])
        for st_ in range(4):
            for half in range(2):
                pst, psn = (ps["a"], "ps_a") if half == 0 else (ps["b"], "ps_b")
                for fc in range(6):
                    em.do("pe", lambda h, fc=fc, st_=st_, half=half, pst=pst: h.matmul(
                        pst[:, :], sb["hE"][:, fc, st_ * 128:(st_ + 1) * 128],
                        sb["wd"][:, fc, half * 512:(half + 1) * 512], start=(fc == 0), stop=(fc == 5)),
                        reads=[rs["hE"], rs["wd"]], writes=[rs[psn]])
                em.do("dve", lambda h, half=half, st_=st_, pst=pst: h.tensor_scalar(
                    sb["ybf"][:, half * 512:(half + 1) * 512], pst[:, :],
                    sb["wcol"][:, 2 * st_:2 * st_ + 1], None, ALU.mult),
                    reads=[rs[psn], rs["wcol"]], writes=[rs["ybf"]])
            em.do("pool", lambda h, e=e, st_=st_: h.dma_start(
                out=y_d[e * CAP + st_ * 128:e * CAP + (st_ + 1) * 128, :], in_=sb["ybf"][:, :]),
                reads=[rs["ybf"]], writes=[rs["y_d"]], dma=True)


def emit_l0ffn_G(nc, em, sb, ps, rs, rbase_d, s8u_d, y_d, x2_out):
    for j in range(NT_OWN):
        em.do("sp", lambda h, j=j: h.dma_start(out=sb["acc"][:, :], in_=rbase_d[j * 128:(j + 1) * 128, :]),
              reads=[rs["rbase_d"]], writes=[rs["acc"]], dma=True)
        em.do("sp", lambda h, j=j: h.dma_start(out=sb["s8u"][:, :], in_=s8u_d[j, :, :]),
              reads=[rs["s8u_d"]], writes=[rs["s8u"]], dma=True)
        for k in range(8):
            em.do("pool", lambda h, k=k: h.indirect_dma_start(
                sb["acc"][:, :], None, y_d[0:128, :], IndirectOffsetOnAxis(sb["s8u"][:, k:k + 1], 0),
                bounds_check=bcreg(em.trk, h, E64 * CAP - 1), oob_is_err=False, compute_op=ALU.add),
                reads=[rs["s8u"], rs["y_d"]], writes=[rs["acc"]], dma=True)
        em.do("sp", lambda h, j=j: h.dma_start(out=x2_out[j * 128:(j + 1) * 128, :], in_=sb["acc"][:, :]),
              reads=[rs["acc"]], writes=[rs["x2_o"]], dma=True)


def build_l0ffn_nc():
    from contextlib import ExitStack
    nc = bass.Bass("TRN2", target_bir_lowering=False)
    x1_full = nc.dram_tensor("x1_full", [NB * T, D], F32, kind="ExternalInput").ap()
    x1_own = nc.dram_tensor("x1_own", [4096, D], F32, kind="ExternalInput").ap()
    rankoh = nc.dram_tensor("rankoh", [1, 512], F32, kind="ExternalInput").ap()
    rwT = nc.dram_tensor("rwT", [D, E64], F32, kind="ExternalInput").ap()
    wsgu = nc.dram_tensor("wsgu", [D, 1536], F32, kind="ExternalInput").ap()
    wsd = nc.dram_tensor("wsd", [768, D], F32, kind="ExternalInput").ap()
    wg = nc.dram_tensor("wg", [E64, D, 768], F32, kind="ExternalInput").ap()
    wu = nc.dram_tensor("wu", [E64, D, 768], F32, kind="ExternalInput").ap()
    wd = nc.dram_tensor("wd", [E64, 768, D], F32, kind="ExternalInput").ap()
    x2_out = nc.dram_tensor("x2_out", [4096, D], F32, kind="ExternalOutput").ap()
    xb_d = nc.dram_tensor("xb_d", [E64 * CAP, D], BF16, kind="Internal").ap()
    w_d = nc.dram_tensor("w_d", [E64 * CAP, 2], F32, kind="Internal").ap()
    y_d = nc.dram_tensor("y_d", [E64 * CAP, D], BF16, kind="Internal").ap()
    s8u_d = nc.dram_tensor("s8u_d", [NT_OWN, 128, 8], U32, kind="Internal").ap()
    rbase_d = nc.dram_tensor("rbase_d", [4096, D], F32, kind="Internal").ap()
    hc = l0ffn_consts(nc)
    es = ExitStack()
    with es:
        sb, ps = alloc_l0ffn(nc, es)
        trk = Trk(nc, es)
        em = Emit(trk)
        rs = {k: Rsrc(k) for k in L0FFN_RS}
        block = es.enter_context(nc.Block())
        emit_l0ffn_R(nc, em, sb, ps, rs, x1_full, rankoh, rwT, hc)
        emit_l0ffn_P(nc, em, sb, ps, rs, x1_own, rwT, xb_d, w_d, s8u_d, hc)
        emit_l0ffn_S(nc, em, sb, ps, rs, x1_own, wsgu, wsd, rbase_d)
        emit_l0ffn_E(nc, em, sb, ps, rs, wg, wu, wd, xb_d, w_d, y_d)
        emit_l0ffn_G(nc, em, sb, ps, rs, rbase_d, s8u_d, y_d, x2_out)
        trk.wait("sp", [rs["x2_o"].w])
        trk.replay(block)
    return nc



L1A_RS = ["ident", "onesf", "konst", "wqkv", "wout", "bq_row", "bo_row",
          "xt", "xbf", "xT", "st", "scr2ln", "qbf", "kbf", "vbf", "prod", "obf",
          "kvrow", "kvb", "rsq", "xr", "acc",
          "ps_a", "ps_b", "ps_t", "ps_kv1", "ps_kv2", "x3_o"]


def alloc_l1attn(nc, es):
    sb, ps = {}, {}
    sb["ident"] = es.enter_context(nc.sbuf_tensor("a_ident", [128, 128], BF16))
    sb["onesf"] = es.enter_context(nc.sbuf_tensor("a_onesf", [128, 128], BF16))
    sb["konst"] = es.enter_context(nc.sbuf_tensor("a_konst", [128, 4], F32))
    sb["wqkv"] = es.enter_context(nc.sbuf_tensor("a_wqkv", [128, 8, 3072], BF16))
    sb["wout"] = es.enter_context(nc.sbuf_tensor("a_wout", [128, 8, 1024], BF16))
    sb["bq_row"] = es.enter_context(nc.sbuf_tensor("a_bqrow", [1, 3072], BF16))
    sb["bo_row"] = es.enter_context(nc.sbuf_tensor("a_borow", [1, 1024], BF16))
    sb["bq_row32"] = es.enter_context(nc.sbuf_tensor("a_bqrow32", [1, 3072], F32))
    sb["bo_row32"] = es.enter_context(nc.sbuf_tensor("a_borow32", [1, 1024], F32))
    sb["xt"] = es.enter_context(nc.sbuf_tensor("a_xt", [128, 1024], F32))
    sb["xbf"] = es.enter_context(nc.sbuf_tensor("a_xbf", [128, 1024], BF16))
    sb["xT"] = es.enter_context(nc.sbuf_tensor("a_xT", [128, 8, 128], BF16))
    sb["st"] = es.enter_context(nc.sbuf_tensor("a_st", [128, 16], F32))
    sb["scr2ln"] = es.enter_context(nc.sbuf_tensor("a_scr2ln", [128, 1024], F32))
    sb["qbf"] = es.enter_context(nc.sbuf_tensor("a_qbf", [128, 16, 1024], BF16))
    sb["kbf"] = es.enter_context(nc.sbuf_tensor("a_kbf", [128, 1024], BF16))
    sb["vbf"] = es.enter_context(nc.sbuf_tensor("a_vbf", [128, 1024], BF16))
    sb["prod"] = es.enter_context(nc.sbuf_tensor("a_prod", [128, 1024], BF16))
    sb["obf"] = es.enter_context(nc.sbuf_tensor("a_obf", [128, 1024], BF16))
    sb["kvrow"] = es.enter_context(nc.sbuf_tensor("a_kvrow", [1, 1024], BF16))
    sb["kvb"] = es.enter_context(nc.sbuf_tensor("a_kvb", [128, 1024], F32))
    sb["rsq"] = es.enter_context(nc.sbuf_tensor("a_rsq", [128, 16], F32))
    sb["xr"] = es.enter_context(nc.sbuf_tensor("a_xr", [128, 1024], F32))
    sb["acc"] = es.enter_context(nc.sbuf_tensor("a_acc", [128, 1024], F32))
    ps["a"] = es.enter_context(nc.psum_tensor("a_ps_a", [128, 512], F32))
    ps["b"] = es.enter_context(nc.psum_tensor("a_ps_b", [128, 512], F32))
    ps["t"] = es.enter_context(nc.psum_tensor("a_ps_t", [128, 128], BF16))
    ps["kv1"] = es.enter_context(nc.psum_tensor("a_ps_kv1", [1, 512], F32))
    ps["kv2"] = es.enter_context(nc.psum_tensor("a_ps_kv2", [1, 512], F32))
    return sb, ps


def emit_ln128(em, sb, rs, xkey="xt", okey="xbf"):
    em.do("act", lambda h: h.activation(out=sb["scr2ln"][:, :], in_=sb[xkey][:, :], func=AF.Square,
                                        accum_out=sb["st"][:, 0:1]),
          reads=[rs[xkey]], writes=[rs["scr2ln"], rs["st"]])
    em.do("dve", lambda h: h.reduce_sum(sb["st"][:, 1:2], sb[xkey][:, :], AX.X, negate=True),
          reads=[rs[xkey]], writes=[rs["st"]])
    em.do("dve", lambda h: h.tensor_scalar(sb["st"][:, 2:3], sb["st"][:, 1:2], 1.0 / D, None, ALU.mult),
          reads=[rs["st"]], writes=[rs["st"]])
    em.do("dve", lambda h: h.tensor_scalar(sb["st"][:, 3:4], sb["st"][:, 0:1], 1.0 / D, None, ALU.mult),
          reads=[rs["st"]], writes=[rs["st"]])
    em.do("dve", lambda h: h.scalar_tensor_tensor(out=sb["st"][:, 4:5], in0=sb["st"][:, 2:3],
                                                  scalar=sb["st"][:, 2:3], in1=sb["st"][:, 3:4],
                                                  op0=ALU.mult, op1=ALU.subtract),
          reads=[rs["st"]], writes=[rs["st"]])
    em.do("act", lambda h: h.activation(out=sb["st"][:, 5:6], in_=sb["st"][:, 4:5], func=AF.Sqrt,
                                        bias=sb["konst"][:, 0:1], scale=sb["konst"][:, 1:2]),
          reads=[rs["st"], rs["konst"]], writes=[rs["st"]])
    em.do("dve", lambda h: h.reciprocal(sb["st"][:, 6:7], sb["st"][:, 5:6]),
          reads=[rs["st"]], writes=[rs["st"]])
    em.do("dve", lambda h: h.tensor_scalar(sb[okey][:, :], sb[xkey][:, :], sb["st"][:, 2:3],
                                           sb["st"][:, 6:7], ALU.add, ALU.mult),
          reads=[rs[xkey], rs["st"]], writes=[rs[okey]])


def emit_l1_attn(nc, em, sb, ps, rs, x_in, x3_out, wqkv_ap, bqkv_ap, wout_ap, bout_ap, nb, hc):
    em.do("sp", lambda h: h.dma_start(out=sb["ident"][:], in_=hc["ident"].ap()), writes=[rs["ident"]], dma=True)
    em.do("sp", lambda h: h.dma_start(out=sb["onesf"][:], in_=hc["ones_full"].ap()), writes=[rs["onesf"]], dma=True)
    em.do("sp", lambda h: h.dma_start(out=sb["konst"][:], in_=hc["konst"].ap()), writes=[rs["konst"]], dma=True)
    em.do("pool", lambda h: h.dma_start(out=sb["wqkv"][:, :, :], in_=wqkv_ap.rearrange("(k p) n -> p k n", p=128)),
          writes=[rs["wqkv"]], dma=True)
    em.do("pool", lambda h: h.dma_start(out=sb["wout"][:, :, :], in_=wout_ap.rearrange("(k p) n -> p k n", p=128)),
          writes=[rs["wout"]], dma=True)
    em.do("sp", lambda h: h.dma_start(out=sb["bq_row32"][0:1, :], in_=bqkv_ap[None, :]), writes=[rs["bq_row"]], dma=True)
    em.do("sp", lambda h: h.dma_start(out=sb["bo_row32"][0:1, :], in_=bout_ap[None, :]), writes=[rs["bo_row"]], dma=True)
    em.do("dve", lambda h: h.tensor_copy(sb["bq_row"][0:1, :], sb["bq_row32"][0:1, :]),
          reads=[rs["bq_row"]], writes=[rs["bq_row"]])
    em.do("dve", lambda h: h.tensor_copy(sb["bo_row"][0:1, :], sb["bo_row32"][0:1, :]),
          reads=[rs["bo_row"]], writes=[rs["bo_row"]])

    NTT = 16
    for b in range(nb):
        for tt in range(NTT):
            em.do("sp", lambda h, b=b, tt=tt: h.dma_start(out=sb["xt"][:, :], in_=x_in[b, tt * 128:(tt + 1) * 128, :]),
                  writes=[rs["xt"]], dma=True)
            emit_ln128(em, sb, rs)
            for ki in range(8):
                em.do("pe", lambda h, ki=ki: h.transpose(ps["t"][:, :], sb["xbf"][:, ki * 128:(ki + 1) * 128],
                                                         sb["ident"][:, :]),
                      reads=[rs["xbf"], rs["ident"]], writes=[rs["ps_t"]])
                eng = "dve" if ki % 2 == 0 else "act"
                fn = (lambda h, ki=ki: h.tensor_copy(sb["xT"][:, ki, :], ps["t"][:, :])) if eng == "dve" else \
                     (lambda h, ki=ki: h.copy(out=sb["xT"][:, ki, :], in_=ps["t"][:, :]))
                em.do(eng, fn, reads=[rs["ps_t"]], writes=[rs["xT"]])
            for s in range(3):
                for half in range(2):
                    pst, psn = (ps["a"], "ps_a") if half == 0 else (ps["b"], "ps_b")
                    c0 = s * 1024 + half * 512
                    for ki in range(8):
                        em.do("pe", lambda h, ki=ki, c0=c0, pst=pst: h.matmul(
                            pst[:, :], sb["xT"][:, ki, :], sb["wqkv"][:, ki, c0:c0 + 512],
                            start=(ki == 0), stop=False),
                            reads=[rs["xT"], rs["wqkv"]], writes=[rs[psn]])
                    em.do("pe", lambda h, c0=c0, pst=pst: h.matmul(
                        pst[:, :], sb["onesf"][0:1, :], sb["bq_row"][0:1, c0:c0 + 512],
                        start=False, stop=True),
                        reads=[rs["onesf"], rs["bq_row"]], writes=[rs[psn]])
                for half in range(2):
                    pst, psn = (ps["a"], "ps_a") if half == 0 else (ps["b"], "ps_b")
                    if s == 0:
                        em.do("act", lambda h, half=half, pst=pst: h.activation(
                            out=sb["scr2ln"][:, half * 512:(half + 1) * 512], in_=pst[:, :], func=AF.Square,
                            accum_out=sb["st"][:, 8 + half:9 + half]),
                            reads=[rs[psn]], writes=[rs["scr2ln"], rs["st"]])
                        em.do("act", lambda h, half=half, tt=tt, pst=pst: h.copy(
                            out=sb["qbf"][:, tt, half * 512:(half + 1) * 512], in_=pst[:, :]),
                            reads=[rs[psn]], writes=[rs["qbf"]])
                    elif s == 1:
                        em.do("act", lambda h, half=half, pst=pst: h.activation(
                            out=sb["scr2ln"][:, half * 512:(half + 1) * 512], in_=pst[:, :], func=AF.Square,
                            accum_out=sb["st"][:, 10 + half:11 + half]),
                            reads=[rs[psn]], writes=[rs["scr2ln"], rs["st"]])
                        em.do("act", lambda h, half=half, pst=pst: h.copy(
                            out=sb["kbf"][:, half * 512:(half + 1) * 512], in_=pst[:, :]),
                            reads=[rs[psn]], writes=[rs["kbf"]])
                    else:
                        em.do("act", lambda h, half=half, pst=pst: h.copy(
                            out=sb["vbf"][:, half * 512:(half + 1) * 512], in_=pst[:, :]),
                            reads=[rs[psn]], writes=[rs["vbf"]])
            em.do("dve", lambda h: h.tensor_add(sb["st"][:, 12:13], sb["st"][:, 8:9], sb["st"][:, 9:10]),
                  reads=[rs["st"]], writes=[rs["st"]])
            em.do("dve", lambda h: h.tensor_add(sb["st"][:, 13:14], sb["st"][:, 10:11], sb["st"][:, 11:12]),
                  reads=[rs["st"]], writes=[rs["st"]])
            em.do("act", lambda h: h.activation(out=sb["st"][:, 14:15], in_=sb["st"][:, 12:13], func=AF.Sqrt),
                  reads=[rs["st"]], writes=[rs["st"]])
            em.do("act", lambda h: h.activation(out=sb["st"][:, 15:16], in_=sb["st"][:, 13:14], func=AF.Sqrt),
                  reads=[rs["st"]], writes=[rs["st"]])
            em.do("dve", lambda h, tt=tt: h.reciprocal(sb["rsq"][:, tt:tt + 1], sb["st"][:, 14:15]),
                  reads=[rs["st"]], writes=[rs["rsq"]])
            em.do("dve", lambda h: h.reciprocal(sb["st"][:, 15:16], sb["st"][:, 15:16]),
                  reads=[rs["st"]], writes=[rs["st"]])
            em.do("dve", lambda h: h.scalar_tensor_tensor(out=sb["prod"][:, :], in0=sb["kbf"][:, :],
                                                          scalar=sb["st"][:, 15:16], in1=sb["vbf"][:, :],
                                                          op0=ALU.mult, op1=ALU.mult),
                  reads=[rs["kbf"], rs["st"], rs["vbf"]], writes=[rs["prod"]])
            em.do("pe", lambda h, b=b, tt=tt: h.matmul(ps["kv1"][0:1, :], sb["onesf"][:, 0:1],
                                                       sb["prod"][:, 0:512],
                                                       start=(tt == 0), stop=(tt == NTT - 1), skip_group_check=True),
                  reads=[rs["onesf"], rs["prod"]], writes=[rs["ps_kv1"]])
            em.do("pe", lambda h, b=b, tt=tt: h.matmul(ps["kv2"][0:1, :], sb["onesf"][:, 0:1],
                                                       sb["prod"][:, 512:1024],
                                                       start=(tt == 0), stop=(tt == NTT - 1), skip_group_check=True),
                  reads=[rs["onesf"], rs["prod"]], writes=[rs["ps_kv2"]])
        em.do("act", lambda h: h.copy(out=sb["kvrow"][0:1, 0:512], in_=ps["kv1"][0:1, :]),
              reads=[rs["ps_kv1"]], writes=[rs["kvrow"]])
        em.do("act", lambda h: h.copy(out=sb["kvrow"][0:1, 512:1024], in_=ps["kv2"][0:1, :]),
              reads=[rs["ps_kv2"]], writes=[rs["kvrow"]])
        for half in range(2):
            pst, psn = (ps["a"], "ps_a") if half == 0 else (ps["b"], "ps_b")
            em.do("pe", lambda h, half=half, pst=pst: h.matmul(
                pst[:, :], sb["onesf"][0:1, :], sb["kvrow"][0:1, half * 512:(half + 1) * 512],
                start=True, stop=True),
                reads=[rs["onesf"], rs["kvrow"]], writes=[rs[psn]])
            em.do("act", lambda h, half=half, pst=pst: h.copy(
                out=sb["kvb"][:, half * 512:(half + 1) * 512], in_=pst[:, :]),
                reads=[rs[psn]], writes=[rs["kvb"]])
        for tt in range(NTT):
            em.do("dve", lambda h, tt=tt: h.scalar_tensor_tensor(
                out=sb["obf"][:, :], in0=sb["qbf"][:, tt, :], scalar=sb["rsq"][:, tt:tt + 1],
                in1=sb["kvb"][:, :], op0=ALU.mult, op1=ALU.mult),
                reads=[rs["qbf"], rs["rsq"], rs["kvb"]], writes=[rs["obf"]])
            for ki in range(8):
                em.do("pe", lambda h, ki=ki: h.transpose(ps["t"][:, :], sb["obf"][:, ki * 128:(ki + 1) * 128],
                                                         sb["ident"][:, :]),
                      reads=[rs["obf"], rs["ident"]], writes=[rs["ps_t"]])
                eng = "dve" if ki % 2 == 0 else "act"
                fn = (lambda h, ki=ki: h.tensor_copy(sb["xT"][:, ki, :], ps["t"][:, :])) if eng == "dve" else \
                     (lambda h, ki=ki: h.copy(out=sb["xT"][:, ki, :], in_=ps["t"][:, :]))
                em.do(eng, fn, reads=[rs["ps_t"]], writes=[rs["xT"]])
            em.do("sp", lambda h, b=b, tt=tt: h.dma_start(out=sb["xr"][:, :], in_=x_in[b, tt * 128:(tt + 1) * 128, :]),
                  writes=[rs["xr"]], dma=True)
            for half in range(2):
                pst, psn = (ps["a"], "ps_a") if half == 0 else (ps["b"], "ps_b")
                for ki in range(8):
                    em.do("pe", lambda h, ki=ki, half=half, pst=pst: h.matmul(
                        pst[:, :], sb["xT"][:, ki, :], sb["wout"][:, ki, half * 512:(half + 1) * 512],
                        start=(ki == 0), stop=False),
                        reads=[rs["xT"], rs["wout"]], writes=[rs[psn]])
                em.do("pe", lambda h, half=half, pst=pst: h.matmul(
                    pst[:, :], sb["onesf"][0:1, :], sb["bo_row"][0:1, half * 512:(half + 1) * 512],
                    start=False, stop=True),
                    reads=[rs["onesf"], rs["bo_row"]], writes=[rs[psn]])
                em.do("dve", lambda h, half=half, pst=pst: h.tensor_add(
                    sb["acc"][:, half * 512:(half + 1) * 512], pst[:, :],
                    sb["xr"][:, half * 512:(half + 1) * 512]),
                    reads=[rs[psn], rs["xr"]], writes=[rs["acc"]])
            em.do("sp", lambda h, b=b, tt=tt: h.dma_start(out=x3_out[b, tt * 128:(tt + 1) * 128, :],
                                                          in_=sb["acc"][:, :]),
                  reads=[rs["acc"]], writes=[rs["x3_o"]], dma=True)


def build_l1attn_nc(nb):
    from contextlib import ExitStack
    nc = bass.Bass("TRN2", target_bir_lowering=False)
    x_in = nc.dram_tensor("x_in", [nb, T, D], F32, kind="ExternalInput").ap()
    x3_out = nc.dram_tensor("x3_out", [nb, T, D], F32, kind="ExternalOutput").ap()
    wqkv = nc.dram_tensor("wqkv", [D, 3 * D], F32, kind="ExternalInput").ap()
    bqkv = nc.dram_tensor("bqkv", [3 * D], F32, kind="ExternalInput").ap()
    wout = nc.dram_tensor("wout", [D, D], F32, kind="ExternalInput").ap()
    bout = nc.dram_tensor("bout", [D], F32, kind="ExternalInput").ap()
    hc = l0ffn_consts(nc)
    es = ExitStack()
    with es:
        sb, ps = alloc_l1attn(nc, es)
        trk = Trk(nc, es)
        em = Emit(trk)
        rs = {k: Rsrc(k) for k in L1A_RS}
        block = es.enter_context(nc.Block())
        emit_l1_attn(nc, em, sb, ps, rs, x_in, x3_out, wqkv, bqkv, wout, bout, nb, hc)
        trk.wait("sp", [rs["x3_o"].w])
        trk.replay(block)
    return nc



L1F_RS = ["ident", "onesf", "konst", "rw8", "w1sb", "w2sb", "xlnT", "gate", "gatebf", "gb",
          "grow", "hbf", "hsp", "yev", "xt", "xbf", "xT", "st", "scr2ln", "xr", "acc",
          "neginf", "l2", "is1", "is2", "gcol",
          "ps_a", "ps_b", "ps_t", "ps_lg8", "y_acc", "out_o"]


def alloc_l1ffn(nc, es):
    sb, ps = {}, {}
    sb["ident"] = es.enter_context(nc.sbuf_tensor("m_ident", [128, 128], BF16))
    sb["onesf"] = es.enter_context(nc.sbuf_tensor("m_onesf", [128, 128], BF16))
    sb["konst"] = es.enter_context(nc.sbuf_tensor("m_konst", [128, 4], F32))
    sb["rw8"] = es.enter_context(nc.sbuf_tensor("m_rw8", [128, 8, 8], BF16))
    sb["w1sb"] = es.enter_context(nc.sbuf_tensor("m_w1sb", [128, 8, 2048], BF16))
    sb["w2sb"] = es.enter_context(nc.sbuf_tensor("m_w2sb", [128, 16, 1024], BF16))
    sb["xlnT"] = es.enter_context(nc.sbuf_tensor("m_xlnT", [128, 8, 4096], BF16))
    sb["gate"] = es.enter_context(nc.sbuf_tensor("m_gate", [128, 32, 8], F32))
    sb["gatebf"] = es.enter_context(nc.sbuf_tensor("m_gatebf", [128, 32, 8], BF16))
    sb["gb"] = es.enter_context(nc.sbuf_tensor("m_gb", [128, 512], BF16))
    sb["grow"] = es.enter_context(nc.sbuf_tensor("m_grow", [1, 512], BF16))
    sb["hbf"] = es.enter_context(nc.sbuf_tensor("m_hbf", [128, 512], BF16))
    sb["hsp"] = es.enter_context(nc.sbuf_tensor("m_hsp", [128, 16, 512], BF16))
    sb["yev"] = es.enter_context(nc.sbuf_tensor("m_yev", [128, 1024], F32))
    sb["xt"] = es.enter_context(nc.sbuf_tensor("m_xt", [128, 1024], F32))
    sb["xbf"] = es.enter_context(nc.sbuf_tensor("m_xbf", [128, 1024], BF16))
    sb["xT"] = es.enter_context(nc.sbuf_tensor("m_xT", [128, 8, 128], BF16))
    sb["st"] = es.enter_context(nc.sbuf_tensor("m_st", [128, 16], F32))
    sb["scr2ln"] = es.enter_context(nc.sbuf_tensor("m_scr2ln", [128, 1024], F32))
    sb["xr"] = es.enter_context(nc.sbuf_tensor("m_xr", [128, 1024], F32))
    sb["acc"] = es.enter_context(nc.sbuf_tensor("m_acc", [128, 1024], F32))
    sb["neginf"] = es.enter_context(nc.sbuf_tensor("m_neginf", [128, 8], F32))
    sb["l2"] = es.enter_context(nc.sbuf_tensor("m_l2", [128, 8], F32))
    sb["is1"] = es.enter_context(nc.sbuf_tensor("m_is1", [128, 8], F32))
    sb["is2"] = es.enter_context(nc.sbuf_tensor("m_is2", [128, 8], F32))
    sb["gcol"] = es.enter_context(nc.sbuf_tensor("m_gcol", [128, 8], F32))
    ps["a"] = es.enter_context(nc.psum_tensor("m_ps_a", [128, 512], F32))
    ps["b"] = es.enter_context(nc.psum_tensor("m_ps_b", [128, 512], F32))
    ps["t"] = es.enter_context(nc.psum_tensor("m_ps_t", [128, 128], BF16))
    ps["lg8"] = es.enter_context(nc.psum_tensor("m_ps_lg8", [128, 8], F32))
    ps["gr"] = es.enter_context(nc.psum_tensor("m_ps_gr", [1, 128], BF16))
    ps["gbp"] = es.enter_context(nc.psum_tensor("m_ps_gbp", [128, 512], F32))
    return sb, ps


def emit_l1_ffn(nc, em, sb, ps, rs, x_in, out_ap, rw8_ap, w1T_ap, w2T_ap, y_acc, hc):
    em.do("sp", lambda h: h.dma_start(out=sb["ident"][:], in_=hc["ident"].ap()), writes=[rs["ident"]], dma=True)
    em.do("sp", lambda h: h.dma_start(out=sb["onesf"][:], in_=hc["ones_full"].ap()), writes=[rs["onesf"]], dma=True)
    em.do("sp", lambda h: h.dma_start(out=sb["konst"][:], in_=hc["konst"].ap()), writes=[rs["konst"]], dma=True)
    em.do("pool", lambda h: h.dma_start(out=sb["rw8"][:, :, :], in_=rw8_ap.rearrange("(k p) n -> p k n", p=128)),
          writes=[rs["rw8"]], dma=True)
    em.do("dve", lambda h: h.memset(sb["neginf"][:, :], -1e30), writes=[rs["neginf"]])

    for j in range(NT_OWN):
        em.do("sp", lambda h, j=j: h.dma_start(out=sb["xt"][:, :], in_=x_in[j * 128:(j + 1) * 128, :]),
              writes=[rs["xt"]], dma=True)
        emit_ln128(em, sb, rs)
        for ki in range(8):
            em.do("pe", lambda h, ki=ki: h.transpose(ps["t"][:, :], sb["xbf"][:, ki * 128:(ki + 1) * 128],
                                                     sb["ident"][:, :]),
                  reads=[rs["xbf"], rs["ident"]], writes=[rs["ps_t"]])
            eng = "dve" if ki % 2 == 0 else "act"
            fn = (lambda h, ki=ki, j=j: h.tensor_copy(sb["xlnT"][:, ki, j * 128:(j + 1) * 128], ps["t"][:, :])) \
                if eng == "dve" else \
                 (lambda h, ki=ki, j=j: h.copy(out=sb["xlnT"][:, ki, j * 128:(j + 1) * 128], in_=ps["t"][:, :]))
            em.do(eng, fn, reads=[rs["ps_t"]], writes=[rs["xlnT"]])
        for ki in range(8):
            em.do("pe", lambda h, ki=ki, j=j: h.matmul(ps["lg8"][:, :], sb["xlnT"][:, ki, j * 128:(j + 1) * 128],
                                                       sb["rw8"][:, ki, :], start=(ki == 0), stop=(ki == 7)),
                  reads=[rs["xlnT"], rs["rw8"]], writes=[rs["ps_lg8"]])
        em.do("dve", lambda h: h.tensor_copy(sb["l2"][:, :], ps["lg8"][:, :]), reads=[rs["ps_lg8"]], writes=[rs["l2"]])
        em.do("dve", lambda h: h.reduce_max(sb["st"][:, 0:1], sb["l2"][:, :], AX.X), reads=[rs["l2"]], writes=[rs["st"]])
        em.do("dve", lambda h: h.tensor_scalar(sb["is1"][:, :], sb["l2"][:, :], sb["st"][:, 0:1], None, ALU.is_equal),
              reads=[rs["l2"], rs["st"]], writes=[rs["is1"]])
        em.do("dve", lambda h: h.scalar_tensor_tensor(out=sb["l2"][:, :], in0=sb["is1"][:, :],
                                                      scalar=-1e30, in1=sb["l2"][:, :],
                                                      op0=ALU.mult, op1=ALU.add),
              reads=[rs["is1"]], writes=[rs["l2"]])
        em.do("dve", lambda h: h.reduce_max(sb["st"][:, 1:2], sb["l2"][:, :], AX.X), reads=[rs["l2"]], writes=[rs["st"]])
        em.do("dve", lambda h: h.tensor_scalar(sb["is2"][:, :], sb["l2"][:, :], sb["st"][:, 1:2], None, ALU.is_equal),
              reads=[rs["l2"], rs["st"]], writes=[rs["is2"]])
        em.do("dve", lambda h: h.tensor_sub(sb["st"][:, 2:3], sb["st"][:, 1:2], sb["st"][:, 0:1]),
              reads=[rs["st"]], writes=[rs["st"]])
        em.do("act", lambda h: h.activation(out=sb["st"][:, 3:4], in_=sb["st"][:, 2:3], func=AF.Exp),
              reads=[rs["st"]], writes=[rs["st"]])
        em.do("dve", lambda h: h.tensor_scalar(sb["st"][:, 4:5], sb["st"][:, 3:4], 1.0, None, ALU.add),
              reads=[rs["st"]], writes=[rs["st"]])
        em.do("dve", lambda h: h.reciprocal(sb["st"][:, 5:6], sb["st"][:, 4:5]), reads=[rs["st"]], writes=[rs["st"]])
        em.do("dve", lambda h: h.tensor_mul(sb["st"][:, 6:7], sb["st"][:, 3:4], sb["st"][:, 5:6]),
              reads=[rs["st"]], writes=[rs["st"]])
        em.do("dve", lambda h: h.tensor_scalar(sb["gcol"][:, :], sb["is1"][:, :], sb["st"][:, 5:6], None, ALU.mult),
              reads=[rs["is1"], rs["st"]], writes=[rs["gcol"]])
        em.do("dve", lambda h, j=j: h.scalar_tensor_tensor(out=sb["gate"][:, j, :], in0=sb["is2"][:, :],
                                                           scalar=sb["st"][:, 6:7], in1=sb["gcol"][:, :],
                                                           op0=ALU.mult, op1=ALU.add),
              reads=[rs["is2"], rs["st"], rs["gcol"]], writes=[rs["gate"]])
        em.do("dve", lambda h, j=j: h.tensor_copy(sb["gatebf"][:, j, :], sb["gate"][:, j, :]),
              reads=[rs["gate"]], writes=[rs["gatebf"]])

    for e in range(8):
        em.do("pool", lambda h, e=e: h.dma_start(out=sb["w1sb"][:, :, :],
                                                 in_=w1T_ap[e].rearrange("(k p) n -> p k n", p=128)),
              writes=[rs["w1sb"]], dma=True)
        em.do("pool", lambda h, e=e: h.dma_start(out=sb["w2sb"][:, :, :],
                                                 in_=w2T_ap[e].rearrange("(k p) n -> p k n", p=128)),
              writes=[rs["w2sb"]], dma=True)
        for sp_ in range(8):
            t0 = sp_ * 512
            for q in range(4):
                em.do("pe", lambda h, e=e, sp_=sp_, q=q: h.transpose(
                    ps["gr"][0:1, :], sb["gatebf"][:, sp_ * 4 + q, e:e + 1], sb["ident"][:, :]),
                    reads=[rs["gatebf"], rs["ident"]], writes=[rs["ps_t"]])
                em.do("act", lambda h, q=q: h.copy(out=sb["grow"][0:1, q * 128:(q + 1) * 128], in_=ps["gr"][0:1, :]),
                      reads=[rs["ps_t"]], writes=[rs["grow"]])
            em.do("pe", lambda h: h.matmul(ps["gbp"][:, :], sb["onesf"][0:1, :], sb["grow"][0:1, :],
                                           start=True, stop=True),
                  reads=[rs["onesf"], rs["grow"]], writes=[rs["ps_a"]])
            em.do("act", lambda h: h.copy(out=sb["gb"][:, :], in_=ps["gbp"][:, :]),
                  reads=[rs["ps_a"]], writes=[rs["gb"]])
            for f in range(16):
                for ki in range(8):
                    em.do("pe", lambda h, f=f, ki=ki, t0=t0: h.matmul(
                        ps["a"][:, :], sb["w1sb"][:, ki, f * 128:(f + 1) * 128],
                        sb["xlnT"][:, ki, t0:t0 + 512], start=(ki == 0), stop=(ki == 7)),
                        reads=[rs["w1sb"], rs["xlnT"]], writes=[rs["ps_a"]])
                em.do("act", lambda h: h.activation(out=sb["hbf"][:, :], in_=ps["a"][:, :], func=AF.Gelu),
                      reads=[rs["ps_a"]], writes=[rs["hbf"]])
                em.do("dve", lambda h, f=f: h.tensor_mul(sb["hsp"][:, f, :], sb["hbf"][:, :], sb["gb"][:, :]),
                      reads=[rs["hbf"], rs["gb"]], writes=[rs["hsp"]])
            for tq in range(4):
                for half in range(2):
                    pst, psn = (ps["a"], "ps_a") if half == 0 else (ps["b"], "ps_b")
                    for f in range(16):
                        em.do("pe", lambda h, f=f, tq=tq, half=half, pst=pst: h.matmul(
                            pst[:, :], sb["hsp"][:, f, tq * 128:(tq + 1) * 128],
                            sb["w2sb"][:, f, half * 512:(half + 1) * 512], start=(f == 0), stop=(f == 15)),
                            reads=[rs["hsp"], rs["w2sb"]], writes=[rs[psn]])
                    em.do("dve", lambda h, half=half, pst=pst: h.tensor_copy(
                        sb["yev"][:, half * 512:(half + 1) * 512], pst[:, :]),
                        reads=[rs[psn]], writes=[rs["yev"]])
                tg = t0 + tq * 128
                if e == 0:
                    em.do("pool", lambda h, tg=tg: h.dma_start(out=y_acc[tg:tg + 128, :], in_=sb["yev"][:, :]),
                          reads=[rs["yev"]], writes=[rs["y_acc"]], dma=True)
                else:
                    em.do("pool", lambda h, tg=tg: h.dma_start(out=y_acc[tg:tg + 128, :], in_=sb["yev"][:, :],
                                                               accum_op=ALU.add),
                          reads=[rs["yev"]], writes=[rs["y_acc"]], dma=True)

    for j in range(NT_OWN):
        em.do("sp", lambda h, j=j: h.dma_start(out=sb["xr"][:, :], in_=x_in[j * 128:(j + 1) * 128, :]),
              writes=[rs["xr"]], dma=True)
        em.do("sp", lambda h, j=j: h.dma_start(out=sb["xt"][:, :], in_=y_acc[j * 128:(j + 1) * 128, :]),
              reads=[rs["y_acc"]], writes=[rs["xt"]], dma=True)
        em.do("dve", lambda h: h.tensor_add(sb["acc"][:, :], sb["xt"][:, :], sb["xr"][:, :]),
              reads=[rs["xt"], rs["xr"]], writes=[rs["acc"]])
        em.do("sp", lambda h, j=j: h.dma_start(out=out_ap[j * 128:(j + 1) * 128, :], in_=sb["acc"][:, :]),
              reads=[rs["acc"]], writes=[rs["out_o"]], dma=True)


def build_l1ffn_nc():
    from contextlib import ExitStack
    nc = bass.Bass("TRN2", target_bir_lowering=False)
    x_in = nc.dram_tensor("x_in", [4096, D], F32, kind="ExternalInput").ap()
    out_ap = nc.dram_tensor("out", [4096, D], F32, kind="ExternalOutput").ap()
    rw8 = nc.dram_tensor("rw8", [D, 8], F32, kind="ExternalInput").ap()
    w1T = nc.dram_tensor("w1T", [8, D, 2048], F32, kind="ExternalInput").ap()
    w2T = nc.dram_tensor("w2T", [8, 2048, D], F32, kind="ExternalInput").ap()
    y_acc = nc.dram_tensor("y_acc", [4096, D], F32, kind="Internal").ap()
    hc = l0ffn_consts(nc)
    es = ExitStack()
    with es:
        sb, ps = alloc_l1ffn(nc, es)
        trk = Trk(nc, es)
        em = Emit(trk)
        rs = {k: Rsrc(k) for k in L1F_RS}
        block = es.enter_context(nc.Block())
        emit_l1_ffn(nc, em, sb, ps, rs, x_in, out_ap, rw8, w1T, w2T, y_acc, hc)
        trk.wait("sp", [rs["out_o"].w])
        trk.replay(block)
    return nc


def trk_barrier(trk):
    trk.barrier()


def build_fused3_nc():
    from contextlib import ExitStack
    nc = bass.Bass("TRN2", target_bir_lowering=False)
    x1_full = nc.dram_tensor("x1_full", [NB * T, D], F32, kind="ExternalInput").ap()
    x1_own = nc.dram_tensor("x1_own", [4096, D], F32, kind="ExternalInput").ap()
    rankoh = nc.dram_tensor("rankoh", [1, 512], F32, kind="ExternalInput").ap()
    rwT = nc.dram_tensor("rwT", [D, E64], F32, kind="ExternalInput").ap()
    wsgu = nc.dram_tensor("wsgu", [D, 1536], F32, kind="ExternalInput").ap()
    wsd = nc.dram_tensor("wsd", [768, D], F32, kind="ExternalInput").ap()
    wg = nc.dram_tensor("wg", [E64, D, 768], F32, kind="ExternalInput").ap()
    wu = nc.dram_tensor("wu", [E64, D, 768], F32, kind="ExternalInput").ap()
    wd = nc.dram_tensor("wd", [E64, 768, D], F32, kind="ExternalInput").ap()
    wqkv = nc.dram_tensor("wqkv", [D, 3 * D], F32, kind="ExternalInput").ap()
    bqkv = nc.dram_tensor("bqkv", [3 * D], F32, kind="ExternalInput").ap()
    wout = nc.dram_tensor("wout", [D, D], F32, kind="ExternalInput").ap()
    bout = nc.dram_tensor("bout", [D], F32, kind="ExternalInput").ap()
    rw8 = nc.dram_tensor("rw8", [D, 8], F32, kind="ExternalInput").ap()
    w1T = nc.dram_tensor("w1T", [8, D, 2048], F32, kind="ExternalInput").ap()
    w2T = nc.dram_tensor("w2T", [8, 2048, D], F32, kind="ExternalInput").ap()
    out_ap = nc.dram_tensor("out", [4096, D], F32, kind="ExternalOutput").ap()

    xb_d = nc.dram_tensor("xb_d", [E64 * CAP, D], BF16, kind="Internal").ap()
    w_d = nc.dram_tensor("w_d", [E64 * CAP, 2], F32, kind="Internal").ap()
    y_d = nc.dram_tensor("y_d", [E64 * CAP, D], BF16, kind="Internal").ap()
    s8u_d = nc.dram_tensor("s8u_d", [NT_OWN, 128, 8], U32, kind="Internal").ap()
    rbase_d = nc.dram_tensor("rbase_d", [4096, D], F32, kind="Internal").ap()
    x2_d = nc.dram_tensor("x2_d", [4096, D], F32, kind="Internal")
    x3_d = nc.dram_tensor("x3_d", [4096, D], F32, kind="Internal")
    y_acc = nc.dram_tensor("y_acc", [4096, D], F32, kind="Internal").ap()

    hc = l0ffn_consts(nc)
    es = ExitStack()
    with es:
        trk = Trk(nc, es)
        em = Emit(trk)
        with ExitStack() as es1:
            sb, ps = alloc_l0ffn(nc, es1)
            rs = {k: Rsrc(k) for k in L0FFN_RS}
            emit_l0ffn_R(nc, em, sb, ps, rs, x1_full, rankoh, rwT, hc)
            emit_l0ffn_P(nc, em, sb, ps, rs, x1_own, rwT, xb_d, w_d, s8u_d, hc)
            emit_l0ffn_S(nc, em, sb, ps, rs, x1_own, wsgu, wsd, rbase_d)
            emit_l0ffn_E(nc, em, sb, ps, rs, wg, wu, wd, xb_d, w_d, y_d)
            emit_l0ffn_G(nc, em, sb, ps, rs, rbase_d, s8u_d, y_d, x2_d.ap())
            trk_barrier(trk)
        with ExitStack() as es2:
            sb, ps = alloc_l1attn(nc, es2)
            rs = {k: Rsrc(k) for k in L1A_RS}
            x2v = x2_d.ap().rearrange("(b t) d -> b t d", b=2)
            x3v = x3_d.ap().rearrange("(b t) d -> b t d", b=2)
            emit_l1_attn(nc, em, sb, ps, rs, x2v, x3v, wqkv, bqkv, wout, bout, 2, hc)
            trk_barrier(trk)
        with ExitStack() as es3:
            sb, ps = alloc_l1ffn(nc, es3)
            rs = {k: Rsrc(k) for k in L1F_RS}
            emit_l1_ffn(nc, em, sb, ps, rs, x3_d.ap(), out_ap, rw8, w1T, w2T, y_acc, hc)
            trk_barrier(trk)
        block = es.enter_context(nc.Block())
        trk.replay(block)
    return nc


def build_fullfused_nc():
    from contextlib import ExitStack
    nc = bass.Bass("TRN2", target_bir_lowering=False)
    x_in = nc.dram_tensor("x_in", [NB, T, D], F32, kind="ExternalInput").ap()
    ownrows = nc.dram_tensor("ownrows", [NT_OWN, 128, 1], U32, kind="ExternalInput").ap()
    rankoh = nc.dram_tensor("rankoh", [1, 512], F32, kind="ExternalInput").ap()
    wq_a = {w: nc.dram_tensor(w, [D, D], F32, kind="ExternalInput").ap()
            for w in ("wq", "wk", "wv", "wo")}
    rwT = nc.dram_tensor("rwT", [D, E64], F32, kind="ExternalInput").ap()
    wsgu = nc.dram_tensor("wsgu", [D, 1536], F32, kind="ExternalInput").ap()
    wsd = nc.dram_tensor("wsd", [768, D], F32, kind="ExternalInput").ap()
    wg = nc.dram_tensor("wg", [E64, D, 768], F32, kind="ExternalInput").ap()
    wu = nc.dram_tensor("wu", [E64, D, 768], F32, kind="ExternalInput").ap()
    wd = nc.dram_tensor("wd", [E64, 768, D], F32, kind="ExternalInput").ap()
    wqkv = nc.dram_tensor("wqkv", [D, 3 * D], F32, kind="ExternalInput").ap()
    bqkv = nc.dram_tensor("bqkv", [3 * D], F32, kind="ExternalInput").ap()
    wout = nc.dram_tensor("wout", [D, D], F32, kind="ExternalInput").ap()
    bout = nc.dram_tensor("bout", [D], F32, kind="ExternalInput").ap()
    rw8 = nc.dram_tensor("rw8", [D, 8], F32, kind="ExternalInput").ap()
    w1T = nc.dram_tensor("w1T", [8, D, 2048], F32, kind="ExternalInput").ap()
    w2T = nc.dram_tensor("w2T", [8, 2048, D], F32, kind="ExternalInput").ap()
    out_ap = nc.dram_tensor("out", [4096, D], F32, kind="ExternalOutput").ap()

    x1_d = nc.dram_tensor("x1_d", [NB * T, D], F32, kind="Internal")
    xb_d = nc.dram_tensor("xb_d", [E64 * CAP, D], BF16, kind="Internal").ap()
    w_d = nc.dram_tensor("w_d", [E64 * CAP, 2], F32, kind="Internal").ap()
    y_d = nc.dram_tensor("y_d", [E64 * CAP, D], BF16, kind="Internal").ap()
    s8u_d = nc.dram_tensor("s8u_d", [NT_OWN, 128, 8], U32, kind="Internal").ap()
    rbase_d = nc.dram_tensor("rbase_d", [4096, D], F32, kind="Internal").ap()
    x2_d = nc.dram_tensor("x2_d", [4096, D], F32, kind="Internal")
    x3_d = nc.dram_tensor("x3_d", [4096, D], F32, kind="Internal")
    y_acc = nc.dram_tensor("y_acc", [4096, D], F32, kind="Internal").ap()

    h_ident, h_tri, h_konst = l0_attn_consts(nc)
    hc = l0ffn_consts(nc)
    es = ExitStack()
    with es:
        trk = Trk(nc, es)
        em = Emit(trk)
        with ExitStack() as es0:
            sb, ps = alloc_l0_attn(nc, es0)
            rs = {k: Rsrc(k) for k in L0_RS_KEYS}
            x1v = x1_d.ap().rearrange("(b t) d -> b t d", b=NB)
            for b in range(NB):
                em.do("pool", lambda h, b=b: h.dma_start(out=x1v[b, 0:QR0, :], in_=x_in[b, 0:QR0, :]),
                      writes=[rs["out_0"]], dma=True)
            emit_l0_attn(nc, em, sb, ps, rs, x_in, x1v, wq_a, NB, h_ident, h_tri, h_konst,
                         out_base=QR0)
            trk_barrier(trk)
        with ExitStack() as es1:
            sb, ps = alloc_l0ffn(nc, es1)
            rs = {k: Rsrc(k) for k in L0FFN_RS}
            emit_l0ffn_R(nc, em, sb, ps, rs, x1_d.ap(), rankoh, rwT, hc)
            emit_l0ffn_P(nc, em, sb, ps, rs, None, rwT, xb_d, w_d, s8u_d, hc,
                         ownrows=ownrows, x1_dram=x1_d.ap())
            emit_l0ffn_S(nc, em, sb, ps, rs, None, wsgu, wsd, rbase_d,
                         ownrows=ownrows, x1_dram=x1_d.ap())
            emit_l0ffn_E(nc, em, sb, ps, rs, wg, wu, wd, xb_d, w_d, y_d)
            emit_l0ffn_G(nc, em, sb, ps, rs, rbase_d, s8u_d, y_d, x2_d.ap())
            trk_barrier(trk)
        with ExitStack() as es2:
            sb, ps = alloc_l1attn(nc, es2)
            rs = {k: Rsrc(k) for k in L1A_RS}
            x2v = x2_d.ap().rearrange("(b t) d -> b t d", b=2)
            x3v = x3_d.ap().rearrange("(b t) d -> b t d", b=2)
            emit_l1_attn(nc, em, sb, ps, rs, x2v, x3v, wqkv, bqkv, wout, bout, 2, hc)
            trk_barrier(trk)
        with ExitStack() as es3:
            sb, ps = alloc_l1ffn(nc, es3)
            rs = {k: Rsrc(k) for k in L1F_RS}
            emit_l1_ffn(nc, em, sb, ps, rs, x3_d.ap(), out_ap, rw8, w1T, w2T, y_acc, hc)
            trk_barrier(trk)
        block = es.enter_context(nc.Block())
        trk.replay(block)
    return nc



def alloc_l0_attn2(nc, es):
    sb, ps = {}, {}
    sb["xlnT"] = es.enter_context(nc.sbuf_tensor("xlnT", [128, 8, 520], BF16))
    sb["qf"] = es.enter_context(nc.sbuf_tensor("qf", [128, 8, 516], BF16))
    sb["kf"] = es.enter_context(nc.sbuf_tensor("kf", [128, 8, 516], BF16))
    sb["vt"] = es.enter_context(nc.sbuf_tensor("vt", [128, 5, 1024], BF16))
    sb["ao"] = es.enter_context(nc.sbuf_tensor("ao", [128, 8, 516], BF16))
    for w in ("wq", "wk", "wv", "wo"):
        sb[w] = es.enter_context(nc.sbuf_tensor(w + "_sb", [128, 8, 1024], BF16))
    for p in range(2):
        sb[f"xt{p}"] = es.enter_context(nc.sbuf_tensor(f"xt{p}", [128, 1024], F32))
        sb[f"xsq{p}"] = es.enter_context(nc.sbuf_tensor(f"xsq{p}", [128, 1024], F32))
        sb[f"xlnb{p}"] = es.enter_context(nc.sbuf_tensor(f"xlnb{p}", [128, 1024], BF16))
        sb[f"st{p}"] = es.enter_context(nc.sbuf_tensor(f"st{p}", [128, 8], F32))
        sb[f"sta{p}"] = es.enter_context(nc.sbuf_tensor(f"sta{p}", [128, 4], F32))
        sb[f"lg{p}"] = es.enter_context(nc.sbuf_tensor(f"lg{p}", [128, 516], F32))
        sb[f"esc{p}"] = es.enter_context(nc.sbuf_tensor(f"esc{p}", [128, 516], BF16))
        sb[f"dg{p}"] = es.enter_context(nc.sbuf_tensor(f"dg{p}", [128, 128], BF16))
        sb[f"wT{p}"] = es.enter_context(nc.sbuf_tensor(f"wT{p}", [128, 5 * 128], BF16))
        sb[f"xr{p}"] = es.enter_context(nc.sbuf_tensor(f"xr{p}", [128, 1024], F32))
        sb[f"xo{p}"] = es.enter_context(nc.sbuf_tensor(f"xo{p}", [128, 1024], F32))
    sb["ident"] = es.enter_context(nc.sbuf_tensor("ident", [128, 128], BF16))
    sb["mtri"] = es.enter_context(nc.sbuf_tensor("mtri", [128, 128], F32))
    sb["konst"] = es.enter_context(nc.sbuf_tensor("konst", [128, 4], F32))
    ps["t"] = es.enter_context(nc.psum_tensor("ps_t", [128, 128], BF16))
    ps["s"] = es.enter_context(nc.psum_tensor("ps_s", [128, 8], F32))
    for p in range(2):
        ps[f"comb{p}"] = es.enter_context(nc.psum_tensor(f"ps_comb{p}", [128, 512], F32))
        ps[f"t2{p}"] = es.enter_context(nc.psum_tensor(f"ps_t2{p}", [128, 128], F32))
        ps[f"h{p}"] = es.enter_context(nc.psum_tensor(f"ps_h{p}", [64, 128], F32))
    return sb, ps


L0_RS2 = (["ident", "mtri", "konst", "xlnT", "qf", "kf", "vt", "ao",
           "wq", "wk", "wv", "wo", "ps_t", "ps_s", "out_0", "out_1"] +
          [f"{k}{p}" for p in range(2) for k in
           ("xt", "xsq", "xlnb", "st", "sta", "lg", "esc", "dg", "wT", "xr", "xo",
            "ps_comb", "ps_t2", "ps_h")])


def emit_l0_attn2(nc, em, sb, ps, rs, x_in, xq_out, w_aps, nb, h_ident, h_tri, h_konst,
                  out_base=0):
    em.do("sp", lambda h: h.dma_start(out=sb["ident"][:], in_=h_ident.ap()), writes=[rs["ident"]], dma=True)
    em.do("sp", lambda h: h.dma_start(out=sb["mtri"][:], in_=h_tri.ap()), writes=[rs["mtri"]], dma=True)
    em.do("sp", lambda h: h.dma_start(out=sb["konst"][:], in_=h_konst.ap()), writes=[rs["konst"]], dma=True)
    for w in ("wq", "wk", "wv", "wo"):
        src = w_aps[w].rearrange("(k p) n -> p k n", p=128)
        em.do("pool", lambda h, w=w, src=src: h.dma_start(out=sb[w][:, :, :], in_=src),
              writes=[rs[w]], dma=True)

    for b in range(nb):
        for ti, (r0, p_) in enumerate(L0_LN_TILES):
            c = ti % 2
            xt, xsq, xlnb, st = f"xt{c}", f"xsq{c}", f"xlnb{c}", f"st{c}"
            em.do("sp", lambda h, b=b, r0=r0, p_=p_, xt=xt: h.dma_start(
                out=sb[xt][0:p_, :], in_=x_in[b, r0:r0 + p_, :]), writes=[rs[xt]], dma=True)
            em.do("act", lambda h, p_=p_, xt=xt, xsq=xsq, st=st: h.activation(
                out=sb[xsq][0:p_, :], in_=sb[xt][0:p_, :], func=AF.Square, accum_out=sb[st][0:p_, 0:1]),
                reads=[rs[xt]], writes=[rs[xsq], rs[st]])
            em.do("dve", lambda h, p_=p_, xt=xt, st=st: h.reduce_sum(
                sb[st][0:p_, 1:2], sb[xt][0:p_, :], AX.X, negate=True),
                reads=[rs[xt]], writes=[rs[st]])
            em.do("dve", lambda h, p_=p_, st=st: h.tensor_scalar(
                sb[st][0:p_, 2:3], sb[st][0:p_, 1:2], 1.0 / D, None, ALU.mult),
                reads=[rs[st]], writes=[rs[st]])
            em.do("dve", lambda h, p_=p_, st=st: h.tensor_scalar(
                sb[st][0:p_, 3:4], sb[st][0:p_, 0:1], 1.0 / D, None, ALU.mult),
                reads=[rs[st]], writes=[rs[st]])
            em.do("dve", lambda h, p_=p_, st=st: h.scalar_tensor_tensor(
                out=sb[st][0:p_, 4:5], in0=sb[st][0:p_, 2:3], scalar=sb[st][0:p_, 2:3],
                in1=sb[st][0:p_, 3:4], op0=ALU.mult, op1=ALU.subtract),
                reads=[rs[st]], writes=[rs[st]])
            em.do("act", lambda h, p_=p_, st=st: h.activation(
                out=sb[st][0:p_, 5:6], in_=sb[st][0:p_, 4:5], func=AF.Sqrt,
                bias=sb["konst"][0:p_, 0:1], scale=sb["konst"][0:p_, 1:2]),
                reads=[rs[st], rs["konst"]], writes=[rs[st]])
            em.do("dve", lambda h, p_=p_, st=st: h.reciprocal(sb[st][0:p_, 6:7], sb[st][0:p_, 5:6]),
                reads=[rs[st]], writes=[rs[st]])
            em.do("dve", lambda h, p_=p_, xt=xt, xlnb=xlnb, st=st: h.tensor_scalar(
                sb[xlnb][0:p_, :], sb[xt][0:p_, :], sb[st][0:p_, 2:3], sb[st][0:p_, 6:7],
                ALU.add, ALU.mult),
                reads=[rs[xt], rs[st]], writes=[rs[xlnb]])
            c0, cn = L0_LNT_COLS[ti]
            for ki in range(8):
                tp = f"ps_t2{ki % 2}"
                tpb = f"t2{ki % 2}"
                em.do("pe", lambda h, ki=ki, p_=p_, xlnb=xlnb, tpb=tpb: h.matmul(
                    ps[tpb][0:128, 0:p_], sb[xlnb][0:p_, ki * 128:(ki + 1) * 128],
                    sb["ident"][0:p_, 0:p_], start=True, stop=True),
                    reads=[rs[xlnb], rs["ident"]], writes=[rs[tp]])
                eng = "dve" if ki % 2 == 0 else "act"
                fn = (lambda h, ki=ki, c0=c0, p_=p_, tpb=tpb: h.tensor_copy(
                    sb["xlnT"][:, ki, c0:c0 + p_], ps[tpb][0:128, 0:p_])) if eng == "dve" else \
                     (lambda h, ki=ki, c0=c0, p_=p_, tpb=tpb: h.copy(
                    out=sb["xlnT"][:, ki, c0:c0 + p_], in_=ps[tpb][0:128, 0:p_]))
                em.do(eng, fn, reads=[rs[tp]], writes=[rs["xlnT"]])

        for mo in range(8):
            c = mo % 2
            cb, cbr = ps[f"comb{c}"], rs[f"ps_comb{c}"]
            for ki in range(8):
                em.do("pe", lambda h, mo=mo, ki=ki, cb=cb: h.matmul(
                    cb[:, 0:512], sb["wq"][:, ki, mo * 128:(mo + 1) * 128],
                    sb["xlnT"][:, ki, 4:516], start=(ki == 0), stop=(ki == 7)),
                    reads=[rs["wq"], rs["xlnT"]], writes=[cbr])
                em.do("pe", lambda h, mo=mo, ki=ki: h.matmul(
                    ps["s"][:, 0:4], sb["wq"][:, ki, mo * 128:(mo + 1) * 128],
                    sb["xlnT"][:, ki, 516:520], start=(ki == 0), stop=(ki == 7),
                    skip_group_check=True),
                    reads=[rs["wq"], rs["xlnT"]], writes=[rs["ps_s"]])
            em.do("act", lambda h, mo=mo, cb=cb: h.activation(
                out=sb["qf"][:, mo, 0:512], in_=cb[:, 0:512], func=AF.Copy, scale=sb["konst"][:, 2:3]),
                reads=[cbr, rs["konst"]], writes=[rs["qf"]])
            em.do("act", lambda h, mo=mo: h.activation(
                out=sb["qf"][:, mo, 512:516], in_=ps["s"][:, 0:4], func=AF.Copy, scale=sb["konst"][:, 2:3]),
                reads=[rs["ps_s"], rs["konst"]], writes=[rs["qf"]])
        for mo in range(8):
            c = mo % 2
            cb, cbr = ps[f"comb{c}"], rs[f"ps_comb{c}"]
            for ki in range(8):
                em.do("pe", lambda h, mo=mo, ki=ki, cb=cb: h.matmul(
                    cb[:, 0:512], sb["wk"][:, ki, mo * 128:(mo + 1) * 128],
                    sb["xlnT"][:, ki, 8:520], start=(ki == 0), stop=(ki == 7)),
                    reads=[rs["wk"], rs["xlnT"]], writes=[cbr])
                em.do("pe", lambda h, mo=mo, ki=ki: h.matmul(
                    ps["s"][:, 0:4], sb["wk"][:, ki, mo * 128:(mo + 1) * 128],
                    sb["xlnT"][:, ki, 0:4], start=(ki == 0), stop=(ki == 7),
                    skip_group_check=True),
                    reads=[rs["wk"], rs["xlnT"]], writes=[rs["ps_s"]])
            em.do("dve", lambda h, mo=mo, cb=cb: h.tensor_copy(sb["kf"][:, mo, 4:516], cb[:, 0:512]),
                  reads=[cbr], writes=[rs["kf"]])
            em.do("dve", lambda h, mo=mo: h.tensor_copy(sb["kf"][:, mo, 0:4], ps["s"][:, 0:4]),
                  reads=[rs["ps_s"]], writes=[rs["kf"]])
        for cti, (c0, ck) in enumerate(L0_CT_XLNT):
            for half in range(2):
                cb, cbr = ps[f"comb{half}"], rs[f"ps_comb{half}"]
                for ki in range(8):
                    em.do("pe", lambda h, cti=cti, c0=c0, ck=ck, half=half, ki=ki, cb=cb: h.matmul(
                        cb[0:ck, 0:512], sb["xlnT"][:, ki, c0:c0 + ck],
                        sb["wv"][:, ki, half * 512:(half + 1) * 512], start=(ki == 0), stop=(ki == 7)),
                        reads=[rs["wv"], rs["xlnT"]], writes=[cbr])
                eng = "dve" if half == 0 else "act"
                fn = (lambda h, cti=cti, ck=ck, half=half, cb=cb: h.tensor_copy(
                    sb["vt"][0:ck, cti, half * 512:(half + 1) * 512], cb[0:ck, 0:512])) if eng == "dve" else \
                     (lambda h, cti=cti, ck=ck, half=half, cb=cb: h.copy(
                    out=sb["vt"][0:ck, cti, half * 512:(half + 1) * 512], in_=cb[0:ck, 0:512]))
                em.do(eng, fn, reads=[cbr], writes=[rs["vt"]])

        it = 0
        for hd in range(16):
            mo, po = hd // 2, (hd % 2) * 64
            for qt, (qoff, qsz) in enumerate(L0_QT):
                c = it % 2
                it += 1
                wq_ = qoff + qsz
                cb, cbr = ps[f"comb{c}"], rs[f"ps_comb{c}"]
                hb, hbr = ps[f"h{c}"], rs[f"ps_h{c}"]
                tb, tbr = ps[f"t2{c}"], rs[f"ps_t2{c}"]
                lg, esc, dg, sta, wT = f"lg{c}", f"esc{c}", f"dg{c}", f"sta{c}", f"wT{c}"
                if qt < 4:
                    em.do("pe", lambda h, mo=mo, po=po, qoff=qoff, qsz=qsz, cb=cb: h.matmul(
                        cb[0:qsz, 0:4], sb["qf"][po:po + 64, mo, qoff:qoff + qsz],
                        sb["kf"][po:po + 64, mo, 0:4], start=True, stop=True, skip_group_check=True),
                        reads=[rs["qf"], rs["kf"]], writes=[cbr])
                    em.do("pe", lambda h, mo=mo, po=po, qoff=qoff, qsz=qsz, wq_=wq_, cb=cb: h.matmul(
                        cb[0:qsz, 4:wq_], sb["qf"][po:po + 64, mo, qoff:qoff + qsz],
                        sb["kf"][po:po + 64, mo, 4:wq_], start=True, stop=True, skip_group_check=True),
                        reads=[rs["qf"], rs["kf"]], writes=[cbr])
                    if qt == 0:
                        em.do("dve", lambda h, qsz=qsz, lg=lg, cb=cb: h.tensor_add(
                            sb[lg][0:qsz, 0:128], cb[0:qsz, 0:128], sb["mtri"][0:qsz, 0:128]),
                            reads=[cbr, rs["mtri"]], writes=[rs[lg]])
                    else:
                        em.do("act", lambda h, qsz=qsz, qoff=qoff, lg=lg, cb=cb: h.copy(
                            out=sb[lg][0:qsz, 0:qoff], in_=cb[0:qsz, 0:qoff]),
                            reads=[cbr], writes=[rs[lg]])
                        em.do("dve", lambda h, qsz=qsz, qoff=qoff, lg=lg, cb=cb: h.tensor_add(
                            sb[lg][0:qsz, qoff:qoff + qsz], cb[0:qsz, qoff:qoff + qsz],
                            sb["mtri"][0:qsz, 0:qsz]),
                            reads=[cbr, rs["mtri"]], writes=[rs[lg]])
                else:
                    em.do("pe", lambda h, mo=mo, po=po, qoff=qoff, qsz=qsz, cb=cb: h.matmul(
                        cb[0:qsz, 0:4], sb["qf"][po:po + 64, mo, qoff:qoff + qsz],
                        sb["kf"][po:po + 64, mo, 0:4], start=True, stop=True, skip_group_check=True),
                        reads=[rs["qf"], rs["kf"]], writes=[cbr])
                    em.do("pe", lambda h, mo=mo, po=po, qoff=qoff, qsz=qsz, cb=cb: h.matmul(
                        cb[0:qsz, 4:512], sb["qf"][po:po + 64, mo, qoff:qoff + qsz],
                        sb["kf"][po:po + 64, mo, 4:512], start=True, stop=True, skip_group_check=True),
                        reads=[rs["qf"], rs["kf"]], writes=[cbr])
                    em.do("pe", lambda h, mo=mo, po=po, qoff=qoff, qsz=qsz: h.matmul(
                        ps["s"][0:qsz, 0:4], sb["qf"][po:po + 64, mo, qoff:qoff + qsz],
                        sb["kf"][po:po + 64, mo, 512:516], start=True, stop=True, skip_group_check=True),
                        reads=[rs["qf"], rs["kf"]], writes=[rs["ps_s"]])
                    em.do("act", lambda h, qsz=qsz, lg=lg, cb=cb: h.copy(
                        out=sb[lg][0:qsz, 0:512], in_=cb[0:qsz, 0:512]),
                        reads=[cbr], writes=[rs[lg]])
                    em.do("dve", lambda h, qsz=qsz, lg=lg: h.tensor_add(
                        sb[lg][0:qsz, 512:516], ps["s"][0:qsz, 0:4], sb["mtri"][0:qsz, 0:qsz]),
                        reads=[rs["ps_s"], rs["mtri"]], writes=[rs[lg]])
                em.do("dve", lambda h, qsz=qsz, wq_=wq_, lg=lg, sta=sta: h.reduce_max(
                    sb[sta][0:qsz, 0:1], sb[lg][0:qsz, 0:wq_], AX.X, negate=True),
                    reads=[rs[lg]], writes=[rs[sta]])
                em.do("act", lambda h, qsz=qsz, wq_=wq_, lg=lg, esc=esc, sta=sta: h.activation(
                    out=sb[esc][0:qsz, 0:wq_], in_=sb[lg][0:qsz, 0:wq_], func=AF.Exp,
                    bias=sb[sta][0:qsz, 0:1], accum_out=sb[sta][0:qsz, 1:2]),
                    reads=[rs[lg], rs[sta]], writes=[rs[esc], rs[sta]])
                em.do("dve", lambda h, qsz=qsz, sta=sta: h.reciprocal(sb[sta][0:qsz, 2:3], sb[sta][0:qsz, 1:2]),
                      reads=[rs[sta]], writes=[rs[sta]])
                em.do("dve", lambda h, qsz=qsz, dg=dg, sta=sta: h.tensor_scalar(
                    sb[dg][0:qsz, 0:qsz], sb["ident"][0:qsz, 0:qsz], sb[sta][0:qsz, 2:3], None, ALU.mult),
                    reads=[rs["ident"], rs[sta]], writes=[rs[dg]])
                pieces = []
                for cti, (c0, ck) in enumerate(L0_CT):
                    if c0 >= wq_:
                        break
                    pieces.append((cti, c0, min(ck, wq_ - c0)))
                for cti, c0, ck_ in pieces:
                    em.do("pe", lambda h, c0=c0, ck_=ck_, qsz=qsz, esc=esc, dg=dg, tb=tb: h.matmul(
                        tb[0:ck_, 0:qsz], sb[esc][0:qsz, c0:c0 + ck_], sb[dg][0:qsz, 0:qsz],
                        start=True, stop=True),
                        reads=[rs[esc], rs[dg]], writes=[tbr])
                    eng = "dve" if cti % 2 == 0 else "act"
                    fn = (lambda h, cti=cti, ck_=ck_, qsz=qsz, wT=wT, tb=tb: h.tensor_copy(
                        sb[wT][0:ck_, cti * 128:cti * 128 + qsz], tb[0:ck_, 0:qsz])) if eng == "dve" else \
                         (lambda h, cti=cti, ck_=ck_, qsz=qsz, wT=wT, tb=tb: h.copy(
                        out=sb[wT][0:ck_, cti * 128:cti * 128 + qsz], in_=tb[0:ck_, 0:qsz]))
                    em.do(eng, fn, reads=[tbr], writes=[rs[wT]])
                for j, (cti, c0, ck_) in enumerate(pieces):
                    em.do("pe", lambda h, cti=cti, ck_=ck_, qsz=qsz, hd=hd, wT=wT, hb=hb,
                          st_=(j == 0), sp_=(j == len(pieces) - 1): h.matmul(
                        hb[0:64, 0:qsz], sb["vt"][0:ck_, cti, hd * 64:(hd + 1) * 64],
                        sb[wT][0:ck_, cti * 128:cti * 128 + qsz], start=st_, stop=sp_),
                        reads=[rs["vt"], rs[wT]], writes=[hbr])
                em.do("act", lambda h, qsz=qsz, mo=mo, po=po, qoff=qoff, hb=hb: h.copy(
                    out=sb["ao"][po:po + 64, mo, qoff:qoff + qsz], in_=hb[0:64, 0:qsz]),
                    reads=[hbr], writes=[rs["ao"]])

        for qt, (qoff, qsz) in enumerate(L0_QT):
            c = qt % 2
            xr, xo = f"xr{c}", f"xo{c}"
            em.do("sp", lambda h, b=b, qoff=qoff, qsz=qsz, xr=xr: h.dma_start(
                out=sb[xr][0:qsz, :], in_=x_in[b, QR0 + qoff:QR0 + qoff + qsz, :]),
                writes=[rs[xr]], dma=True)
            for half in range(2):
                cb, cbr = ps[f"comb{half}"], rs[f"ps_comb{half}"]
                for ki in range(8):
                    em.do("pe", lambda h, qoff=qoff, qsz=qsz, half=half, ki=ki, cb=cb: h.matmul(
                        cb[0:qsz, 0:512], sb["ao"][:, ki, qoff:qoff + qsz],
                        sb["wo"][:, ki, half * 512:(half + 1) * 512], start=(ki == 0), stop=(ki == 7)),
                        reads=[rs["ao"], rs["wo"]], writes=[cbr])
                em.do("dve", lambda h, qsz=qsz, half=half, cb=cb, xr=xr, xo=xo: h.tensor_add(
                    sb[xo][0:qsz, half * 512:(half + 1) * 512], cb[0:qsz, 0:512],
                    sb[xr][0:qsz, half * 512:(half + 1) * 512]),
                    reads=[cbr, rs[xr]], writes=[rs[xo]])
            em.do("sp", lambda h, b=b, qoff=qoff, qsz=qsz, xo=xo: h.dma_start(
                out=xq_out[b, out_base + qoff:out_base + qoff + qsz, :], in_=sb[xo][0:qsz, :]),
                reads=[rs[xo]], writes=[rs["out_" + str(b % 2)]], dma=True)


def build_l0_attn2_nc(nb):
    from contextlib import ExitStack
    nc = bass.Bass("TRN2", target_bir_lowering=False)
    x_in = nc.dram_tensor("x_in", [nb, T, D], F32, kind="ExternalInput").ap()
    xq_out = nc.dram_tensor("xq_out", [nb, NQ, D], F32, kind="ExternalOutput").ap()
    w_aps = {w: nc.dram_tensor(w, [D, D], F32, kind="ExternalInput").ap()
             for w in ("wq", "wk", "wv", "wo")}
    h_ident, h_tri, h_konst = l0_attn_consts(nc)
    es = ExitStack()
    with es:
        sb, ps = alloc_l0_attn2(nc, es)
        trk = Trk(nc, es)
        em = Emit(trk)
        rs = {k: Rsrc(k) for k in L0_RS2}
        block = es.enter_context(nc.Block())
        emit_l0_attn2(nc, em, sb, ps, rs, x_in, xq_out, w_aps, nb, h_ident, h_tri, h_konst)
        trk.wait("sp", [rs["out_0"].w, rs["out_1"].w])
        trk.replay(block)
    return nc



L1F_RS2 = (["ident", "onesf", "konst", "rw8", "w1sb0", "w1sb1", "w2sb0", "w2sb1",
            "xlnT", "gate", "gatebf", "grow", "hsp",
            "neginf", "l2", "is1", "is2", "gcol",
            "ps_t", "ps_lg8", "ps_gr", "y_acc", "out_o"] +
           [f"{k}{p}" for p in range(2) for k in
            ("xt", "xbf", "xT", "st", "scr2ln", "xr", "acc", "hbf", "yev", "gb",
             "ps_a", "ps_b")])


def alloc_l1ffn2(nc, es):
    sb, ps = {}, {}
    sb["ident"] = es.enter_context(nc.sbuf_tensor("m_ident", [128, 128], BF16))
    sb["onesf"] = es.enter_context(nc.sbuf_tensor("m_onesf", [128, 128], BF16))
    sb["konst"] = es.enter_context(nc.sbuf_tensor("m_konst", [128, 4], F32))
    sb["rw8"] = es.enter_context(nc.sbuf_tensor("m_rw8", [128, 8, 8], BF16))
    for p in range(2):
        sb[f"w1sb{p}"] = es.enter_context(nc.sbuf_tensor(f"m_w1sb{p}", [128, 8, 2048], BF16))
    w2s = es.enter_context(nc.sbuf_tensor("m_w2sb", [128, 16, 1024], BF16))
    sb["w2sb0"] = w2s
    sb["w2sb1"] = w2s
    sb["xlnT"] = es.enter_context(nc.sbuf_tensor("m_xlnT", [128, 8, 4096], BF16))
    sb["gate"] = es.enter_context(nc.sbuf_tensor("m_gate", [128, 32, 8], F32))
    sb["gatebf"] = es.enter_context(nc.sbuf_tensor("m_gatebf", [128, 32, 8], BF16))
    sb["grow"] = es.enter_context(nc.sbuf_tensor("m_grow", [1, 512], BF16))
    sb["hsp"] = es.enter_context(nc.sbuf_tensor("m_hsp", [128, 16, 512], BF16))
    sb["neginf"] = es.enter_context(nc.sbuf_tensor("m_neginf", [128, 8], F32))
    sb["l2"] = es.enter_context(nc.sbuf_tensor("m_l2", [128, 8], F32))
    sb["is1"] = es.enter_context(nc.sbuf_tensor("m_is1", [128, 8], F32))
    sb["is2"] = es.enter_context(nc.sbuf_tensor("m_is2", [128, 8], F32))
    sb["gcol"] = es.enter_context(nc.sbuf_tensor("m_gcol", [128, 8], F32))
    for p in range(2):
        sb[f"xt{p}"] = es.enter_context(nc.sbuf_tensor(f"m_xt{p}", [128, 1024], F32))
        sb[f"xbf{p}"] = es.enter_context(nc.sbuf_tensor(f"m_xbf{p}", [128, 1024], BF16))
        sb[f"st{p}"] = es.enter_context(nc.sbuf_tensor(f"m_st{p}", [128, 16], F32))

        sb[f"hbf{p}"] = es.enter_context(nc.sbuf_tensor(f"m_hbf{p}", [128, 512], BF16))
        sb[f"gb{p}"] = es.enter_context(nc.sbuf_tensor(f"m_gb{p}", [128, 512], BF16))
        ps[f"a{p}"] = es.enter_context(nc.psum_tensor(f"m_ps_a{p}", [128, 512], F32))
        ps[f"b{p}"] = es.enter_context(nc.psum_tensor(f"m_ps_b{p}", [128, 512], F32))
    xr = es.enter_context(nc.sbuf_tensor("m_xr", [128, 1024], F32))
    sb["xr0"] = xr
    sb["xr1"] = xr
    sb["scr2ln0"] = xr
    sb["scr2ln1"] = xr
    acc = es.enter_context(nc.sbuf_tensor("m_acc", [128, 1024], F32))
    sb["acc0"] = acc
    sb["acc1"] = acc
    sb["yev0"] = acc
    sb["yev1"] = acc
    ps["t"] = es.enter_context(nc.psum_tensor("m_ps_t", [128, 128], BF16))
    ps["lg8"] = es.enter_context(nc.psum_tensor("m_ps_lg8", [128, 8], F32))
    ps["gr"] = es.enter_context(nc.psum_tensor("m_ps_gr", [1, 128], BF16))
    return sb, ps


def emit_ln128v(em, sb, rs, c, xkey="xt", okey="xbf"):
    xk, ok, st, sc = f"{xkey}{c}", f"{okey}{c}", f"st{c}", f"scr2ln{c}"
    em.do("act", lambda h: h.activation(out=sb[sc][:, :], in_=sb[xk][:, :], func=AF.Square,
                                        accum_out=sb[st][:, 0:1]),
          reads=[rs[xk]], writes=[rs[sc], rs[st]])
    em.do("dve", lambda h: h.reduce_sum(sb[st][:, 1:2], sb[xk][:, :], AX.X, negate=True),
          reads=[rs[xk]], writes=[rs[st]])
    em.do("dve", lambda h: h.tensor_scalar(sb[st][:, 2:3], sb[st][:, 1:2], 1.0 / D, None, ALU.mult),
          reads=[rs[st]], writes=[rs[st]])
    em.do("dve", lambda h: h.tensor_scalar(sb[st][:, 3:4], sb[st][:, 0:1], 1.0 / D, None, ALU.mult),
          reads=[rs[st]], writes=[rs[st]])
    em.do("dve", lambda h: h.scalar_tensor_tensor(out=sb[st][:, 4:5], in0=sb[st][:, 2:3],
                                                  scalar=sb[st][:, 2:3], in1=sb[st][:, 3:4],
                                                  op0=ALU.mult, op1=ALU.subtract),
          reads=[rs[st]], writes=[rs[st]])
    em.do("act", lambda h: h.activation(out=sb[st][:, 5:6], in_=sb[st][:, 4:5], func=AF.Sqrt,
                                        bias=sb["konst"][:, 0:1], scale=sb["konst"][:, 1:2]),
          reads=[rs[st], rs["konst"]], writes=[rs[st]])
    em.do("dve", lambda h: h.reciprocal(sb[st][:, 6:7], sb[st][:, 5:6]),
          reads=[rs[st]], writes=[rs[st]])
    em.do("dve", lambda h: h.tensor_scalar(sb[ok][:, :], sb[xk][:, :], sb[st][:, 2:3],
                                           sb[st][:, 6:7], ALU.add, ALU.mult),
          reads=[rs[xk], rs[st]], writes=[rs[ok]])


def emit_l1_ffn2(nc, em, sb, ps, rs, x_in, out_ap, rw8_ap, w1T_ap, w2T_ap, y_acc, hc):
    em.do("sp", lambda h: h.dma_start(out=sb["ident"][:], in_=hc["ident"].ap()), writes=[rs["ident"]], dma=True)
    em.do("sp", lambda h: h.dma_start(out=sb["onesf"][:], in_=hc["ones_full"].ap()), writes=[rs["onesf"]], dma=True)
    em.do("sp", lambda h: h.dma_start(out=sb["konst"][:], in_=hc["konst"].ap()), writes=[rs["konst"]], dma=True)
    em.do("pool", lambda h: h.dma_start(out=sb["rw8"][:, :, :], in_=rw8_ap.rearrange("(k p) n -> p k n", p=128)),
          writes=[rs["rw8"]], dma=True)
    em.do("dve", lambda h: h.memset(sb["neginf"][:, :], -1e30), writes=[rs["neginf"]])

    for j in range(NT_OWN):
        c = j % 2
        xbf = f"xbf{c}"
        em.do("sp", lambda h, j=j, c=c: h.dma_start(out=sb[f"xt{c}"][:, :], in_=x_in[j * 128:(j + 1) * 128, :]),
              writes=[rs[f"xt{c}"]], dma=True)
        emit_ln128v(em, sb, rs, c)
        for ki in range(8):
            pc = ki % 2
            tb, tbr = ps[f"b{pc}"], rs[f"ps_b{pc}"]
            em.do("pe", lambda h, ki=ki, xbf=xbf, tb=tb: h.matmul(
                tb[0:128, 0:128], sb[xbf][:, ki * 128:(ki + 1) * 128], sb["ident"][:, :],
                start=True, stop=True),
                reads=[rs[xbf], rs["ident"]], writes=[tbr])
            eng = "dve" if ki % 2 == 0 else "act"
            fn = (lambda h, ki=ki, j=j, tb=tb: h.tensor_copy(
                sb["xlnT"][:, ki, j * 128:(j + 1) * 128], tb[0:128, 0:128])) \
                if eng == "dve" else \
                 (lambda h, ki=ki, j=j, tb=tb: h.copy(
                out=sb["xlnT"][:, ki, j * 128:(j + 1) * 128], in_=tb[0:128, 0:128]))
            em.do(eng, fn, reads=[tbr], writes=[rs["xlnT"]])
            em.do("pe", lambda h, ki=ki, j=j: h.matmul(
                ps["lg8"][:, :], sb["xlnT"][:, ki, j * 128:(j + 1) * 128], sb["rw8"][:, ki, :],
                start=(ki == 0), stop=(ki == 7), skip_group_check=True),
                reads=[rs["xlnT"], rs["rw8"]], writes=[rs["ps_lg8"]])
        st = f"st{c}"
        em.do("dve", lambda h: h.tensor_copy(sb["l2"][:, :], ps["lg8"][:, :]), reads=[rs["ps_lg8"]], writes=[rs["l2"]])
        em.do("dve", lambda h, st=st: h.reduce_max(sb[st][:, 8:9], sb["l2"][:, :], AX.X),
              reads=[rs["l2"]], writes=[rs[st]])
        em.do("dve", lambda h, st=st: h.tensor_scalar(sb["is1"][:, :], sb["l2"][:, :], sb[st][:, 8:9], None, ALU.is_equal),
              reads=[rs["l2"], rs[st]], writes=[rs["is1"]])
        em.do("dve", lambda h: h.scalar_tensor_tensor(out=sb["l2"][:, :], in0=sb["is1"][:, :],
                                                      scalar=-1e30, in1=sb["l2"][:, :],
                                                      op0=ALU.mult, op1=ALU.add),
              reads=[rs["is1"]], writes=[rs["l2"]])
        em.do("dve", lambda h, st=st: h.reduce_max(sb[st][:, 9:10], sb["l2"][:, :], AX.X),
              reads=[rs["l2"]], writes=[rs[st]])
        em.do("dve", lambda h, st=st: h.tensor_scalar(sb["is2"][:, :], sb["l2"][:, :], sb[st][:, 9:10], None, ALU.is_equal),
              reads=[rs["l2"], rs[st]], writes=[rs["is2"]])
        em.do("dve", lambda h, st=st: h.tensor_sub(sb[st][:, 10:11], sb[st][:, 9:10], sb[st][:, 8:9]),
              reads=[rs[st]], writes=[rs[st]])
        em.do("act", lambda h, st=st: h.activation(out=sb[st][:, 11:12], in_=sb[st][:, 10:11], func=AF.Exp),
              reads=[rs[st]], writes=[rs[st]])
        em.do("dve", lambda h, st=st: h.tensor_scalar(sb[st][:, 12:13], sb[st][:, 11:12], 1.0, None, ALU.add),
              reads=[rs[st]], writes=[rs[st]])
        em.do("dve", lambda h, st=st: h.reciprocal(sb[st][:, 13:14], sb[st][:, 12:13]),
              reads=[rs[st]], writes=[rs[st]])
        em.do("dve", lambda h, st=st: h.tensor_mul(sb[st][:, 14:15], sb[st][:, 11:12], sb[st][:, 13:14]),
              reads=[rs[st]], writes=[rs[st]])
        em.do("dve", lambda h, st=st: h.tensor_scalar(sb["gcol"][:, :], sb["is1"][:, :], sb[st][:, 13:14], None, ALU.mult),
              reads=[rs["is1"], rs[st]], writes=[rs["gcol"]])
        em.do("dve", lambda h, j=j, st=st: h.scalar_tensor_tensor(out=sb["gate"][:, j, :], in0=sb["is2"][:, :],
                                                                  scalar=sb[st][:, 14:15], in1=sb["gcol"][:, :],
                                                                  op0=ALU.mult, op1=ALU.add),
              reads=[rs["is2"], rs[st], rs["gcol"]], writes=[rs["gate"]])
        em.do("dve", lambda h, j=j: h.tensor_copy(sb["gatebf"][:, j, :], sb["gate"][:, j, :]),
              reads=[rs["gate"]], writes=[rs["gatebf"]])

    for e in range(8):
        we = e % 2
        w1sb, w2sb = f"w1sb{we}", f"w2sb{we}"
        em.do("pool", lambda h, e=e, w1sb=w1sb: h.dma_start(out=sb[w1sb][:, :, :],
                                                            in_=w1T_ap[e].rearrange("(k p) n -> p k n", p=128)),
              writes=[rs[w1sb]], dma=True)
        em.do("pool", lambda h, e=e, w2sb=w2sb: h.dma_start(out=sb[w2sb][:, :, :],
                                                            in_=w2T_ap[e].rearrange("(k p) n -> p k n", p=128)),
              writes=[rs[w2sb]], dma=True)
        for sp_ in range(8):
            t0 = sp_ * 512
            gc = sp_ % 2
            gb = f"gb{gc}"
            for q in range(4):
                em.do("pe", lambda h, e=e, sp_=sp_, q=q: h.transpose(
                    ps["gr"][0:1, :], sb["gatebf"][:, sp_ * 4 + q, e:e + 1], sb["ident"][:, :]),
                    reads=[rs["gatebf"], rs["ident"]], writes=[rs["ps_gr"]])
                em.do("act", lambda h, q=q: h.copy(out=sb["grow"][0:1, q * 128:(q + 1) * 128], in_=ps["gr"][0:1, :]),
                      reads=[rs["ps_gr"]], writes=[rs["grow"]])
            em.do("pe", lambda h, gc=gc: h.matmul(ps[f"b{gc}"][:, :], sb["onesf"][0:1, :], sb["grow"][0:1, :],
                                                  start=True, stop=True),
                  reads=[rs["onesf"], rs["grow"]], writes=[rs[f"ps_b{gc}"]])
            em.do("act", lambda h, gb=gb, gc=gc: h.copy(out=sb[gb][:, :], in_=ps[f"b{gc}"][:, :]),
                  reads=[rs[f"ps_b{gc}"]], writes=[rs[gb]])
            for f in range(16):
                fc = f % 2
                pa, par = ps[f"a{fc}"], rs[f"ps_a{fc}"]
                hbf = f"hbf{fc}"
                for ki in range(8):
                    em.do("pe", lambda h, f=f, ki=ki, t0=t0, pa=pa, w1sb=w1sb: h.matmul(
                        pa[:, :], sb[w1sb][:, ki, f * 128:(f + 1) * 128],
                        sb["xlnT"][:, ki, t0:t0 + 512], start=(ki == 0), stop=(ki == 7)),
                        reads=[rs[w1sb], rs["xlnT"]], writes=[par])
                em.do("act", lambda h, pa=pa, hbf=hbf: h.activation(out=sb[hbf][:, :], in_=pa[:, :], func=AF.Gelu),
                      reads=[par], writes=[rs[hbf]])
                em.do("dve", lambda h, f=f, hbf=hbf, gb=gb: h.tensor_mul(sb["hsp"][:, f, :], sb[hbf][:, :], sb[gb][:, :]),
                      reads=[rs[hbf], rs[gb]], writes=[rs["hsp"]])
            for tq in range(4):
                yc = tq % 2
                yev = f"yev{yc}"
                for half in range(2):
                    pb, pbr = ps[f"b{half}"], rs[f"ps_b{half}"]
                    for f in range(16):
                        em.do("pe", lambda h, f=f, tq=tq, half=half, pb=pb, w2sb=w2sb: h.matmul(
                            pb[:, :], sb["hsp"][:, f, tq * 128:(tq + 1) * 128],
                            sb[w2sb][:, f, half * 512:(half + 1) * 512], start=(f == 0), stop=(f == 15)),
                            reads=[rs["hsp"], rs[w2sb]], writes=[pbr])
                    em.do("dve", lambda h, half=half, pb=pb, yev=yev: h.tensor_copy(
                        sb[yev][:, half * 512:(half + 1) * 512], pb[:, :]),
                        reads=[pbr], writes=[rs[yev]])
                tg = t0 + tq * 128
                if e == 0:
                    em.do("pool", lambda h, tg=tg, yev=yev: h.dma_start(out=y_acc[tg:tg + 128, :], in_=sb[yev][:, :]),
                          reads=[rs[yev]], writes=[rs["y_acc"]], dma=True)
                else:
                    em.do("pool", lambda h, tg=tg, yev=yev: h.dma_start(out=y_acc[tg:tg + 128, :], in_=sb[yev][:, :],
                                                                        accum_op=ALU.add),
                          reads=[rs[yev]], writes=[rs["y_acc"]], dma=True)

    for j in range(NT_OWN):
        c = j % 2
        xr, xt, acc = f"xr{c}", f"xt{c}", f"acc{c}"
        em.do("sp", lambda h, j=j, xr=xr: h.dma_start(out=sb[xr][:, :], in_=x_in[j * 128:(j + 1) * 128, :]),
              writes=[rs[xr]], dma=True)
        em.do("sp", lambda h, j=j, xt=xt: h.dma_start(out=sb[xt][:, :], in_=y_acc[j * 128:(j + 1) * 128, :]),
              reads=[rs["y_acc"]], writes=[rs[xt]], dma=True)
        em.do("dve", lambda h, xt=xt, xr=xr, acc=acc: h.tensor_add(sb[acc][:, :], sb[xt][:, :], sb[xr][:, :]),
              reads=[rs[xt], rs[xr]], writes=[rs[acc]])
        em.do("sp", lambda h, j=j, acc=acc: h.dma_start(out=out_ap[j * 128:(j + 1) * 128, :], in_=sb[acc][:, :]),
              reads=[rs[acc]], writes=[rs["out_o"]], dma=True)


def build_fullfused2_nc():
    from contextlib import ExitStack
    nc = bass.Bass("TRN2", target_bir_lowering=False)
    x_in = nc.dram_tensor("x_in", [NB, T, D], F32, kind="ExternalInput").ap()
    ownrows = nc.dram_tensor("ownrows", [NT_OWN, 128, 1], U32, kind="ExternalInput").ap()
    rankoh = nc.dram_tensor("rankoh", [1, 512], F32, kind="ExternalInput").ap()
    wq_a = {w: nc.dram_tensor(w, [D, D], F32, kind="ExternalInput").ap()
            for w in ("wq", "wk", "wv", "wo")}
    rwT = nc.dram_tensor("rwT", [D, E64], F32, kind="ExternalInput").ap()
    wsgu = nc.dram_tensor("wsgu", [D, 1536], F32, kind="ExternalInput").ap()
    wsd = nc.dram_tensor("wsd", [768, D], F32, kind="ExternalInput").ap()
    wg = nc.dram_tensor("wg", [E64, D, 768], F32, kind="ExternalInput").ap()
    wu = nc.dram_tensor("wu", [E64, D, 768], F32, kind="ExternalInput").ap()
    wd = nc.dram_tensor("wd", [E64, 768, D], F32, kind="ExternalInput").ap()
    wqkv = nc.dram_tensor("wqkv", [D, 3 * D], F32, kind="ExternalInput").ap()
    bqkv = nc.dram_tensor("bqkv", [3 * D], F32, kind="ExternalInput").ap()
    wout = nc.dram_tensor("wout", [D, D], F32, kind="ExternalInput").ap()
    bout = nc.dram_tensor("bout", [D], F32, kind="ExternalInput").ap()
    rw8 = nc.dram_tensor("rw8", [D, 8], F32, kind="ExternalInput").ap()
    w1T = nc.dram_tensor("w1T", [8, D, 2048], F32, kind="ExternalInput").ap()
    w2T = nc.dram_tensor("w2T", [8, 2048, D], F32, kind="ExternalInput").ap()
    out_ap = nc.dram_tensor("out", [4096, D], F32, kind="ExternalOutput").ap()

    x1_d = nc.dram_tensor("x1_d", [NB * T, D], F32, kind="Internal")
    xb_d = nc.dram_tensor("xb_d", [E64 * CAP, D], BF16, kind="Internal").ap()
    w_d = nc.dram_tensor("w_d", [E64 * CAP, 2], F32, kind="Internal").ap()
    y_d = nc.dram_tensor("y_d", [E64 * CAP, D], BF16, kind="Internal").ap()
    s8u_d = nc.dram_tensor("s8u_d", [NT_OWN, 128, 8], U32, kind="Internal").ap()
    rbase_d = nc.dram_tensor("rbase_d", [4096, D], F32, kind="Internal").ap()
    x2_d = nc.dram_tensor("x2_d", [4096, D], F32, kind="Internal")
    x3_d = nc.dram_tensor("x3_d", [4096, D], F32, kind="Internal")
    y_acc = nc.dram_tensor("y_acc", [4096, D], F32, kind="Internal").ap()

    h_ident, h_tri, h_konst = l0_attn_consts(nc)
    hc = l0ffn_consts(nc)
    es = ExitStack()
    with es:
        trk = Trk(nc, es)
        em = Emit(trk)
        with ExitStack() as es0:
            sb, ps = alloc_l0_attn2(nc, es0)
            rs = {k: Rsrc(k) for k in L0_RS2}
            x1v = x1_d.ap().rearrange("(b t) d -> b t d", b=NB)
            for b in range(NB):
                em.do("pool", lambda h, b=b: h.dma_start(out=x1v[b, 0:QR0, :], in_=x_in[b, 0:QR0, :]),
                      writes=[rs["out_0"]], dma=True)
            emit_l0_attn2(nc, em, sb, ps, rs, x_in, x1v, wq_a, NB, h_ident, h_tri, h_konst,
                          out_base=QR0)
            trk_barrier(trk)
        with ExitStack() as es1:
            sb, ps = alloc_l0ffn(nc, es1)
            rs = {k: Rsrc(k) for k in L0FFN_RS}
            emit_l0ffn_R(nc, em, sb, ps, rs, x1_d.ap(), rankoh, rwT, hc)
            emit_l0ffn_P(nc, em, sb, ps, rs, None, rwT, xb_d, w_d, s8u_d, hc,
                         ownrows=ownrows, x1_dram=x1_d.ap())
            emit_l0ffn_S(nc, em, sb, ps, rs, None, wsgu, wsd, rbase_d,
                         ownrows=ownrows, x1_dram=x1_d.ap())
            emit_l0ffn_E(nc, em, sb, ps, rs, wg, wu, wd, xb_d, w_d, y_d)
            emit_l0ffn_G(nc, em, sb, ps, rs, rbase_d, s8u_d, y_d, x2_d.ap())
            trk_barrier(trk)
        with ExitStack() as es2:
            sb, ps = alloc_l1attn(nc, es2)
            rs = {k: Rsrc(k) for k in L1A_RS}
            x2v = x2_d.ap().rearrange("(b t) d -> b t d", b=2)
            x3v = x3_d.ap().rearrange("(b t) d -> b t d", b=2)
            emit_l1_attn(nc, em, sb, ps, rs, x2v, x3v, wqkv, bqkv, wout, bout, 2, hc)
            trk_barrier(trk)
        with ExitStack() as es3:
            sb, ps = alloc_l1ffn2(nc, es3)
            rs = {k: Rsrc(k) for k in L1F_RS2}
            rs["w2sb1"] = rs["w2sb0"]
            rs["scr2ln1"] = rs["scr2ln0"]
            rs["scr2ln0"] = rs["xr0"]
            rs["scr2ln1"] = rs["xr0"]
            rs["xr1"] = rs["xr0"]
            rs["acc1"] = rs["acc0"]
            rs["yev0"] = rs["acc0"]
            rs["yev1"] = rs["acc0"]
            emit_l1_ffn2(nc, em, sb, ps, rs, x3_d.ap(), out_ap, rw8, w1T, w2T, y_acc, hc)
            trk_barrier(trk)
        block = es.enter_context(nc.Block())
        trk.replay(block)
    return nc


def build_l1ffn2_nc():
    from contextlib import ExitStack
    nc = bass.Bass("TRN2", target_bir_lowering=False)
    x_in = nc.dram_tensor("x_in", [4096, D], F32, kind="ExternalInput").ap()
    out_ap = nc.dram_tensor("out", [4096, D], F32, kind="ExternalOutput").ap()
    rw8 = nc.dram_tensor("rw8", [D, 8], F32, kind="ExternalInput").ap()
    w1T = nc.dram_tensor("w1T", [8, D, 2048], F32, kind="ExternalInput").ap()
    w2T = nc.dram_tensor("w2T", [8, 2048, D], F32, kind="ExternalInput").ap()
    y_acc = nc.dram_tensor("y_acc", [4096, D], F32, kind="Internal").ap()
    hc = l0ffn_consts(nc)
    es = ExitStack()
    with es:
        sb, ps = alloc_l1ffn2(nc, es)
        trk = Trk(nc, es)
        em = Emit(trk)
        rs = {k: Rsrc(k) for k in L1F_RS2}
        rs["w2sb1"] = rs["w2sb0"]
        rs["scr2ln0"] = rs["xr0"]
        rs["scr2ln1"] = rs["xr0"]
        rs["xr1"] = rs["xr0"]
        rs["acc1"] = rs["acc0"]
        rs["yev0"] = rs["acc0"]
        rs["yev1"] = rs["acc0"]
        block = es.enter_context(nc.Block())
        emit_l1_ffn2(nc, em, sb, ps, rs, x_in, out_ap, rw8, w1T, w2T, y_acc, hc)
        trk.wait("sp", [rs["out_o"].w])
        trk.replay(block)
    return nc



L1A_RS2 = (["ident", "onesf", "konst", "wqkv", "wout", "bq_row", "bo_row",
            "qbf", "kvrow", "kvb", "rsq", "ps_kv1", "ps_kv2", "x3_o"] +
           [f"{k}{p}" for p in range(2) for k in
            ("xt", "xbf", "xT", "st", "scr2ln", "kbf", "vbf", "prod", "obf", "xr", "acc",
             "ps_a", "ps_b", "ps_t")])


def alloc_l1attn2(nc, es):
    sb, ps = {}, {}
    sb["ident"] = es.enter_context(nc.sbuf_tensor("a_ident", [128, 128], BF16))
    sb["onesf"] = es.enter_context(nc.sbuf_tensor("a_onesf", [128, 128], BF16))
    sb["konst"] = es.enter_context(nc.sbuf_tensor("a_konst", [128, 4], F32))
    sb["wqkv"] = es.enter_context(nc.sbuf_tensor("a_wqkv", [128, 8, 3072], BF16))
    sb["wout"] = es.enter_context(nc.sbuf_tensor("a_wout", [128, 8, 1024], BF16))
    sb["bq_row"] = es.enter_context(nc.sbuf_tensor("a_bqrow", [1, 3072], BF16))
    sb["bo_row"] = es.enter_context(nc.sbuf_tensor("a_borow", [1, 1024], BF16))
    sb["bq_row32"] = es.enter_context(nc.sbuf_tensor("a_bqrow32", [1, 3072], F32))
    sb["bo_row32"] = es.enter_context(nc.sbuf_tensor("a_borow32", [1, 1024], F32))
    sb["qbf"] = es.enter_context(nc.sbuf_tensor("a_qbf", [128, 16, 1024], BF16))
    sb["kvrow"] = es.enter_context(nc.sbuf_tensor("a_kvrow", [1, 1024], BF16))
    sb["kvb"] = es.enter_context(nc.sbuf_tensor("a_kvb", [128, 1024], F32))
    sb["rsq"] = es.enter_context(nc.sbuf_tensor("a_rsq", [128, 16], F32))
    for p in range(2):
        sb[f"xt{p}"] = es.enter_context(nc.sbuf_tensor(f"a_xt{p}", [128, 1024], F32))
        sb[f"xbf{p}"] = es.enter_context(nc.sbuf_tensor(f"a_xbf{p}", [128, 1024], BF16))
        sb[f"xT{p}"] = es.enter_context(nc.sbuf_tensor(f"a_xT{p}", [128, 8, 128], BF16))
        sb[f"st{p}"] = es.enter_context(nc.sbuf_tensor(f"a_st{p}", [128, 16], F32))
        sb[f"scr2ln{p}"] = es.enter_context(nc.sbuf_tensor(f"a_scr2ln{p}", [128, 1024], F32))
        sb[f"kbf{p}"] = es.enter_context(nc.sbuf_tensor(f"a_kbf{p}", [128, 1024], BF16))
        sb[f"vbf{p}"] = es.enter_context(nc.sbuf_tensor(f"a_vbf{p}", [128, 1024], BF16))
        sb[f"prod{p}"] = es.enter_context(nc.sbuf_tensor(f"a_prod{p}", [128, 1024], BF16))
        sb[f"obf{p}"] = es.enter_context(nc.sbuf_tensor(f"a_obf{p}", [128, 1024], BF16))
        sb[f"xr{p}"] = es.enter_context(nc.sbuf_tensor(f"a_xr{p}", [128, 1024], F32))
        sb[f"acc{p}"] = es.enter_context(nc.sbuf_tensor(f"a_acc{p}", [128, 1024], F32))
        ps[f"a{p}"] = es.enter_context(nc.psum_tensor(f"a_ps_a{p}", [128, 512], F32))
        ps[f"b{p}"] = es.enter_context(nc.psum_tensor(f"a_ps_b{p}", [128, 512], F32))
        ps[f"t{p}"] = es.enter_context(nc.psum_tensor(f"a_ps_t{p}", [128, 128], F32))
    ps["kv1"] = es.enter_context(nc.psum_tensor("a_ps_kv1", [1, 512], F32))
    ps["kv2"] = es.enter_context(nc.psum_tensor("a_ps_kv2", [1, 512], F32))
    return sb, ps


def emit_l1_attn2(nc, em, sb, ps, rs, x_in, x3_out, wqkv_ap, bqkv_ap, wout_ap, bout_ap, nb, hc):
    em.do("sp", lambda h: h.dma_start(out=sb["ident"][:], in_=hc["ident"].ap()), writes=[rs["ident"]], dma=True)
    em.do("sp", lambda h: h.dma_start(out=sb["onesf"][:], in_=hc["ones_full"].ap()), writes=[rs["onesf"]], dma=True)
    em.do("sp", lambda h: h.dma_start(out=sb["konst"][:], in_=hc["konst"].ap()), writes=[rs["konst"]], dma=True)
    em.do("pool", lambda h: h.dma_start(out=sb["wqkv"][:, :, :], in_=wqkv_ap.rearrange("(k p) n -> p k n", p=128)),
          writes=[rs["wqkv"]], dma=True)
    em.do("pool", lambda h: h.dma_start(out=sb["wout"][:, :, :], in_=wout_ap.rearrange("(k p) n -> p k n", p=128)),
          writes=[rs["wout"]], dma=True)
    em.do("sp", lambda h: h.dma_start(out=sb["bq_row32"][0:1, :], in_=bqkv_ap[None, :]), writes=[rs["bq_row"]], dma=True)
    em.do("sp", lambda h: h.dma_start(out=sb["bo_row32"][0:1, :], in_=bout_ap[None, :]), writes=[rs["bo_row"]], dma=True)
    em.do("dve", lambda h: h.tensor_copy(sb["bq_row"][0:1, :], sb["bq_row32"][0:1, :]),
          reads=[rs["bq_row"]], writes=[rs["bq_row"]])
    em.do("dve", lambda h: h.tensor_copy(sb["bo_row"][0:1, :], sb["bo_row32"][0:1, :]),
          reads=[rs["bo_row"]], writes=[rs["bo_row"]])

    NTT = 16
    for b in range(nb):
        for tt in range(NTT):
            c = tt % 2
            xt, xbf, xT, st = f"xt{c}", f"xbf{c}", f"xT{c}", f"st{c}"
            kbf, vbf, prod = f"kbf{c}", f"vbf{c}", f"prod{c}"
            em.do("sp", lambda h, b=b, tt=tt, xt=xt: h.dma_start(out=sb[xt][:, :], in_=x_in[b, tt * 128:(tt + 1) * 128, :]),
                  writes=[rs[xt]], dma=True)
            em.do("act", lambda h, xt=xt, st=st, c=c: h.activation(
                out=sb[f"scr2ln{c}"][:, :], in_=sb[xt][:, :], func=AF.Square, accum_out=sb[st][:, 0:1]),
                reads=[rs[xt]], writes=[rs[f"scr2ln{c}"], rs[st]])
            em.do("dve", lambda h, xt=xt, st=st: h.reduce_sum(sb[st][:, 1:2], sb[xt][:, :], AX.X, negate=True),
                  reads=[rs[xt]], writes=[rs[st]])
            em.do("dve", lambda h, st=st: h.tensor_scalar(sb[st][:, 2:3], sb[st][:, 1:2], 1.0 / D, None, ALU.mult),
                  reads=[rs[st]], writes=[rs[st]])
            em.do("dve", lambda h, st=st: h.tensor_scalar(sb[st][:, 3:4], sb[st][:, 0:1], 1.0 / D, None, ALU.mult),
                  reads=[rs[st]], writes=[rs[st]])
            em.do("dve", lambda h, st=st: h.scalar_tensor_tensor(
                out=sb[st][:, 4:5], in0=sb[st][:, 2:3], scalar=sb[st][:, 2:3],
                in1=sb[st][:, 3:4], op0=ALU.mult, op1=ALU.subtract),
                reads=[rs[st]], writes=[rs[st]])
            em.do("act", lambda h, st=st: h.activation(out=sb[st][:, 5:6], in_=sb[st][:, 4:5], func=AF.Sqrt,
                                                       bias=sb["konst"][:, 0:1], scale=sb["konst"][:, 1:2]),
                  reads=[rs[st], rs["konst"]], writes=[rs[st]])
            em.do("dve", lambda h, st=st: h.reciprocal(sb[st][:, 6:7], sb[st][:, 5:6]),
                  reads=[rs[st]], writes=[rs[st]])
            em.do("dve", lambda h, xt=xt, xbf=xbf, st=st: h.tensor_scalar(
                sb[xbf][:, :], sb[xt][:, :], sb[st][:, 2:3], sb[st][:, 6:7], ALU.add, ALU.mult),
                reads=[rs[xt], rs[st]], writes=[rs[xbf]])
            for ki in range(8):
                tc = ki % 2
                tb, tbr = ps[f"t{tc}"], rs[f"ps_t{tc}"]
                em.do("pe", lambda h, ki=ki, xbf=xbf, tb=tb: h.matmul(
                    tb[0:128, 0:128], sb[xbf][:, ki * 128:(ki + 1) * 128], sb["ident"][:, :],
                    start=True, stop=True),
                    reads=[rs[xbf], rs["ident"]], writes=[tbr])
                eng = "dve" if ki % 2 == 0 else "act"
                fn = (lambda h, ki=ki, xT=xT, tb=tb: h.tensor_copy(sb[xT][:, ki, :], tb[0:128, 0:128])) \
                    if eng == "dve" else \
                     (lambda h, ki=ki, xT=xT, tb=tb: h.copy(out=sb[xT][:, ki, :], in_=tb[0:128, 0:128]))
                em.do(eng, fn, reads=[tbr], writes=[rs[xT]])
            for s in range(3):
                for half in range(2):
                    pn = f"{'a' if half == 0 else 'b'}{c}"
                    pst, psn = ps[pn], rs[f"ps_{pn}"]
                    c0 = s * 1024 + half * 512
                    for ki in range(8):
                        em.do("pe", lambda h, ki=ki, c0=c0, pst=pst, xT=xT: h.matmul(
                            pst[:, :], sb[xT][:, ki, :], sb["wqkv"][:, ki, c0:c0 + 512],
                            start=(ki == 0), stop=False),
                            reads=[rs[xT], rs["wqkv"]], writes=[psn])
                    em.do("pe", lambda h, c0=c0, pst=pst: h.matmul(
                        pst[:, :], sb["onesf"][0:1, :], sb["bq_row"][0:1, c0:c0 + 512],
                        start=False, stop=True),
                        reads=[rs["onesf"], rs["bq_row"]], writes=[psn])
                    if s == 0:
                        em.do("act", lambda h, half=half, pst=pst, st=st, c=c: h.activation(
                            out=sb[f"scr2ln{c}"][:, half * 512:(half + 1) * 512], in_=pst[:, :], func=AF.Square,
                            accum_out=sb[st][:, 8 + half:9 + half]),
                            reads=[psn], writes=[rs[f"scr2ln{c}"], rs[st]])
                        em.do("act", lambda h, half=half, tt=tt, pst=pst: h.copy(
                            out=sb["qbf"][:, tt, half * 512:(half + 1) * 512], in_=pst[:, :]),
                            reads=[psn], writes=[rs["qbf"]])
                    elif s == 1:
                        em.do("act", lambda h, half=half, pst=pst, st=st, c=c: h.activation(
                            out=sb[f"scr2ln{c}"][:, half * 512:(half + 1) * 512], in_=pst[:, :], func=AF.Square,
                            accum_out=sb[st][:, 10 + half:11 + half]),
                            reads=[psn], writes=[rs[f"scr2ln{c}"], rs[st]])
                        em.do("act", lambda h, half=half, pst=pst, kbf=kbf: h.copy(
                            out=sb[kbf][:, half * 512:(half + 1) * 512], in_=pst[:, :]),
                            reads=[psn], writes=[rs[kbf]])
                    else:
                        em.do("act", lambda h, half=half, pst=pst, vbf=vbf: h.copy(
                            out=sb[vbf][:, half * 512:(half + 1) * 512], in_=pst[:, :]),
                            reads=[psn], writes=[rs[vbf]])
            em.do("dve", lambda h, st=st: h.tensor_add(sb[st][:, 12:13], sb[st][:, 8:9], sb[st][:, 9:10]),
                  reads=[rs[st]], writes=[rs[st]])
            em.do("dve", lambda h, st=st: h.tensor_add(sb[st][:, 13:14], sb[st][:, 10:11], sb[st][:, 11:12]),
                  reads=[rs[st]], writes=[rs[st]])
            em.do("act", lambda h, st=st: h.activation(out=sb[st][:, 14:15], in_=sb[st][:, 12:13], func=AF.Sqrt),
                  reads=[rs[st]], writes=[rs[st]])
            em.do("act", lambda h, st=st: h.activation(out=sb[st][:, 15:16], in_=sb[st][:, 13:14], func=AF.Sqrt),
                  reads=[rs[st]], writes=[rs[st]])
            em.do("dve", lambda h, tt=tt, st=st: h.reciprocal(sb["rsq"][:, tt:tt + 1], sb[st][:, 14:15]),
                  reads=[rs[st]], writes=[rs["rsq"]])
            em.do("dve", lambda h, st=st: h.reciprocal(sb[st][:, 15:16], sb[st][:, 15:16]),
                  reads=[rs[st]], writes=[rs[st]])
            em.do("dve", lambda h, kbf=kbf, vbf=vbf, prod=prod, st=st: h.scalar_tensor_tensor(
                out=sb[prod][:, :], in0=sb[kbf][:, :], scalar=sb[st][:, 15:16], in1=sb[vbf][:, :],
                op0=ALU.mult, op1=ALU.mult),
                reads=[rs[kbf], rs[st], rs[vbf]], writes=[rs[prod]])
            em.do("pe", lambda h, b=b, tt=tt, prod=prod: h.matmul(
                ps["kv1"][0:1, :], sb["onesf"][:, 0:1], sb[prod][:, 0:512],
                start=(tt == 0), stop=(tt == NTT - 1), skip_group_check=True),
                reads=[rs["onesf"], rs[prod]], writes=[rs["ps_kv1"]])
            em.do("pe", lambda h, b=b, tt=tt, prod=prod: h.matmul(
                ps["kv2"][0:1, :], sb["onesf"][:, 0:1], sb[prod][:, 512:1024],
                start=(tt == 0), stop=(tt == NTT - 1), skip_group_check=True),
                reads=[rs["onesf"], rs[prod]], writes=[rs["ps_kv2"]])
        em.do("act", lambda h: h.copy(out=sb["kvrow"][0:1, 0:512], in_=ps["kv1"][0:1, :]),
              reads=[rs["ps_kv1"]], writes=[rs["kvrow"]])
        em.do("act", lambda h: h.copy(out=sb["kvrow"][0:1, 512:1024], in_=ps["kv2"][0:1, :]),
              reads=[rs["ps_kv2"]], writes=[rs["kvrow"]])
        for half in range(2):
            pn = f"{'a' if half == 0 else 'b'}0"
            pst, psn = ps[pn], rs[f"ps_{pn}"]
            em.do("pe", lambda h, half=half, pst=pst: h.matmul(
                pst[:, :], sb["onesf"][0:1, :], sb["kvrow"][0:1, half * 512:(half + 1) * 512],
                start=True, stop=True),
                reads=[rs["onesf"], rs["kvrow"]], writes=[psn])
            em.do("act", lambda h, half=half, pst=pst: h.copy(
                out=sb["kvb"][:, half * 512:(half + 1) * 512], in_=pst[:, :]),
                reads=[psn], writes=[rs["kvb"]])
        for tt in range(NTT):
            c = tt % 2
            obf, xT, xr, acc = f"obf{c}", f"xT{c}", f"xr{c}", f"acc{c}"
            em.do("dve", lambda h, tt=tt, obf=obf: h.scalar_tensor_tensor(
                out=sb[obf][:, :], in0=sb["qbf"][:, tt, :], scalar=sb["rsq"][:, tt:tt + 1],
                in1=sb["kvb"][:, :], op0=ALU.mult, op1=ALU.mult),
                reads=[rs["qbf"], rs["rsq"], rs["kvb"]], writes=[rs[obf]])
            for ki in range(8):
                tc = ki % 2
                tb, tbr = ps[f"t{tc}"], rs[f"ps_t{tc}"]
                em.do("pe", lambda h, ki=ki, obf=obf, tb=tb: h.matmul(
                    tb[0:128, 0:128], sb[obf][:, ki * 128:(ki + 1) * 128], sb["ident"][:, :],
                    start=True, stop=True),
                    reads=[rs[obf], rs["ident"]], writes=[tbr])
                eng = "dve" if ki % 2 == 0 else "act"
                fn = (lambda h, ki=ki, xT=xT, tb=tb: h.tensor_copy(sb[xT][:, ki, :], tb[0:128, 0:128])) \
                    if eng == "dve" else \
                     (lambda h, ki=ki, xT=xT, tb=tb: h.copy(out=sb[xT][:, ki, :], in_=tb[0:128, 0:128]))
                em.do(eng, fn, reads=[tbr], writes=[rs[xT]])
            em.do("sp", lambda h, b=b, tt=tt, xr=xr: h.dma_start(out=sb[xr][:, :], in_=x_in[b, tt * 128:(tt + 1) * 128, :]),
                  writes=[rs[xr]], dma=True)
            for half in range(2):
                pn = f"{'a' if half == 0 else 'b'}{c}"
                pst, psn = ps[pn], rs[f"ps_{pn}"]
                for ki in range(8):
                    em.do("pe", lambda h, ki=ki, half=half, pst=pst, xT=xT: h.matmul(
                        pst[:, :], sb[xT][:, ki, :], sb["wout"][:, ki, half * 512:(half + 1) * 512],
                        start=(ki == 0), stop=False),
                        reads=[rs[xT], rs["wout"]], writes=[psn])
                em.do("pe", lambda h, half=half, pst=pst: h.matmul(
                    pst[:, :], sb["onesf"][0:1, :], sb["bo_row"][0:1, half * 512:(half + 1) * 512],
                    start=False, stop=True),
                    reads=[rs["onesf"], rs["bo_row"]], writes=[psn])
                em.do("dve", lambda h, half=half, pst=pst, xr=xr, acc=acc: h.tensor_add(
                    sb[acc][:, half * 512:(half + 1) * 512], pst[:, :],
                    sb[xr][:, half * 512:(half + 1) * 512]),
                    reads=[psn, rs[xr]], writes=[rs[acc]])
            em.do("sp", lambda h, b=b, tt=tt, acc=acc: h.dma_start(out=x3_out[b, tt * 128:(tt + 1) * 128, :],
                                                                   in_=sb[acc][:, :]),
                  reads=[rs[acc]], writes=[rs["x3_o"]], dma=True)



L0FFN_RS2 = (["rwT", "ident", "onesf", "triu", "iota512", "iotae", "colsumB", "konst",
              "runrow", "exc", "excm", "rankoh", "baseown",
              "i8u", "i8f", "G", "S", "slot", "inv", "s8", "w8", "w8x2", "s8u", "scr",
              "xlnT", "wsgu", "wsd", "hsh", "xr", "acc", "oidx", "x1_d", "m80", "m81",
              "gbf0", "gbf1",
              "ps_lg", "ps_lg2", "ps_run", "ps_runP", "ps_S",
              "xb_d", "w_d", "y_d", "s8u_d", "rbase_d", "x2_o", "hE", "wcol"] +
             [f"{k}{p}" for p in range(2) for k in
              ("xt", "xbf", "xT", "st", "lgr", "rep", "Mbf", "scr2",
               "ps_a", "ps_b", "ps_c", "ps_d", "wg", "wu", "wd", "xbT", "gbf", "ybf")])


def alloc_l0ffn_core(nc, es):
    sb, ps = {}, {}
    sb["rwT"] = es.enter_context(nc.sbuf_tensor("rwT_sb", [128, 8, E64], BF16))
    sb["ident"] = es.enter_context(nc.sbuf_tensor("f_ident", [128, 128], BF16))
    sb["onesf"] = es.enter_context(nc.sbuf_tensor("f_onesf", [128, 128], BF16))
    sb["triu"] = es.enter_context(nc.sbuf_tensor("f_triu", [128, 128], BF16))
    sb["iota512"] = es.enter_context(nc.sbuf_tensor("f_iota512", [128, E64], F32))
    sb["iotae"] = es.enter_context(nc.sbuf_tensor("f_iotae", [128, E64], F32))
    sb["colsumB"] = es.enter_context(nc.sbuf_tensor("f_colsumB", [128, E64], F32))
    sb["konst"] = es.enter_context(nc.sbuf_tensor("f_konst", [128, 4], F32))
    sb["runrow"] = es.enter_context(nc.sbuf_tensor("f_runrow", [1, E64], BF16))
    sb["exc"] = es.enter_context(nc.sbuf_tensor("f_exc", [1, 8, E64], F32))
    sb["excm"] = es.enter_context(nc.sbuf_tensor("f_excm", [1, 8, E64], F32))
    sb["rankoh"] = es.enter_context(nc.sbuf_tensor("f_rankoh", [1, 512], F32))
    sb["baseown"] = es.enter_context(nc.sbuf_tensor("f_baseown", [1, E64], BF16))
    sb["i8u"] = es.enter_context(nc.sbuf_tensor("f_i8u", [128, 8], U32))
    sb["i8f"] = es.enter_context(nc.sbuf_tensor("f_i8f", [128, 8], F32))
    sb["G"] = es.enter_context(nc.sbuf_tensor("f_G", [128, E64], F32))
    sb["S"] = es.enter_context(nc.sbuf_tensor("f_S", [128, E64], F32))
    sb["slot"] = es.enter_context(nc.sbuf_tensor("f_slot", [128, E64], F32))
    sb["inv"] = es.enter_context(nc.sbuf_tensor("f_inv", [128, E64], F32))
    sb["s8"] = es.enter_context(nc.sbuf_tensor("f_s8", [128, 8], F32))
    sb["w8"] = es.enter_context(nc.sbuf_tensor("f_w8", [128, 8], F32))
    sb["w8x2"] = es.enter_context(nc.sbuf_tensor("f_w8x2", [128, 8, 2], F32))
    sb["s8u"] = es.enter_context(nc.sbuf_tensor("f_s8u", [128, 8], U32))
    sb["scr"] = es.enter_context(nc.sbuf_tensor("f_scr", [128, E64], F32))
    sb["oidx"] = es.enter_context(nc.sbuf_tensor("f_oidx", [128, 1], U32))
    sb["xr"] = es.enter_context(nc.sbuf_tensor("f_xr", [128, 1024], F32))
    sb["acc"] = es.enter_context(nc.sbuf_tensor("f_acc", [128, 1024], F32))
    sb["wcol"] = es.enter_context(nc.sbuf_tensor("f_wcol", [128, 8], F32))
    sb["gbf0"] = es.enter_context(nc.sbuf_tensor("f_gbf0", [128, 512], BF16))
    sb["gbf1"] = es.enter_context(nc.sbuf_tensor("f_gbf1", [128, 512], BF16))
    for p in range(2):
        sb[f"xt{p}"] = es.enter_context(nc.sbuf_tensor(f"f_xt{p}", [128, 1024], F32))
        sb[f"xbf{p}"] = es.enter_context(nc.sbuf_tensor(f"f_xbf{p}", [128, 1024], BF16))
        sb[f"xT{p}"] = es.enter_context(nc.sbuf_tensor(f"f_xT{p}", [128, 8, 128], BF16))
        sb[f"st{p}"] = es.enter_context(nc.sbuf_tensor(f"f_st{p}", [128, 8], F32))
        sb[f"lgr{p}"] = es.enter_context(nc.sbuf_tensor(f"f_lgr{p}", [128, E64], F32))
        sb[f"rep{p}"] = es.enter_context(nc.sbuf_tensor(f"f_rep{p}", [128, E64], F32))
        sb[f"Mbf{p}"] = es.enter_context(nc.sbuf_tensor(f"f_Mbf{p}", [128, E64], BF16))
        sb[f"scr2{p}"] = es.enter_context(nc.sbuf_tensor(f"f_scr2{p}", [128, 1024], F32))
        sb[f"m8{p}"] = es.enter_context(nc.sbuf_tensor(f"f_m8{p}", [128, 8], F32))
        ps[f"a{p}"] = es.enter_context(nc.psum_tensor(f"f_ps_a{p}", [128, 512], F32))
    return sb, ps


def emit_l0ffn_R2(nc, em, sb, ps, rs, x1_full, rankoh_ap, rwT_ap, hc):
    em.do("sp", lambda h: h.dma_start(out=sb["onesf"][:], in_=hc["ones_full"].ap()), writes=[rs["onesf"]], dma=True)
    em.do("sp", lambda h: h.dma_start(out=sb["triu"][:], in_=hc["triu"].ap()), writes=[rs["triu"]], dma=True)
    em.do("sp", lambda h: h.dma_start(out=sb["ident"][:], in_=hc["ident"].ap()), writes=[rs["ident"]], dma=True)
    em.do("sp", lambda h: h.dma_start(out=sb["konst"][:], in_=hc["konst"].ap()), writes=[rs["konst"]], dma=True)
    em.do("sp", lambda h: h.dma_start(out=sb["iota512"][:], in_=hc["iota512"].ap()), writes=[rs["iota512"]], dma=True)
    em.do("sp", lambda h: h.dma_start(out=sb["iotae"][:], in_=hc["iota_e"].ap()), writes=[rs["iotae"]], dma=True)
    em.do("sp", lambda h: h.dma_start(out=sb["rankoh"][0:1, :], in_=rankoh_ap[0:1, :]), writes=[rs["rankoh"]], dma=True)
    em.do("pool", lambda h: h.dma_start(out=sb["rwT"][:, :, :], in_=rwT_ap.rearrange("(k p) n -> p k n", p=128)),
          writes=[rs["rwT"]], dma=True)
    for dk in range(8):
        em.do("pe", lambda h, dk=dk: h.matmul(ps["lg"][:, :], sb["onesf"][:, :], sb["rwT"][:, dk, :],
                                              start=(dk == 0), stop=(dk == 7)),
              reads=[rs["onesf"], rs["rwT"]], writes=[rs["ps_lg"]])
    em.do("dve", lambda h: h.tensor_copy(sb["colsumB"][:], ps["lg"][:, :]), reads=[rs["ps_lg"]], writes=[rs["colsumB"]])

    for ti in range(NT_GLOB):
        g, gj = ti // 32, ti % 32
        c = ti % 2
        xt, xbf, xT, st = f"xt{c}", f"xbf{c}", f"xT{c}", f"st{c}"
        lgr, rep, Mbf = f"lgr{c}", f"rep{c}", f"Mbf{c}"
        lgp, lgpr = (ps["lg"], rs["ps_lg"]) if c == 0 else (ps["lg2"], rs["ps_lg2"])
        em.do("sp", lambda h, ti=ti, xt=xt: h.dma_start(out=sb[xt][:, :], in_=x1_full[ti * 128:(ti + 1) * 128, :]),
              writes=[rs[xt]], dma=True)
        em.do("act", lambda h, xt=xt, xbf=xbf, st=st: h.activation(out=sb[xbf][:, :], in_=sb[xt][:, :], func=AF.Copy,
                                                                   accum_out=sb[st][:, 0:1]),
              reads=[rs[xt]], writes=[rs[xbf], rs[st]])
        em.do("dve", lambda h, st=st: h.tensor_scalar(sb[st][:, 1:2], sb[st][:, 0:1], -1.0 / D, None, ALU.mult),
              reads=[rs[st]], writes=[rs[st]])
        for ki in range(8):
            tc = ki % 2
            tb, tbr = ps[f"a{tc}"], rs[f"ps_a{tc}"]
            em.do("pe", lambda h, ki=ki, xbf=xbf, tb=tb: h.matmul(
                tb[0:128, 0:128], sb[xbf][:, ki * 128:(ki + 1) * 128], sb["ident"][:, :],
                start=True, stop=True),
                reads=[rs[xbf], rs["ident"]], writes=[tbr])
            eng = "dve" if ki % 2 == 0 else "act"
            fn = (lambda h, ki=ki, xT=xT, tb=tb: h.tensor_copy(sb[xT][:, ki, :], tb[0:128, 0:128])) \
                if eng == "dve" else \
                 (lambda h, ki=ki, xT=xT, tb=tb: h.copy(out=sb[xT][:, ki, :], in_=tb[0:128, 0:128]))
            em.do(eng, fn, reads=[tbr], writes=[rs[xT]])
        for ki in range(8):
            em.do("pe", lambda h, ki=ki, xT=xT, lgp=lgp: h.matmul(lgp[:, :], sb[xT][:, ki, :], sb["rwT"][:, ki, :],
                                                                  start=(ki == 0), stop=(ki == 7),
                                                                  skip_group_check=True),
                  reads=[rs[xT], rs["rwT"]], writes=[lgpr])
        em.do("dve", lambda h, st=st, lgr=lgr, lgp=lgp: h.scalar_tensor_tensor(
            out=sb[lgr][:, :], in0=sb["colsumB"][:, :], scalar=sb[st][:, 1:2], in1=lgp[:, :],
            op0=ALU.mult, op1=ALU.add),
            reads=[rs["colsumB"], rs[st], lgpr], writes=[rs[lgr]])
        m8 = f"m8{c}"
        em.do("dve", lambda h, lgr=lgr, m8=m8: h.max(out=sb[m8][:, :], in_=sb[lgr][:, :]),
              reads=[rs[lgr]], writes=[rs[m8]])
        em.do("dve", lambda h, lgr=lgr, rep=rep, m8=m8: h.match_replace(out=sb[rep][:, :], in_to_replace=sb[m8][:, :],
                                                                        in_values=sb[lgr][:, :], imm_value=-1e30),
              reads=[rs[lgr], rs[m8]], writes=[rs[rep]])
        em.do("dve", lambda h, lgr=lgr, rep=rep, Mbf=Mbf: h.tensor_tensor(sb[Mbf][:, :], sb[lgr][:, :], sb[rep][:, :], ALU.not_equal),
              reads=[rs[lgr], rs[rep]], writes=[rs[Mbf]])
        em.do("pe", lambda h, ti=ti, Mbf=Mbf: h.matmul(ps["run"][0:1, :], sb["onesf"][:, 0:1], sb[Mbf][:, :],
                                                       start=(ti == 0), stop=(ti == NT_GLOB - 1), skip_group_check=True),
              reads=[rs["onesf"], rs[Mbf]], writes=[rs["ps_run"]])
        if gj == 31:
            em.do("act", lambda h, g=g: h.copy(out=sb["exc"][0:1, g, :], in_=ps["run"][0:1, :]),
                  reads=[rs["ps_run"]], writes=[rs["exc"]])
    em.do("dve", lambda h: h.tensor_mul(sb["excm"][0:1, :, :], sb["exc"][0:1, :, :],
                                        sb["rankoh"][0:1, :].rearrange("a (g e) -> a g e", e=E64)),
          reads=[rs["rankoh"], rs["exc"]], writes=[rs["excm"]])
    em.do("dve", lambda h: h.tensor_reduce(sb["scr"][0:1, 0:E64],
                                           sb["excm"][0:1, :, :].transpose([0, 2, 1]), AX.X, ALU.add),
          reads=[rs["excm"]], writes=[rs["scr"]])
    em.do("dve", lambda h: h.tensor_copy(sb["baseown"][0:1, :], sb["scr"][0:1, 0:E64]),
          reads=[rs["scr"]], writes=[rs["baseown"]])


def emit_l0ffn_P2(nc, em, sb, ps, rs, rwT_ap, xb_d, w_d, s8u_d, hc, ownrows, x1_dram):
    for j in range(NT_OWN):
        c = j % 2
        xt, xbf, st = f"xt{c}", f"xbf{c}", f"st{c}"
        lgr, rep, m8 = f"lgr{c}", f"rep{c}", f"m8{c}"
        lgp, lgpr = (ps["lg"], rs["ps_lg"]) if c == 0 else (ps["lg2"], rs["ps_lg2"])
        em.do("sp", lambda h, j=j: h.dma_start(out=sb["oidx"][:, :], in_=ownrows[j, :, :]),
              writes=[rs["oidx"]], dma=True)
        em.do("pool", lambda h, j=j, xt=xt: h.indirect_dma_start(
            sb[xt][:, :], None, x1_dram[0:128, :], IndirectOffsetOnAxis(sb["oidx"][:, :], 0),
            bounds_check=bcreg(em.trk, h, NB * T - 1), oob_is_err=False),
            reads=[rs["oidx"], rs["x1_d"]], writes=[rs[xt]], dma=True)
        em.do("act", lambda h, xt=xt, st=st, c=c: h.activation(out=sb[f"scr2{c}"][:, :], in_=sb[xt][:, :],
                                                               func=AF.Square, accum_out=sb[st][:, 0:1]),
              reads=[rs[xt]], writes=[rs[f"scr2{c}"], rs[st]])
        em.do("dve", lambda h, xt=xt, st=st: h.reduce_sum(sb[st][:, 1:2], sb[xt][:, :], AX.X, negate=True),
              reads=[rs[xt]], writes=[rs[st]])
        em.do("dve", lambda h, st=st: h.tensor_scalar(sb[st][:, 2:3], sb[st][:, 1:2], 1.0 / D, None, ALU.mult),
              reads=[rs[st]], writes=[rs[st]])
        em.do("dve", lambda h, st=st: h.tensor_scalar(sb[st][:, 3:4], sb[st][:, 0:1], 1.0 / D, None, ALU.mult),
              reads=[rs[st]], writes=[rs[st]])
        em.do("dve", lambda h, st=st: h.scalar_tensor_tensor(out=sb[st][:, 4:5], in0=sb[st][:, 2:3],
                                                             scalar=sb[st][:, 2:3], in1=sb[st][:, 3:4],
                                                             op0=ALU.mult, op1=ALU.subtract),
              reads=[rs[st]], writes=[rs[st]])
        em.do("act", lambda h, st=st: h.activation(out=sb[st][:, 5:6], in_=sb[st][:, 4:5], func=AF.Sqrt,
                                                   bias=sb["konst"][:, 0:1], scale=sb["konst"][:, 1:2]),
              reads=[rs[st], rs["konst"]], writes=[rs[st]])
        em.do("dve", lambda h, st=st: h.reciprocal(sb[st][:, 6:7], sb[st][:, 5:6]),
              reads=[rs[st]], writes=[rs[st]])
        em.do("dve", lambda h, xt=xt, xbf=xbf, st=st: h.tensor_scalar(
            sb[xbf][:, :], sb[xt][:, :], sb[st][:, 2:3], sb[st][:, 6:7], ALU.add, ALU.mult),
            reads=[rs[xt], rs[st]], writes=[rs[xbf]])
        for ki in range(8):
            tc = ki % 2
            tb, tbr = ps[f"a{tc}"], rs[f"ps_a{tc}"]
            em.do("pe", lambda h, ki=ki, xbf=xbf, tb=tb: h.matmul(
                tb[0:128, 0:128], sb[xbf][:, ki * 128:(ki + 1) * 128], sb["ident"][:, :],
                start=True, stop=True),
                reads=[rs[xbf], rs["ident"]], writes=[tbr])
            eng = "dve" if ki % 2 == 0 else "act"
            fn = (lambda h, ki=ki, j=j, tb=tb: h.tensor_copy(sb["xlnT"][:, ki, j * 128:(j + 1) * 128], tb[0:128, 0:128])) \
                if eng == "dve" else \
                 (lambda h, ki=ki, j=j, tb=tb: h.copy(out=sb["xlnT"][:, ki, j * 128:(j + 1) * 128], in_=tb[0:128, 0:128]))
            em.do(eng, fn, reads=[tbr], writes=[rs["xlnT"]])
        for ki in range(8):
            em.do("pe", lambda h, ki=ki, j=j, lgp=lgp: h.matmul(
                lgp[:, :], sb["xlnT"][:, ki, j * 128:(j + 1) * 128], sb["rwT"][:, ki, :],
                start=(ki == 0), stop=(ki == 7), skip_group_check=True),
                reads=[rs["xlnT"], rs["rwT"]], writes=[lgpr])
        em.do("dve", lambda h, lgr=lgr, lgp=lgp: h.tensor_copy(sb[lgr][:, :], lgp[:, :]),
              reads=[lgpr], writes=[rs[lgr]])
        em.do("dve", lambda h, lgr=lgr, m8=m8: h.max(out=sb[m8][:, :], in_=sb[lgr][:, :]),
              reads=[rs[lgr]], writes=[rs[m8]])
        em.do("dve", lambda h, lgr=lgr, m8=m8: h.max_index(out=sb["i8u"][:, :], in_max=sb[m8][:, :], in_values=sb[lgr][:, :]),
              reads=[rs[lgr], rs[m8]], writes=[rs["i8u"]])
        em.do("dve", lambda h: h.tensor_copy(sb["i8f"][:, :], sb["i8u"][:, :]), reads=[rs["i8u"]], writes=[rs["i8f"]])
        em.do("dve", lambda h, lgr=lgr, rep=rep, m8=m8: h.match_replace(out=sb[rep][:, :], in_to_replace=sb[m8][:, :],
                                                                        in_values=sb[lgr][:, :], imm_value=-1e30),
              reads=[rs[lgr], rs[m8]], writes=[rs[rep]])
        em.do("dve", lambda h, lgr=lgr, rep=rep: h.tensor_tensor(sb["scr"][:, :], sb[lgr][:, :], sb[rep][:, :], ALU.not_equal),
              reads=[rs[lgr], rs[rep]], writes=[rs["scr"]])
        em.do("dve", lambda h, c=c: h.tensor_copy(sb[f"Mbf{c}"][:, :], sb["scr"][:, :]),
              reads=[rs["scr"]], writes=[rs[f"Mbf{c}"]])
        em.do("dve", lambda h, st=st, m8=m8: h.tensor_scalar(sb[st][:, 7:8], sb[m8][:, 0:1], -1.0, None, ALU.mult),
              reads=[rs[m8]], writes=[rs[st]])
        em.do("act", lambda h, lgr=lgr, rep=rep, st=st: h.activation(out=sb[rep][:, :], in_=sb[lgr][:, :], func=AF.Exp,
                                                                     bias=sb[st][:, 7:8]),
              reads=[rs[lgr], rs[st]], writes=[rs[rep]])
        em.do("dve", lambda h, rep=rep: h.tensor_mul(sb["G"][:, :], sb["scr"][:, :], sb[rep][:, :]),
              reads=[rs["scr"], rs[rep]], writes=[rs["G"]])
        em.do("dve", lambda h, st=st: h.tensor_reduce(sb[st][:, 7:8], sb["G"][:, :], AX.X, ALU.add),
              reads=[rs["G"]], writes=[rs[st]])
        em.do("dve", lambda h, st=st: h.reciprocal(sb[st][:, 7:8], sb[st][:, 7:8]),
              reads=[rs[st]], writes=[rs[st]])
        em.do("dve", lambda h, st=st: h.tensor_scalar(sb["G"][:, :], sb["G"][:, :], sb[st][:, 7:8], None, ALU.mult),
              reads=[rs[st]], writes=[rs["G"]])
        if j == 0:
            em.do("dve", lambda h: h.memset(sb["runrow"][0:1, :], 0.0), writes=[rs["runrow"]])
        else:
            em.do("act", lambda h: h.copy(out=sb["runrow"][0:1, :], in_=ps["runP"][0:1, :]),
                  reads=[rs["ps_runP"]], writes=[rs["runrow"]])
        em.do("pe", lambda h, c=c: h.matmul(ps["S"][:, :], sb["triu"][:, :], sb[f"Mbf{c}"][:, :], start=True, stop=False),
              reads=[rs["triu"], rs[f"Mbf{c}"]], writes=[rs["ps_S"]])
        em.do("pe", lambda h: h.matmul(ps["S"][:, :], sb["onesf"][0:1, :], sb["runrow"][0:1, :],
                                       start=False, stop=False),
              reads=[rs["onesf"], rs["runrow"]], writes=[rs["ps_S"]])
        em.do("pe", lambda h: h.matmul(ps["S"][:, :], sb["onesf"][0:1, :], sb["baseown"][0:1, :],
                                       start=False, stop=True),
              reads=[rs["onesf"], rs["baseown"]], writes=[rs["ps_S"]])
        em.do("pe", lambda h, j=j, c=c: h.matmul(ps["runP"][0:1, :], sb["onesf"][:, 0:1], sb[f"Mbf{c}"][:, :],
                                                 start=(j == 0), stop=(j == NT_OWN - 1), skip_group_check=True),
              reads=[rs["onesf"], rs[f"Mbf{c}"]], writes=[rs["ps_runP"]])
        em.do("dve", lambda h: h.tensor_copy(sb["S"][:, :], ps["S"][:, :]), reads=[rs["ps_S"]], writes=[rs["S"]])
        em.do("dve", lambda h: h.tensor_scalar(sb["inv"][:, :], sb["S"][:, :], float(CAP), None, ALU.is_ge),
              reads=[rs["S"]], writes=[rs["inv"]])
        em.do("dve", lambda h: h.tensor_add(sb["slot"][:, :], sb["S"][:, :], sb["iota512"][:, :]),
              reads=[rs["S"], rs["iota512"]], writes=[rs["slot"]])
        em.do("dve", lambda h: h.scalar_tensor_tensor(out=sb["slot"][:, :], in0=sb["inv"][:, :],
                                                      scalar=HUGE_SLOT, in1=sb["slot"][:, :],
                                                      op0=ALU.mult, op1=ALU.add),
              reads=[rs["inv"]], writes=[rs["slot"]])
        for k in range(8):
            em.do("dve", lambda h, k=k: h.scalar_tensor_tensor(out=sb["scr"][:, :], in0=sb["iotae"][:, :],
                                                               scalar=sb["i8f"][:, k:k + 1], in1=sb["slot"][:, :],
                                                               op0=ALU.is_equal, op1=ALU.mult,
                                                               accum_out=sb["s8"][:, k:k + 1]),
                  reads=[rs["iotae"], rs["i8f"], rs["slot"]], writes=[rs["scr"], rs["s8"]])
            em.do("dve", lambda h, k=k: h.scalar_tensor_tensor(out=sb["scr"][:, :], in0=sb["iotae"][:, :],
                                                               scalar=sb["i8f"][:, k:k + 1], in1=sb["G"][:, :],
                                                               op0=ALU.is_equal, op1=ALU.mult,
                                                               accum_out=sb["w8"][:, k:k + 1]),
                  reads=[rs["iotae"], rs["i8f"], rs["G"]], writes=[rs["scr"], rs["w8"]])
        em.do("dve", lambda h: h.tensor_copy(sb["s8u"][:, :], sb["s8"][:, :]), reads=[rs["s8"]], writes=[rs["s8u"]])
        em.do("dve", lambda h: h.tensor_copy(sb["w8x2"][:, :, 0], sb["w8"][:, :]), reads=[rs["w8"]], writes=[rs["w8x2"]])
        em.do("dve", lambda h: h.tensor_copy(sb["w8x2"][:, :, 1], sb["w8"][:, :]), reads=[rs["w8"]], writes=[rs["w8x2"]])
        em.do("sp", lambda h, j=j: h.dma_start(out=s8u_d[j, :, :], in_=sb["s8u"][:, :]),
              reads=[rs["s8u"]], writes=[rs["s8u_d"]], dma=True)
        for k in range(8):
            em.do("pool", lambda h, k=k, xbf=xbf: h.indirect_dma_start(
                xb_d[0:128, :], IndirectOffsetOnAxis(sb["s8u"][:, k:k + 1], 0),
                sb[xbf][:, :], None, bounds_check=bcreg(em.trk, h, E64 * CAP - 1), oob_is_err=False),
                reads=[rs["s8u"], rs[xbf]], writes=[rs["xb_d"]], dma=True)
            em.do("pool", lambda h, k=k: h.indirect_dma_start(
                w_d[0:128, :], IndirectOffsetOnAxis(sb["s8u"][:, k:k + 1], 0),
                sb["w8x2"][:, k, :], None, bounds_check=bcreg(em.trk, h, E64 * CAP - 1), oob_is_err=False),
                reads=[rs["s8u"], rs["w8x2"]], writes=[rs["w_d"]], dma=True)


def emit_l0ffn_S2(nc, em, sb, ps, rs, wsgu_ap, wsd_ap, rbase_d, ownrows, x1_dram):
    em.do("pool", lambda h: h.dma_start(out=sb["wsgu"][:, :, :], in_=wsgu_ap.rearrange("(k p) n -> p k n", p=128)),
          writes=[rs["wsgu"]], dma=True)
    em.do("pool", lambda h: h.dma_start(out=sb["wsd"][:, :, :], in_=wsd_ap.rearrange("(k p) n -> p k n", p=128)),
          writes=[rs["wsd"]], dma=True)
    SF = 768
    for sp_ in range(8):
        t0 = sp_ * 512
        for fc in range(6):
            pg, pgr = ps[f"a{0}"], rs["ps_a0"]
            pu, pur = ps[f"a{1}"], rs["ps_a1"]
            for ki in range(8):
                em.do("pe", lambda h, fc=fc, ki=ki, t0=t0, pg=pg: h.matmul(
                    pg[:, :], sb["wsgu"][:, ki, fc * 128:(fc + 1) * 128],
                    sb["xlnT"][:, ki, t0:t0 + 512], start=(ki == 0), stop=(ki == 7)),
                    reads=[rs["wsgu"], rs["xlnT"]], writes=[pgr])
                em.do("pe", lambda h, fc=fc, ki=ki, t0=t0, pu=pu: h.matmul(
                    pu[:, :], sb["wsgu"][:, ki, SF + fc * 128:SF + (fc + 1) * 128],
                    sb["xlnT"][:, ki, t0:t0 + 512], start=(ki == 0), stop=(ki == 7)),
                    reads=[rs["wsgu"], rs["xlnT"]], writes=[pur])
            em.do("act", lambda h, fc=fc, pg=pg: h.activation(out=sb[f"gbf{fc % 2}"][:, :], in_=pg[:, :], func=AF.Silu),
                  reads=[pgr], writes=[rs[f"gbf{fc % 2}"]])
            em.do("dve", lambda h, fc=fc, pu=pu: h.tensor_mul(sb["hsh"][:, fc, :], sb[f"gbf{fc % 2}"][:, :], pu[:, :]),
                  reads=[rs[f"gbf{fc % 2}"], pur], writes=[rs["hsh"]])
        for tt in range(4):
            tg = t0 + tt * 128
            if ownrows is None:
                pass
            else:
                em.do("sp", lambda h, tg=tg: h.dma_start(out=sb["oidx"][:, :], in_=ownrows[tg // 128, :, :]),
                      writes=[rs["oidx"]], dma=True)
                em.do("pool", lambda h: h.indirect_dma_start(
                    sb["xr"][:, :], None, x1_dram[0:128, :], IndirectOffsetOnAxis(sb["oidx"][:, :], 0),
                    bounds_check=bcreg(em.trk, h, NB * T - 1), oob_is_err=False),
                    reads=[rs["oidx"], rs["x1_d"]], writes=[rs["xr"]], dma=True)
            for half in range(2):
                pt, ptr = ps[f"a{half}"], rs[f"ps_a{half}"]
                for fc in range(6):
                    em.do("pe", lambda h, fc=fc, tt=tt, half=half, pt=pt: h.matmul(
                        pt[:, :], sb["hsh"][:, fc, tt * 128:(tt + 1) * 128],
                        sb["wsd"][:, fc, half * 512:(half + 1) * 512], start=(fc == 0), stop=(fc == 5)),
                        reads=[rs["hsh"], rs["wsd"]], writes=[ptr])
                em.do("dve", lambda h, half=half, pt=pt: h.tensor_add(
                    sb["acc"][:, half * 512:(half + 1) * 512], pt[:, :],
                    sb["xr"][:, half * 512:(half + 1) * 512]),
                    reads=[ptr, rs["xr"]], writes=[rs["acc"]])
            em.do("sp", lambda h, tg=tg: h.dma_start(out=rbase_d[tg:tg + 128, :], in_=sb["acc"][:, :]),
                  reads=[rs["acc"]], writes=[rs["rbase_d"]], dma=True)


def emit_l0ffn_E2(nc, em, sb, ps, rs, wg_ap, wu_ap, wd_ap, xb_d, w_d, y_d):
    for e in range(E64):
        we = e % 2
        wg, wu, wd = f"wg{we}", f"wu{we}", f"wd{we}"
        xbT = f"xbT{we}"
        em.do("pool", lambda h, e=e, wg=wg: h.dma_start(out=sb[wg][:, :, :],
                                                        in_=wg_ap[e].rearrange("(k p) n -> p k n", p=128)),
              writes=[rs[wg]], dma=True)
        em.do("pool", lambda h, e=e, wu=wu: h.dma_start(out=sb[wu][:, :, :],
                                                        in_=wu_ap[e].rearrange("(k p) n -> p k n", p=128)),
              writes=[rs[wu]], dma=True)
        em.do("pool", lambda h, e=e, wd=wd: h.dma_start(out=sb[wd][:, :, :],
                                                        in_=wd_ap[e].rearrange("(k p) n -> p k n", p=128)),
              writes=[rs[wd]], dma=True)
        for dk in range(8):
            em.do("sp", lambda h, e=e, dk=dk, xbT=xbT: h.dma_start(
                out=sb[xbT][:, dk, :], in_=xb_d[e * CAP:(e + 1) * CAP, dk * 128:(dk + 1) * 128],
                transpose=True),
                reads=[rs["xb_d"]], writes=[rs[xbT]], dma=True)
        for st_ in range(4):
            em.do("sp", lambda h, e=e, st_=st_: h.dma_start(
                out=sb["wcol"][:, 2 * st_:2 * st_ + 2],
                in_=w_d[e * CAP + st_ * 128:e * CAP + (st_ + 1) * 128, 0:2]),
                reads=[rs["w_d"]], writes=[rs["wcol"]], dma=True)
        for fc in range(6):
            pc = fc % 2
            pg, pgr = ps[f"c{pc}"], rs[f"ps_c{pc}"]
            pu, pur = ps[f"d{pc}"], rs[f"ps_d{pc}"]
            gbf = f"gbf{pc}"
            for ki in range(8):
                em.do("pe", lambda h, fc=fc, ki=ki, pg=pg, wg=wg, xbT=xbT: h.matmul(
                    pg[:, :], sb[wg][:, ki, fc * 128:(fc + 1) * 128],
                    sb[xbT][:, ki, :], start=(ki == 0), stop=(ki == 7)),
                    reads=[rs[wg], rs[xbT]], writes=[pgr])
                em.do("pe", lambda h, fc=fc, ki=ki, pu=pu, wu=wu, xbT=xbT: h.matmul(
                    pu[:, :], sb[wu][:, ki, fc * 128:(fc + 1) * 128],
                    sb[xbT][:, ki, :], start=(ki == 0), stop=(ki == 7)),
                    reads=[rs[wu], rs[xbT]], writes=[pur])
            em.do("act", lambda h, pg=pg, gbf=gbf: h.activation(out=sb[gbf][:, :], in_=pg[:, :], func=AF.Silu),
                  reads=[pgr], writes=[rs[gbf]])
            em.do("dve", lambda h, fc=fc, pu=pu, gbf=gbf: h.tensor_mul(sb["hE"][:, fc, :], sb[gbf][:, :], pu[:, :]),
                  reads=[rs[gbf], pur], writes=[rs["hE"]])
        for st_ in range(4):
            yc = st_ % 2
            ybf = f"ybf{yc}"
            for half in range(2):
                pt, ptr = ps[f"a{half}"], rs[f"ps_a{half}"]
                for fc in range(6):
                    em.do("pe", lambda h, fc=fc, st_=st_, half=half, pt=pt, wd=wd: h.matmul(
                        pt[:, :], sb["hE"][:, fc, st_ * 128:(st_ + 1) * 128],
                        sb[wd][:, fc, half * 512:(half + 1) * 512], start=(fc == 0), stop=(fc == 5)),
                        reads=[rs["hE"], rs[wd]], writes=[ptr])
                em.do("dve", lambda h, half=half, st_=st_, pt=pt, ybf=ybf: h.tensor_scalar(
                    sb[ybf][:, half * 512:(half + 1) * 512], pt[:, :],
                    sb["wcol"][:, 2 * st_:2 * st_ + 1], None, ALU.mult),
                    reads=[ptr, rs["wcol"]], writes=[rs[ybf]])
            em.do("pool", lambda h, e=e, st_=st_, ybf=ybf: h.dma_start(
                out=y_d[e * CAP + st_ * 128:e * CAP + (st_ + 1) * 128, :], in_=sb[ybf][:, :]),
                reads=[rs[ybf]], writes=[rs["y_d"]], dma=True)


def emit_l0ffn_G2(nc, em, sb, ps, rs, rbase_d, s8u_d, y_d, x2_out):
    for j in range(NT_OWN):
        c = j % 2
        xt = f"xt{c}"
        em.do("sp", lambda h, j=j, xt=xt: h.dma_start(out=sb[xt][:, :], in_=rbase_d[j * 128:(j + 1) * 128, :]),
              reads=[rs["rbase_d"]], writes=[rs[xt]], dma=True)
        em.do("sp", lambda h, j=j: h.dma_start(out=sb["s8u"][:, :], in_=s8u_d[j, :, :]),
              reads=[rs["s8u_d"]], writes=[rs["s8u"]], dma=True)
        for k in range(8):
            em.do("pool", lambda h, k=k, xt=xt: h.indirect_dma_start(
                sb[xt][:, :], None, y_d[0:128, :], IndirectOffsetOnAxis(sb["s8u"][:, k:k + 1], 0),
                bounds_check=bcreg(em.trk, h, E64 * CAP - 1), oob_is_err=False, compute_op=ALU.add),
                reads=[rs["s8u"], rs["y_d"]], writes=[rs[xt]], dma=True)
        em.do("sp", lambda h, j=j, xt=xt: h.dma_start(out=x2_out[j * 128:(j + 1) * 128, :], in_=sb[xt][:, :]),
              reads=[rs[xt]], writes=[rs["x2_o"]], dma=True)


def build_fullfused3_nc():
    from contextlib import ExitStack
    nc = bass.Bass("TRN2", target_bir_lowering=False)
    x_in = nc.dram_tensor("x_in", [NB, T, D], F32, kind="ExternalInput").ap()
    ownrows = nc.dram_tensor("ownrows", [NT_OWN, 128, 1], U32, kind="ExternalInput").ap()
    rankoh = nc.dram_tensor("rankoh", [1, 512], F32, kind="ExternalInput").ap()
    wq_a = {w: nc.dram_tensor(w, [D, D], F32, kind="ExternalInput").ap()
            for w in ("wq", "wk", "wv", "wo")}
    rwT = nc.dram_tensor("rwT", [D, E64], F32, kind="ExternalInput").ap()
    wsgu = nc.dram_tensor("wsgu", [D, 1536], F32, kind="ExternalInput").ap()
    wsd = nc.dram_tensor("wsd", [768, D], F32, kind="ExternalInput").ap()
    wg = nc.dram_tensor("wg", [E64, D, 768], F32, kind="ExternalInput").ap()
    wu = nc.dram_tensor("wu", [E64, D, 768], F32, kind="ExternalInput").ap()
    wd = nc.dram_tensor("wd", [E64, 768, D], F32, kind="ExternalInput").ap()
    wqkv = nc.dram_tensor("wqkv", [D, 3 * D], F32, kind="ExternalInput").ap()
    bqkv = nc.dram_tensor("bqkv", [3 * D], F32, kind="ExternalInput").ap()
    wout = nc.dram_tensor("wout", [D, D], F32, kind="ExternalInput").ap()
    bout = nc.dram_tensor("bout", [D], F32, kind="ExternalInput").ap()
    rw8 = nc.dram_tensor("rw8", [D, 8], F32, kind="ExternalInput").ap()
    w1T = nc.dram_tensor("w1T", [8, D, 2048], F32, kind="ExternalInput").ap()
    w2T = nc.dram_tensor("w2T", [8, 2048, D], F32, kind="ExternalInput").ap()
    out_ap = nc.dram_tensor("out", [4096, D], F32, kind="ExternalOutput").ap()

    x1_d = nc.dram_tensor("x1_d", [NB * T, D], F32, kind="Internal")
    xb_d = nc.dram_tensor("xb_d", [E64 * CAP, D], BF16, kind="Internal").ap()
    w_d = nc.dram_tensor("w_d", [E64 * CAP, 2], F32, kind="Internal").ap()
    y_d = nc.dram_tensor("y_d", [E64 * CAP, D], BF16, kind="Internal").ap()
    s8u_d = nc.dram_tensor("s8u_d", [NT_OWN, 128, 8], U32, kind="Internal").ap()
    rbase_d = nc.dram_tensor("rbase_d", [4096, D], F32, kind="Internal").ap()
    x2_d = nc.dram_tensor("x2_d", [4096, D], F32, kind="Internal")
    x3_d = nc.dram_tensor("x3_d", [4096, D], F32, kind="Internal")
    y_acc = nc.dram_tensor("y_acc", [4096, D], F32, kind="Internal").ap()

    h_ident, h_tri, h_konst = l0_attn_consts(nc)
    hc = l0ffn_consts(nc)
    es = ExitStack()
    with es:
        trk = Trk(nc, es)
        em = Emit(trk)
        with ExitStack() as es0:
            sb, ps = alloc_l0_attn2(nc, es0)
            rs = {k: Rsrc(k) for k in L0_RS2}
            x1v = x1_d.ap().rearrange("(b t) d -> b t d", b=NB)
            for b in range(NB):
                em.do("pool", lambda h, b=b: h.dma_start(out=x1v[b, 0:QR0, :], in_=x_in[b, 0:QR0, :]),
                      writes=[rs["out_0"]], dma=True)
            emit_l0_attn2(nc, em, sb, ps, rs, x_in, x1v, wq_a, NB, h_ident, h_tri, h_konst,
                          out_base=QR0)
            trk.barrier()
        with ExitStack() as es1:
            sb, ps = alloc_l0ffn_core(nc, es1)
            rs = {k: Rsrc(k) for k in L0FFN_RS2}
            with ExitStack() as esR:
                ps["lg"] = esR.enter_context(nc.psum_tensor("f_ps_lg", [128, E64], F32))
                ps["lg2"] = esR.enter_context(nc.psum_tensor("f_ps_lg2", [128, E64], F32))
                ps["run"] = esR.enter_context(nc.psum_tensor("f_ps_run", [1, E64], F32))
                emit_l0ffn_R2(nc, em, sb, ps, rs, x1_d.ap(), rankoh, rwT, hc)
                trk.barrier()
            with ExitStack() as esP:
                ps["lg"] = esP.enter_context(nc.psum_tensor("f_ps_lgp", [128, E64], F32))
                ps["lg2"] = esP.enter_context(nc.psum_tensor("f_ps_lg2p", [128, E64], F32))
                ps["S"] = esP.enter_context(nc.psum_tensor("f_ps_S", [128, E64], F32))
                ps["runP"] = esP.enter_context(nc.psum_tensor("f_ps_runP", [1, E64], F32))
                sb["xlnT"] = esP.enter_context(nc.sbuf_tensor("f_xlnT", [128, 8, 4096], BF16))
                sb["wsgu"] = esP.enter_context(nc.sbuf_tensor("f_wsgu", [128, 8, 1536], BF16))
                sb["wsd"] = esP.enter_context(nc.sbuf_tensor("f_wsd", [128, 6, 1024], BF16))
                sb["hsh"] = esP.enter_context(nc.sbuf_tensor("f_hsh", [128, 6, 512], BF16))
                rs["ps_lg"] = Rsrc("ps_lg")
                rs["ps_lg2"] = Rsrc("ps_lg2")
                emit_l0ffn_P2(nc, em, sb, ps, rs, rwT, xb_d, w_d, s8u_d, hc, ownrows, x1_d.ap())
                emit_l0ffn_S2(nc, em, sb, ps, rs, wsgu, wsd, rbase_d, ownrows, x1_d.ap())
                trk.barrier()
            with ExitStack() as esE:
                for p in range(2):
                    ps[f"c{p}"] = esE.enter_context(nc.psum_tensor(f"f_ps_c{p}", [128, 512], F32))
                    ps[f"d{p}"] = esE.enter_context(nc.psum_tensor(f"f_ps_d{p}", [128, 512], F32))
                    sb[f"wg{p}"] = esE.enter_context(nc.sbuf_tensor(f"f_wg{p}", [128, 8, 768], BF16))
                    sb[f"wu{p}"] = esE.enter_context(nc.sbuf_tensor(f"f_wu{p}", [128, 8, 768], BF16))
                    sb[f"wd{p}"] = esE.enter_context(nc.sbuf_tensor(f"f_wd{p}", [128, 6, 1024], BF16))
                    sb[f"xbT{p}"] = esE.enter_context(nc.sbuf_tensor(f"f_xbT{p}", [128, 8, 512], BF16))
                    sb[f"ybf{p}"] = esE.enter_context(nc.sbuf_tensor(f"f_ybf{p}", [128, 1024], BF16))
                sb["hE"] = esE.enter_context(nc.sbuf_tensor("f_hE", [128, 6, 512], BF16))
                emit_l0ffn_E2(nc, em, sb, ps, rs, wg, wu, wd, xb_d, w_d, y_d)
                emit_l0ffn_G2(nc, em, sb, ps, rs, rbase_d, s8u_d, y_d, x2_d.ap())
                trk.barrier()
        with ExitStack() as es2:
            sb, ps = alloc_l1attn2(nc, es2)
            rs = {k: Rsrc(k) for k in L1A_RS2}
            x2v = x2_d.ap().rearrange("(b t) d -> b t d", b=2)
            x3v = x3_d.ap().rearrange("(b t) d -> b t d", b=2)
            emit_l1_attn2(nc, em, sb, ps, rs, x2v, x3v, wqkv, bqkv, wout, bout, 2, hc)
            trk.barrier()
        with ExitStack() as es3:
            sb, ps = alloc_l1ffn2(nc, es3)
            rs = {k: Rsrc(k) for k in L1F_RS2}
            rs["w2sb1"] = rs["w2sb0"]
            rs["scr2ln0"] = rs["xr0"]
            rs["scr2ln1"] = rs["xr0"]
            rs["xr1"] = rs["xr0"]
            rs["acc1"] = rs["acc0"]
            rs["yev0"] = rs["acc0"]
            rs["yev1"] = rs["acc0"]
            emit_l1_ffn2(nc, em, sb, ps, rs, x3_d.ap(), out_ap, rw8, w1T, w2T, y_acc, hc)
            trk.barrier()
        block = es.enter_context(nc.Block())
        trk.replay(block)
    return nc


def build_l1attn2_nc(nb):
    from contextlib import ExitStack
    nc = bass.Bass("TRN2", target_bir_lowering=False)
    x_in = nc.dram_tensor("x_in", [nb, T, D], F32, kind="ExternalInput").ap()
    x3_out = nc.dram_tensor("x3_out", [nb, T, D], F32, kind="ExternalOutput").ap()
    wqkv = nc.dram_tensor("wqkv", [D, 3 * D], F32, kind="ExternalInput").ap()
    bqkv = nc.dram_tensor("bqkv", [3 * D], F32, kind="ExternalInput").ap()
    wout = nc.dram_tensor("wout", [D, D], F32, kind="ExternalInput").ap()
    bout = nc.dram_tensor("bout", [D], F32, kind="ExternalInput").ap()
    hc = l0ffn_consts(nc)
    es = ExitStack()
    with es:
        sb, ps = alloc_l1attn2(nc, es)
        trk = Trk(nc, es)
        em = Emit(trk)
        rs = {k: Rsrc(k) for k in L1A_RS2}
        block = es.enter_context(nc.Block())
        emit_l1_attn2(nc, em, sb, ps, rs, x_in, x3_out, wqkv, bqkv, wout, bout, nb, hc)
        trk.wait("sp", [rs["x3_o"].w])
        trk.replay(block)
    return nc


def build_l0ffn2_nc():
    from contextlib import ExitStack
    nc = bass.Bass("TRN2", target_bir_lowering=False)
    x1_full = nc.dram_tensor("x1_full", [NB * T, D], F32, kind="ExternalInput").ap()
    ownrows = nc.dram_tensor("ownrows", [NT_OWN, 128, 1], U32, kind="ExternalInput").ap()
    rankoh = nc.dram_tensor("rankoh", [1, 512], F32, kind="ExternalInput").ap()
    rwT = nc.dram_tensor("rwT", [D, E64], F32, kind="ExternalInput").ap()
    wsgu = nc.dram_tensor("wsgu", [D, 1536], F32, kind="ExternalInput").ap()
    wsd = nc.dram_tensor("wsd", [768, D], F32, kind="ExternalInput").ap()
    wg = nc.dram_tensor("wg", [E64, D, 768], F32, kind="ExternalInput").ap()
    wu = nc.dram_tensor("wu", [E64, D, 768], F32, kind="ExternalInput").ap()
    wd = nc.dram_tensor("wd", [E64, 768, D], F32, kind="ExternalInput").ap()
    x2_out = nc.dram_tensor("x2_out", [4096, D], F32, kind="ExternalOutput").ap()
    xb_d = nc.dram_tensor("xb_d", [E64 * CAP, D], BF16, kind="Internal").ap()
    w_d = nc.dram_tensor("w_d", [E64 * CAP, 2], F32, kind="Internal").ap()
    y_d = nc.dram_tensor("y_d", [E64 * CAP, D], BF16, kind="Internal").ap()
    s8u_d = nc.dram_tensor("s8u_d", [NT_OWN, 128, 8], U32, kind="Internal").ap()
    rbase_d = nc.dram_tensor("rbase_d", [4096, D], F32, kind="Internal").ap()
    hc = l0ffn_consts(nc)
    es = ExitStack()
    with es:
        trk = Trk(nc, es)
        em = Emit(trk)
        with ExitStack() as es1:
            sb, ps = alloc_l0ffn_core(nc, es1)
            rs = {k: Rsrc(k) for k in L0FFN_RS2}
            with ExitStack() as esR:
                ps["lg"] = esR.enter_context(nc.psum_tensor("f_ps_lg", [128, E64], F32))
                ps["lg2"] = esR.enter_context(nc.psum_tensor("f_ps_lg2", [128, E64], F32))
                ps["run"] = esR.enter_context(nc.psum_tensor("f_ps_run", [1, E64], F32))
                emit_l0ffn_R2(nc, em, sb, ps, rs, x1_full, rankoh, rwT, hc)
                trk.barrier()
            with ExitStack() as esP:
                ps["lg"] = esP.enter_context(nc.psum_tensor("f_ps_lgp", [128, E64], F32))
                ps["lg2"] = esP.enter_context(nc.psum_tensor("f_ps_lg2p", [128, E64], F32))
                ps["S"] = esP.enter_context(nc.psum_tensor("f_ps_S", [128, E64], F32))
                ps["runP"] = esP.enter_context(nc.psum_tensor("f_ps_runP", [1, E64], F32))
                sb["xlnT"] = esP.enter_context(nc.sbuf_tensor("f_xlnT", [128, 8, 4096], BF16))
                sb["wsgu"] = esP.enter_context(nc.sbuf_tensor("f_wsgu", [128, 8, 1536], BF16))
                sb["wsd"] = esP.enter_context(nc.sbuf_tensor("f_wsd", [128, 6, 1024], BF16))
                sb["hsh"] = esP.enter_context(nc.sbuf_tensor("f_hsh", [128, 6, 512], BF16))
                rs["ps_lg"] = Rsrc("ps_lg")
                rs["ps_lg2"] = Rsrc("ps_lg2")
                emit_l0ffn_P2(nc, em, sb, ps, rs, rwT, xb_d, w_d, s8u_d, hc, ownrows, x1_full)
                emit_l0ffn_S2(nc, em, sb, ps, rs, wsgu, wsd, rbase_d, ownrows, x1_full)
                trk.barrier()
            with ExitStack() as esE:
                for p in range(2):
                    ps[f"c{p}"] = esE.enter_context(nc.psum_tensor(f"f_ps_c{p}", [128, 512], F32))
                    ps[f"d{p}"] = esE.enter_context(nc.psum_tensor(f"f_ps_d{p}", [128, 512], F32))
                    sb[f"wg{p}"] = esE.enter_context(nc.sbuf_tensor(f"f_wg{p}", [128, 8, 768], BF16))
                    sb[f"wu{p}"] = esE.enter_context(nc.sbuf_tensor(f"f_wu{p}", [128, 8, 768], BF16))
                    sb[f"wd{p}"] = esE.enter_context(nc.sbuf_tensor(f"f_wd{p}", [128, 6, 1024], BF16))
                    sb[f"xbT{p}"] = esE.enter_context(nc.sbuf_tensor(f"f_xbT{p}", [128, 8, 512], BF16))
                    sb[f"ybf{p}"] = esE.enter_context(nc.sbuf_tensor(f"f_ybf{p}", [128, 1024], BF16))
                sb["hE"] = esE.enter_context(nc.sbuf_tensor("f_hE", [128, 6, 512], BF16))
                emit_l0ffn_E2(nc, em, sb, ps, rs, wg, wu, wd, xb_d, w_d, y_d)
                emit_l0ffn_G2(nc, em, sb, ps, rs, rbase_d, s8u_d, y_d, x2_out)
        trk.wait("sp", [rs["x2_o"].w])
        trk.replay(block := es.enter_context(nc.Block()))
    return nc



_NC_CACHE = {}


def _get_nc(name, builder):
    if name not in _NC_CACHE:
        _NC_CACHE[name] = builder()
    return _NC_CACHE[name]


def kernel(L0_attn_x, L0_attn_Wq, L0_attn_Wk, L0_attn_Wv, L0_attn_Wo,
           L0_ffn_router_w, L0_ffn_w_gate, L0_ffn_w_up, L0_ffn_w_down,
           L0_ffn_ws_gate_up, L0_ffn_ws_down,
           L1_attn_Wqkv, L1_attn_bqkv, L1_attn_Wout, L1_attn_bout,
           L1_ffn_router_w, L1_ffn_w1, L1_ffn_w2):
    x = np.ascontiguousarray(L0_attn_x.astype(np.float32, copy=False))
    nc = _get_nc("full3", build_fullfused3_nc)
    rwT = np.ascontiguousarray(L0_ffn_router_w.T)
    rw8 = np.ascontiguousarray(L1_ffn_router_w.T)
    w1T = np.ascontiguousarray(L1_ffn_w1.transpose(0, 2, 1))
    w2T = np.ascontiguousarray(L1_ffn_w2.transpose(0, 2, 1))
    in_maps = []
    for r in range(8):
        oh = np.zeros((8, E64), np.float32)
        if r > 0:
            oh[r - 1] = 1.0
        own = (np.arange(4096, dtype=np.uint32) + 4096 * r).reshape(NT_OWN, 128, 1)
        in_maps.append({
            "x_in": x, "ownrows": own,
            "rankoh": np.ascontiguousarray(oh.reshape(1, 512)),
            "wq": L0_attn_Wq, "wk": L0_attn_Wk, "wv": L0_attn_Wv, "wo": L0_attn_Wo,
            "rwT": rwT, "wsgu": L0_ffn_ws_gate_up, "wsd": L0_ffn_ws_down,
            "wg": L0_ffn_w_gate, "wu": L0_ffn_w_up, "wd": L0_ffn_w_down,
            "wqkv": L1_attn_Wqkv, "bqkv": L1_attn_bqkv,
            "wout": L1_attn_Wout, "bout": L1_attn_bout,
            "rw8": rw8, "w1T": w1T, "w2T": w2T,
        })
    res = run_bass_kernel_spmd(nc, in_maps, core_ids=list(range(8)))
    out = np.concatenate([res.results[r]["out"] for r in range(8)], axis=0)
    return np.ascontiguousarray(out.reshape(NB, T, D)).astype(np.float32, copy=False)
